# version 12
# baseline (speedup 1.0000x reference)
import numpy as np
import ml_dtypes
import concourse.bass as bass
import concourse.mybir as mybir
from concourse.bass_utils import run_bass_kernel_spmd

F32 = mybir.dt.float32
BF16 = mybir.dt.bfloat16
AF = mybir.ActivationFunctionType
ALU = mybir.AluOpType
BFNP = ml_dtypes.bfloat16

T = 2048
D = 1024
NCH = 8

COMPUTE_ENGINES = ("pe", "act", "dve", "pool", "sp")
DMA_QUEUES = {"qsp": "sp", "qpool": "pool"}
ALL_ENTITIES = list(COMPUTE_ENGINES) + list(DMA_QUEUES)


class _Op:
    __slots__ = ("idx", "entity", "issuer", "emit", "reads", "writes", "waits",
                 "marked", "ent_idx", "know", "qwait")

    def __init__(self, idx, entity, issuer, emit, reads, writes):
        self.idx = idx
        self.entity = entity
        self.issuer = issuer
        self.emit = emit
        self.reads = reads
        self.writes = writes
        self.waits = []
        self.marked = False
        self.ent_idx = -1
        self.know = None


class Builder:
    def __init__(self):
        self.ops = []
        self.last_write = {}
        self.readers = {}
        self.strict_ranges = []

    def in_strict(self, idx):
        return any(lo <= idx < hi for lo, hi in self.strict_ranges)

    def op(self, eng, emit, r=(), w=()):
        o = _Op(len(self.ops), eng, eng, emit, tuple(r), tuple(w))
        self.ops.append(o)
        return o

    def dma(self, queue, emit, r=(), w=()):
        o = _Op(len(self.ops), queue, DMA_QUEUES[queue], emit, tuple(r), tuple(w))
        self.ops.append(o)
        return o

    def finalize(self, nc):
        for o in self.ops:
            deps = []
            for t in o.reads:
                p = self.last_write.get(t)
                if p is not None:
                    deps.append(p)
            for t in o.writes:
                p = self.last_write.get(t)
                if p is not None:
                    deps.append(p)
                deps.extend(self.readers.get(t, ()))
            strict = self.in_strict(o.idx)
            seen = {}
            for p in deps:
                if (p.entity == o.entity and o.entity not in DMA_QUEUES
                        and not strict):
                    continue
                if p is o:
                    continue
                cur = seen.get(p.entity)
                if cur is None or p.idx > cur.idx:
                    seen[p.entity] = p
            o.waits = list(seen.values())
            for t in o.reads:
                self.readers.setdefault(t, []).append(o)
            for t in o.writes:
                self.last_write[t] = o
                self.readers[t] = []

        import os as _os
        if _os.environ.get("KSERIAL"):
            for i in range(1, len(self.ops)):
                o = self.ops[i]
                p = self.ops[i - 1]
                if all(q is not p for q in o.waits):
                    o.waits.append(p)

        def dma_key(o):
            return (o.entity, o.writes)

        key_rank = {}
        key_id = {}
        for o in self.ops:
            if o.entity in DMA_QUEUES:
                o.marked = True
                k = dma_key(o)
                if k not in key_id:
                    key_id[k] = len(key_id)
                key_rank[k] = key_rank.get(k, 0) + 1
                o.ent_idx = key_rank[k]
                o.entity = f"D{key_id[k]}_{o.entity}"

        for o in self.ops:
            seen = {}
            for p in o.waits:
                cur = seen.get(p.entity)
                if cur is None or p.idx > cur.idx:
                    seen[p.entity] = p
            o.waits = list(seen.values())

        issuer_know = {e: {} for e in COMPUTE_ENGINES}
        for o in self.ops:
            kn = issuer_know[o.issuer]
            pruned = []
            for p in sorted(o.waits, key=lambda q: q.idx):
                if kn.get(p.entity, -1) >= p.idx:
                    continue
                pruned.append(p)
                p.marked = True
                if p.know:
                    for e2, v in p.know.items():
                        if kn.get(e2, -1) < v:
                            kn[e2] = v
                if kn.get(p.entity, -1) < p.idx:
                    kn[p.entity] = p.idx
            o.waits = pruned
            snap = dict(kn)
            snap[o.entity] = o.idx
            o.know = snap

        cnt = {}
        for o in self.ops:
            if o.issuer == o.entity or o.entity in COMPUTE_ENGINES:
                if o.marked:
                    cnt[o.entity] = cnt.get(o.entity, 0) + 1
                o.ent_idx = cnt.get(o.entity, 0)

        entities = sorted(set(o.entity for o in self.ops)
                          | set(e for o in self.ops for e in
                                [p.entity for p in o.waits]))
        sems = {e: nc.alloc_semaphore(f"sem_{e}"[:48]) for e in entities}
        streams = {e: [] for e in COMPUTE_ENGINES}
        for o in self.ops:
            streams[o.issuer].append(o)

        def amt(e):
            return 1 if e in COMPUTE_ENGINES else 16

        def run(ns, issuer):
            for o in streams[issuer]:
                for p in o.waits:
                    ns.wait_ge(sems[p.entity], p.ent_idx * amt(p.entity))
                instr = o.emit(ns)
                if o.marked:
                    instr.then_inc(sems[o.entity], amt(o.entity))

        with nc.Block() as block:
            @block.tensor
            def _(e):
                run(e, "pe")

            @block.scalar
            def _(e):
                run(e, "act")

            @block.vector
            def _(e):
                run(e, "dve")

            @block.gpsimd
            def _(e):
                run(e, "pool")

            @block.sync
            def _(e):
                run(e, "sp")


def _onefc():
    a = np.zeros((128, 4), np.float32)
    a[:, 0] = 1.0
    return a


def _onefr():
    a = np.zeros((128, 128), np.float32)
    for p in (0, 32, 64):
        a[p, :] = 1.0
    return a


def _onebr():
    a = np.zeros((128, 384), np.float64)
    a[0, 0:128] = 1.0
    a[64, 0:128] = 1.0
    a[0, 128:192] = 1.0
    a[0, 320:384] = 1.0
    return a.astype(BFNP)


def _blk64():
    a = np.zeros((128, 128), np.float64)
    a[0:64, 0:64] = 1.0
    a[64:128, 64:128] = 1.0
    return a.astype(BFNP)


def _host_consts():
    DH, WIN = 64, 512
    half = np.arange(0, DH, 2, dtype=np.float64)
    inv_freq = 1.0 / 10000.0 ** (half / DH)
    t = np.arange(T, dtype=np.float64)
    freqs = np.concatenate([t[:, None] * inv_freq[None, :]] * 2, axis=-1)
    base = (half + 0.4 * DH) / (1.4 * DH)
    power = (t - T // 2) / float(WIN // 2)
    scale = np.concatenate([base[None, :] ** power[:, None]] * 2, axis=-1)
    cos, sin = np.cos(freqs), np.sin(freqs)
    sgn = np.where(np.arange(DH) < 32, -1.0, 1.0)[None, :]

    def dup(a):
        return np.concatenate([a.T, a.T], axis=0).astype(BFNP)

    rot = np.stack([dup(cos * scale), dup(sgn * sin * scale),
                    dup(cos / scale), dup(sgn * sin / scale)])

    p = np.arange(128)[:, None]
    c = np.arange(128)[None, :]
    mska = np.where(p >= c, 0.0, -1e30).astype(BFNP)
    mskb = np.where(p <= c, 0.0, -1e30).astype(BFNP)
    c2 = np.arange(512)[None, :]
    mskc = np.zeros((4, 128, 512), dtype=np.float64)
    for k in range(4):
        mskc[k] = np.where(128 * k + p > c2, -4e9, 0.0)
    mskc = mskc.reshape(512, 512).astype(BFNP)
    return rot, mska, mskb, mskc


def build_program(debug=False):
    nc = bass.Bass("TRN2", target_bir_lowering=False)
    B = Builder()

    def din(name, shape, dt=F32):
        return nc.dram_tensor(name, shape, dt, kind="ExternalInput").ap()

    xin = din("xT", [D, T])
    wqT = din("wqT", [D, 2048])
    wkvT = din("wkvT", [D, 64])
    wkupT = din("wkupT", [64, 2048])
    wvupT = din("wvupT", [64, 2048])
    woT = din("woT", [2048, D])
    bw_qkv = din("bw_qkv", [D, 3072])
    bb_qkv = din("bb_qkv", [1, 3072])
    bw_out = din("bw_out", [D, D])
    bb_out = din("bb_out", [1, D])
    fwqkv = din("fwqkv", [2, D, 1536])
    fwperm = din("fwperm", [2, D, 1024])
    fqs = din("fqs", [2, 1, 64])
    fks = din("fks", [2, 1, 64])
    fwo = din("fwo", [2, 512, D])
    flng = din("flng", [2, 1, D])
    flnb = din("flnb", [2, 1, D])
    fw1 = din("fw1", [2, D, 5460])
    fb1a = din("fb1a", [2, 1, 2730])
    fb1g = din("fb1g", [2, 1, 2730])
    fw2 = din("fw2", [2, 2730, D])
    fb2 = din("fb2", [2, 1, D])
    rot_t = din("rot", [4, 128, T], BF16)
    mska_t = din("mska", [128, 128], BF16)
    mskb_t = din("mskb", [128, 128], BF16)
    mskc_t = din("mskc", [512, 512], BF16)
    identb_t = din("identb", [128, 128], BF16)
    onefc_t = din("onefc", [128, 4], F32)
    onebc_t = din("onebc", [128, 4], BF16)
    onefr_t = din("onefr", [128, 128], F32)
    onebr_t = din("onebr", [128, 384], BF16)
    blk64_t = din("blk64", [128, 128], BF16)

    yT = nc.dram_tensor("yT", [D, T], F32, kind="ExternalOutput").ap()
    kind_dbg = "ExternalOutput" if debug else "Internal"
    xs1 = nc.dram_tensor("xs1", [D, T], F32, kind=kind_dbg).ap()
    auxs = nc.dram_tensor("auxs", [8, T], BF16, kind="Internal").ap()
    auxdump = nc.dram_tensor("auxdump", [16, T], BF16, kind="ExternalOutput").ap() if debug else None
    dumpK = nc.dram_tensor("dumpK", [68, T], BF16, kind="ExternalOutput").ap() if debug else None
    dumpQ = nc.dram_tensor("dumpQ", [68, T], BF16, kind="ExternalOutput").ap() if debug else None
    dumpE = nc.dram_tensor("dumpE", [128, 512], BF16, kind="ExternalOutput").ap() if debug else None
    dumpO = nc.dram_tensor("dumpO", [128, 2 * T], BF16, kind="ExternalOutput").ap() if debug else None
    dumpS = nc.dram_tensor("dumpS", [2, 512], F32, kind="ExternalOutput").ap() if debug else None
    xs2 = nc.dram_tensor("xs2", [D, T], F32, kind=kind_dbg).ap()
    xt1 = nc.dram_tensor("xt1", [D, T], F32, kind=kind_dbg).ap() if debug else None
    xt2p = nc.dram_tensor("xt2p", [D, T], F32, kind=kind_dbg).ap() if debug else None
    xt3m = nc.dram_tensor("xt3m", [D, T], F32, kind=kind_dbg).ap() if debug else None

    def sbt(name, shape, dt):
        return nc.alloc_sbuf_tensor(name, shape, dt)

    XN = sbt("XN", [128, 8 * T], BF16)
    XF = sbt("XF", [128, 8 * T], F32)
    XFb = XF.bitcast(BF16)
    SH = sbt("SH", [128, 16384], BF16)
    WS = sbt("WS", [128, 8192], BF16)
    SA = sbt("SA", [128, T], F32)
    SB_ = sbt("SBs", [128, T], F32)
    SC = sbt("SCs", [128, T], F32)
    SCb = SC.bitcast(BF16)
    XL = sbt("XL", [128, 2 * T], F32)
    ROTB = XL.bitcast(BF16)
    ROWS = sbt("ROWS", [128, T], F32)
    ROWSB = sbt("ROWSB", [128, 2 * T], BF16)
    MSK = sbt("MSK", [128, 2304], BF16)
    BCOL = sbt("BCOL", [128, 192], F32)
    ONEF = sbt("ONEF", [128, 4], F32)
    ONEB = sbt("ONEB", [128, 4], BF16)
    ONEFR = sbt("ONEFR", [128, 128], F32)
    ONEBR = sbt("ONEBR", [128, 384], BF16)
    IDENTB = sbt("IDENTB", [128, 128], BF16)
    BLK64 = sbt("BLK64", [128, 128], BF16)
    ZEROB = sbt("ZEROB", [128, 4], BF16)

    PB = [nc.alloc_psum_tensor(f"PB{i}", [128, 512], F32) for i in range(6)]
    PBS = nc.alloc_psum_tensor("PBS", [128, 1024], F32)

    CONSTS = sbt("CONSTS", [128, 4], F32)
    CONST_VALS = (64.0 + 1e-6, -64.0, -70.0)
    for j, v in enumerate(CONST_VALS):
        nc.const_aps.register(F32, v, CONSTS[:, j:j + 1])

    def pbt(i):
        return f"PB{i}"

    def sh(c0, n, rows=None):
        ap = SH[:, c0:c0 + n] if rows is None else SH[rows[0]:rows[1], c0:c0 + n]
        return ap, [f"SH:{i}" for i in range(c0 // 512, (c0 + n + 511) // 512)]

    def ws(c0, n, rows=None):
        ap = WS[:, c0:c0 + n] if rows is None else WS[rows[0]:rows[1], c0:c0 + n]
        return ap, [f"WS:{i}" for i in range(c0 // 128, (c0 + n + 127) // 128)]

    def XFc(c):
        return XF[:, c * T:(c + 1) * T]

    def XNc(c):
        return XN[:, c * T:(c + 1) * T]

    ROWP = (0, 32, 64)

    def row(i, off=0, n=T):
        p = ROWP[i]
        return ROWS[p:p + 1, off:off + n]

    def onesrow_f(i):
        return ONEFR[ROWP[i]:ROWP[i] + 1, 0:128]

    def rowb(i, off=0, n=T):
        return ROWSB[0:1, i * T + off:i * T + off + n]

    BVROW = ROWSB

    OFF_BQKV, OFF_BOUT, OFF_B2 = 0, 24, 32
    OFF_LNG, OFF_LNB = 48, 64
    OFF_B1 = 80
    OFF_QS, OFF_KS, OFF_QSP, OFF_KSP, OFF_MBK = 168, 170, 172, 174, 176

    B.dma("qsp", lambda e: e.dma_start(out=MSK[:, 0:128], in_=mska_t), w=["MSK"])
    B.dma("qsp", lambda e: e.dma_start(out=MSK[:, 128:256], in_=mskb_t), w=["MSK"])
    B.dma("qsp", lambda e: e.dma_start(
        out=MSK[:, 256:2304].rearrange("p (k n) -> p k n", k=4),
        in_=mskc_t.rearrange("(k p) n -> p k n", p=128)), w=["MSK"])
    B.dma("qsp", lambda e: e.dma_start(out=IDENTB[:], in_=identb_t), w=["IDENTB"])
    B.op("dve", lambda e: e.memset(ZEROB[:], 0.0), w=["ZEROB"])
    B.dma("qpool", lambda e: e.dma_start(out=ROWSB[64:65, 0:1024],
                                         in_=bb_qkv[:, 2048:3072]),
          r=["bb_qkv"], w=["RBV"])
    B.dma("qsp", lambda e: e.dma_start(out=ONEF[:], in_=onefc_t), w=["ONE"])
    B.dma("qsp", lambda e: e.dma_start(out=ONEB[:], in_=onebc_t), w=["ONE"])
    B.dma("qsp", lambda e: e.dma_start(out=ONEFR[:], in_=onefr_t), w=["ONE"])
    B.dma("qsp", lambda e: e.dma_start(out=ONEBR[:], in_=onebr_t), w=["ONE"])
    B.dma("qsp", lambda e: e.dma_start(out=BLK64[:], in_=blk64_t), w=["BLK"])

    for j, v in enumerate(CONST_VALS):
        B.op("dve", lambda e, j=j, v=v: e.memset(CONSTS[:, j:j + 1], v), w=["CONSTS"])

    def load_row_to_cols(src_ap, src_tag, n, bcol_off):
        for s0 in range(0, n, 2048):
            seg = min(2048, n - s0)
            def eml(e, s0=s0, seg=seg):
                return e.dma_start(out=SA[0:1, 0:seg], in_=src_ap[:, s0:s0 + seg])
            B.dma("qsp", eml, r=[src_tag], w=["SA"])
            for c in range((seg + 127) // 128):
                wdt = min(128, seg - c * 128)
                def emm(e, c=c, wdt=wdt):
                    return e.matmul(PB[5][0:wdt, 0:1], SA[0:1, c * 128:c * 128 + wdt],
                                    ONEFR[0:1, 0:1], start=True, stop=True)
                B.op("pe", emm, r=["SA", "ONE"], w=[pbt(5)])
                def emv(e, c=c, wdt=wdt, s0=s0):
                    col = bcol_off + s0 // 128 + c
                    return e.tensor_copy(out=BCOL[0:wdt, col:col + 1],
                                         in_=PB[5][0:wdt, 0:1])
                B.op("dve", emv, r=[pbt(5)], w=["BCOL"])

    load_row_to_cols(bb_qkv, "bb_qkv", 3072, OFF_BQKV)
    load_row_to_cols(bb_out, "bb_out", D, OFF_BOUT)
    for l in range(2):
        load_row_to_cols(fb1a[l], "fb1a", 2730, OFF_B1 + 44 * l)
        load_row_to_cols(fb1g[l], "fb1g", 2730, OFF_B1 + 44 * l + 22)
        load_row_to_cols(fb2[l], "fb2", D, OFF_B2 + 8 * l)
        load_row_to_cols(flng[l], "flng", D, OFF_LNG + 8 * l)
        load_row_to_cols(flnb[l], "flnb", D, OFF_LNB + 8 * l)
        load_row_to_cols(fqs[l], "fqs", 64, OFF_QS + l)
        load_row_to_cols(fks[l], "fks", 64, OFF_KS + l)
    for l in range(2):
        for src, dst in ((OFF_QS + l, OFF_QSP + l), (OFF_KS + l, OFF_KSP + l)):
            def e1(e, src=src, dst=dst):
                return e.tensor_copy(out=BCOL[0:32, dst:dst + 1],
                                     in_=BCOL[32:64, src:src + 1])
            B.op("dve", e1, r=["BCOL"], w=["BCOL"])
            def e2(e, src=src, dst=dst):
                return e.tensor_copy(out=BCOL[32:64, dst:dst + 1],
                                     in_=BCOL[0:32, src:src + 1])
            B.op("dve", e2, r=["BCOL"], w=["BCOL"])
        for cc in (OFF_QS + l, OFF_KS + l, OFF_QSP + l, OFF_KSP + l):
            def e3(e, cc=cc):
                return e.tensor_copy(out=BCOL[64:128, cc:cc + 1],
                                     in_=BCOL[0:64, cc:cc + 1])
            B.op("dve", e3, r=["BCOL"], w=["BCOL"])
    for hp in range(8):
        def e4(e, hp=hp):
            return e.activation(out=BCOL[:, OFF_MBK + hp:OFF_MBK + hp + 1],
                                in_=BCOL[:, OFF_BQKV + 8 + hp:OFF_BQKV + 8 + hp + 1],
                                func=AF.Identity, scale=-2.0)
        B.op("act", e4, r=["BCOL"], w=["BCOL"])

    def stats_init():
        B.op("dve", lambda e: e.memset(row(0), 0.0), w=["R0"])
        B.op("dve", lambda e: e.memset(row(1), 0.0), w=["R1"])

    def stats_from_bf16(src_ap_fn, src_tags):
        B.op("act", lambda e: e.activation(out=SCb[:, T:2 * T], in_=src_ap_fn(),
                                           func=AF.Square), r=list(src_tags), w=["SC"])
        for nch in range(4):
            def em1(e, nch=nch, fn=src_ap_fn):
                full = fn()
                return e.matmul(PBS[0:1, 0:512], ONEB[:, 0:1],
                                full[:, nch * 512:(nch + 1) * 512], start=True, stop=True)
            B.op("pe", em1, r=list(src_tags) + ["ONE"], w=["PBSa"])
            def em2(e, nch=nch):
                return e.matmul(PBS[0:1, 512:1024], ONEB[:, 0:1],
                                SCb[:, T + nch * 512:T + (nch + 1) * 512],
                                start=True, stop=True)
            B.op("pe", em2, r=["SC", "ONE"], w=["PBSb"])
            def ev1(e, nch=nch):
                return e.tensor_tensor(out=row(0, nch * 512, 512),
                                       in0=row(0, nch * 512, 512),
                                       in1=PBS[0:1, 0:512], op=ALU.add)
            B.op("dve", ev1, r=["R0", "PBSa"], w=["R0"])
            def ev2(e, nch=nch):
                return e.tensor_tensor(out=row(1, nch * 512, 512),
                                       in0=row(1, nch * 512, 512),
                                       in1=PBS[0:1, 512:1024], op=ALU.add)
            B.op("dve", ev2, r=["R1", "PBSb"], w=["R1"])

    def stats_chunk(src_ap_fn, src_tags):
        B.op("act", lambda e: e.activation(out=SCb[:, 0:T], in_=src_ap_fn(),
                                           func=AF.Copy), r=list(src_tags), w=["SC"])
        stats_from_bf16(lambda: SCb[:, 0:T], ["SC"])

    def stats_rows(eps=1e-5):
        B.op("act", lambda e: e.activation(out=row(0), in_=row(0), func=AF.Identity,
                                           scale=1.0 / D), r=["R0"], w=["R0"])
        B.op("act", lambda e: e.activation(out=row(1), in_=row(1), func=AF.Identity,
                                           scale=1.0 / D), r=["R1"], w=["R1"])
        B.op("dve", lambda e: e.tensor_tensor(out=SA[0:1, 0:T], in0=row(0), in1=row(0),
                                              op=ALU.mult), r=["R0"], w=["SA"])
        B.op("act", lambda e: e.activation(out=SB_[0:1, 0:T], in_=row(1),
                                           func=AF.Copy), r=["R1"], w=["SB"])
        B.op("dve", lambda e: e.tensor_tensor(out=row(1), in0=SB_[0:1, 0:T],
                                              in1=SA[0:1, 0:T], op=ALU.subtract),
             r=["SA", "SB"], w=["R1"])
        B.op("dve", lambda e: e.tensor_scalar(out=row(1), in0=row(1), scalar1=eps,
                                              scalar2=None, op0=ALU.add),
             r=["R1"], w=["R1"])
        B.op("act", lambda e: e.activation(out=row(1), in_=row(1), func=AF.Ln),
             r=["R1"], w=["R1"])
        B.op("act", lambda e: e.activation(out=row(1), in_=row(1), func=AF.Exp,
                                           scale=-0.5), r=["R1"], w=["R1"])

    def normalize(dst_fn, src_fn=None, gain_bias=None):
        for nch in range(4):
            def emb(e, nch=nch):
                return e.matmul(PB[4][:, 0:512], onesrow_f(0),
                                row(0, nch * 512, 512), start=True, stop=True)
            B.op("pe", emb, r=["R0", "ONE"], w=[pbt(4)])
            def emb2(e, nch=nch):
                return e.matmul(PB[5][:, 0:512], onesrow_f(1),
                                row(1, nch * 512, 512), start=True, stop=True)
            B.op("pe", emb2, r=["R1", "ONE"], w=[pbt(5)])
            for c in range(NCH):
                if src_fn is None:
                    src_ap = XFc(c)[:, nch * 512:(nch + 1) * 512]
                    src_tags, shifted = [f"XF:{c}"], False
                else:
                    src_ap, src_tags, shifted = src_fn(c, nch)
                if not shifted:
                    def em1(e, src_ap=src_ap):
                        return e.tensor_tensor(out=SA[:, 0:512], in0=src_ap,
                                               in1=PB[4][:, 0:512], op=ALU.subtract)
                    B.op("dve", em1, r=src_tags + [pbt(4)], w=["SA"])
                else:
                    def em1(e, src_ap=src_ap):
                        return e.tensor_tensor(out=SA[:, 1:512], in0=src_ap,
                                               in1=PB[4][:, 1:512], op=ALU.subtract)
                    B.op("dve", em1, r=src_tags + [pbt(4)], w=["SA"])
                    def em0(e):
                        return e.tensor_scalar(out=SA[:, 0:1], in0=PB[4][:, 0:1],
                                               scalar1=-1.0, scalar2=None, op0=ALU.mult)
                    B.op("dve", em0, r=[pbt(4)], w=["SA"])
                dst_ap, dst_tags = dst_fn(c, nch)
                if gain_bias is None:
                    def em2(e, dst_ap=dst_ap):
                        return e.tensor_tensor(out=dst_ap, in0=SA[:, 0:512],
                                               in1=PB[5][:, 0:512], op=ALU.mult)
                    B.op("dve", em2, r=["SA", pbt(5)], w=dst_tags)
                else:
                    g_ap, b_ap = gain_bias(c)
                    def em2(e):
                        return e.tensor_tensor(out=SB_[:, 0:512], in0=SA[:, 0:512],
                                               in1=PB[5][:, 0:512], op=ALU.mult)
                    B.op("dve", em2, r=["SA", pbt(5)], w=["SB"])
                    def em3(e, dst_ap=dst_ap, g_ap=g_ap, b_ap=b_ap):
                        return e.activation(out=dst_ap, in_=SB_[:, 0:512],
                                            func=AF.Identity, scale=g_ap, bias=b_ap)
                    B.op("act", em3, r=["SB", "BCOL"], w=dst_tags)

    def norm_to_XN():
        normalize(lambda c, nch: (XNc(c)[:, nch * 512:(nch + 1) * 512], [f"XN:{c}"]))

    def norm_to_XF():
        normalize(lambda c, nch: (XFc(c)[:, nch * 512:(nch + 1) * 512], [f"XF:{c}"]))

    stats_init()
    for c in range(NCH):
        def eml(e, c=c):
            return e.dma_start(out=XFc(c), in_=xin[c * 128:(c + 1) * 128, :])
        B.dma("qsp", eml, r=["xin"], w=[f"XF:{c}"])
        stats_chunk(lambda c=c: XFc(c), [f"XF:{c}"])
    stats_rows()
    norm_to_XN()

    def attn_a():
        QT0, KT0, VT0, ET0, KV0 = 0, 4096, 8192, 13312, 14336
        kv_ap, kv_tags = ws(7168, 512)
        B.dma("qpool", lambda e: e.dma_start(
            out=kv_ap.rearrange("p (c n) -> p c n", c=8),
            in_=wkvT.rearrange("(c p) n -> p c n", p=128)),
            r=["wkvT"], w=kv_tags)
        for nch in range(4):
            for kc in range(NCH):
                def em(e, nch=nch, kc=kc):
                    a, _ = ws(7168 + kc * 64, 64)
                    return e.matmul(PB[0][0:64, :], a,
                                    XNc(kc)[:, nch * 512:(nch + 1) * 512],
                                    start=(kc == 0), stop=(kc == NCH - 1))
                B.op("pe", em, r=kv_tags + [f"XN:{kc}"], w=[pbt(0)])
            def eme(e, nch=nch):
                a, tg = sh(KV0 + nch * 512, 512, rows=(0, 64))
                return e.activation(out=a, in_=PB[0][0:64, :], func=AF.Copy)
            B.op("act", eme, r=[pbt(0)], w=sh(KV0, 2048)[1])
        kup_ap, kup_tags = ws(0, 2048, rows=(0, 64))
        B.dma("qpool", lambda e: e.dma_start(out=kup_ap, in_=wkupT),
              r=["wkupT"], w=kup_tags)
        vup_ap, vup_tags = ws(2048, 2048, rows=(0, 64))
        B.dma("qpool", lambda e: e.dma_start(out=vup_ap, in_=wvupT),
              r=["wvupT"], w=vup_tags)
        vt_tags = sh(VT0, 4112)[1]
        for jt in range(16):
            def emo(e, jt=jt):
                return e.memset(SH[:, VT0 + jt * 257 + 256:VT0 + jt * 257 + 257], 1.0)
            B.op("dve", emo, w=vt_tags)

        kvs_tags = sh(KV0, 2048)[1]
        qt_tags = sh(QT0, 4096)[1]
        kt_tags = sh(KT0, 4096)[1]
        for h in range(8):
            wq_ap, wq_tags = ws(4096, 2048)
            def emw(e, h=h):
                return e.dma_start(
                    out=wq_ap.rearrange("p (c n) -> p c n", c=8),
                    in_=wqT.rearrange("(c p) n -> p c n", p=128)[:, :, h * 256:(h + 1) * 256])
            B.dma("qpool", emw, r=["wqT"], w=wq_tags)
            for m in range(2):
                for nch in range(4):
                    for kc in range(NCH):
                        def em(e, m=m, nch=nch, kc=kc):
                            a, _ = ws(4096 + kc * 256 + m * 128, 128)
                            return e.matmul(PB[m][:, :], a,
                                            XNc(kc)[:, nch * 512:(nch + 1) * 512],
                                            start=(kc == 0), stop=(kc == NCH - 1))
                        B.op("pe", em, r=wq_tags + [f"XN:{kc}"], w=[pbt(m)])
                    def eme(e, m=m, nch=nch):
                        return e.activation(
                            out=SH[:, QT0 + m * T + nch * 512:QT0 + m * T + (nch + 1) * 512],
                            in_=PB[m][:, :], func=AF.Copy)
                    B.op("act", eme, r=[pbt(m)], w=qt_tags)
                for nch in range(4):
                    def em(e, m=m, nch=nch, h=h):
                        a, _ = ws(h * 256 + m * 128, 128, rows=(0, 64))
                        return e.matmul(PB[2][:, :], a,
                                        SH[0:64, KV0 + nch * 512:KV0 + (nch + 1) * 512],
                                        start=True, stop=True)
                    B.op("pe", em, r=kup_tags + kvs_tags, w=[pbt(2)])
                    def eme(e, m=m, nch=nch):
                        return e.activation(
                            out=SH[:, KT0 + m * T + nch * 512:KT0 + m * T + (nch + 1) * 512],
                            in_=PB[2][:, :], func=AF.Copy)
                    B.op("act", eme, r=[pbt(2)], w=kt_tags)
            for jt in range(16):
                def em(e, jt=jt, h=h):
                    a, _ = ws(2048 + h * 256, 256, rows=(0, 64))
                    return e.matmul(PB[3][:, 0:256],
                                    SH[0:64, KV0 + jt * 128:KV0 + (jt + 1) * 128],
                                    a, start=True, stop=True)
                B.op("pe", em, r=kvs_tags + vup_tags, w=[pbt(3)])
                def eme(e, jt=jt):
                    return e.activation(out=SH[:, VT0 + jt * 257:VT0 + jt * 257 + 256],
                                        in_=PB[3][:, 0:256], func=AF.Copy)
                B.op("act", eme, r=[pbt(3)], w=vt_tags)
            for ic in range(4):
                for jt in range(16):
                    pb = jt % 2
                    for kc in range(2):
                        def em(e, pb=pb, kc=kc, jt=jt, ic=ic):
                            return e.matmul(
                                PB[pb][:, :],
                                SH[:, KT0 + kc * T + jt * 128:KT0 + kc * T + (jt + 1) * 128],
                                SH[:, QT0 + kc * T + ic * 512:QT0 + kc * T + (ic + 1) * 512],
                                start=(kc == 0), stop=(kc == 1))
                        B.op("pe", em, r=kt_tags + qt_tags, w=[pbt(pb)])
                    def eme(e, pb=pb):
                        return e.activation(out=SH[:, ET0 + pb * 512:ET0 + (pb + 1) * 512],
                                            in_=PB[pb][:, :], func=AF.Exp, scale=1.0 / 16.0)
                    B.op("act", eme, r=[pbt(pb)], w=sh(ET0 + pb * 512, 512)[1])
                    for mt in range(3):
                        o0, wdt, opb = ((0, 128, 3), (128, 128, 4), (256, 1, 5))[mt]
                        def em(e, jt=jt, o0=o0, wdt=wdt, opb=opb, pb=pb):
                            return e.matmul(
                                PB[opb][0:(128 if wdt == 128 else 1), 0:512],
                                SH[:, VT0 + jt * 257 + o0:VT0 + jt * 257 + o0 + wdt],
                                SH[:, ET0 + pb * 512:ET0 + (pb + 1) * 512],
                                start=(jt == 0), stop=(jt == 15))
                        B.op("pe", em, r=vt_tags + sh(ET0 + pb * 512, 512)[1], w=[pbt(opb)])
                def emf(e):
                    return e.tensor_scalar(out=row(2, 0, 512), in0=PB[5][0:1, 0:512],
                                           scalar1=1e-35, scalar2=None, op0=ALU.max)
                B.op("dve", emf, r=[pbt(5)], w=["R2"])
                B.op("act", lambda e: e.activation(out=row(2, 0, 512), in_=row(2, 0, 512),
                                                   func=AF.Ln), r=["R2"], w=["R2"])
                B.op("act", lambda e: e.activation(out=row(2, 0, 512), in_=row(2, 0, 512),
                                                   func=AF.Exp, scale=-1.0),
                     r=["R2"], w=["R2"])
                def emb(e):
                    return e.matmul(PB[2][:, 0:512], onesrow_f(2), row(2, 0, 512),
                                    start=True, stop=True)
                B.op("pe", emb, r=["R2", "ONE"], w=[pbt(2)])
                B.op("act", lambda e: e.activation(out=SA[:, 0:512], in_=PB[2][:, 0:512],
                                                   func=AF.Copy), r=[pbt(2)], w=["SA"])
                for mt in range(2):
                    def em(e, mt=mt, ic=ic, h=h):
                        oc = 2 * h + mt
                        return e.tensor_tensor(
                            out=XFb[:, oc * T + ic * 512:oc * T + ic * 512 + 512],
                            in0=PB[3 + mt][:, 0:512], in1=SA[:, 0:512], op=ALU.mult)
                    B.op("dve", em, r=[pbt(3 + mt), "SA"], w=[f"XF:{h}"])
        stats_init()
        for m in range(NCH):
            def eml(e, m=m):
                return e.dma_start(out=XL[:, (m % 2) * T:(m % 2) * T + T],
                                   in_=xin[m * 128:(m + 1) * 128, :])
            B.dma("qsp", eml, r=["xin"], w=[f"XL:{m % 2}"])
            for kc in range(16):
                sl_ap, sl_tags = ws(6144 + (kc % 2) * 128, 128)
                def emw(e, m=m, kc=kc, sl_ap=sl_ap):
                    return e.dma_start(out=sl_ap,
                                       in_=woT[kc * 128:(kc + 1) * 128, m * 128:(m + 1) * 128])
                B.dma("qpool", emw, r=["woT"], w=sl_tags)
                for sp in range(4):
                    def em(e, m=m, kc=kc, sp=sp, sl_ap=sl_ap):
                        return e.matmul(PB[sp][:, :], sl_ap,
                                        XFb[:, kc * T + sp * 512:kc * T + sp * 512 + 512],
                                        start=(kc == 0), stop=(kc == 15))
                    B.op("pe", em, r=sl_tags + [f"XF:{kc // 2}"], w=[pbt(sp)])
            for sp in range(4):
                def emr(e, m=m, sp=sp):
                    return e.tensor_tensor(
                        out=SB_[:, sp * 512:(sp + 1) * 512], in0=PB[sp][:, :],
                        in1=XL[:, (m % 2) * T + sp * 512:(m % 2) * T + (sp + 1) * 512],
                        op=ALU.add)
                B.op("dve", emr, r=[pbt(sp), f"XL:{m % 2}"], w=["SB"])
            def emw2(e, m=m):
                return e.dma_start(out=xs1[m * 128:(m + 1) * 128, :], in_=SB_[:, 0:T])
            B.dma("qsp", emw2, r=["SB"], w=[f"xs1:{m}"])
            stats_chunk(lambda: SB_[:, 0:T], ["SB"])
        stats_rows()
        for c in range(NCH):
            def eml(e, c=c):
                return e.dma_start(out=XFc(c), in_=xs1[c * 128:(c + 1) * 128, :])
            B.dma("qsp", eml, r=[f"xs1:{c}"], w=[f"XF:{c}"])
            if debug:
                def emt(e, c=c):
                    return e.dma_start(out=xt1[c * 128:(c + 1) * 128, :], in_=XFc(c))
                B.dma("qsp", emt, r=[f"XF:{c}"], w=["xt1"])
        norm_to_XF()

    attn_a()

    def shift_to_XN():
        for c in range(NCH):
            if c < 4:
                def em(e, c=c):
                    return e.activation(out=XNc(c), in_=XFc(c), func=AF.Copy)
                B.op("act", em, r=[f"XF:{c}"], w=[f"XN:{c}"])
            else:
                def em(e, c=c):
                    return e.activation(out=XNc(c)[:, 1:T], in_=XFc(c)[:, 0:T - 1],
                                        func=AF.Copy)
                B.op("act", em, r=[f"XF:{c}"], w=[f"XN:{c}"])
                B.op("dve", lambda e, c=c: e.memset(XNc(c)[:, 0:1], 0.0), w=[f"XN:{c}"])

    def ffn_layer(l):
        QQ0, KK0, VS0, ET0, OT0 = 0, 2048, 4096, 7168, 8192
        qq_tags = sh(QQ0, 2048)[1]
        kk_tags = sh(KK0, 2048)[1]
        vs_tags = sh(VS0, 2080)[1]
        et_tags = ["SH:7"]
        wqkv_l = fwqkv[l].rearrange("(c p) n -> p c n", p=128)
        wperm_l = fwperm[l].rearrange("(c p) n -> p c n", p=128)

        shift_to_XN()
        for hp in range(4):
            slot_info = []
            for si, (src, base) in enumerate((
                    (wqkv_l, hp * 128), (wqkv_l, 512 + hp * 128),
                    (wqkv_l, 1024 + hp * 128),
                    (wperm_l, hp * 128), (wperm_l, 512 + hp * 128))):
                s_ap, s_tags = ws(si * 1024, 1024)
                slot_info.append((si, s_tags))
                def emw(e, s_ap=s_ap, src=src, base=base):
                    return e.dma_start(out=s_ap.rearrange("p (c n) -> p c n", c=8),
                                       in_=src[:, :, base:base + 128])
                B.dma("qpool", emw, r=["fwqkv"], w=s_tags)
            for which in range(2):
                dst0 = (QQ0, KK0)[which]
                dtags = (qq_tags, kk_tags)[which]
                cs_off = (0, 2 * T)[which]
                sc = (OFF_QS, OFF_KS)[which] + l
                scp = (OFF_QSP, OFF_KSP)[which] + l
                wtag = slot_info[which][1]
                wptag = slot_info[3 + which][1]
                for nch in range(4):
                    for kc in range(NCH):
                        def em(e, which=which, nch=nch, kc=kc):
                            a, _ = ws(which * 1024 + kc * 128, 128)
                            return e.matmul(PB[nch][:, :], a,
                                            XNc(kc)[:, nch * 512:(nch + 1) * 512],
                                            start=(kc == 0), stop=(kc == NCH - 1))
                        B.op("pe", em, r=wtag + [f"XN:{kc}"], w=[pbt(nch)])
                for nch in range(4):
                    def em(e, nch=nch):
                        return e.activation(out=SCb[:, nch * 512:(nch + 1) * 512],
                                            in_=PB[nch][:, :], func=AF.Square)
                    B.op("act", em, r=[pbt(nch)], w=["SC"])
                for nch in range(4):
                    def em(e, nch=nch):
                        return e.matmul(PB[4 + nch % 2][:, 0:512], BLK64[:, :],
                                        SCb[:, nch * 512:(nch + 1) * 512],
                                        start=True, stop=True)
                    B.op("pe", em, r=["SC", "BLK"], w=[pbt(4 + nch % 2)])
                    def el(e, nch=nch):
                        return e.activation(out=SA[:, nch * 512:(nch + 1) * 512],
                                            in_=PB[4 + nch % 2][:, 0:512], func=AF.Ln)
                    B.op("act", el, r=[pbt(4 + nch % 2)], w=["SA"])
                for nch in range(4):
                    def ex(e, nch=nch):
                        return e.activation(out=SA[:, nch * 512:(nch + 1) * 512],
                                            in_=SA[:, nch * 512:(nch + 1) * 512],
                                            func=AF.Exp, scale=-0.5)
                    B.op("act", ex, r=["SA"], w=["SA"])
                for nch in range(4):
                    def em(e, nch=nch):
                        return e.tensor_tensor(out=SB_[:, nch * 512:(nch + 1) * 512],
                                               in0=PB[nch][:, :],
                                               in1=SA[:, nch * 512:(nch + 1) * 512],
                                               op=ALU.mult)
                    B.op("dve", em, r=[pbt(nch), "SA"], w=["SB"])
                def esc(e, sc=sc):
                    return e.activation(out=SB_[:], in_=SB_[:], func=AF.Identity,
                                        scale=BCOL[:, sc:sc + 1])
                B.op("act", esc, r=["SB", "BCOL"], w=["SB"])
                for nch in range(4):
                    for kc in range(NCH):
                        def em(e, which=which, nch=nch, kc=kc):
                            a, _ = ws((3 + which) * 1024 + kc * 128, 128)
                            return e.matmul(PB[nch][:, :], a,
                                            XNc(kc)[:, nch * 512:(nch + 1) * 512],
                                            start=(kc == 0), stop=(kc == NCH - 1))
                        B.op("pe", em, r=wptag + [f"XN:{kc}"], w=[pbt(nch)])
                    def em2(e, nch=nch):
                        return e.tensor_tensor(out=SC[:, nch * 512:(nch + 1) * 512],
                                               in0=PB[nch][:, :],
                                               in1=SA[:, nch * 512:(nch + 1) * 512],
                                               op=ALU.mult)
                    B.op("dve", em2, r=[pbt(nch), "SA"], w=["SC"])
                def esc2(e, scp=scp):
                    return e.activation(out=SC[:], in_=SC[:], func=AF.Identity,
                                        scale=BCOL[:, scp:scp + 1])
                B.op("act", esc2, r=["SC", "BCOL"], w=["SC"])
                def eu(e, cs_off=cs_off):
                    return e.tensor_tensor(out=SB_[:], in0=SB_[:],
                                           in1=ROTB[:, cs_off:cs_off + T], op=ALU.mult)
                B.op("dve", eu, r=["SB", "XL:0", "XL:1"], w=["SB"])
                def ev2(e, cs_off=cs_off):
                    return e.tensor_tensor(out=SC[:], in0=SC[:],
                                           in1=ROTB[:, cs_off + T:cs_off + 2 * T],
                                           op=ALU.mult)
                B.op("dve", ev2, r=["SC", "XL:0", "XL:1"], w=["SC"])
                def eq(e, dst0=dst0):
                    return e.tensor_tensor(out=SH[:, dst0:dst0 + T], in0=SB_[:],
                                           in1=SC[:], op=ALU.add)
                B.op("dve", eq, r=["SB", "SC"], w=dtags)
            if hp == 0:
                for bi in range(32):
                    def emo(e, bi=bi):
                        return e.memset(SH[:, VS0 + bi * 65 + 64:VS0 + bi * 65 + 65], 1.0)
                    B.op("dve", emo, w=vs_tags)
            vtag = slot_info[2][1]
            for jt in range(16):
                for kc in range(NCH):
                    def em(e, jt=jt, kc=kc):
                        a, _ = ws(2 * 1024 + kc * 128, 128)
                        return e.matmul(PB[5][:, 0:128], XNc(kc)[:, jt * 128:(jt + 1) * 128],
                                        a, start=(kc == 0), stop=(kc == NCH - 1))
                    B.op("pe", em, r=vtag + [f"XN:{kc}"], w=[pbt(5)])
                for hh in range(2):
                    def emc(e, jt=jt, hh=hh):
                        off = VS0 + (hh * 16 + jt) * 65
                        return e.activation(out=SH[:, off:off + 64],
                                            in_=PB[5][:, hh * 64:hh * 64 + 64],
                                            func=AF.Copy)
                    B.op("act", emc, r=[pbt(5)], w=vs_tags)
            def foacc(gg):
                return (PB[4], pbt(4)) if gg % 2 == 0 else (PBS, "PBSa")

            def fepilogue(hh, w, gg):
                h = 2 * hp + hh
                acc, acct = foacc(gg)
                r0 = (gg % 2) * 512
                rt = ["R2"] if gg % 2 == 0 else ["R2b"]
                def emf(e, acc=acc, r0=r0):
                    return e.tensor_scalar(out=row(2, r0, 512), in0=acc[64:65, 0:512],
                                           scalar1=1e-35, scalar2=None, op0=ALU.max)
                B.op("dve", emf, r=[acct], w=rt)
                def er1(e, r0=r0):
                    return e.activation(out=row(2, r0, 512), in_=row(2, r0, 512),
                                        func=AF.Ln)
                B.op("act", er1, r=rt, w=rt)
                def er2(e, r0=r0):
                    return e.activation(out=row(2, r0, 512), in_=row(2, r0, 512),
                                        func=AF.Exp, scale=-1.0)
                B.op("act", er2, r=rt, w=rt)
                if gg % 2 == 0:
                    bt_ap, btt = PB[5][:, 0:512], pbt(5)
                else:
                    bt_ap, btt = PBS[:, 512:1024], "PBSb"
                def emb(e, r0=r0, bt_ap=bt_ap):
                    return e.matmul(bt_ap, onesrow_f(2), row(2, r0, 512),
                                    start=True, stop=True)
                B.op("pe", emb, r=rt + ["ONE"], w=[btt])
                def ecp(e, bt_ap=bt_ap):
                    return e.activation(out=SA[:, 0:512], in_=bt_ap, func=AF.Copy)
                B.op("act", ecp, r=[btt], w=["SA"])
                def emo2(e, h=h, w=w, acc=acc):
                    oc = h // 2
                    pr = (h % 2) * 64
                    off = OT0 + oc * T + w * 512
                    return e.tensor_tensor(out=SH[pr:pr + 64, off:off + 512],
                                           in0=acc[0:64, 0:512], in1=SA[0:64, 0:512],
                                           op=ALU.mult)
                B.op("dve", emo2, r=[acct, "SA"],
                     w=sh(OT0 + (h // 2) * T + w * 512, 512)[1])

            def fflush(t):
                hh, w, gg, jglob, pb, i0, N, first, last = t
                acc, acct = foacc(gg)
                def emo(e, jglob=jglob, pb=pb, hh=hh, i0=i0, N=N,
                        first=first, last=last, acc=acc):
                    off = VS0 + (hh * 16 + jglob) * 65
                    return e.matmul(acc[0:65, i0:i0 + N],
                                    SH[:, off:off + 65],
                                    SH[:, ET0 + pb * 512:ET0 + pb * 512 + N],
                                    start=first, stop=last)
                B.op("pe", emo, r=vs_tags + sh(ET0 + pb * 512, 512)[1], w=[acct])
                if last:
                    fepilogue(hh, w, gg)

            fpend = None
            for hh in range(2):
                for w in range(4):
                    gg = hh * 4 + w
                    qb = hh * 64
                    jtls = list(range(4, 8)) if w == 0 else list(range(8))
                    for ti, jtl in enumerate(jtls):
                        jglob = 4 * (w - 1) + jtl
                        pb = ti % 2
                        i0 = max(0, (jtl - 4) * 128)
                        i1 = min(512, (jtl + 1) * 128)
                        N = i1 - i0
                        def em(e, pb=pb, jglob=jglob, w=w, qb=qb, i0=i0, N=N):
                            return e.matmul(
                                PB[pb][:, 0:N],
                                SH[qb:qb + 64, KK0 + jglob * 128:KK0 + (jglob + 1) * 128],
                                SH[qb:qb + 64, QQ0 + w * 512 + i0:QQ0 + w * 512 + i0 + N],
                                start=True, stop=False)
                        B.op("pe", em, r=qq_tags + kk_tags, w=[pbt(pb)])
                        if w > 0 and jtl <= 3:
                            moff, mpos = 0, jtl * 128 - i0
                        elif jtl >= 4:
                            moff, mpos = 128, (jtl - 4) * 128 - i0
                        else:
                            moff = None
                        if moff is not None:
                            def emk(e, pb=pb, moff=moff, mpos=mpos):
                                return e.matmul(PB[pb][:, mpos:mpos + 128],
                                                IDENTB[:, :], MSK[:, moff:moff + 128],
                                                start=False, stop=True)
                            B.op("pe", emk, r=["IDENTB", "MSK"], w=[pbt(pb)])
                        else:
                            def emk(e, pb=pb):
                                return e.matmul(PB[pb][0:1, 0:1], IDENTB[0:1, 0:1],
                                                ZEROB[0:1, 0:1], start=False, stop=True)
                            B.op("pe", emk, r=["IDENTB", "ZEROB"], w=[pbt(pb)])
                        if fpend is not None:
                            fflush(fpend)
                        def eme(e, pb=pb, N=N):
                            return e.activation(out=SH[:, ET0 + pb * 512:ET0 + pb * 512 + N],
                                                in_=PB[pb][:, 0:N], func=AF.Exp, scale=8.0)
                        B.op("act", eme, r=[pbt(pb)], w=sh(ET0 + pb * 512, 512)[1])
                        fpend = (hh, w, gg, jglob, pb, i0, N,
                                 ti == 0, ti == len(jtls) - 1)
            fflush(fpend)
        for m in range(NCH):
            for kc in range(4):
                sl_ap, sl_tags = ws(6144 + (kc % 2) * 128, 128)
                def emw(e, m=m, kc=kc, sl_ap=sl_ap):
                    return e.dma_start(out=sl_ap,
                                       in_=fwo[l][kc * 128:(kc + 1) * 128, m * 128:(m + 1) * 128])
                B.dma("qpool", emw, r=["fwo"], w=sl_tags)
                for sp in range(4):
                    def em(e, m=m, kc=kc, sp=sp, sl_ap=sl_ap):
                        return e.matmul(PB[sp][:, :], sl_ap,
                                        SH[:, OT0 + kc * T + sp * 512:OT0 + kc * T + (sp + 1) * 512],
                                        start=(kc == 0), stop=(kc == 3))
                    B.op("pe", em, r=sl_tags + sh(8192 + kc * T, T)[1], w=[pbt(sp)])
            for sp in range(4):
                def ea(e, m=m, sp=sp):
                    return e.tensor_tensor(
                        out=XFc(m)[:, sp * 512:(sp + 1) * 512],
                        in0=XFc(m)[:, sp * 512:(sp + 1) * 512],
                        in1=PB[sp][:, :], op=ALU.add)
                B.op("dve", ea, r=[pbt(sp), f"XF:{m}"], w=[f"XF:{m}"])
        stats_init()
        shift_to_XN()
        for c in range(NCH):
            stats_from_bf16(lambda c=c: XNc(c), [f"XN:{c}"])
        stats_rows()

        def srcfn(c, nch):
            if c < 4:
                return (XFc(c)[:, nch * 512:(nch + 1) * 512], [f"XF:{c}"], False)
            if nch == 0:
                return (XFc(c)[:, 0:511], [f"XF:{c}"], True)
            return (XFc(c)[:, nch * 512 - 1:(nch + 1) * 512 - 1], [f"XF:{c}"], False)

        normalize(lambda c, nch: (XNc(c)[:, nch * 512:(nch + 1) * 512], [f"XN:{c}"]),
                  src_fn=srcfn,
                  gain_bias=lambda c: (BCOL[:, OFF_LNG + 8 * l + c:OFF_LNG + 8 * l + c + 1],
                                       BCOL[:, OFF_LNB + 8 * l + c:OFF_LNB + 8 * l + c + 1]))

        w1r = fw1[l].rearrange("(c p) n -> p c n", p=128)
        HID = 2730
        groups = [list(range(0, 8)), list(range(8, 16)), list(range(16, 22))]
        for g in groups:
            for gi, cch in enumerate(g):
                wdt = min(128, HID - cch * 128)
                wa_ap, wa_tags = ws(5 * 1024, 8 * wdt)
                wg_ap, wg_tags = ws(6 * 1024, 8 * wdt)
                def emwa(e, wa_ap=wa_ap, cch=cch, wdt=wdt):
                    return e.dma_start(out=wa_ap.rearrange("p (c n) -> p c n", c=8),
                                       in_=w1r[:, :, cch * 128:cch * 128 + wdt])
                B.dma("qpool", emwa, r=["fw1"], w=wa_tags)
                def emwg(e, wg_ap=wg_ap, cch=cch, wdt=wdt):
                    return e.dma_start(out=wg_ap.rearrange("p (c n) -> p c n", c=8),
                                       in_=w1r[:, :, HID + cch * 128:HID + cch * 128 + wdt])
                B.dma("qpool", emwg, r=["fw1"], w=wg_tags)
                for nch in range(4):
                    for kc in range(NCH):
                        def ema(e, nch=nch, kc=kc, wdt=wdt):
                            a, _ = ws(5 * 1024 + kc * wdt, wdt)
                            return e.matmul(PB[nch % 2][0:wdt, :], a,
                                            XNc(kc)[:, nch * 512:(nch + 1) * 512],
                                            start=(kc == 0), stop=(kc == NCH - 1))
                        B.op("pe", ema, r=wa_tags + [f"XN:{kc}"], w=[pbt(nch % 2)])
                        def emg(e, nch=nch, kc=kc, wdt=wdt):
                            a, _ = ws(6 * 1024 + kc * wdt, wdt)
                            return e.matmul(PB[2 + nch % 2][0:wdt, :], a,
                                            XNc(kc)[:, nch * 512:(nch + 1) * 512],
                                            start=(kc == 0), stop=(kc == NCH - 1))
                        B.op("pe", emg, r=wg_tags + [f"XN:{kc}"], w=[pbt(2 + nch % 2)])
                    def eg1(e, nch=nch, wdt=wdt, cch=cch):
                        bcg = OFF_B1 + 44 * l + 22 + cch
                        return e.activation(out=SCb[0:wdt, nch * 512:(nch + 1) * 512],
                                            in_=PB[2 + nch % 2][0:wdt, :], func=AF.Gelu,
                                            bias=BCOL[0:wdt, bcg:bcg + 1])
                    B.op("act", eg1, r=[pbt(2 + nch % 2), "BCOL"], w=["SC"])
                    def eg2(e, nch=nch, wdt=wdt, cch=cch, gi=gi):
                        bca = OFF_B1 + 44 * l + cch
                        a, _ = sh(gi * T + nch * 512, 512, rows=(0, wdt))
                        return e.scalar_tensor_tensor(
                            out=a, in0=PB[nch % 2][0:wdt, :],
                            scalar=BCOL[0:wdt, bca:bca + 1],
                            in1=SCb[0:wdt, nch * 512:(nch + 1) * 512],
                            op0=ALU.add, op1=ALU.mult)
                    B.op("dve", eg2, r=[pbt(nch % 2), "SC", "BCOL"],
                         w=sh(gi * T, 2048)[1])
            last_group = (g[0] == 16)
            for m in range(NCH):
                for gi, cch in enumerate(g):
                    wdt = min(128, HID - cch * 128)
                    sl_ap, sl_tags = ws(7 * 1024 + (gi % 2) * 128, 128, rows=(0, wdt))
                    def emw(e, m=m, cch=cch, wdt=wdt, sl_ap=sl_ap):
                        return e.dma_start(out=sl_ap,
                                           in_=fw2[l][cch * 128:cch * 128 + wdt,
                                                      m * 128:(m + 1) * 128])
                    B.dma("qpool", emw, r=["fw2"], w=sl_tags)
                    for sp in range(4):
                        def em(e, m=m, sp=sp, gi=gi, wdt=wdt, sl_ap=sl_ap, n=len(g)):
                            a, _ = sh(gi * T + sp * 512, 512, rows=(0, wdt))
                            return e.matmul(PB[2 + sp][:, :], sl_ap, a,
                                            start=(gi == 0), stop=(gi == n - 1))
                        B.op("pe", em, r=sl_tags + sh(gi * T, 2048)[1], w=[pbt(2 + sp)])
                for sp in range(4):
                    if last_group:
                        def ea(e, m=m, sp=sp):
                            return e.scalar_tensor_tensor(
                                out=XFc(m)[:, sp * 512:(sp + 1) * 512],
                                in0=PB[2 + sp][:, :],
                                scalar=BCOL[:, OFF_B2 + 8 * l + m:OFF_B2 + 8 * l + m + 1],
                                in1=XFc(m)[:, sp * 512:(sp + 1) * 512],
                                op0=ALU.add, op1=ALU.add)
                        B.op("dve", ea, r=[pbt(2 + sp), f"XF:{m}", "BCOL"], w=[f"XF:{m}"])
                    else:
                        def ea(e, m=m, sp=sp):
                            return e.tensor_tensor(
                                out=XFc(m)[:, sp * 512:(sp + 1) * 512],
                                in0=XFc(m)[:, sp * 512:(sp + 1) * 512],
                                in1=PB[2 + sp][:, :], op=ALU.add)
                        B.op("dve", ea, r=[pbt(2 + sp), f"XF:{m}"], w=[f"XF:{m}"])

    def ffn_call(src_x, src_fmt, dst_x, dst_fmt, final):
        for bi in range(4):
            def emr(e, bi=bi):
                return e.dma_start(out=ROTB[:, bi * T:(bi + 1) * T], in_=rot_t[bi])
            B.dma("qsp", emr, r=["rot"], w=["XL:0" if bi < 2 else "XL:1"])
        for l in range(2):
            ffn_layer(l)
        if not final:
            stats_init()
        for c in range(NCH):
            def eml(e, c=c):
                return e.dma_start(out=XL[:, (c % 2) * T:(c % 2) * T + T],
                                   in_=src_x[c * 128:(c + 1) * 128, :])
            B.dma("qsp", eml, r=[src_fmt.format(c)], w=[f"XL:{c % 2}"])
            def emr2(e, c=c):
                return e.tensor_tensor(out=XFc(c), in0=XFc(c),
                                       in1=XL[:, (c % 2) * T:(c % 2) * T + T], op=ALU.add)
            B.op("dve", emr2, r=[f"XF:{c}", f"XL:{c % 2}"], w=[f"XF:{c}"])
            def emw(e, c=c):
                return e.dma_start(out=dst_x[c * 128:(c + 1) * 128, :], in_=XFc(c))
            B.dma("qsp", emw, r=[f"XF:{c}"], w=[dst_fmt.format(c)])
            if not final:
                stats_chunk(lambda c=c: XFc(c), [f"XF:{c}"])
        if not final:
            stats_rows()

    ffn_call(xs1, "xs1:{}", xs2, "xs2:{}", False)
    norm_to_XN()

    def attn_b():
        Q0, K0, V0, ET0, OT0 = 0, 4096, 8192, 11264, 12288
        EPS = 1e-6
        CEXP = 70.0
        wq = bw_qkv.rearrange("(c p) n -> p c n", p=128)
        v_tags = sh(V0, 2080)[1]
        if debug:
            for c in range(NCH):
                def emt(e, c=c):
                    return e.dma_start(out=xt2p[c * 128:(c + 1) * 128, :], in_=XFc(c))
                B.dma("qsp", emt, r=[f"XF:{c}"], w=["xt2p"])
        for grp in range(4):
            if debug and grp == 2:
                for c in range(NCH):
                    def emt(e, c=c):
                        return e.dma_start(out=xt3m[c * 128:(c + 1) * 128, :], in_=XFc(c))
                    B.dma("qsp", emt, r=[f"XF:{c}"], w=["xt3m"])
            ot_tags = [sh(OT0, 2048)[1], sh(OT0 + 2048, 2048)[1]]
            for gh in range(2):
                hp = grp * 2 + gh
                slot_tags = []
                for si, base in enumerate((hp * 128, 1024 + hp * 128, 2048 + hp * 128)):
                    s_ap, s_tags = ws(si * 1024, 1024)
                    slot_tags.append(s_tags)
                    def emw(e, s_ap=s_ap, base=base):
                        return e.dma_start(out=s_ap.rearrange("p (c n) -> p c n", c=8),
                                           in_=wq[:, :, base:base + 128])
                    B.dma("qpool", emw, r=["bw_qkv"], w=s_tags)
                for which in range(2):
                    for nch in range(4):
                        for kc in range(NCH):
                            def em(e, which=which, nch=nch, kc=kc):
                                a, _ = ws(which * 1024 + kc * 128, 128)
                                return e.matmul(PB[nch][:, :], a,
                                                XNc(kc)[:, nch * 512:(nch + 1) * 512],
                                                start=(kc == 0), stop=(kc == NCH - 1))
                            B.op("pe", em, r=slot_tags[which] + [f"XN:{kc}"], w=[pbt(nch)])
                    for hh in range(2):
                        dst0 = (Q0, K0)[which] + hh * 2048
                        dtags = sh(dst0, 2048)[1]
                        if which == 0:
                            bias_col = OFF_BQKV + hp
                            scl = 1.0
                        else:
                            bias_col = OFF_MBK + hp
                            scl = -2.0
                        for nch in range(4):
                            def ee(e, hh=hh, nch=nch, dst0=dst0, bias_col=bias_col, scl=scl):
                                return e.activation(
                                    out=SH[0:64, dst0 + nch * 512:dst0 + (nch + 1) * 512],
                                    in_=PB[nch][hh * 64:hh * 64 + 64, :],
                                    func=AF.Identity, scale=scl,
                                    bias=BCOL[hh * 64:hh * 64 + 64, bias_col:bias_col + 1])
                            B.op("act", ee, r=[pbt(nch), "BCOL"], w=dtags)
                for hh in range(2):
                    qd = Q0 + hh * 2048
                    kd = K0 + hh * 2048
                    qtags = sh(qd, 2048)[1]
                    ktags = sh(kd, 2048)[1]
                    B.op("dve", lambda e, qd=qd: e.memset(SH[64:68, qd:qd + T], 1.0),
                         w=qtags)
                    B.op("dve", lambda e, kd=kd: e.memset(SH[64:68, kd:kd + T], 1.0),
                         w=ktags)
                    B.op("act", lambda e, qd=qd: e.activation(out=SCb[0:64, 0:T],
                                                              in_=SH[0:64, qd:qd + T],
                                                              func=AF.Square),
                         r=qtags, w=["SC", "SC:0", "SC:1"])
                    for nch in range(4):
                        def em(e, nch=nch):
                            return e.matmul(PBS[0:1, 0:512], ONEB[0:64, 0:1],
                                            SCb[0:64, nch * 512:(nch + 1) * 512],
                                            start=True, stop=True)
                        B.op("pe", em, r=["SC", "SC:0", "SC:1", "ONE"], w=["PBSa"])
                        def ee(e, hh=hh, nch=nch):
                            return e.activation(out=row(hh, nch * 512, 512),
                                                in_=PBS[0:1, 0:512], func=AF.Identity,
                                                scale=1.0, bias=64.0 + EPS)
                        B.op("act", ee, r=["PBSa", "CONSTS"], w=[f"R{hh}"])
                    p0 = hh * 32
                    def eqh(e, hh=hh, p0=p0):
                        return e.activation(out=ROWSB[p0:p0 + 1, 0:T], in_=row(hh),
                                            func=AF.Copy)
                    B.op("act", eqh, r=[f"R{hh}"], w=["RBQ"])
                    def eql(e, hh=hh, p0=p0):
                        return e.tensor_tensor(out=ROWSB[p0:p0 + 1, T:2 * T], in0=row(hh),
                                               in1=ROWSB[p0:p0 + 1, 0:T], op=ALU.subtract)
                    B.op("dve", eql, r=[f"R{hh}", "RBQ"], w=["RBQ2"])
                    def ew1(e, hh=hh, p0=p0):
                        return e.dma_start(out=auxs[4 * hh + 0], in_=ROWSB[p0:p0 + 1, 0:T])
                    B.dma("qsp", ew1, r=["RBQ"], w=[f"aux{4 * hh}"])
                    def ew2(e, hh=hh, p0=p0):
                        return e.dma_start(out=auxs[4 * hh + 1],
                                           in_=ROWSB[p0:p0 + 1, T:2 * T])
                    B.dma("qsp", ew2, r=["RBQ2"], w=[f"aux{4 * hh + 1}"])
                    def edq(e, qd=qd, hh=hh):
                        return e.dma_start(out=SH[66:67, qd:qd + T], in_=auxs[4 * hh + 0])
                    B.dma("qsp", edq, r=[f"aux{4 * hh}"], w=qtags)
                    def edq2(e, qd=qd, hh=hh):
                        return e.dma_start(out=SH[67:68, qd:qd + T], in_=auxs[4 * hh + 1])
                    B.dma("qsp", edq2, r=[f"aux{4 * hh + 1}"], w=qtags)
                    B.op("act", lambda e, kd=kd: e.activation(out=SCb[0:64, 0:T],
                                                              in_=SH[0:64, kd:kd + T],
                                                              func=AF.Square),
                         r=ktags, w=["SC", "SC:0", "SC:1"])
                    for nch in range(4):
                        def em(e, nch=nch):
                            return e.matmul(PBS[0:1, 512:1024], ONEB[0:64, 0:1],
                                            SCb[0:64, nch * 512:(nch + 1) * 512],
                                            start=True, stop=True)
                        B.op("pe", em, r=["SC", "SC:0", "SC:1", "ONE"], w=["PBSb"])
                        def ee(e, nch=nch):
                            return e.activation(out=row(2, nch * 512, 512),
                                                in_=PBS[0:1, 512:1024], func=AF.Identity,
                                                scale=0.25, bias=-64.0)
                        B.op("act", ee, r=["PBSb", "CONSTS"], w=["R2", "R2b"])
                    def ekh(e, kd=kd):
                        return e.activation(out=SH[64:65, kd:kd + T], in_=row(2),
                                            func=AF.Copy)
                    B.op("act", ekh, r=["R2"], w=ktags)
                    def ekl(e, kd=kd):
                        return e.tensor_tensor(out=ROWSB[64:65, 1024:1024 + T], in0=row(2),
                                               in1=SH[64:65, kd:kd + T], op=ALU.subtract)
                    B.op("dve", ekl, r=["R2"] + ktags, w=["RBK"])
                    def ew3(e, hh=hh):
                        return e.dma_start(out=auxs[4 * hh + 2],
                                           in_=ROWSB[64:65, 1024:1024 + T])
                    B.dma("qsp", ew3, r=["RBK"], w=[f"aux{4 * hh + 2}"])
                    def edk(e, kd=kd, hh=hh):
                        return e.dma_start(out=SH[65:66, kd:kd + T], in_=auxs[4 * hh + 2])
                    B.dma("qsp", edk, r=[f"aux{4 * hh + 2}"], w=ktags)
                if hp == 0 and debug:
                    def edK(e):
                        return e.dma_start(out=dumpK, in_=SH[0:68, K0:K0 + T])
                    B.dma("qsp", edK, r=sh(K0, 2048)[1], w=["dumpK"])
                    def edQ(e):
                        return e.dma_start(out=dumpQ, in_=SH[0:68, Q0:Q0 + T])
                    B.dma("qsp", edQ, r=sh(Q0, 2048)[1], w=["dumpQ"])
                    for r_ in range(4):
                        def ed1(e, r_=r_):
                            return e.dma_start(out=auxdump[r_],
                                               in_=SH[64 + r_:65 + r_, K0:K0 + T])
                        B.dma("qsp", ed1, r=sh(K0, 2048)[1], w=["auxdump"])
                        def ed2(e, r_=r_):
                            return e.dma_start(out=auxdump[4 + r_],
                                               in_=SH[64 + r_:65 + r_, Q0:Q0 + T])
                        B.dma("qsp", ed2, r=sh(Q0, 2048)[1], w=["auxdump"])
                        def ed3(e, r_=r_):
                            return e.dma_start(out=auxdump[8 + r_],
                                               in_=SH[r_:r_ + 1, K0:K0 + T])
                        B.dma("qsp", ed3, r=sh(K0, 2048)[1], w=["auxdump"])
                        def ed4(e, r_=r_):
                            return e.dma_start(out=auxdump[12 + r_],
                                               in_=SH[r_:r_ + 1, Q0:Q0 + T])
                        B.dma("qsp", ed4, r=sh(Q0, 2048)[1], w=["auxdump"])
                if hp == 0:
                    for bi in range(32):
                        def emo(e, bi=bi):
                            return e.memset(SH[:, V0 + bi * 65 + 64:V0 + bi * 65 + 65], 1.0)
                        B.op("dve", emo, w=v_tags)
                for jt in range(16):
                    for kc in range(NCH):
                        def em(e, jt=jt, kc=kc):
                            a, _ = ws(2 * 1024 + kc * 128, 128)
                            return e.matmul(PB[5][:, 0:128],
                                            XNc(kc)[:, jt * 128:(jt + 1) * 128],
                                            a, start=(kc == 0), stop=(kc == NCH - 1))
                        B.op("pe", em, r=slot_tags[2] + [f"XN:{kc}"], w=[pbt(5)])
                    for hh in range(2):
                        def emc(e, jt=jt, hh=hh):
                            off = V0 + (hh * 16 + jt) * 65
                            return e.activation(out=SH[:, off:off + 64],
                                                in_=PB[5][:, hh * 64:hh * 64 + 64],
                                                func=AF.Copy)
                        B.op("act", emc, r=[pbt(5)], w=v_tags)
                for jt in range(16):
                    def emb2(e, hp=hp):
                        return e.matmul(PB[5][:, 0:128], ONEBR[64:65, 0:128],
                                        ROWSB[64:65, hp * 128:(hp + 1) * 128],
                                        start=True, stop=True)
                    B.op("pe", emb2, r=["ONE", "RBV"], w=[pbt(5)])
                    for hh in range(2):
                        def ea(e, jt=jt, hh=hh):
                            off = V0 + (hh * 16 + jt) * 65
                            return e.tensor_tensor(out=SH[:, off:off + 64],
                                                   in0=PB[5][:, hh * 64:hh * 64 + 64],
                                                   in1=SH[:, off:off + 64], op=ALU.add)
                        B.op("dve", ea, r=[pbt(5)] + v_tags, w=v_tags)
                tiles = []
                for hh in range(2):
                    for ic in range(4):
                        for jt in range(4 * ic + 4):
                            tiles.append((hh, ic, jt))

                def oacc(g):
                    return (PB[4], pbt(4)) if g % 2 == 0 else (PBS, "PBSa")

                def epilogue(hh, ic, g):
                    acc, acct = oacc(g)
                    r0 = (g % 2) * 512
                    rt = ["R2"] if g % 2 == 0 else ["R2b"]
                    def emf(e, acc=acc, r0=r0):
                        return e.tensor_scalar(out=row(2, r0, 512),
                                               in0=acc[64:65, 0:512],
                                               scalar1=1e-35, scalar2=2.0 ** 64,
                                               op0=ALU.max, op1=ALU.mult)
                    B.op("dve", emf, r=[acct], w=rt)
                    if debug and hp == 0 and hh == 0 and ic == 0:
                        def edS1(e):
                            return e.dma_start(out=dumpS[1:2, :], in_=row(2, 0, 512))
                        B.dma("qsp", edS1, r=rt, w=["dumpS2"])
                    def er1(e, r0=r0):
                        return e.activation(out=row(2, r0, 512), in_=row(2, r0, 512),
                                            func=AF.Ln)
                    B.op("act", er1, r=rt, w=rt)
                    def er2(e, r0=r0):
                        return e.activation(out=row(2, r0, 512), in_=row(2, r0, 512),
                                            func=AF.Exp, scale=-1.0)
                    B.op("act", er2, r=rt, w=rt)
                    if debug and hp == 0 and hh == 0 and ic == 0:
                        def edS0(e):
                            return e.dma_start(out=dumpS[0:1, :], in_=row(2, 0, 512))
                        B.dma("qsp", edS0, r=rt, w=["dumpS"])
                    if g % 2 == 0:
                        bt_ap, btt = PB[5][:, 0:512], pbt(5)
                    else:
                        bt_ap, btt = PBS[:, 512:1024], "PBSb"
                    def emb3(e, r0=r0, bt_ap=bt_ap):
                        return e.matmul(bt_ap, onesrow_f(2), row(2, r0, 512),
                                        start=True, stop=True)
                    B.op("pe", emb3, r=rt + ["ONE"], w=[btt])
                    sbo = (g % 2) * 512
                    sbt = [f"SB:{g % 2}"]
                    def ecp(e, bt_ap=bt_ap, sbo=sbo):
                        return e.activation(out=SB_[:, sbo:sbo + 512], in_=bt_ap,
                                            func=AF.Identity, scale=2.0 ** 64)
                    B.op("act", ecp, r=[btt], w=sbt)
                    def emo(e, hh=hh, ic=ic, acc=acc, sbo=sbo, gh=gh):
                        off = OT0 + gh * T + ic * 512
                        pr = hh * 64
                        return e.tensor_tensor(out=SH[pr:pr + 64, off:off + 512],
                                               in0=acc[0:64, 0:512],
                                               in1=SB_[0:64, sbo:sbo + 512],
                                               op=ALU.mult)
                    B.op("dve", emo, r=[acct] + sbt, w=ot_tags[gh])

                ET3 = 10752

                def flush5(t):
                    idx, hh, ic, jt = t
                    s = idx % 4
                    ep = idx % 3
                    sa_ap = SA[:, s * 512:(s + 1) * 512]
                    sat = [f"SA:{s}"]
                    ett = sh(ET3 + ep * 512, 512)[1]
                    def e5(e, sa_ap=sa_ap, ep=ep):
                        return e.activation(
                            out=SH[:, ET3 + ep * 512:ET3 + (ep + 1) * 512],
                            in_=sa_ap, func=AF.Exp, scale=0.25, bias=-CEXP)
                    B.op("act", e5, r=sat + ["CONSTS"], w=ett)
                    if debug and hp == 0 and idx == 0:
                        def edE(e):
                            return e.dma_start(out=dumpE, in_=SH[:, ET3:ET3 + 512])
                        B.dma("qsp", edE, r=ett, w=["dumpE"])

                def flush6(t):
                    idx, hh, ic, jt = t
                    g = hh * 4 + ic
                    n = 4 * ic + 4
                    ep = idx % 3
                    ett = sh(ET3 + ep * 512, 512)[1]
                    acc, acct = oacc(g)
                    def e6(e, jt=jt, ep=ep, hh=hh, n=n, acc=acc):
                        off = V0 + (hh * 16 + jt) * 65
                        return e.matmul(acc[0:65, 0:512], SH[:, off:off + 65],
                                        SH[:, ET3 + ep * 512:ET3 + (ep + 1) * 512],
                                        start=(jt == 0), stop=(jt == n - 1))
                    B.op("pe", e6, r=v_tags + ett, w=[acct])
                    if jt == n - 1:
                        epilogue(hh, ic, g)

                pend5 = None
                pend6 = None
                for idx, (hh, ic, jt) in enumerate(tiles):
                    qd = Q0 + hh * 2048
                    kd = K0 + hh * 2048
                    qtags = sh(qd, 2048)[1]
                    ktags = sh(kd, 2048)[1]
                    pbp = idx % 2
                    s = idx % 4
                    sa_ap = SA[:, s * 512:(s + 1) * 512]
                    sc_ap = SC[:, s * 512:(s + 1) * 512]
                    sat = [f"SA:{s}"]
                    sct = [f"SC:{s}"]
                    def emd(e, jt=jt, ic=ic, qd=qd, kd=kd, pbp=pbp):
                        return e.matmul(PB[pbp][:, :],
                                        SH[0:68, kd + jt * 128:kd + (jt + 1) * 128],
                                        SH[0:68, qd + ic * 512:qd + (ic + 1) * 512],
                                        start=True, stop=True)
                    B.op("pe", emd, r=ktags + qtags, w=[pbt(pbp)])
                    def emn(e, jt=jt, ic=ic, qd=qd, kd=kd, pbp=pbp):
                        return e.matmul(PB[2 + pbp][:, :],
                                        SH[0:64, kd + jt * 128:kd + (jt + 1) * 128],
                                        SH[0:64, qd + ic * 512:qd + (ic + 1) * 512],
                                        start=True, stop=True)
                    B.op("pe", emn, r=ktags + qtags, w=[pbt(2 + pbp)])
                    def e1b(e, pbp=pbp, sa_ap=sa_ap):
                        return e.activation(out=sa_ap, in_=PB[pbp][:, :], func=AF.Ln)
                    B.op("act", e1b, r=[pbt(pbp)], w=sat)
                    def e1c(e, sa_ap=sa_ap):
                        return e.activation(out=sa_ap, in_=sa_ap, func=AF.Exp,
                                            scale=-1.0)
                    B.op("act", e1c, r=sat, w=sat)
                    def e2(e, pbp=pbp, sc_ap=sc_ap):
                        return e.activation(out=sc_ap, in_=PB[2 + pbp][:, :],
                                            func=AF.Square)
                    B.op("act", e2, r=[pbt(2 + pbp)], w=sct)
                    def e3(e, sa_ap=sa_ap, sc_ap=sc_ap):
                        return e.tensor_tensor(out=sa_ap, in0=sc_ap, in1=sa_ap,
                                               op=ALU.mult)
                    B.op("dve", e3, r=sat + sct, w=sat)
                    if jt >= 4 * ic:
                        kk = jt - 4 * ic
                        def e4(e, kk=kk, sa_ap=sa_ap):
                            return e.tensor_tensor(
                                out=sa_ap, in0=sa_ap,
                                in1=MSK[:, 256 + kk * 512:256 + (kk + 1) * 512],
                                op=ALU.add)
                        B.op("dve", e4, r=sat + ["MSK"], w=sat)
                    if pend6 is not None:
                        flush6(pend6)
                    if pend5 is not None:
                        flush5(pend5)
                    pend6, pend5 = pend5, (idx, hh, ic, jt)
                flush6(pend6)
                flush5(pend5)
                flush6(pend5)
            for m in range(NCH):
                for kci in range(2):
                    kc = 2 * grp + kci
                    sl_ap, sl_tags = ws(3 * 1024 + kci * 128, 128)
                    def emw(e, m=m, kc=kc, sl_ap=sl_ap):
                        return e.dma_start(
                            out=sl_ap,
                            in_=bw_out[kc * 128:(kc + 1) * 128, m * 128:(m + 1) * 128])
                    B.dma("qpool", emw, r=["bw_out"], w=sl_tags)
                    for sp in range(4):
                        def em(e, m=m, sp=sp, kci=kci, sl_ap=sl_ap):
                            return e.matmul(
                                PB[sp][:, :], sl_ap,
                                SH[:, OT0 + kci * T + sp * 512:OT0 + kci * T + (sp + 1) * 512],
                                start=(kci == 0), stop=(kci == 1))
                        B.op("pe", em, r=sl_tags + ot_tags[kci], w=[pbt(sp)])
                for sp in range(4):
                    if grp == 3:
                        def ea(e, m=m, sp=sp):
                            return e.scalar_tensor_tensor(
                                out=XFc(m)[:, sp * 512:(sp + 1) * 512],
                                in0=PB[sp][:, :],
                                scalar=BCOL[:, OFF_BOUT + m:OFF_BOUT + m + 1],
                                in1=XFc(m)[:, sp * 512:(sp + 1) * 512],
                                op0=ALU.add, op1=ALU.add)
                        B.op("dve", ea, r=[pbt(sp), f"XF:{m}", "BCOL"], w=[f"XF:{m}"])
                    else:
                        def ea(e, m=m, sp=sp):
                            return e.tensor_tensor(
                                out=XFc(m)[:, sp * 512:(sp + 1) * 512],
                                in0=XFc(m)[:, sp * 512:(sp + 1) * 512],
                                in1=PB[sp][:, :], op=ALU.add)
                        B.op("dve", ea, r=[pbt(sp), f"XF:{m}"], w=[f"XF:{m}"])
        stats_init()
        for c in range(NCH):
            def emw(e, c=c):
                return e.dma_start(out=xs1[c * 128:(c + 1) * 128, :], in_=XFc(c))
            B.dma("qsp", emw, r=[f"XF:{c}"], w=[f"xs1:{c}"])
            stats_chunk(lambda c=c: XFc(c), [f"XF:{c}"])
        stats_rows()
        norm_to_XF()

    attn_b()
    ffn_call(xs1, "xs1:{}", yT, "yT:{}", True)

    extra = (["xt1", "xt2p", "xt3m", "auxdump", "dumpK", "dumpQ", "dumpE", "dumpO", "dumpS"] + [f"xs1:{c}" for c in range(NCH)] + [f"xs2:{c}" for c in range(NCH)]) if debug else []
    B.op("sp", lambda e: e.nop(), r=[f"yT:{c}" for c in range(NCH)] + extra)
    import os
    if not os.environ.get("KLOOSE"):
        B.strict_ranges.append((0, len(B.ops)))
    B.finalize(nc)
    return nc


_CACHE = {}


def kernel(**inputs):
    x = np.asarray(inputs["L0_attn_a_x"], dtype=np.float32)
    nb = x.shape[0]
    rot, mska, mskb, mskc = _host_consts()

    def f32(a):
        return np.ascontiguousarray(np.asarray(a, dtype=np.float32))

    wqkv = f32(inputs["L0_ffn_wqkv"])
    perm = np.concatenate([(h * 64 + (np.arange(64) + 32) % 64) for h in range(8)])
    fwperm = np.ascontiguousarray(
        np.concatenate([wqkv[:, :, perm], wqkv[:, :, 512 + perm]], axis=2))
    b1 = f32(inputs["L0_ffn_b1"])

    common = {
        "wqT": f32(np.asarray(inputs["L0_attn_a_Wq"]).T),
        "wkvT": f32(np.asarray(inputs["L0_attn_a_Wkv_down"]).T),
        "wkupT": f32(np.asarray(inputs["L0_attn_a_Wk_up"]).T),
        "wvupT": f32(np.asarray(inputs["L0_attn_a_Wv_up"]).T),
        "woT": f32(np.asarray(inputs["L0_attn_a_Wo"]).T),
        "bw_qkv": f32(inputs["L0_attn_b_w_qkv"]),
        "bb_qkv": f32(inputs["L0_attn_b_b_qkv"]).reshape(1, -1),
        "bw_out": f32(inputs["L0_attn_b_w_out"]),
        "bb_out": f32(inputs["L0_attn_b_b_out"]).reshape(1, -1),
        "fwqkv": wqkv,
        "fwperm": fwperm,
        "fqs": f32(inputs["L0_ffn_q_scale"]).reshape(2, 1, 64),
        "fks": f32(inputs["L0_ffn_k_scale"]).reshape(2, 1, 64),
        "fwo": f32(inputs["L0_ffn_wo"]),
        "flng": f32(inputs["L0_ffn_ln_g"]).reshape(2, 1, D),
        "flnb": f32(inputs["L0_ffn_ln_b"]).reshape(2, 1, D),
        "fw1": f32(inputs["L0_ffn_w1"]),
        "fb1a": np.ascontiguousarray(b1[:, 0:2730]).reshape(2, 1, 2730),
        "fb1g": np.ascontiguousarray(b1[:, 2730:5460]).reshape(2, 1, 2730),
        "fw2": f32(inputs["L0_ffn_w2"]),
        "fb2": f32(inputs["L0_ffn_b2"]).reshape(2, 1, D),
        "rot": rot, "mska": mska, "mskb": mskb, "mskc": mskc,
        "identb": np.eye(128, dtype=BFNP),
        "onefc": _onefc(), "onebc": _onefc().astype(BFNP),
        "onefr": _onefr(), "onebr": _onebr(), "blk64": _blk64(),
    }

    debug = bool(inputs.pop("_debug", False))
    key = ("nc", debug)
    if key not in _CACHE:
        _CACHE[key] = build_program(debug)
    nc = _CACHE[key]

    in_maps = []
    for b in range(nb):
        m = dict(common)
        m["xT"] = np.ascontiguousarray(x[b].T)
        in_maps.append(m)

    res = run_bass_kernel_spmd(nc, in_maps, core_ids=list(range(nb)))
    if debug:
        taps = {nm: np.stack([np.ascontiguousarray(r[nm].T) for r in res.results])
                for nm in ("xt1", "xs2", "xs1", "xt2p", "xt3m")}
        for nm2 in ("auxdump", "dumpK", "dumpQ", "dumpE", "dumpO", "dumpS"):
            taps[nm2] = np.stack([np.asarray(r[nm2]) for r in res.results])
        taps["out"] = np.stack([np.ascontiguousarray(r["yT"].T) for r in res.results])
        return taps
    return np.stack([np.ascontiguousarray(r["yT"].T) for r in res.results])


# version 13
# speedup vs baseline: 1.2311x; 1.2311x over previous
import numpy as np
import ml_dtypes
import concourse.bass as bass
import concourse.mybir as mybir
from concourse.bass_utils import run_bass_kernel_spmd

F32 = mybir.dt.float32
BF16 = mybir.dt.bfloat16
AF = mybir.ActivationFunctionType
ALU = mybir.AluOpType
BFNP = ml_dtypes.bfloat16

T = 2048
D = 1024
NCH = 8

COMPUTE_ENGINES = ("pe", "act", "dve", "pool", "sp")
DMA_QUEUES = {"qsp": "sp", "qpool": "pool"}
ALL_ENTITIES = list(COMPUTE_ENGINES) + list(DMA_QUEUES)


class _Op:
    __slots__ = ("idx", "entity", "issuer", "emit", "reads", "writes", "waits",
                 "marked", "ent_idx", "know", "qwait")

    def __init__(self, idx, entity, issuer, emit, reads, writes):
        self.idx = idx
        self.entity = entity
        self.issuer = issuer
        self.emit = emit
        self.reads = reads
        self.writes = writes
        self.waits = []
        self.marked = False
        self.ent_idx = -1
        self.know = None


class Builder:
    def __init__(self):
        self.ops = []
        self.last_write = {}
        self.readers = {}
        self.strict_ranges = []

    def in_strict(self, idx):
        return any(lo <= idx < hi for lo, hi in self.strict_ranges)

    def op(self, eng, emit, r=(), w=()):
        o = _Op(len(self.ops), eng, eng, emit, tuple(r), tuple(w))
        self.ops.append(o)
        return o

    def dma(self, queue, emit, r=(), w=()):
        o = _Op(len(self.ops), queue, DMA_QUEUES[queue], emit, tuple(r), tuple(w))
        self.ops.append(o)
        return o

    def finalize(self, nc):
        for o in self.ops:
            deps = []
            for t in o.reads:
                p = self.last_write.get(t)
                if p is not None:
                    deps.append(p)
            for t in o.writes:
                p = self.last_write.get(t)
                if p is not None:
                    deps.append(p)
                deps.extend(self.readers.get(t, ()))
            strict = self.in_strict(o.idx)
            seen = {}
            for p in deps:
                if (p.entity == o.entity and o.entity not in DMA_QUEUES
                        and not strict):
                    continue
                if p is o:
                    continue
                cur = seen.get(p.entity)
                if cur is None or p.idx > cur.idx:
                    seen[p.entity] = p
            o.waits = list(seen.values())
            for t in o.reads:
                self.readers.setdefault(t, []).append(o)
            for t in o.writes:
                self.last_write[t] = o
                self.readers[t] = []

        import os as _os
        if _os.environ.get("KSERIAL"):
            for i in range(1, len(self.ops)):
                o = self.ops[i]
                p = self.ops[i - 1]
                if all(q is not p for q in o.waits):
                    o.waits.append(p)

        def dma_key(o):
            return (o.entity, o.writes)

        key_rank = {}
        key_id = {}
        for o in self.ops:
            if o.entity in DMA_QUEUES:
                o.marked = True
                k = dma_key(o)
                if k not in key_id:
                    key_id[k] = len(key_id)
                key_rank[k] = key_rank.get(k, 0) + 1
                o.ent_idx = key_rank[k]
                o.entity = f"D{key_id[k]}_{o.entity}"

        for o in self.ops:
            seen = {}
            for p in o.waits:
                cur = seen.get(p.entity)
                if cur is None or p.idx > cur.idx:
                    seen[p.entity] = p
            o.waits = list(seen.values())

        issuer_know = {e: {} for e in COMPUTE_ENGINES}
        for o in self.ops:
            kn = issuer_know[o.issuer]
            pruned = []
            for p in sorted(o.waits, key=lambda q: q.idx):
                if kn.get(p.entity, -1) >= p.idx:
                    continue
                pruned.append(p)
                p.marked = True
                if p.know:
                    for e2, v in p.know.items():
                        if kn.get(e2, -1) < v:
                            kn[e2] = v
                if kn.get(p.entity, -1) < p.idx:
                    kn[p.entity] = p.idx
            o.waits = pruned
            snap = dict(kn)
            snap[o.entity] = o.idx
            o.know = snap

        cnt = {}
        for o in self.ops:
            if o.issuer == o.entity or o.entity in COMPUTE_ENGINES:
                if o.marked:
                    cnt[o.entity] = cnt.get(o.entity, 0) + 1
                o.ent_idx = cnt.get(o.entity, 0)

        entities = sorted(set(o.entity for o in self.ops)
                          | set(e for o in self.ops for e in
                                [p.entity for p in o.waits]))
        sems = {e: nc.alloc_semaphore(f"sem_{e}"[:48]) for e in entities}
        streams = {e: [] for e in COMPUTE_ENGINES}
        for o in self.ops:
            streams[o.issuer].append(o)

        def amt(e):
            return 1 if e in COMPUTE_ENGINES else 16

        def run(ns, issuer):
            for o in streams[issuer]:
                for p in o.waits:
                    ns.wait_ge(sems[p.entity], p.ent_idx * amt(p.entity))
                instr = o.emit(ns)
                if o.marked:
                    instr.then_inc(sems[o.entity], amt(o.entity))

        with nc.Block() as block:
            @block.tensor
            def _(e):
                run(e, "pe")

            @block.scalar
            def _(e):
                run(e, "act")

            @block.vector
            def _(e):
                run(e, "dve")

            @block.gpsimd
            def _(e):
                run(e, "pool")

            @block.sync
            def _(e):
                run(e, "sp")


def _onefc():
    a = np.zeros((128, 4), np.float32)
    a[:, 0] = 1.0
    a[:, 1] = 1.0 / 1024.0
    return a


def _onefr():
    a = np.zeros((128, 128), np.float32)
    for p in (0, 32, 64):
        a[p, :] = 1.0
    return a


def _onebr():
    a = np.zeros((128, 384), np.float64)
    a[0, 0:128] = 1.0
    a[64, 0:128] = 1.0
    a[0, 128:192] = 1.0
    a[0, 320:384] = 1.0
    return a.astype(BFNP)


def _blk64():
    a = np.zeros((128, 128), np.float64)
    a[0:64, 0:64] = 1.0
    a[64:128, 64:128] = 1.0
    return a.astype(BFNP)


def _host_consts():
    DH, WIN = 64, 512
    half = np.arange(0, DH, 2, dtype=np.float64)
    inv_freq = 1.0 / 10000.0 ** (half / DH)
    t = np.arange(T, dtype=np.float64)
    freqs = np.concatenate([t[:, None] * inv_freq[None, :]] * 2, axis=-1)
    base = (half + 0.4 * DH) / (1.4 * DH)
    power = (t - T // 2) / float(WIN // 2)
    scale = np.concatenate([base[None, :] ** power[:, None]] * 2, axis=-1)
    cos, sin = np.cos(freqs), np.sin(freqs)
    sgn = np.where(np.arange(DH) < 32, -1.0, 1.0)[None, :]

    def dup(a):
        return np.concatenate([a.T, a.T], axis=0).astype(BFNP)

    rot = np.stack([dup(cos * scale), dup(sgn * sin * scale),
                    dup(cos / scale), dup(sgn * sin / scale)])

    p = np.arange(128)[:, None]
    c = np.arange(128)[None, :]
    mska = np.where(p >= c, 0.0, -1e30).astype(BFNP)
    mskb = np.where(p <= c, 0.0, -1e30).astype(BFNP)
    c2 = np.arange(512)[None, :]
    mskc = np.zeros((4, 128, 512), dtype=np.float64)
    for k in range(4):
        mskc[k] = np.where(128 * k + p > c2, -4e9, 0.0)
    mskc = mskc.reshape(512, 512).astype(BFNP)
    return rot, mska, mskb, mskc


def build_program(debug=False):
    nc = bass.Bass("TRN2", target_bir_lowering=False)
    B = Builder()

    def din(name, shape, dt=F32):
        return nc.dram_tensor(name, shape, dt, kind="ExternalInput").ap()

    xin = din("xT", [D, T])
    wqT = din("wqT", [D, 2048])
    wkvT = din("wkvT", [D, 64])
    wkupT = din("wkupT", [64, 2048])
    wvupT = din("wvupT", [64, 2048])
    woT = din("woT", [2048, D])
    bw_qkv = din("bw_qkv", [D, 3072])
    bb_qkv = din("bb_qkv", [1, 3072])
    bw_out = din("bw_out", [D, D])
    bb_out = din("bb_out", [1, D])
    fwqkv = din("fwqkv", [2, D, 1536])
    fwperm = din("fwperm", [2, D, 1024])
    fqs = din("fqs", [2, 1, 64])
    fks = din("fks", [2, 1, 64])
    fwo = din("fwo", [2, 512, D])
    flng = din("flng", [2, 1, D])
    flnb = din("flnb", [2, 1, D])
    fw1 = din("fw1", [2, D, 5460])
    fb1a = din("fb1a", [2, 1, 2730])
    fb1g = din("fb1g", [2, 1, 2730])
    fw2 = din("fw2", [2, 2730, D])
    fb2 = din("fb2", [2, 1, D])
    rot_t = din("rot", [4, 128, T], BF16)
    mska_t = din("mska", [128, 128], BF16)
    mskb_t = din("mskb", [128, 128], BF16)
    mskc_t = din("mskc", [512, 512], BF16)
    identb_t = din("identb", [128, 128], BF16)
    onefc_t = din("onefc", [128, 4], F32)
    onebc_t = din("onebc", [128, 4], BF16)
    onefr_t = din("onefr", [128, 128], F32)
    onebr_t = din("onebr", [128, 384], BF16)
    blk64_t = din("blk64", [128, 128], BF16)

    yT = nc.dram_tensor("yT", [D, T], F32, kind="ExternalOutput").ap()
    kind_dbg = "ExternalOutput" if debug else "Internal"
    xs1 = nc.dram_tensor("xs1", [D, T], F32, kind=kind_dbg).ap()
    auxs = nc.dram_tensor("auxs", [8, T], BF16, kind="Internal").ap()
    auxdump = nc.dram_tensor("auxdump", [16, T], BF16, kind="ExternalOutput").ap() if debug else None
    dumpK = nc.dram_tensor("dumpK", [68, T], BF16, kind="ExternalOutput").ap() if debug else None
    dumpQ = nc.dram_tensor("dumpQ", [68, T], BF16, kind="ExternalOutput").ap() if debug else None
    dumpE = nc.dram_tensor("dumpE", [128, 512], BF16, kind="ExternalOutput").ap() if debug else None
    dumpO = nc.dram_tensor("dumpO", [128, 2 * T], BF16, kind="ExternalOutput").ap() if debug else None
    dumpS = nc.dram_tensor("dumpS", [2, 512], F32, kind="ExternalOutput").ap() if debug else None
    xs2 = nc.dram_tensor("xs2", [D, T], F32, kind=kind_dbg).ap()
    xt1 = nc.dram_tensor("xt1", [D, T], F32, kind=kind_dbg).ap() if debug else None
    xt2p = nc.dram_tensor("xt2p", [D, T], F32, kind=kind_dbg).ap() if debug else None
    xt3m = nc.dram_tensor("xt3m", [D, T], F32, kind=kind_dbg).ap() if debug else None

    def sbt(name, shape, dt):
        return nc.alloc_sbuf_tensor(name, shape, dt)

    XN = sbt("XN", [128, 8 * T], BF16)
    XF = sbt("XF", [128, 8 * T], F32)
    XFb = XF.bitcast(BF16)
    SH = sbt("SH", [128, 16384], BF16)
    WS = sbt("WS", [128, 8192], BF16)
    SA = sbt("SA", [128, T], F32)
    SB_ = sbt("SBs", [128, T], F32)
    SC = sbt("SCs", [128, T], F32)
    SCb = SC.bitcast(BF16)
    XL = sbt("XL", [128, 2 * T], F32)
    ROTB = XL.bitcast(BF16)
    ROWS = sbt("ROWS", [128, T], F32)
    ROWSB = sbt("ROWSB", [128, 2 * T], BF16)
    MSK = sbt("MSK", [128, 2304], BF16)
    BCOL = sbt("BCOL", [128, 192], F32)
    ONEF = sbt("ONEF", [128, 4], F32)
    ONEB = sbt("ONEB", [128, 4], BF16)
    ONEFR = sbt("ONEFR", [128, 128], F32)
    ONEBR = sbt("ONEBR", [128, 384], BF16)
    IDENTB = sbt("IDENTB", [128, 128], BF16)
    BLK64 = sbt("BLK64", [128, 128], BF16)
    ZEROB = sbt("ZEROB", [128, 4], BF16)

    PB = [nc.alloc_psum_tensor(f"PB{i}", [128, 512], F32) for i in range(6)]
    PBS = nc.alloc_psum_tensor("PBS", [128, 1024], F32)

    CONSTS = sbt("CONSTS", [128, 4], F32)
    CONST_VALS = (64.0 + 1e-6, -64.0, -70.0)
    for j, v in enumerate(CONST_VALS):
        nc.const_aps.register(F32, v, CONSTS[:, j:j + 1])

    def pbt(i):
        return f"PB{i}"

    def sh(c0, n, rows=None):
        ap = SH[:, c0:c0 + n] if rows is None else SH[rows[0]:rows[1], c0:c0 + n]
        return ap, [f"SH:{i}" for i in range(c0 // 512, (c0 + n + 511) // 512)]

    def ws(c0, n, rows=None):
        ap = WS[:, c0:c0 + n] if rows is None else WS[rows[0]:rows[1], c0:c0 + n]
        return ap, [f"WS:{i}" for i in range(c0 // 128, (c0 + n + 127) // 128)]

    def XFc(c):
        return XF[:, c * T:(c + 1) * T]

    def XNc(c):
        return XN[:, c * T:(c + 1) * T]

    ROWP = (0, 32, 64)

    def row(i, off=0, n=T):
        p = ROWP[i]
        return ROWS[p:p + 1, off:off + n]

    def onesrow_f(i):
        return ONEFR[ROWP[i]:ROWP[i] + 1, 0:128]

    def rowb(i, off=0, n=T):
        return ROWSB[0:1, i * T + off:i * T + off + n]

    BVROW = ROWSB

    OFF_BQKV, OFF_BOUT, OFF_B2 = 0, 24, 32
    OFF_LNG, OFF_LNB = 48, 64
    OFF_B1 = 80
    OFF_QS, OFF_KS, OFF_QSP, OFF_KSP, OFF_MBK = 168, 170, 172, 174, 176

    B.dma("qsp", lambda e: e.dma_start(out=MSK[:, 0:128], in_=mska_t), w=["MSK"])
    B.dma("qsp", lambda e: e.dma_start(out=MSK[:, 128:256], in_=mskb_t), w=["MSK"])
    B.dma("qsp", lambda e: e.dma_start(
        out=MSK[:, 256:2304].rearrange("p (k n) -> p k n", k=4),
        in_=mskc_t.rearrange("(k p) n -> p k n", p=128)), w=["MSK"])
    B.dma("qsp", lambda e: e.dma_start(out=IDENTB[:], in_=identb_t), w=["IDENTB"])
    B.op("dve", lambda e: e.memset(ZEROB[:], 0.0), w=["ZEROB"])
    B.dma("qpool", lambda e: e.dma_start(out=ROWSB[64:65, 0:1024],
                                         in_=bb_qkv[:, 2048:3072]),
          r=["bb_qkv"], w=["RBV"])
    B.dma("qsp", lambda e: e.dma_start(out=ONEF[:], in_=onefc_t), w=["ONE"])
    B.dma("qsp", lambda e: e.dma_start(out=ONEB[:], in_=onebc_t), w=["ONE"])
    B.dma("qsp", lambda e: e.dma_start(out=ONEFR[:], in_=onefr_t), w=["ONE"])
    B.dma("qsp", lambda e: e.dma_start(out=ONEBR[:], in_=onebr_t), w=["ONE"])
    B.dma("qsp", lambda e: e.dma_start(out=BLK64[:], in_=blk64_t), w=["BLK"])

    for j, v in enumerate(CONST_VALS):
        B.op("dve", lambda e, j=j, v=v: e.memset(CONSTS[:, j:j + 1], v), w=["CONSTS"])

    def load_row_to_cols(src_ap, src_tag, n, bcol_off):
        for s0 in range(0, n, 2048):
            seg = min(2048, n - s0)
            def eml(e, s0=s0, seg=seg):
                return e.dma_start(out=SA[0:1, 0:seg], in_=src_ap[:, s0:s0 + seg])
            B.dma("qsp", eml, r=[src_tag], w=["SA"])
            for c in range((seg + 127) // 128):
                wdt = min(128, seg - c * 128)
                def emm(e, c=c, wdt=wdt):
                    return e.matmul(PB[5][0:wdt, 0:1], SA[0:1, c * 128:c * 128 + wdt],
                                    ONEFR[0:1, 0:1], start=True, stop=True)
                B.op("pe", emm, r=["SA", "ONE"], w=[pbt(5)])
                def emv(e, c=c, wdt=wdt, s0=s0):
                    col = bcol_off + s0 // 128 + c
                    return e.tensor_copy(out=BCOL[0:wdt, col:col + 1],
                                         in_=PB[5][0:wdt, 0:1])
                B.op("dve", emv, r=[pbt(5)], w=["BCOL"])

    load_row_to_cols(bb_qkv, "bb_qkv", 3072, OFF_BQKV)
    load_row_to_cols(bb_out, "bb_out", D, OFF_BOUT)
    for l in range(2):
        load_row_to_cols(fb1a[l], "fb1a", 2730, OFF_B1 + 44 * l)
        load_row_to_cols(fb1g[l], "fb1g", 2730, OFF_B1 + 44 * l + 22)
        load_row_to_cols(fb2[l], "fb2", D, OFF_B2 + 8 * l)
        load_row_to_cols(flng[l], "flng", D, OFF_LNG + 8 * l)
        load_row_to_cols(flnb[l], "flnb", D, OFF_LNB + 8 * l)
        load_row_to_cols(fqs[l], "fqs", 64, OFF_QS + l)
        load_row_to_cols(fks[l], "fks", 64, OFF_KS + l)
    for l in range(2):
        for src, dst in ((OFF_QS + l, OFF_QSP + l), (OFF_KS + l, OFF_KSP + l)):
            def e1(e, src=src, dst=dst):
                return e.tensor_copy(out=BCOL[0:32, dst:dst + 1],
                                     in_=BCOL[32:64, src:src + 1])
            B.op("dve", e1, r=["BCOL"], w=["BCOL"])
            def e2(e, src=src, dst=dst):
                return e.tensor_copy(out=BCOL[32:64, dst:dst + 1],
                                     in_=BCOL[0:32, src:src + 1])
            B.op("dve", e2, r=["BCOL"], w=["BCOL"])
        for cc in (OFF_QS + l, OFF_KS + l, OFF_QSP + l, OFF_KSP + l):
            def e3(e, cc=cc):
                return e.tensor_copy(out=BCOL[64:128, cc:cc + 1],
                                     in_=BCOL[0:64, cc:cc + 1])
            B.op("dve", e3, r=["BCOL"], w=["BCOL"])
    for hp in range(8):
        def e4(e, hp=hp):
            return e.activation(out=BCOL[:, OFF_MBK + hp:OFF_MBK + hp + 1],
                                in_=BCOL[:, OFF_BQKV + 8 + hp:OFF_BQKV + 8 + hp + 1],
                                func=AF.Identity, scale=-2.0)
        B.op("act", e4, r=["BCOL"], w=["BCOL"])

    def stats_init():
        B.op("dve", lambda e: e.memset(row(0), 0.0),
             w=["R0"] + [f"R0c{i}" for i in range(4)])
        B.op("dve", lambda e: e.memset(row(1), 0.0),
             w=["R1"] + [f"R1c{i}" for i in range(4)])

    def stats_from_bf16(src_ap_fn, src_tags):
        B.op("act", lambda e: e.activation(out=SCb[:, T:2 * T], in_=src_ap_fn(),
                                           func=AF.Square), r=list(src_tags), w=["SC"])
        for nch in range(4):
            def em1(e, nch=nch, fn=src_ap_fn):
                full = fn()
                return e.matmul(PBS[0:1, 0:512], ONEB[:, 1:2],
                                full[:, nch * 512:(nch + 1) * 512], start=True, stop=True)
            B.op("pe", em1, r=list(src_tags) + ["ONE"], w=["PBSa"])
            def em2(e, nch=nch):
                return e.matmul(PBS[0:1, 512:1024], ONEB[:, 1:2],
                                SCb[:, T + nch * 512:T + (nch + 1) * 512],
                                start=True, stop=True)
            B.op("pe", em2, r=["SC", "ONE"], w=["PBSb"])
            def ev1(e, nch=nch):
                return e.tensor_tensor(out=row(0, nch * 512, 512),
                                       in0=row(0, nch * 512, 512),
                                       in1=PBS[0:1, 0:512], op=ALU.add)
            B.op("dve", ev1, r=[f"R0c{nch}", "PBSa"], w=["R0", f"R0c{nch}"])
            def ev2(e, nch=nch):
                return e.tensor_tensor(out=row(1, nch * 512, 512),
                                       in0=row(1, nch * 512, 512),
                                       in1=PBS[0:1, 512:1024], op=ALU.add)
            B.op("dve", ev2, r=[f"R1c{nch}", "PBSb"], w=["R1", f"R1c{nch}"])

    def stats_chunk(src_ap_fn, src_tags):
        B.op("act", lambda e: e.activation(out=SCb[:, 0:T], in_=src_ap_fn(),
                                           func=AF.Copy), r=list(src_tags), w=["SC"])
        stats_from_bf16(lambda: SCb[:, 0:T], ["SC"])

    def stats_rows(eps=1e-5):
        for nch in range(4):
            s0 = nch * 512
            def ev1(e, s0=s0):
                return e.tensor_tensor(out=SA[0:1, s0:s0 + 512], in0=row(0, s0, 512),
                                       in1=row(0, s0, 512), op=ALU.mult)
            B.op("dve", ev1, r=[f"R0c{nch}"], w=["SA"])
            def ev2(e, s0=s0):
                return e.activation(out=SB_[0:1, s0:s0 + 512], in_=row(1, s0, 512),
                                    func=AF.Copy)
            B.op("act", ev2, r=[f"R1c{nch}"], w=["SB"])
            def ev3(e, s0=s0):
                return e.scalar_tensor_tensor(out=row(1, s0, 512),
                                              in0=SB_[0:1, s0:s0 + 512], scalar=eps,
                                              in1=SA[0:1, s0:s0 + 512],
                                              op0=ALU.add, op1=ALU.subtract)
            B.op("dve", ev3, r=["SA", "SB"], w=["R1", f"R1c{nch}"])
            def ev4(e, s0=s0):
                return e.activation(out=row(1, s0, 512), in_=row(1, s0, 512),
                                    func=AF.Ln)
            B.op("act", ev4, r=[f"R1c{nch}"], w=["R1", f"R1c{nch}"])
            def ev5(e, s0=s0):
                return e.activation(out=row(1, s0, 512), in_=row(1, s0, 512),
                                    func=AF.Exp, scale=-0.5)
            B.op("act", ev5, r=[f"R1c{nch}"], w=["R1", f"R1c{nch}"])

    def normalize(dst_fn, src_fn=None, gain_bias=None):
        for nch in range(4):
            def emb(e, nch=nch):
                return e.matmul(PB[4][:, 0:512], onesrow_f(0),
                                row(0, nch * 512, 512), start=True, stop=True)
            B.op("pe", emb, r=[f"R0c{nch}", "ONE"], w=[pbt(4)])
            def emb2(e, nch=nch):
                return e.matmul(PB[5][:, 0:512], onesrow_f(1),
                                row(1, nch * 512, 512), start=True, stop=True)
            B.op("pe", emb2, r=[f"R1c{nch}", "ONE"], w=[pbt(5)])
            for c in range(NCH):
                if src_fn is None:
                    src_ap = XFc(c)[:, nch * 512:(nch + 1) * 512]
                    src_tags, shifted = [f"XF:{c}"], False
                else:
                    src_ap, src_tags, shifted = src_fn(c, nch)
                if not shifted:
                    def em1(e, src_ap=src_ap):
                        return e.tensor_tensor(out=SA[:, 0:512], in0=src_ap,
                                               in1=PB[4][:, 0:512], op=ALU.subtract)
                    B.op("dve", em1, r=src_tags + [pbt(4)], w=["SA"])
                else:
                    def em1(e, src_ap=src_ap):
                        return e.tensor_tensor(out=SA[:, 1:512], in0=src_ap,
                                               in1=PB[4][:, 1:512], op=ALU.subtract)
                    B.op("dve", em1, r=src_tags + [pbt(4)], w=["SA"])
                    def em0(e):
                        return e.tensor_scalar(out=SA[:, 0:1], in0=PB[4][:, 0:1],
                                               scalar1=-1.0, scalar2=None, op0=ALU.mult)
                    B.op("dve", em0, r=[pbt(4)], w=["SA"])
                dst_ap, dst_tags = dst_fn(c, nch)
                if gain_bias is None:
                    def em2(e, dst_ap=dst_ap):
                        return e.tensor_tensor(out=dst_ap, in0=SA[:, 0:512],
                                               in1=PB[5][:, 0:512], op=ALU.mult)
                    B.op("dve", em2, r=["SA", pbt(5)], w=dst_tags)
                else:
                    g_ap, b_ap = gain_bias(c)
                    def em2(e):
                        return e.tensor_tensor(out=SB_[:, 0:512], in0=SA[:, 0:512],
                                               in1=PB[5][:, 0:512], op=ALU.mult)
                    B.op("dve", em2, r=["SA", pbt(5)], w=["SB"])
                    def em3(e, dst_ap=dst_ap, g_ap=g_ap, b_ap=b_ap):
                        return e.activation(out=dst_ap, in_=SB_[:, 0:512],
                                            func=AF.Identity, scale=g_ap, bias=b_ap)
                    B.op("act", em3, r=["SB", "BCOL"], w=dst_tags)

    def norm_to_XN():
        normalize(lambda c, nch: (XNc(c)[:, nch * 512:(nch + 1) * 512], [f"XN:{c}"]))

    def norm_to_XF():
        normalize(lambda c, nch: (XFc(c)[:, nch * 512:(nch + 1) * 512], [f"XF:{c}"]))

    stats_init()
    for c in range(NCH):
        def eml(e, c=c):
            return e.dma_start(out=XFc(c), in_=xin[c * 128:(c + 1) * 128, :])
        B.dma("qsp", eml, r=["xin"], w=[f"XF:{c}"])
        stats_chunk(lambda c=c: XFc(c), [f"XF:{c}"])
    stats_rows()
    norm_to_XN()

    def attn_a():
        QT0, KT0, VT0, ET0, KV0 = 0, 4096, 8192, 13312, 14336
        kv_ap, kv_tags = ws(7168, 512)
        B.dma("qpool", lambda e: e.dma_start(
            out=kv_ap.rearrange("p (c n) -> p c n", c=8),
            in_=wkvT.rearrange("(c p) n -> p c n", p=128)),
            r=["wkvT"], w=kv_tags)
        for nch in range(4):
            for kc in range(NCH):
                def em(e, nch=nch, kc=kc):
                    a, _ = ws(7168 + kc * 64, 64)
                    return e.matmul(PB[0][0:64, :], a,
                                    XNc(kc)[:, nch * 512:(nch + 1) * 512],
                                    start=(kc == 0), stop=(kc == NCH - 1))
                B.op("pe", em, r=kv_tags + [f"XN:{kc}"], w=[pbt(0)])
            def eme(e, nch=nch):
                a, tg = sh(KV0 + nch * 512, 512, rows=(0, 64))
                return e.activation(out=a, in_=PB[0][0:64, :], func=AF.Copy)
            B.op("act", eme, r=[pbt(0)], w=sh(KV0, 2048)[1])
        kup_ap, kup_tags = ws(0, 2048, rows=(0, 64))
        B.dma("qpool", lambda e: e.dma_start(out=kup_ap, in_=wkupT),
              r=["wkupT"], w=kup_tags)
        vup_ap, vup_tags = ws(2048, 2048, rows=(0, 64))
        B.dma("qpool", lambda e: e.dma_start(out=vup_ap, in_=wvupT),
              r=["wvupT"], w=vup_tags)
        vt_tags = sh(VT0, 4112)[1]
        for jt in range(16):
            def emo(e, jt=jt):
                return e.memset(SH[:, VT0 + jt * 257 + 256:VT0 + jt * 257 + 257], 1.0)
            B.op("dve", emo, w=vt_tags)

        kvs_tags = sh(KV0, 2048)[1]
        qt_tags = sh(QT0, 4096)[1]
        kt_tags = sh(KT0, 4096)[1]
        for h in range(8):
            wq_ap, wq_tags = ws(4096, 2048)
            def emw(e, h=h):
                return e.dma_start(
                    out=wq_ap.rearrange("p (c n) -> p c n", c=8),
                    in_=wqT.rearrange("(c p) n -> p c n", p=128)[:, :, h * 256:(h + 1) * 256])
            B.dma("qpool", emw, r=["wqT"], w=wq_tags)
            for m in range(2):
                for nch in range(4):
                    for kc in range(NCH):
                        def em(e, m=m, nch=nch, kc=kc):
                            a, _ = ws(4096 + kc * 256 + m * 128, 128)
                            return e.matmul(PB[m][:, :], a,
                                            XNc(kc)[:, nch * 512:(nch + 1) * 512],
                                            start=(kc == 0), stop=(kc == NCH - 1))
                        B.op("pe", em, r=wq_tags + [f"XN:{kc}"], w=[pbt(m)])
                    def eme(e, m=m, nch=nch):
                        return e.activation(
                            out=SH[:, QT0 + m * T + nch * 512:QT0 + m * T + (nch + 1) * 512],
                            in_=PB[m][:, :], func=AF.Copy)
                    B.op("act", eme, r=[pbt(m)], w=qt_tags)
                for nch in range(4):
                    def em(e, m=m, nch=nch, h=h):
                        a, _ = ws(h * 256 + m * 128, 128, rows=(0, 64))
                        return e.matmul(PB[2][:, :], a,
                                        SH[0:64, KV0 + nch * 512:KV0 + (nch + 1) * 512],
                                        start=True, stop=True)
                    B.op("pe", em, r=kup_tags + kvs_tags, w=[pbt(2)])
                    def eme(e, m=m, nch=nch):
                        return e.activation(
                            out=SH[:, KT0 + m * T + nch * 512:KT0 + m * T + (nch + 1) * 512],
                            in_=PB[2][:, :], func=AF.Copy)
                    B.op("act", eme, r=[pbt(2)], w=kt_tags)
            for jt in range(16):
                def em(e, jt=jt, h=h):
                    a, _ = ws(2048 + h * 256, 256, rows=(0, 64))
                    return e.matmul(PB[3][:, 0:256],
                                    SH[0:64, KV0 + jt * 128:KV0 + (jt + 1) * 128],
                                    a, start=True, stop=True)
                B.op("pe", em, r=kvs_tags + vup_tags, w=[pbt(3)])
                def eme(e, jt=jt):
                    return e.activation(out=SH[:, VT0 + jt * 257:VT0 + jt * 257 + 256],
                                        in_=PB[3][:, 0:256], func=AF.Copy)
                B.op("act", eme, r=[pbt(3)], w=vt_tags)
            for ic in range(4):
                for jt in range(16):
                    pb = jt % 2
                    for kc in range(2):
                        def em(e, pb=pb, kc=kc, jt=jt, ic=ic):
                            return e.matmul(
                                PB[pb][:, :],
                                SH[:, KT0 + kc * T + jt * 128:KT0 + kc * T + (jt + 1) * 128],
                                SH[:, QT0 + kc * T + ic * 512:QT0 + kc * T + (ic + 1) * 512],
                                start=(kc == 0), stop=(kc == 1))
                        B.op("pe", em, r=kt_tags + qt_tags, w=[pbt(pb)])
                    def eme(e, pb=pb):
                        return e.activation(out=SH[:, ET0 + pb * 512:ET0 + (pb + 1) * 512],
                                            in_=PB[pb][:, :], func=AF.Exp, scale=1.0 / 16.0)
                    B.op("act", eme, r=[pbt(pb)], w=sh(ET0 + pb * 512, 512)[1])
                    for mt in range(3):
                        o0, wdt, opb = ((0, 128, 3), (128, 128, 4), (256, 1, 5))[mt]
                        def em(e, jt=jt, o0=o0, wdt=wdt, opb=opb, pb=pb):
                            return e.matmul(
                                PB[opb][0:(128 if wdt == 128 else 1), 0:512],
                                SH[:, VT0 + jt * 257 + o0:VT0 + jt * 257 + o0 + wdt],
                                SH[:, ET0 + pb * 512:ET0 + (pb + 1) * 512],
                                start=(jt == 0), stop=(jt == 15))
                        B.op("pe", em, r=vt_tags + sh(ET0 + pb * 512, 512)[1], w=[pbt(opb)])
                def emf(e):
                    return e.tensor_scalar(out=row(2, 0, 512), in0=PB[5][0:1, 0:512],
                                           scalar1=1e-35, scalar2=None, op0=ALU.max)
                B.op("dve", emf, r=[pbt(5)], w=["R2"])
                B.op("act", lambda e: e.activation(out=row(2, 0, 512), in_=row(2, 0, 512),
                                                   func=AF.Ln), r=["R2"], w=["R2"])
                B.op("act", lambda e: e.activation(out=row(2, 0, 512), in_=row(2, 0, 512),
                                                   func=AF.Exp, scale=-1.0),
                     r=["R2"], w=["R2"])
                def emb(e):
                    return e.matmul(PB[2][:, 0:512], onesrow_f(2), row(2, 0, 512),
                                    start=True, stop=True)
                B.op("pe", emb, r=["R2", "ONE"], w=[pbt(2)])
                B.op("act", lambda e: e.activation(out=SA[:, 0:512], in_=PB[2][:, 0:512],
                                                   func=AF.Copy), r=[pbt(2)], w=["SA"])
                for mt in range(2):
                    def em(e, mt=mt, ic=ic, h=h):
                        oc = 2 * h + mt
                        return e.tensor_tensor(
                            out=XFb[:, oc * T + ic * 512:oc * T + ic * 512 + 512],
                            in0=PB[3 + mt][:, 0:512], in1=SA[:, 0:512], op=ALU.mult)
                    B.op("dve", em, r=[pbt(3 + mt), "SA"], w=[f"XF:{h}"])
        stats_init()
        for m in range(NCH):
            def eml(e, m=m):
                return e.dma_start(out=XL[:, (m % 2) * T:(m % 2) * T + T],
                                   in_=xin[m * 128:(m + 1) * 128, :])
            B.dma("qsp", eml, r=["xin"], w=[f"XL:{m % 2}"])
            for kc in range(16):
                sl_ap, sl_tags = ws(6144 + (kc % 2) * 128, 128)
                def emw(e, m=m, kc=kc, sl_ap=sl_ap):
                    return e.dma_start(out=sl_ap,
                                       in_=woT[kc * 128:(kc + 1) * 128, m * 128:(m + 1) * 128])
                B.dma("qpool", emw, r=["woT"], w=sl_tags)
                for sp in range(4):
                    def em(e, m=m, kc=kc, sp=sp, sl_ap=sl_ap):
                        return e.matmul(PB[sp][:, :], sl_ap,
                                        XFb[:, kc * T + sp * 512:kc * T + sp * 512 + 512],
                                        start=(kc == 0), stop=(kc == 15))
                    B.op("pe", em, r=sl_tags + [f"XF:{kc // 2}"], w=[pbt(sp)])
            for sp in range(4):
                def emr(e, m=m, sp=sp):
                    return e.tensor_tensor(
                        out=SB_[:, sp * 512:(sp + 1) * 512], in0=PB[sp][:, :],
                        in1=XL[:, (m % 2) * T + sp * 512:(m % 2) * T + (sp + 1) * 512],
                        op=ALU.add)
                B.op("dve", emr, r=[pbt(sp), f"XL:{m % 2}"], w=["SB"])
            def emw2(e, m=m):
                return e.dma_start(out=xs1[m * 128:(m + 1) * 128, :], in_=SB_[:, 0:T])
            B.dma("qsp", emw2, r=["SB"], w=[f"xs1:{m}"])
            stats_chunk(lambda: SB_[:, 0:T], ["SB"])
        stats_rows()
        for c in range(NCH):
            def eml(e, c=c):
                return e.dma_start(out=XFc(c), in_=xs1[c * 128:(c + 1) * 128, :])
            B.dma("qsp", eml, r=[f"xs1:{c}"], w=[f"XF:{c}"])
            if debug:
                def emt(e, c=c):
                    return e.dma_start(out=xt1[c * 128:(c + 1) * 128, :], in_=XFc(c))
                B.dma("qsp", emt, r=[f"XF:{c}"], w=["xt1"])
        norm_to_XF()

    attn_a()

    def shift_to_XN():
        for c in range(NCH):
            if c < 4:
                def em(e, c=c):
                    return e.activation(out=XNc(c), in_=XFc(c), func=AF.Copy)
                B.op("act", em, r=[f"XF:{c}"], w=[f"XN:{c}"])
            else:
                def em(e, c=c):
                    return e.activation(out=XNc(c)[:, 1:T], in_=XFc(c)[:, 0:T - 1],
                                        func=AF.Copy)
                B.op("act", em, r=[f"XF:{c}"], w=[f"XN:{c}"])
                B.op("dve", lambda e, c=c: e.memset(XNc(c)[:, 0:1], 0.0), w=[f"XN:{c}"])

    def ffn_layer(l):
        QQ0, KK0, VS0, ET0, OT0 = 0, 2048, 4096, 7168, 8192
        qq_tags = sh(QQ0, 2048)[1]
        kk_tags = sh(KK0, 2048)[1]
        vs_tags = sh(VS0, 2080)[1]
        et_tags = ["SH:7"]
        wqkv_l = fwqkv[l].rearrange("(c p) n -> p c n", p=128)
        wperm_l = fwperm[l].rearrange("(c p) n -> p c n", p=128)

        shift_to_XN()
        for hp in range(4):
            slot_info = []
            for si, (src, base) in enumerate((
                    (wqkv_l, hp * 128), (wqkv_l, 512 + hp * 128),
                    (wqkv_l, 1024 + hp * 128),
                    (wperm_l, hp * 128), (wperm_l, 512 + hp * 128))):
                s_ap, s_tags = ws(si * 1024, 1024)
                slot_info.append((si, s_tags))
                def emw(e, s_ap=s_ap, src=src, base=base):
                    return e.dma_start(out=s_ap.rearrange("p (c n) -> p c n", c=8),
                                       in_=src[:, :, base:base + 128])
                B.dma("qpool", emw, r=["fwqkv"], w=s_tags)
            for which in range(2):
                dst0 = (QQ0, KK0)[which]
                dtags = (qq_tags, kk_tags)[which]
                cs_off = (0, 2 * T)[which]
                sc = (OFF_QS, OFF_KS)[which] + l
                scp = (OFF_QSP, OFF_KSP)[which] + l
                wtag = slot_info[which][1]
                wptag = slot_info[3 + which][1]
                for nch in range(4):
                    for kc in range(NCH):
                        def em(e, which=which, nch=nch, kc=kc):
                            a, _ = ws(which * 1024 + kc * 128, 128)
                            return e.matmul(PB[nch][:, :], a,
                                            XNc(kc)[:, nch * 512:(nch + 1) * 512],
                                            start=(kc == 0), stop=(kc == NCH - 1))
                        B.op("pe", em, r=wtag + [f"XN:{kc}"], w=[pbt(nch)])
                for nch in range(4):
                    def em(e, nch=nch):
                        return e.activation(out=SCb[:, nch * 512:(nch + 1) * 512],
                                            in_=PB[nch][:, :], func=AF.Square)
                    B.op("act", em, r=[pbt(nch)], w=["SC"])
                for nch in range(4):
                    def em(e, nch=nch):
                        return e.matmul(PB[4 + nch % 2][:, 0:512], BLK64[:, :],
                                        SCb[:, nch * 512:(nch + 1) * 512],
                                        start=True, stop=True)
                    B.op("pe", em, r=["SC", "BLK"], w=[pbt(4 + nch % 2)])
                    def el(e, nch=nch):
                        return e.activation(out=SA[:, nch * 512:(nch + 1) * 512],
                                            in_=PB[4 + nch % 2][:, 0:512], func=AF.Ln)
                    B.op("act", el, r=[pbt(4 + nch % 2)], w=["SA"])
                for nch in range(4):
                    def ex(e, nch=nch):
                        return e.activation(out=SA[:, nch * 512:(nch + 1) * 512],
                                            in_=SA[:, nch * 512:(nch + 1) * 512],
                                            func=AF.Exp, scale=-0.5)
                    B.op("act", ex, r=["SA"], w=["SA"])
                for nch in range(4):
                    def em(e, nch=nch):
                        return e.tensor_tensor(out=SB_[:, nch * 512:(nch + 1) * 512],
                                               in0=PB[nch][:, :],
                                               in1=SA[:, nch * 512:(nch + 1) * 512],
                                               op=ALU.mult)
                    B.op("dve", em, r=[pbt(nch), "SA"], w=["SB"])
                def esc(e, sc=sc):
                    return e.activation(out=SB_[:], in_=SB_[:], func=AF.Identity,
                                        scale=BCOL[:, sc:sc + 1])
                B.op("act", esc, r=["SB", "BCOL"], w=["SB"])
                for nch in range(4):
                    for kc in range(NCH):
                        def em(e, which=which, nch=nch, kc=kc):
                            a, _ = ws((3 + which) * 1024 + kc * 128, 128)
                            return e.matmul(PB[nch][:, :], a,
                                            XNc(kc)[:, nch * 512:(nch + 1) * 512],
                                            start=(kc == 0), stop=(kc == NCH - 1))
                        B.op("pe", em, r=wptag + [f"XN:{kc}"], w=[pbt(nch)])
                    def em2(e, nch=nch):
                        return e.tensor_tensor(out=SC[:, nch * 512:(nch + 1) * 512],
                                               in0=PB[nch][:, :],
                                               in1=SA[:, nch * 512:(nch + 1) * 512],
                                               op=ALU.mult)
                    B.op("dve", em2, r=[pbt(nch), "SA"], w=["SC"])
                def esc2(e, scp=scp):
                    return e.activation(out=SC[:], in_=SC[:], func=AF.Identity,
                                        scale=BCOL[:, scp:scp + 1])
                B.op("act", esc2, r=["SC", "BCOL"], w=["SC"])
                def eu(e, cs_off=cs_off):
                    return e.tensor_tensor(out=SB_[:], in0=SB_[:],
                                           in1=ROTB[:, cs_off:cs_off + T], op=ALU.mult)
                B.op("dve", eu, r=["SB", "XL:0", "XL:1"], w=["SB"])
                def ev2(e, cs_off=cs_off):
                    return e.tensor_tensor(out=SC[:], in0=SC[:],
                                           in1=ROTB[:, cs_off + T:cs_off + 2 * T],
                                           op=ALU.mult)
                B.op("dve", ev2, r=["SC", "XL:0", "XL:1"], w=["SC"])
                def eq(e, dst0=dst0):
                    return e.tensor_tensor(out=SH[:, dst0:dst0 + T], in0=SB_[:],
                                           in1=SC[:], op=ALU.add)
                B.op("dve", eq, r=["SB", "SC"], w=dtags)
            if hp == 0:
                for bi in range(32):
                    def emo(e, bi=bi):
                        return e.memset(SH[:, VS0 + bi * 65 + 64:VS0 + bi * 65 + 65], 1.0)
                    B.op("dve", emo, w=vs_tags)
            vtag = slot_info[2][1]
            for jt in range(16):
                for kc in range(NCH):
                    def em(e, jt=jt, kc=kc):
                        a, _ = ws(2 * 1024 + kc * 128, 128)
                        return e.matmul(PB[5][:, 0:128], XNc(kc)[:, jt * 128:(jt + 1) * 128],
                                        a, start=(kc == 0), stop=(kc == NCH - 1))
                    B.op("pe", em, r=vtag + [f"XN:{kc}"], w=[pbt(5)])
                for hh in range(2):
                    def emc(e, jt=jt, hh=hh):
                        off = VS0 + (hh * 16 + jt) * 65
                        return e.activation(out=SH[:, off:off + 64],
                                            in_=PB[5][:, hh * 64:hh * 64 + 64],
                                            func=AF.Copy)
                    B.op("act", emc, r=[pbt(5)], w=vs_tags)
            def foacc(gg):
                return (PB[4], pbt(4)) if gg % 2 == 0 else (PBS, "PBSa")

            def fepilogue(hh, w, gg):
                h = 2 * hp + hh
                acc, acct = foacc(gg)
                r0 = (gg % 2) * 512
                rt = ["R2"] if gg % 2 == 0 else ["R2b"]
                def emf(e, acc=acc, r0=r0):
                    return e.tensor_scalar(out=row(2, r0, 512), in0=acc[64:65, 0:512],
                                           scalar1=1e-35, scalar2=None, op0=ALU.max)
                B.op("dve", emf, r=[acct], w=rt)
                def er1(e, r0=r0):
                    return e.activation(out=row(2, r0, 512), in_=row(2, r0, 512),
                                        func=AF.Ln)
                B.op("act", er1, r=rt, w=rt)
                def er2(e, r0=r0):
                    return e.activation(out=row(2, r0, 512), in_=row(2, r0, 512),
                                        func=AF.Exp, scale=-1.0)
                B.op("act", er2, r=rt, w=rt)
                if gg % 2 == 0:
                    bt_ap, btt = PB[5][:, 0:512], pbt(5)
                else:
                    bt_ap, btt = PBS[:, 512:1024], "PBSb"
                def emb(e, r0=r0, bt_ap=bt_ap):
                    return e.matmul(bt_ap, onesrow_f(2), row(2, r0, 512),
                                    start=True, stop=True)
                B.op("pe", emb, r=rt + ["ONE"], w=[btt])
                def ecp(e, bt_ap=bt_ap):
                    return e.activation(out=SA[:, 0:512], in_=bt_ap, func=AF.Copy)
                B.op("act", ecp, r=[btt], w=["SA"])
                def emo2(e, h=h, w=w, acc=acc):
                    oc = h // 2
                    pr = (h % 2) * 64
                    off = OT0 + oc * T + w * 512
                    return e.tensor_tensor(out=SH[pr:pr + 64, off:off + 512],
                                           in0=acc[0:64, 0:512], in1=SA[0:64, 0:512],
                                           op=ALU.mult)
                B.op("dve", emo2, r=[acct, "SA"],
                     w=sh(OT0 + (h // 2) * T + w * 512, 512)[1])

            def fflush(t):
                hh, w, gg, jglob, pb, i0, N, first, last = t
                acc, acct = foacc(gg)
                def emo(e, jglob=jglob, pb=pb, hh=hh, i0=i0, N=N,
                        first=first, last=last, acc=acc):
                    off = VS0 + (hh * 16 + jglob) * 65
                    return e.matmul(acc[0:65, i0:i0 + N],
                                    SH[:, off:off + 65],
                                    SH[:, ET0 + pb * 512:ET0 + pb * 512 + N],
                                    start=first, stop=last)
                B.op("pe", emo, r=vs_tags + sh(ET0 + pb * 512, 512)[1], w=[acct])
                if last:
                    fepilogue(hh, w, gg)

            fpend = None
            for hh in range(2):
                for w in range(4):
                    gg = hh * 4 + w
                    qb = hh * 64
                    jtls = list(range(4, 8)) if w == 0 else list(range(8))
                    for ti, jtl in enumerate(jtls):
                        jglob = 4 * (w - 1) + jtl
                        pb = ti % 2
                        i0 = max(0, (jtl - 4) * 128)
                        i1 = min(512, (jtl + 1) * 128)
                        N = i1 - i0
                        def em(e, pb=pb, jglob=jglob, w=w, qb=qb, i0=i0, N=N):
                            return e.matmul(
                                PB[pb][:, 0:N],
                                SH[qb:qb + 64, KK0 + jglob * 128:KK0 + (jglob + 1) * 128],
                                SH[qb:qb + 64, QQ0 + w * 512 + i0:QQ0 + w * 512 + i0 + N],
                                start=True, stop=False)
                        B.op("pe", em, r=qq_tags + kk_tags, w=[pbt(pb)])
                        if w > 0 and jtl <= 3:
                            moff, mpos = 0, jtl * 128 - i0
                        elif jtl >= 4:
                            moff, mpos = 128, (jtl - 4) * 128 - i0
                        else:
                            moff = None
                        if moff is not None:
                            def emk(e, pb=pb, moff=moff, mpos=mpos):
                                return e.matmul(PB[pb][:, mpos:mpos + 128],
                                                IDENTB[:, :], MSK[:, moff:moff + 128],
                                                start=False, stop=True)
                            B.op("pe", emk, r=["IDENTB", "MSK"], w=[pbt(pb)])
                        else:
                            def emk(e, pb=pb):
                                return e.matmul(PB[pb][0:1, 0:1], IDENTB[0:1, 0:1],
                                                ZEROB[0:1, 0:1], start=False, stop=True)
                            B.op("pe", emk, r=["IDENTB", "ZEROB"], w=[pbt(pb)])
                        if fpend is not None:
                            fflush(fpend)
                        def eme(e, pb=pb, N=N):
                            return e.activation(out=SH[:, ET0 + pb * 512:ET0 + pb * 512 + N],
                                                in_=PB[pb][:, 0:N], func=AF.Exp, scale=8.0)
                        B.op("act", eme, r=[pbt(pb)], w=sh(ET0 + pb * 512, 512)[1])
                        fpend = (hh, w, gg, jglob, pb, i0, N,
                                 ti == 0, ti == len(jtls) - 1)
            fflush(fpend)
        for m in range(NCH):
            for kc in range(4):
                sl_ap, sl_tags = ws(6144 + (kc % 2) * 128, 128)
                def emw(e, m=m, kc=kc, sl_ap=sl_ap):
                    return e.dma_start(out=sl_ap,
                                       in_=fwo[l][kc * 128:(kc + 1) * 128, m * 128:(m + 1) * 128])
                B.dma("qpool", emw, r=["fwo"], w=sl_tags)
                for sp in range(4):
                    def em(e, m=m, kc=kc, sp=sp, sl_ap=sl_ap):
                        return e.matmul(PB[sp][:, :], sl_ap,
                                        SH[:, OT0 + kc * T + sp * 512:OT0 + kc * T + (sp + 1) * 512],
                                        start=(kc == 0), stop=(kc == 3))
                    B.op("pe", em, r=sl_tags + sh(8192 + kc * T, T)[1], w=[pbt(sp)])
            for sp in range(4):
                def ea(e, m=m, sp=sp):
                    return e.tensor_tensor(
                        out=XFc(m)[:, sp * 512:(sp + 1) * 512],
                        in0=XFc(m)[:, sp * 512:(sp + 1) * 512],
                        in1=PB[sp][:, :], op=ALU.add)
                B.op("dve", ea, r=[pbt(sp), f"XF:{m}"], w=[f"XF:{m}"])
        stats_init()
        shift_to_XN()
        for c in range(NCH):
            stats_from_bf16(lambda c=c: XNc(c), [f"XN:{c}"])
        stats_rows()

        def srcfn(c, nch):
            if c < 4:
                return (XFc(c)[:, nch * 512:(nch + 1) * 512], [f"XF:{c}"], False)
            if nch == 0:
                return (XFc(c)[:, 0:511], [f"XF:{c}"], True)
            return (XFc(c)[:, nch * 512 - 1:(nch + 1) * 512 - 1], [f"XF:{c}"], False)

        normalize(lambda c, nch: (XNc(c)[:, nch * 512:(nch + 1) * 512], [f"XN:{c}"]),
                  src_fn=srcfn,
                  gain_bias=lambda c: (BCOL[:, OFF_LNG + 8 * l + c:OFF_LNG + 8 * l + c + 1],
                                       BCOL[:, OFF_LNB + 8 * l + c:OFF_LNB + 8 * l + c + 1]))

        w1r = fw1[l].rearrange("(c p) n -> p c n", p=128)
        HID = 2730
        groups = [list(range(0, 8)), list(range(8, 16)), list(range(16, 22))]
        for g in groups:
            for gi, cch in enumerate(g):
                wdt = min(128, HID - cch * 128)
                wa_ap, wa_tags = ws(5 * 1024, 8 * wdt)
                wg_ap, wg_tags = ws(6 * 1024, 8 * wdt)
                def emwa(e, wa_ap=wa_ap, cch=cch, wdt=wdt):
                    return e.dma_start(out=wa_ap.rearrange("p (c n) -> p c n", c=8),
                                       in_=w1r[:, :, cch * 128:cch * 128 + wdt])
                B.dma("qpool", emwa, r=["fw1"], w=wa_tags)
                def emwg(e, wg_ap=wg_ap, cch=cch, wdt=wdt):
                    return e.dma_start(out=wg_ap.rearrange("p (c n) -> p c n", c=8),
                                       in_=w1r[:, :, HID + cch * 128:HID + cch * 128 + wdt])
                B.dma("qpool", emwg, r=["fw1"], w=wg_tags)
                for nch in range(4):
                    for kc in range(NCH):
                        def ema(e, nch=nch, kc=kc, wdt=wdt):
                            a, _ = ws(5 * 1024 + kc * wdt, wdt)
                            return e.matmul(PB[nch % 2][0:wdt, :], a,
                                            XNc(kc)[:, nch * 512:(nch + 1) * 512],
                                            start=(kc == 0), stop=(kc == NCH - 1))
                        B.op("pe", ema, r=wa_tags + [f"XN:{kc}"], w=[pbt(nch % 2)])
                        def emg(e, nch=nch, kc=kc, wdt=wdt):
                            a, _ = ws(6 * 1024 + kc * wdt, wdt)
                            return e.matmul(PB[2 + nch % 2][0:wdt, :], a,
                                            XNc(kc)[:, nch * 512:(nch + 1) * 512],
                                            start=(kc == 0), stop=(kc == NCH - 1))
                        B.op("pe", emg, r=wg_tags + [f"XN:{kc}"], w=[pbt(2 + nch % 2)])
                    def eg1(e, nch=nch, wdt=wdt, cch=cch):
                        bcg = OFF_B1 + 44 * l + 22 + cch
                        return e.activation(out=SCb[0:wdt, nch * 512:(nch + 1) * 512],
                                            in_=PB[2 + nch % 2][0:wdt, :], func=AF.Gelu,
                                            bias=BCOL[0:wdt, bcg:bcg + 1])
                    B.op("act", eg1, r=[pbt(2 + nch % 2), "BCOL"], w=["SC"])
                    def eg2(e, nch=nch, wdt=wdt, cch=cch, gi=gi):
                        bca = OFF_B1 + 44 * l + cch
                        a, _ = sh(gi * T + nch * 512, 512, rows=(0, wdt))
                        return e.scalar_tensor_tensor(
                            out=a, in0=PB[nch % 2][0:wdt, :],
                            scalar=BCOL[0:wdt, bca:bca + 1],
                            in1=SCb[0:wdt, nch * 512:(nch + 1) * 512],
                            op0=ALU.add, op1=ALU.mult)
                    B.op("dve", eg2, r=[pbt(nch % 2), "SC", "BCOL"],
                         w=sh(gi * T, 2048)[1])
            last_group = (g[0] == 16)
            for m in range(NCH):
                for gi, cch in enumerate(g):
                    wdt = min(128, HID - cch * 128)
                    sl_ap, sl_tags = ws(7 * 1024 + (gi % 2) * 128, 128, rows=(0, wdt))
                    def emw(e, m=m, cch=cch, wdt=wdt, sl_ap=sl_ap):
                        return e.dma_start(out=sl_ap,
                                           in_=fw2[l][cch * 128:cch * 128 + wdt,
                                                      m * 128:(m + 1) * 128])
                    B.dma("qpool", emw, r=["fw2"], w=sl_tags)
                    for sp in range(4):
                        def em(e, m=m, sp=sp, gi=gi, wdt=wdt, sl_ap=sl_ap, n=len(g)):
                            a, _ = sh(gi * T + sp * 512, 512, rows=(0, wdt))
                            return e.matmul(PB[2 + sp][:, :], sl_ap, a,
                                            start=(gi == 0), stop=(gi == n - 1))
                        B.op("pe", em, r=sl_tags + sh(gi * T, 2048)[1], w=[pbt(2 + sp)])
                for sp in range(4):
                    if last_group:
                        def ea(e, m=m, sp=sp):
                            return e.scalar_tensor_tensor(
                                out=XFc(m)[:, sp * 512:(sp + 1) * 512],
                                in0=PB[2 + sp][:, :],
                                scalar=BCOL[:, OFF_B2 + 8 * l + m:OFF_B2 + 8 * l + m + 1],
                                in1=XFc(m)[:, sp * 512:(sp + 1) * 512],
                                op0=ALU.add, op1=ALU.add)
                        B.op("dve", ea, r=[pbt(2 + sp), f"XF:{m}", "BCOL"], w=[f"XF:{m}"])
                    else:
                        def ea(e, m=m, sp=sp):
                            return e.tensor_tensor(
                                out=XFc(m)[:, sp * 512:(sp + 1) * 512],
                                in0=XFc(m)[:, sp * 512:(sp + 1) * 512],
                                in1=PB[2 + sp][:, :], op=ALU.add)
                        B.op("dve", ea, r=[pbt(2 + sp), f"XF:{m}"], w=[f"XF:{m}"])

    def ffn_call(src_x, src_fmt, dst_x, dst_fmt, final):
        for bi in range(4):
            def emr(e, bi=bi):
                return e.dma_start(out=ROTB[:, bi * T:(bi + 1) * T], in_=rot_t[bi])
            B.dma("qsp", emr, r=["rot"], w=["XL:0" if bi < 2 else "XL:1"])
        for l in range(2):
            ffn_layer(l)
        if not final:
            stats_init()
        for c in range(NCH):
            def eml(e, c=c):
                return e.dma_start(out=XL[:, (c % 2) * T:(c % 2) * T + T],
                                   in_=src_x[c * 128:(c + 1) * 128, :])
            B.dma("qsp", eml, r=[src_fmt.format(c)], w=[f"XL:{c % 2}"])
            def emr2(e, c=c):
                return e.tensor_tensor(out=XFc(c), in0=XFc(c),
                                       in1=XL[:, (c % 2) * T:(c % 2) * T + T], op=ALU.add)
            B.op("dve", emr2, r=[f"XF:{c}", f"XL:{c % 2}"], w=[f"XF:{c}"])
            def emw(e, c=c):
                return e.dma_start(out=dst_x[c * 128:(c + 1) * 128, :], in_=XFc(c))
            B.dma("qsp", emw, r=[f"XF:{c}"], w=[dst_fmt.format(c)])
            if not final:
                stats_chunk(lambda c=c: XFc(c), [f"XF:{c}"])
        if not final:
            stats_rows()

    ffn_call(xs1, "xs1:{}", xs2, "xs2:{}", False)
    norm_to_XN()

    def attn_b():
        Q0, K0, V0, ET0, OT0 = 0, 4096, 8192, 11264, 12288
        EPS = 1e-6
        CEXP = 70.0
        wq = bw_qkv.rearrange("(c p) n -> p c n", p=128)
        v_tags = sh(V0, 2080)[1]
        if debug:
            for c in range(NCH):
                def emt(e, c=c):
                    return e.dma_start(out=xt2p[c * 128:(c + 1) * 128, :], in_=XFc(c))
                B.dma("qsp", emt, r=[f"XF:{c}"], w=["xt2p"])
        for grp in range(4):
            if debug and grp == 2:
                for c in range(NCH):
                    def emt(e, c=c):
                        return e.dma_start(out=xt3m[c * 128:(c + 1) * 128, :], in_=XFc(c))
                    B.dma("qsp", emt, r=[f"XF:{c}"], w=["xt3m"])
            ot_tags = [sh(OT0, 2048)[1], sh(OT0 + 2048, 2048)[1]]
            for gh in range(2):
                hp = grp * 2 + gh
                slot_tags = []
                for si, base in enumerate((hp * 128, 1024 + hp * 128, 2048 + hp * 128)):
                    s_ap, s_tags = ws(si * 1024, 1024)
                    slot_tags.append(s_tags)
                    def emw(e, s_ap=s_ap, base=base):
                        return e.dma_start(out=s_ap.rearrange("p (c n) -> p c n", c=8),
                                           in_=wq[:, :, base:base + 128])
                    B.dma("qpool", emw, r=["bw_qkv"], w=s_tags)
                for which in range(2):
                    for nch in range(4):
                        for kc in range(NCH):
                            def em(e, which=which, nch=nch, kc=kc):
                                a, _ = ws(which * 1024 + kc * 128, 128)
                                return e.matmul(PB[nch][:, :], a,
                                                XNc(kc)[:, nch * 512:(nch + 1) * 512],
                                                start=(kc == 0), stop=(kc == NCH - 1))
                            B.op("pe", em, r=slot_tags[which] + [f"XN:{kc}"], w=[pbt(nch)])
                    for hh in range(2):
                        dst0 = (Q0, K0)[which] + hh * 2048
                        dtags = sh(dst0, 2048)[1]
                        if which == 0:
                            bias_col = OFF_BQKV + hp
                            scl = 1.0
                        else:
                            bias_col = OFF_MBK + hp
                            scl = -2.0
                        for nch in range(4):
                            def ee(e, hh=hh, nch=nch, dst0=dst0, bias_col=bias_col, scl=scl):
                                return e.activation(
                                    out=SH[0:64, dst0 + nch * 512:dst0 + (nch + 1) * 512],
                                    in_=PB[nch][hh * 64:hh * 64 + 64, :],
                                    func=AF.Identity, scale=scl,
                                    bias=BCOL[hh * 64:hh * 64 + 64, bias_col:bias_col + 1])
                            B.op("act", ee, r=[pbt(nch), "BCOL"], w=dtags)
                for hh in range(2):
                    qd = Q0 + hh * 2048
                    kd = K0 + hh * 2048
                    qtags = sh(qd, 2048)[1]
                    ktags = sh(kd, 2048)[1]
                    B.op("dve", lambda e, qd=qd: e.memset(SH[64:68, qd:qd + T], 1.0),
                         w=qtags)
                    B.op("dve", lambda e, kd=kd: e.memset(SH[64:68, kd:kd + T], 1.0),
                         w=ktags)
                    B.op("act", lambda e, qd=qd: e.activation(out=SCb[0:64, 0:T],
                                                              in_=SH[0:64, qd:qd + T],
                                                              func=AF.Square),
                         r=qtags, w=["SC", "SC:0", "SC:1"])
                    for nch in range(4):
                        def em(e, nch=nch):
                            return e.matmul(PBS[0:1, 0:512], ONEB[0:64, 0:1],
                                            SCb[0:64, nch * 512:(nch + 1) * 512],
                                            start=True, stop=True)
                        B.op("pe", em, r=["SC", "SC:0", "SC:1", "ONE"], w=["PBSa"])
                        def ee(e, hh=hh, nch=nch):
                            return e.activation(out=row(hh, nch * 512, 512),
                                                in_=PBS[0:1, 0:512], func=AF.Identity,
                                                scale=1.0, bias=64.0 + EPS)
                        B.op("act", ee, r=["PBSa", "CONSTS"], w=[f"R{hh}"])
                    p0 = hh * 32
                    def eqh(e, hh=hh, p0=p0):
                        return e.activation(out=ROWSB[p0:p0 + 1, 0:T], in_=row(hh),
                                            func=AF.Copy)
                    B.op("act", eqh, r=[f"R{hh}"], w=["RBQ"])
                    def eql(e, hh=hh, p0=p0):
                        return e.tensor_tensor(out=ROWSB[p0:p0 + 1, T:2 * T], in0=row(hh),
                                               in1=ROWSB[p0:p0 + 1, 0:T], op=ALU.subtract)
                    B.op("dve", eql, r=[f"R{hh}", "RBQ"], w=["RBQ2"])
                    def ew1(e, hh=hh, p0=p0):
                        return e.dma_start(out=auxs[4 * hh + 0], in_=ROWSB[p0:p0 + 1, 0:T])
                    B.dma("qsp", ew1, r=["RBQ"], w=[f"aux{4 * hh}"])
                    def ew2(e, hh=hh, p0=p0):
                        return e.dma_start(out=auxs[4 * hh + 1],
                                           in_=ROWSB[p0:p0 + 1, T:2 * T])
                    B.dma("qsp", ew2, r=["RBQ2"], w=[f"aux{4 * hh + 1}"])
                    def edq(e, qd=qd, hh=hh):
                        return e.dma_start(out=SH[66:67, qd:qd + T], in_=auxs[4 * hh + 0])
                    B.dma("qsp", edq, r=[f"aux{4 * hh}"], w=qtags)
                    def edq2(e, qd=qd, hh=hh):
                        return e.dma_start(out=SH[67:68, qd:qd + T], in_=auxs[4 * hh + 1])
                    B.dma("qsp", edq2, r=[f"aux{4 * hh + 1}"], w=qtags)
                    B.op("act", lambda e, kd=kd: e.activation(out=SCb[0:64, 0:T],
                                                              in_=SH[0:64, kd:kd + T],
                                                              func=AF.Square),
                         r=ktags, w=["SC", "SC:0", "SC:1"])
                    for nch in range(4):
                        def em(e, nch=nch):
                            return e.matmul(PBS[0:1, 512:1024], ONEB[0:64, 0:1],
                                            SCb[0:64, nch * 512:(nch + 1) * 512],
                                            start=True, stop=True)
                        B.op("pe", em, r=["SC", "SC:0", "SC:1", "ONE"], w=["PBSb"])
                        def ee(e, nch=nch):
                            return e.activation(out=row(2, nch * 512, 512),
                                                in_=PBS[0:1, 512:1024], func=AF.Identity,
                                                scale=0.25, bias=-64.0)
                        B.op("act", ee, r=["PBSb", "CONSTS"], w=["R2", "R2b"])
                    def ekh(e, kd=kd):
                        return e.activation(out=SH[64:65, kd:kd + T], in_=row(2),
                                            func=AF.Copy)
                    B.op("act", ekh, r=["R2"], w=ktags)
                    def ekl(e, kd=kd):
                        return e.tensor_tensor(out=ROWSB[64:65, 1024:1024 + T], in0=row(2),
                                               in1=SH[64:65, kd:kd + T], op=ALU.subtract)
                    B.op("dve", ekl, r=["R2"] + ktags, w=["RBK"])
                    def ew3(e, hh=hh):
                        return e.dma_start(out=auxs[4 * hh + 2],
                                           in_=ROWSB[64:65, 1024:1024 + T])
                    B.dma("qsp", ew3, r=["RBK"], w=[f"aux{4 * hh + 2}"])
                    def edk(e, kd=kd, hh=hh):
                        return e.dma_start(out=SH[65:66, kd:kd + T], in_=auxs[4 * hh + 2])
                    B.dma("qsp", edk, r=[f"aux{4 * hh + 2}"], w=ktags)
                if hp == 0 and debug:
                    def edK(e):
                        return e.dma_start(out=dumpK, in_=SH[0:68, K0:K0 + T])
                    B.dma("qsp", edK, r=sh(K0, 2048)[1], w=["dumpK"])
                    def edQ(e):
                        return e.dma_start(out=dumpQ, in_=SH[0:68, Q0:Q0 + T])
                    B.dma("qsp", edQ, r=sh(Q0, 2048)[1], w=["dumpQ"])
                    for r_ in range(4):
                        def ed1(e, r_=r_):
                            return e.dma_start(out=auxdump[r_],
                                               in_=SH[64 + r_:65 + r_, K0:K0 + T])
                        B.dma("qsp", ed1, r=sh(K0, 2048)[1], w=["auxdump"])
                        def ed2(e, r_=r_):
                            return e.dma_start(out=auxdump[4 + r_],
                                               in_=SH[64 + r_:65 + r_, Q0:Q0 + T])
                        B.dma("qsp", ed2, r=sh(Q0, 2048)[1], w=["auxdump"])
                        def ed3(e, r_=r_):
                            return e.dma_start(out=auxdump[8 + r_],
                                               in_=SH[r_:r_ + 1, K0:K0 + T])
                        B.dma("qsp", ed3, r=sh(K0, 2048)[1], w=["auxdump"])
                        def ed4(e, r_=r_):
                            return e.dma_start(out=auxdump[12 + r_],
                                               in_=SH[r_:r_ + 1, Q0:Q0 + T])
                        B.dma("qsp", ed4, r=sh(Q0, 2048)[1], w=["auxdump"])
                if hp == 0:
                    for bi in range(32):
                        def emo(e, bi=bi):
                            return e.memset(SH[:, V0 + bi * 65 + 64:V0 + bi * 65 + 65], 1.0)
                        B.op("dve", emo, w=v_tags)
                for jt in range(16):
                    for kc in range(NCH):
                        def em(e, jt=jt, kc=kc):
                            a, _ = ws(2 * 1024 + kc * 128, 128)
                            return e.matmul(PB[5][:, 0:128],
                                            XNc(kc)[:, jt * 128:(jt + 1) * 128],
                                            a, start=(kc == 0), stop=(kc == NCH - 1))
                        B.op("pe", em, r=slot_tags[2] + [f"XN:{kc}"], w=[pbt(5)])
                    for hh in range(2):
                        def emc(e, jt=jt, hh=hh):
                            off = V0 + (hh * 16 + jt) * 65
                            return e.activation(out=SH[:, off:off + 64],
                                                in_=PB[5][:, hh * 64:hh * 64 + 64],
                                                func=AF.Copy)
                        B.op("act", emc, r=[pbt(5)], w=v_tags)
                for jt in range(16):
                    def emb2(e, hp=hp):
                        return e.matmul(PB[5][:, 0:128], ONEBR[64:65, 0:128],
                                        ROWSB[64:65, hp * 128:(hp + 1) * 128],
                                        start=True, stop=True)
                    B.op("pe", emb2, r=["ONE", "RBV"], w=[pbt(5)])
                    for hh in range(2):
                        def ea(e, jt=jt, hh=hh):
                            off = V0 + (hh * 16 + jt) * 65
                            return e.tensor_tensor(out=SH[:, off:off + 64],
                                                   in0=PB[5][:, hh * 64:hh * 64 + 64],
                                                   in1=SH[:, off:off + 64], op=ALU.add)
                        B.op("dve", ea, r=[pbt(5)] + v_tags, w=v_tags)
                tiles = []
                for hh in range(2):
                    for ic in range(4):
                        for jt in range(4 * ic + 4):
                            tiles.append((hh, ic, jt))

                def oacc(g):
                    return (PB[4], pbt(4)) if g % 2 == 0 else (PBS, "PBSa")

                def epilogue(hh, ic, g):
                    acc, acct = oacc(g)
                    r0 = (g % 2) * 512
                    rt = ["R2"] if g % 2 == 0 else ["R2b"]
                    def emf(e, acc=acc, r0=r0):
                        return e.tensor_scalar(out=row(2, r0, 512),
                                               in0=acc[64:65, 0:512],
                                               scalar1=1e-35, scalar2=2.0 ** 64,
                                               op0=ALU.max, op1=ALU.mult)
                    B.op("dve", emf, r=[acct], w=rt)
                    if debug and hp == 0 and hh == 0 and ic == 0:
                        def edS1(e):
                            return e.dma_start(out=dumpS[1:2, :], in_=row(2, 0, 512))
                        B.dma("qsp", edS1, r=rt, w=["dumpS2"])
                    def er1(e, r0=r0):
                        return e.activation(out=row(2, r0, 512), in_=row(2, r0, 512),
                                            func=AF.Ln)
                    B.op("act", er1, r=rt, w=rt)
                    def er2(e, r0=r0):
                        return e.activation(out=row(2, r0, 512), in_=row(2, r0, 512),
                                            func=AF.Exp, scale=-1.0)
                    B.op("act", er2, r=rt, w=rt)
                    if debug and hp == 0 and hh == 0 and ic == 0:
                        def edS0(e):
                            return e.dma_start(out=dumpS[0:1, :], in_=row(2, 0, 512))
                        B.dma("qsp", edS0, r=rt, w=["dumpS"])
                    if g % 2 == 0:
                        bt_ap, btt = PB[5][:, 0:512], pbt(5)
                    else:
                        bt_ap, btt = PBS[:, 512:1024], "PBSb"
                    def emb3(e, r0=r0, bt_ap=bt_ap):
                        return e.matmul(bt_ap, onesrow_f(2), row(2, r0, 512),
                                        start=True, stop=True)
                    B.op("pe", emb3, r=rt + ["ONE"], w=[btt])
                    sbo = (g % 2) * 512
                    sbt = [f"SB:{g % 2}"]
                    def ecp(e, bt_ap=bt_ap, sbo=sbo):
                        return e.activation(out=SB_[:, sbo:sbo + 512], in_=bt_ap,
                                            func=AF.Identity, scale=2.0 ** 64)
                    B.op("act", ecp, r=[btt], w=sbt)
                    def emo(e, hh=hh, ic=ic, acc=acc, sbo=sbo, gh=gh):
                        off = OT0 + gh * T + ic * 512
                        pr = hh * 64
                        return e.tensor_tensor(out=SH[pr:pr + 64, off:off + 512],
                                               in0=acc[0:64, 0:512],
                                               in1=SB_[0:64, sbo:sbo + 512],
                                               op=ALU.mult)
                    B.op("dve", emo, r=[acct] + sbt, w=ot_tags[gh])

                ET3 = 10752

                def flush5(t):
                    idx, hh, ic, jt = t
                    s = idx % 4
                    ep = idx % 3
                    sa_ap = SA[:, s * 512:(s + 1) * 512]
                    sat = [f"SA:{s}"]
                    ett = sh(ET3 + ep * 512, 512)[1]
                    def e5(e, sa_ap=sa_ap, ep=ep):
                        return e.activation(
                            out=SH[:, ET3 + ep * 512:ET3 + (ep + 1) * 512],
                            in_=sa_ap, func=AF.Exp, scale=0.25, bias=-CEXP)
                    B.op("act", e5, r=sat + ["CONSTS"], w=ett)
                    if debug and hp == 0 and idx == 0:
                        def edE(e):
                            return e.dma_start(out=dumpE, in_=SH[:, ET3:ET3 + 512])
                        B.dma("qsp", edE, r=ett, w=["dumpE"])

                def flush6(t):
                    idx, hh, ic, jt = t
                    g = hh * 4 + ic
                    n = 4 * ic + 4
                    ep = idx % 3
                    ett = sh(ET3 + ep * 512, 512)[1]
                    acc, acct = oacc(g)
                    def e6(e, jt=jt, ep=ep, hh=hh, n=n, acc=acc):
                        off = V0 + (hh * 16 + jt) * 65
                        return e.matmul(acc[0:65, 0:512], SH[:, off:off + 65],
                                        SH[:, ET3 + ep * 512:ET3 + (ep + 1) * 512],
                                        start=(jt == 0), stop=(jt == n - 1))
                    B.op("pe", e6, r=v_tags + ett, w=[acct])
                    if jt == n - 1:
                        epilogue(hh, ic, g)

                pend5 = None
                pend6 = None
                for idx, (hh, ic, jt) in enumerate(tiles):
                    qd = Q0 + hh * 2048
                    kd = K0 + hh * 2048
                    qtags = sh(qd, 2048)[1]
                    ktags = sh(kd, 2048)[1]
                    pbp = idx % 2
                    s = idx % 4
                    sa_ap = SA[:, s * 512:(s + 1) * 512]
                    sc_ap = SC[:, s * 512:(s + 1) * 512]
                    sat = [f"SA:{s}"]
                    sct = [f"SC:{s}"]
                    def emd(e, jt=jt, ic=ic, qd=qd, kd=kd, pbp=pbp):
                        return e.matmul(PB[pbp][:, :],
                                        SH[0:68, kd + jt * 128:kd + (jt + 1) * 128],
                                        SH[0:68, qd + ic * 512:qd + (ic + 1) * 512],
                                        start=True, stop=True)
                    B.op("pe", emd, r=ktags + qtags, w=[pbt(pbp)])
                    def emn(e, jt=jt, ic=ic, qd=qd, kd=kd, pbp=pbp):
                        return e.matmul(PB[2 + pbp][:, :],
                                        SH[0:64, kd + jt * 128:kd + (jt + 1) * 128],
                                        SH[0:64, qd + ic * 512:qd + (ic + 1) * 512],
                                        start=True, stop=True)
                    B.op("pe", emn, r=ktags + qtags, w=[pbt(2 + pbp)])
                    def e1b(e, pbp=pbp, sa_ap=sa_ap):
                        return e.activation(out=sa_ap, in_=PB[pbp][:, :], func=AF.Ln)
                    B.op("act", e1b, r=[pbt(pbp)], w=sat)
                    def e1c(e, sa_ap=sa_ap):
                        return e.activation(out=sa_ap, in_=sa_ap, func=AF.Exp,
                                            scale=-1.0)
                    B.op("act", e1c, r=sat, w=sat)
                    def e2(e, pbp=pbp, sc_ap=sc_ap):
                        return e.activation(out=sc_ap, in_=PB[2 + pbp][:, :],
                                            func=AF.Square)
                    B.op("act", e2, r=[pbt(2 + pbp)], w=sct)
                    def e3(e, sa_ap=sa_ap, sc_ap=sc_ap):
                        return e.tensor_tensor(out=sa_ap, in0=sc_ap, in1=sa_ap,
                                               op=ALU.mult)
                    B.op("dve", e3, r=sat + sct, w=sat)
                    if jt >= 4 * ic:
                        kk = jt - 4 * ic
                        def e4(e, kk=kk, sa_ap=sa_ap):
                            return e.tensor_tensor(
                                out=sa_ap, in0=sa_ap,
                                in1=MSK[:, 256 + kk * 512:256 + (kk + 1) * 512],
                                op=ALU.add)
                        B.op("dve", e4, r=sat + ["MSK"], w=sat)
                    if pend6 is not None:
                        flush6(pend6)
                    if pend5 is not None:
                        flush5(pend5)
                    pend6, pend5 = pend5, (idx, hh, ic, jt)
                flush6(pend6)
                flush5(pend5)
                flush6(pend5)
            for m in range(NCH):
                for kci in range(2):
                    kc = 2 * grp + kci
                    sl_ap, sl_tags = ws(3 * 1024 + kci * 128, 128)
                    def emw(e, m=m, kc=kc, sl_ap=sl_ap):
                        return e.dma_start(
                            out=sl_ap,
                            in_=bw_out[kc * 128:(kc + 1) * 128, m * 128:(m + 1) * 128])
                    B.dma("qpool", emw, r=["bw_out"], w=sl_tags)
                    for sp in range(4):
                        def em(e, m=m, sp=sp, kci=kci, sl_ap=sl_ap):
                            return e.matmul(
                                PB[sp][:, :], sl_ap,
                                SH[:, OT0 + kci * T + sp * 512:OT0 + kci * T + (sp + 1) * 512],
                                start=(kci == 0), stop=(kci == 1))
                        B.op("pe", em, r=sl_tags + ot_tags[kci], w=[pbt(sp)])
                for sp in range(4):
                    if grp == 3:
                        def ea(e, m=m, sp=sp):
                            return e.scalar_tensor_tensor(
                                out=XFc(m)[:, sp * 512:(sp + 1) * 512],
                                in0=PB[sp][:, :],
                                scalar=BCOL[:, OFF_BOUT + m:OFF_BOUT + m + 1],
                                in1=XFc(m)[:, sp * 512:(sp + 1) * 512],
                                op0=ALU.add, op1=ALU.add)
                        B.op("dve", ea, r=[pbt(sp), f"XF:{m}", "BCOL"], w=[f"XF:{m}"])
                    else:
                        def ea(e, m=m, sp=sp):
                            return e.tensor_tensor(
                                out=XFc(m)[:, sp * 512:(sp + 1) * 512],
                                in0=XFc(m)[:, sp * 512:(sp + 1) * 512],
                                in1=PB[sp][:, :], op=ALU.add)
                        B.op("dve", ea, r=[pbt(sp), f"XF:{m}"], w=[f"XF:{m}"])
        stats_init()
        for c in range(NCH):
            def emw(e, c=c):
                return e.dma_start(out=xs1[c * 128:(c + 1) * 128, :], in_=XFc(c))
            B.dma("qsp", emw, r=[f"XF:{c}"], w=[f"xs1:{c}"])
            stats_chunk(lambda c=c: XFc(c), [f"XF:{c}"])
        stats_rows()
        norm_to_XF()

    attn_b()
    ffn_call(xs1, "xs1:{}", yT, "yT:{}", True)

    extra = (["xt1", "xt2p", "xt3m", "auxdump", "dumpK", "dumpQ", "dumpE", "dumpO", "dumpS"] + [f"xs1:{c}" for c in range(NCH)] + [f"xs2:{c}" for c in range(NCH)]) if debug else []
    B.op("sp", lambda e: e.nop(), r=[f"yT:{c}" for c in range(NCH)] + extra)
    import os
    if os.environ.get("KSTRICT"):
        B.strict_ranges.append((0, len(B.ops)))
    B.finalize(nc)
    return nc


_CACHE = {}


def kernel(**inputs):
    x = np.asarray(inputs["L0_attn_a_x"], dtype=np.float32)
    nb = x.shape[0]
    rot, mska, mskb, mskc = _host_consts()

    def f32(a):
        return np.ascontiguousarray(np.asarray(a, dtype=np.float32))

    wqkv = f32(inputs["L0_ffn_wqkv"])
    perm = np.concatenate([(h * 64 + (np.arange(64) + 32) % 64) for h in range(8)])
    fwperm = np.ascontiguousarray(
        np.concatenate([wqkv[:, :, perm], wqkv[:, :, 512 + perm]], axis=2))
    b1 = f32(inputs["L0_ffn_b1"])

    common = {
        "wqT": f32(np.asarray(inputs["L0_attn_a_Wq"]).T),
        "wkvT": f32(np.asarray(inputs["L0_attn_a_Wkv_down"]).T),
        "wkupT": f32(np.asarray(inputs["L0_attn_a_Wk_up"]).T),
        "wvupT": f32(np.asarray(inputs["L0_attn_a_Wv_up"]).T),
        "woT": f32(np.asarray(inputs["L0_attn_a_Wo"]).T),
        "bw_qkv": f32(inputs["L0_attn_b_w_qkv"]),
        "bb_qkv": f32(inputs["L0_attn_b_b_qkv"]).reshape(1, -1),
        "bw_out": f32(inputs["L0_attn_b_w_out"]),
        "bb_out": f32(inputs["L0_attn_b_b_out"]).reshape(1, -1),
        "fwqkv": wqkv,
        "fwperm": fwperm,
        "fqs": f32(inputs["L0_ffn_q_scale"]).reshape(2, 1, 64),
        "fks": f32(inputs["L0_ffn_k_scale"]).reshape(2, 1, 64),
        "fwo": f32(inputs["L0_ffn_wo"]),
        "flng": f32(inputs["L0_ffn_ln_g"]).reshape(2, 1, D),
        "flnb": f32(inputs["L0_ffn_ln_b"]).reshape(2, 1, D),
        "fw1": f32(inputs["L0_ffn_w1"]),
        "fb1a": np.ascontiguousarray(b1[:, 0:2730]).reshape(2, 1, 2730),
        "fb1g": np.ascontiguousarray(b1[:, 2730:5460]).reshape(2, 1, 2730),
        "fw2": f32(inputs["L0_ffn_w2"]),
        "fb2": f32(inputs["L0_ffn_b2"]).reshape(2, 1, D),
        "rot": rot, "mska": mska, "mskb": mskb, "mskc": mskc,
        "identb": np.eye(128, dtype=BFNP),
        "onefc": _onefc(), "onebc": _onefc().astype(BFNP),
        "onefr": _onefr(), "onebr": _onebr(), "blk64": _blk64(),
    }

    debug = bool(inputs.pop("_debug", False))
    key = ("nc", debug)
    if key not in _CACHE:
        _CACHE[key] = build_program(debug)
    nc = _CACHE[key]

    in_maps = []
    for b in range(nb):
        m = dict(common)
        m["xT"] = np.ascontiguousarray(x[b].T)
        in_maps.append(m)

    res = run_bass_kernel_spmd(nc, in_maps, core_ids=list(range(nb)))
    if debug:
        taps = {nm: np.stack([np.ascontiguousarray(r[nm].T) for r in res.results])
                for nm in ("xt1", "xs2", "xs1", "xt2p", "xt3m")}
        for nm2 in ("auxdump", "dumpK", "dumpQ", "dumpE", "dumpO", "dumpS"):
            taps[nm2] = np.stack([np.asarray(r[nm2]) for r in res.results])
        taps["out"] = np.stack([np.ascontiguousarray(r["yT"].T) for r in res.results])
        return taps
    return np.stack([np.ascontiguousarray(r["yT"].T) for r in res.results])


# version 14
# speedup vs baseline: 1.2994x; 1.0555x over previous
import numpy as np
import ml_dtypes
import concourse.bass as bass
import concourse.mybir as mybir
from concourse.bass_utils import run_bass_kernel_spmd

F32 = mybir.dt.float32
BF16 = mybir.dt.bfloat16
AF = mybir.ActivationFunctionType
ALU = mybir.AluOpType
BFNP = ml_dtypes.bfloat16

T = 2048
D = 1024
NCH = 8

COMPUTE_ENGINES = ("pe", "act", "dve", "pool", "sp")
DMA_QUEUES = {"qsp": "sp", "qpool": "pool"}
ALL_ENTITIES = list(COMPUTE_ENGINES) + list(DMA_QUEUES)


class _Op:
    __slots__ = ("idx", "entity", "issuer", "emit", "reads", "writes", "waits",
                 "marked", "ent_idx", "know", "qwait")

    def __init__(self, idx, entity, issuer, emit, reads, writes):
        self.idx = idx
        self.entity = entity
        self.issuer = issuer
        self.emit = emit
        self.reads = reads
        self.writes = writes
        self.waits = []
        self.marked = False
        self.ent_idx = -1
        self.know = None


class Builder:
    def __init__(self):
        self.ops = []
        self.last_write = {}
        self.readers = {}
        self.strict_ranges = []

    def in_strict(self, idx):
        return any(lo <= idx < hi for lo, hi in self.strict_ranges)

    def op(self, eng, emit, r=(), w=()):
        o = _Op(len(self.ops), eng, eng, emit, tuple(r), tuple(w))
        self.ops.append(o)
        return o

    def dma(self, queue, emit, r=(), w=()):
        o = _Op(len(self.ops), queue, DMA_QUEUES[queue], emit, tuple(r), tuple(w))
        self.ops.append(o)
        return o

    def finalize(self, nc):
        for o in self.ops:
            deps = []
            for t in o.reads:
                p = self.last_write.get(t)
                if p is not None:
                    deps.append(p)
            for t in o.writes:
                p = self.last_write.get(t)
                if p is not None:
                    deps.append(p)
                deps.extend(self.readers.get(t, ()))
            strict = self.in_strict(o.idx)
            seen = {}
            for p in deps:
                if (p.entity == o.entity and o.entity not in DMA_QUEUES
                        and not strict):
                    continue
                if p is o:
                    continue
                cur = seen.get(p.entity)
                if cur is None or p.idx > cur.idx:
                    seen[p.entity] = p
            o.waits = list(seen.values())
            for t in o.reads:
                self.readers.setdefault(t, []).append(o)
            for t in o.writes:
                self.last_write[t] = o
                self.readers[t] = []

        import os as _os
        if _os.environ.get("KSERIAL"):
            for i in range(1, len(self.ops)):
                o = self.ops[i]
                p = self.ops[i - 1]
                if all(q is not p for q in o.waits):
                    o.waits.append(p)

        def dma_key(o):
            return (o.entity, o.writes)

        key_rank = {}
        key_id = {}
        for o in self.ops:
            if o.entity in DMA_QUEUES:
                o.marked = True
                k = dma_key(o)
                if k not in key_id:
                    key_id[k] = len(key_id)
                key_rank[k] = key_rank.get(k, 0) + 1
                o.ent_idx = key_rank[k]
                o.entity = f"D{key_id[k]}_{o.entity}"

        for o in self.ops:
            seen = {}
            for p in o.waits:
                cur = seen.get(p.entity)
                if cur is None or p.idx > cur.idx:
                    seen[p.entity] = p
            o.waits = list(seen.values())

        issuer_know = {e: {} for e in COMPUTE_ENGINES}
        for o in self.ops:
            kn = issuer_know[o.issuer]
            pruned = []
            for p in sorted(o.waits, key=lambda q: q.idx):
                if kn.get(p.entity, -1) >= p.idx:
                    continue
                pruned.append(p)
                p.marked = True
                if p.know:
                    for e2, v in p.know.items():
                        if kn.get(e2, -1) < v:
                            kn[e2] = v
                if kn.get(p.entity, -1) < p.idx:
                    kn[p.entity] = p.idx
            o.waits = pruned
            snap = dict(kn)
            snap[o.entity] = o.idx
            o.know = snap

        cnt = {}
        for o in self.ops:
            if o.issuer == o.entity or o.entity in COMPUTE_ENGINES:
                if o.marked:
                    cnt[o.entity] = cnt.get(o.entity, 0) + 1
                o.ent_idx = cnt.get(o.entity, 0)

        entities = sorted(set(o.entity for o in self.ops)
                          | set(e for o in self.ops for e in
                                [p.entity for p in o.waits]))
        sems = {e: nc.alloc_semaphore(f"sem_{e}"[:48]) for e in entities}
        streams = {e: [] for e in COMPUTE_ENGINES}
        for o in self.ops:
            streams[o.issuer].append(o)

        def amt(e):
            return 1 if e in COMPUTE_ENGINES else 16

        def run(ns, issuer):
            for o in streams[issuer]:
                for p in o.waits:
                    ns.wait_ge(sems[p.entity], p.ent_idx * amt(p.entity))
                instr = o.emit(ns)
                if o.marked:
                    instr.then_inc(sems[o.entity], amt(o.entity))

        with nc.Block() as block:
            @block.tensor
            def _(e):
                run(e, "pe")

            @block.scalar
            def _(e):
                run(e, "act")

            @block.vector
            def _(e):
                run(e, "dve")

            @block.gpsimd
            def _(e):
                run(e, "pool")

            @block.sync
            def _(e):
                run(e, "sp")


def _onefc():
    a = np.zeros((128, 4), np.float32)
    a[:, 0] = 1.0
    a[:, 1] = 1.0 / 1024.0
    return a


def _onefr():
    a = np.zeros((128, 128), np.float32)
    for p in (0, 32, 64):
        a[p, :] = 1.0
    return a


def _onebr():
    a = np.zeros((128, 384), np.float64)
    a[0, 0:128] = 1.0
    a[64, 0:128] = 1.0
    a[0, 128:192] = 1.0
    a[0, 320:384] = 1.0
    return a.astype(BFNP)


def _blk64():
    a = np.zeros((128, 128), np.float64)
    a[0:64, 0:64] = 1.0
    a[64:128, 64:128] = 1.0
    return a.astype(BFNP)


def _host_consts():
    DH, WIN = 64, 512
    half = np.arange(0, DH, 2, dtype=np.float64)
    inv_freq = 1.0 / 10000.0 ** (half / DH)
    t = np.arange(T, dtype=np.float64)
    freqs = np.concatenate([t[:, None] * inv_freq[None, :]] * 2, axis=-1)
    base = (half + 0.4 * DH) / (1.4 * DH)
    power = (t - T // 2) / float(WIN // 2)
    scale = np.concatenate([base[None, :] ** power[:, None]] * 2, axis=-1)
    cos, sin = np.cos(freqs), np.sin(freqs)
    sgn = np.where(np.arange(DH) < 32, -1.0, 1.0)[None, :]

    def dup(a):
        return np.concatenate([a.T, a.T], axis=0).astype(BFNP)

    rot = np.stack([dup(cos * scale), dup(sgn * sin * scale),
                    dup(cos / scale), dup(sgn * sin / scale)])

    p = np.arange(128)[:, None]
    c = np.arange(128)[None, :]
    mska = np.where(p >= c, 0.0, -1e30).astype(BFNP)
    mskb = np.where(p <= c, 0.0, -1e30).astype(BFNP)
    c2 = np.arange(512)[None, :]
    mskc = np.zeros((4, 128, 512), dtype=np.float64)
    for k in range(4):
        mskc[k] = np.where(128 * k + p > c2, -4e9, 0.0)
    mskc = mskc.reshape(512, 512).astype(BFNP)
    return rot, mska, mskb, mskc


def build_program(debug=False):
    nc = bass.Bass("TRN2", target_bir_lowering=False)
    B = Builder()

    def din(name, shape, dt=F32):
        return nc.dram_tensor(name, shape, dt, kind="ExternalInput").ap()

    xin = din("xT", [D, T])
    wqT = din("wqT", [D, 2048])
    wkvT = din("wkvT", [D, 64])
    wkupT = din("wkupT", [64, 2048])
    wvupT = din("wvupT", [64, 2048])
    woT = din("woT", [2048, D])
    bw_qkv = din("bw_qkv", [D, 3072])
    bb_qkv = din("bb_qkv", [1, 3072])
    bw_out = din("bw_out", [D, D])
    bb_out = din("bb_out", [1, D])
    fwqkv = din("fwqkv", [2, D, 1536])
    fwperm = din("fwperm", [2, D, 1024])
    fqs = din("fqs", [2, 1, 64])
    fks = din("fks", [2, 1, 64])
    fwo = din("fwo", [2, 512, D])
    flng = din("flng", [2, 1, D])
    flnb = din("flnb", [2, 1, D])
    fw1 = din("fw1", [2, D, 5460])
    fb1a = din("fb1a", [2, 1, 2730])
    fb1g = din("fb1g", [2, 1, 2730])
    fw2 = din("fw2", [2, 2730, D])
    fb2 = din("fb2", [2, 1, D])
    rot_t = din("rot", [4, 128, T], BF16)
    mska_t = din("mska", [128, 128], BF16)
    mskb_t = din("mskb", [128, 128], BF16)
    mskc_t = din("mskc", [512, 512], BF16)
    identb_t = din("identb", [128, 128], BF16)
    onefc_t = din("onefc", [128, 4], F32)
    onebc_t = din("onebc", [128, 4], BF16)
    onefr_t = din("onefr", [128, 128], F32)
    onebr_t = din("onebr", [128, 384], BF16)
    blk64_t = din("blk64", [128, 128], BF16)

    yT = nc.dram_tensor("yT", [D, T], F32, kind="ExternalOutput").ap()
    kind_dbg = "ExternalOutput" if debug else "Internal"
    xs1 = nc.dram_tensor("xs1", [D, T], F32, kind=kind_dbg).ap()
    auxs = nc.dram_tensor("auxs", [8, T], BF16, kind="Internal").ap()
    auxdump = nc.dram_tensor("auxdump", [16, T], BF16, kind="ExternalOutput").ap() if debug else None
    dumpK = nc.dram_tensor("dumpK", [68, T], BF16, kind="ExternalOutput").ap() if debug else None
    dumpQ = nc.dram_tensor("dumpQ", [68, T], BF16, kind="ExternalOutput").ap() if debug else None
    dumpE = nc.dram_tensor("dumpE", [128, 512], BF16, kind="ExternalOutput").ap() if debug else None
    dumpO = nc.dram_tensor("dumpO", [128, 2 * T], BF16, kind="ExternalOutput").ap() if debug else None
    dumpS = nc.dram_tensor("dumpS", [2, 512], F32, kind="ExternalOutput").ap() if debug else None
    xs2 = nc.dram_tensor("xs2", [D, T], F32, kind=kind_dbg).ap()
    xt1 = nc.dram_tensor("xt1", [D, T], F32, kind=kind_dbg).ap() if debug else None
    xt2p = nc.dram_tensor("xt2p", [D, T], F32, kind=kind_dbg).ap() if debug else None
    xt3m = nc.dram_tensor("xt3m", [D, T], F32, kind=kind_dbg).ap() if debug else None

    def sbt(name, shape, dt):
        return nc.alloc_sbuf_tensor(name, shape, dt)

    XN = sbt("XN", [128, 8 * T], BF16)
    XF = sbt("XF", [128, 8 * T], F32)
    XFb = XF.bitcast(BF16)
    SH = sbt("SH", [128, 16384], BF16)
    WS = sbt("WS", [128, 8192], BF16)
    SA = sbt("SA", [128, T], F32)
    SB_ = sbt("SBs", [128, T], F32)
    SC = sbt("SCs", [128, T], F32)
    SCb = SC.bitcast(BF16)
    XL = sbt("XL", [128, 2 * T], F32)
    ROTB = XL.bitcast(BF16)
    ROWS = sbt("ROWS", [128, T], F32)
    ROWSB = sbt("ROWSB", [128, 2 * T], BF16)
    MSK = sbt("MSK", [128, 2304], BF16)
    BCOL = sbt("BCOL", [128, 192], F32)
    ONEF = sbt("ONEF", [128, 4], F32)
    ONEB = sbt("ONEB", [128, 4], BF16)
    ONEFR = sbt("ONEFR", [128, 128], F32)
    ONEBR = sbt("ONEBR", [128, 384], BF16)
    IDENTB = sbt("IDENTB", [128, 128], BF16)
    BLK64 = sbt("BLK64", [128, 128], BF16)
    ZEROB = sbt("ZEROB", [128, 4], BF16)

    PB = [nc.alloc_psum_tensor(f"PB{i}", [128, 512], F32) for i in range(6)]
    PBS = nc.alloc_psum_tensor("PBS", [128, 1024], F32)

    CONSTS = sbt("CONSTS", [128, 4], F32)
    CONST_VALS = (64.0 + 1e-6, -64.0, -70.0)
    for j, v in enumerate(CONST_VALS):
        nc.const_aps.register(F32, v, CONSTS[:, j:j + 1])

    def pbt(i):
        return f"PB{i}"

    def sh(c0, n, rows=None):
        ap = SH[:, c0:c0 + n] if rows is None else SH[rows[0]:rows[1], c0:c0 + n]
        return ap, [f"SH:{i}" for i in range(c0 // 512, (c0 + n + 511) // 512)]

    def ws(c0, n, rows=None):
        ap = WS[:, c0:c0 + n] if rows is None else WS[rows[0]:rows[1], c0:c0 + n]
        return ap, [f"WS:{i}" for i in range(c0 // 128, (c0 + n + 127) // 128)]

    def XFc(c):
        return XF[:, c * T:(c + 1) * T]

    def XNc(c):
        return XN[:, c * T:(c + 1) * T]

    ROWP = (0, 32, 64)

    def row(i, off=0, n=T):
        p = ROWP[i]
        return ROWS[p:p + 1, off:off + n]

    def onesrow_f(i):
        return ONEFR[ROWP[i]:ROWP[i] + 1, 0:128]

    def rowb(i, off=0, n=T):
        return ROWSB[0:1, i * T + off:i * T + off + n]

    BVROW = ROWSB

    OFF_BQKV, OFF_BOUT, OFF_B2 = 0, 24, 32
    OFF_LNG, OFF_LNB = 48, 64
    OFF_B1 = 80
    OFF_QS, OFF_KS, OFF_QSP, OFF_KSP, OFF_MBK = 168, 170, 172, 174, 176

    B.dma("qsp", lambda e: e.dma_start(out=MSK[:, 0:128], in_=mska_t), w=["MSK"])
    B.dma("qsp", lambda e: e.dma_start(out=MSK[:, 128:256], in_=mskb_t), w=["MSK"])
    B.dma("qsp", lambda e: e.dma_start(
        out=MSK[:, 256:2304].rearrange("p (k n) -> p k n", k=4),
        in_=mskc_t.rearrange("(k p) n -> p k n", p=128)), w=["MSK"])
    B.dma("qsp", lambda e: e.dma_start(out=IDENTB[:], in_=identb_t), w=["IDENTB"])
    B.op("dve", lambda e: e.memset(ZEROB[:], 0.0), w=["ZEROB"])
    B.dma("qpool", lambda e: e.dma_start(out=ROWSB[64:65, 0:1024],
                                         in_=bb_qkv[:, 2048:3072]),
          r=["bb_qkv"], w=["RBV"])
    B.dma("qsp", lambda e: e.dma_start(out=ONEF[:], in_=onefc_t), w=["ONE"])
    B.dma("qsp", lambda e: e.dma_start(out=ONEB[:], in_=onebc_t), w=["ONE"])
    B.dma("qsp", lambda e: e.dma_start(out=ONEFR[:], in_=onefr_t), w=["ONE"])
    B.dma("qsp", lambda e: e.dma_start(out=ONEBR[:], in_=onebr_t), w=["ONE"])
    B.dma("qsp", lambda e: e.dma_start(out=BLK64[:], in_=blk64_t), w=["BLK"])

    for j, v in enumerate(CONST_VALS):
        B.op("dve", lambda e, j=j, v=v: e.memset(CONSTS[:, j:j + 1], v), w=["CONSTS"])

    def load_row_to_cols(src_ap, src_tag, n, bcol_off):
        for s0 in range(0, n, 2048):
            seg = min(2048, n - s0)
            def eml(e, s0=s0, seg=seg):
                return e.dma_start(out=SA[0:1, 0:seg], in_=src_ap[:, s0:s0 + seg])
            B.dma("qsp", eml, r=[src_tag], w=["SA"])
            for c in range((seg + 127) // 128):
                wdt = min(128, seg - c * 128)
                def emm(e, c=c, wdt=wdt):
                    return e.matmul(PB[5][0:wdt, 0:1], SA[0:1, c * 128:c * 128 + wdt],
                                    ONEFR[0:1, 0:1], start=True, stop=True)
                B.op("pe", emm, r=["SA", "ONE"], w=[pbt(5)])
                def emv(e, c=c, wdt=wdt, s0=s0):
                    col = bcol_off + s0 // 128 + c
                    return e.tensor_copy(out=BCOL[0:wdt, col:col + 1],
                                         in_=PB[5][0:wdt, 0:1])
                B.op("dve", emv, r=[pbt(5)], w=["BCOL"])

    load_row_to_cols(bb_qkv, "bb_qkv", 3072, OFF_BQKV)
    load_row_to_cols(bb_out, "bb_out", D, OFF_BOUT)
    for l in range(2):
        load_row_to_cols(fb1a[l], "fb1a", 2730, OFF_B1 + 44 * l)
        load_row_to_cols(fb1g[l], "fb1g", 2730, OFF_B1 + 44 * l + 22)
        load_row_to_cols(fb2[l], "fb2", D, OFF_B2 + 8 * l)
        load_row_to_cols(flng[l], "flng", D, OFF_LNG + 8 * l)
        load_row_to_cols(flnb[l], "flnb", D, OFF_LNB + 8 * l)
        load_row_to_cols(fqs[l], "fqs", 64, OFF_QS + l)
        load_row_to_cols(fks[l], "fks", 64, OFF_KS + l)
    for l in range(2):
        for src, dst in ((OFF_QS + l, OFF_QSP + l), (OFF_KS + l, OFF_KSP + l)):
            def e1(e, src=src, dst=dst):
                return e.tensor_copy(out=BCOL[0:32, dst:dst + 1],
                                     in_=BCOL[32:64, src:src + 1])
            B.op("dve", e1, r=["BCOL"], w=["BCOL"])
            def e2(e, src=src, dst=dst):
                return e.tensor_copy(out=BCOL[32:64, dst:dst + 1],
                                     in_=BCOL[0:32, src:src + 1])
            B.op("dve", e2, r=["BCOL"], w=["BCOL"])
        for cc in (OFF_QS + l, OFF_KS + l, OFF_QSP + l, OFF_KSP + l):
            def e3(e, cc=cc):
                return e.tensor_copy(out=BCOL[64:128, cc:cc + 1],
                                     in_=BCOL[0:64, cc:cc + 1])
            B.op("dve", e3, r=["BCOL"], w=["BCOL"])
    for hp in range(8):
        def e4(e, hp=hp):
            return e.activation(out=BCOL[:, OFF_MBK + hp:OFF_MBK + hp + 1],
                                in_=BCOL[:, OFF_BQKV + 8 + hp:OFF_BQKV + 8 + hp + 1],
                                func=AF.Identity, scale=-2.0)
        B.op("act", e4, r=["BCOL"], w=["BCOL"])

    def stats_init():
        B.op("dve", lambda e: e.memset(row(0), 0.0),
             w=["R0"] + [f"R0c{i}" for i in range(4)])
        B.op("dve", lambda e: e.memset(row(1), 0.0),
             w=["R1"] + [f"R1c{i}" for i in range(4)])

    def stats_from_bf16(src_ap_fn, src_tags):
        B.op("act", lambda e: e.activation(out=SCb[:, T:2 * T], in_=src_ap_fn(),
                                           func=AF.Square), r=list(src_tags), w=["SC"])
        for nch in range(4):
            def em1(e, nch=nch, fn=src_ap_fn):
                full = fn()
                return e.matmul(PBS[0:1, 0:512], ONEB[:, 1:2],
                                full[:, nch * 512:(nch + 1) * 512], start=True, stop=True)
            B.op("pe", em1, r=list(src_tags) + ["ONE"], w=["PBSa"])
            def em2(e, nch=nch):
                return e.matmul(PBS[0:1, 512:1024], ONEB[:, 1:2],
                                SCb[:, T + nch * 512:T + (nch + 1) * 512],
                                start=True, stop=True)
            B.op("pe", em2, r=["SC", "ONE"], w=["PBSb"])
            def ev1(e, nch=nch):
                return e.tensor_tensor(out=row(0, nch * 512, 512),
                                       in0=row(0, nch * 512, 512),
                                       in1=PBS[0:1, 0:512], op=ALU.add)
            B.op("dve", ev1, r=[f"R0c{nch}", "PBSa"], w=["R0", f"R0c{nch}"])
            def ev2(e, nch=nch):
                return e.tensor_tensor(out=row(1, nch * 512, 512),
                                       in0=row(1, nch * 512, 512),
                                       in1=PBS[0:1, 512:1024], op=ALU.add)
            B.op("dve", ev2, r=[f"R1c{nch}", "PBSb"], w=["R1", f"R1c{nch}"])

    def stats_chunk(src_ap_fn, src_tags):
        B.op("act", lambda e: e.activation(out=SCb[:, 0:T], in_=src_ap_fn(),
                                           func=AF.Copy), r=list(src_tags), w=["SC"])
        stats_from_bf16(lambda: SCb[:, 0:T], ["SC"])

    def stats_rows(eps=1e-5):
        for nch in range(4):
            s0 = nch * 512
            def ev1(e, s0=s0):
                return e.tensor_tensor(out=SA[0:1, s0:s0 + 512], in0=row(0, s0, 512),
                                       in1=row(0, s0, 512), op=ALU.mult)
            B.op("dve", ev1, r=[f"R0c{nch}"], w=["SA"])
            def ev2(e, s0=s0):
                return e.activation(out=SB_[0:1, s0:s0 + 512], in_=row(1, s0, 512),
                                    func=AF.Copy)
            B.op("act", ev2, r=[f"R1c{nch}"], w=["SB"])
            def ev3(e, s0=s0):
                return e.scalar_tensor_tensor(out=row(1, s0, 512),
                                              in0=SB_[0:1, s0:s0 + 512], scalar=eps,
                                              in1=SA[0:1, s0:s0 + 512],
                                              op0=ALU.add, op1=ALU.subtract)
            B.op("dve", ev3, r=["SA", "SB"], w=["R1", f"R1c{nch}"])
            def ev4(e, s0=s0):
                return e.activation(out=row(1, s0, 512), in_=row(1, s0, 512),
                                    func=AF.Ln)
            B.op("act", ev4, r=[f"R1c{nch}"], w=["R1", f"R1c{nch}"])
            def ev5(e, s0=s0):
                return e.activation(out=row(1, s0, 512), in_=row(1, s0, 512),
                                    func=AF.Exp, scale=-0.5)
            B.op("act", ev5, r=[f"R1c{nch}"], w=["R1", f"R1c{nch}"])

    def normalize(dst_fn, src_fn=None, gain_bias=None):
        for nch in range(4):
            def emb(e, nch=nch):
                return e.matmul(PB[4][:, 0:512], onesrow_f(0),
                                row(0, nch * 512, 512), start=True, stop=True)
            B.op("pe", emb, r=[f"R0c{nch}", "ONE"], w=[pbt(4)])
            def emb2(e, nch=nch):
                return e.matmul(PB[5][:, 0:512], onesrow_f(1),
                                row(1, nch * 512, 512), start=True, stop=True)
            B.op("pe", emb2, r=[f"R1c{nch}", "ONE"], w=[pbt(5)])
            for c in range(NCH):
                if src_fn is None:
                    src_ap = XFc(c)[:, nch * 512:(nch + 1) * 512]
                    src_tags, shifted = [f"XF:{c}"], False
                else:
                    src_ap, src_tags, shifted = src_fn(c, nch)
                if not shifted:
                    def em1(e, src_ap=src_ap):
                        return e.tensor_tensor(out=SA[:, 0:512], in0=src_ap,
                                               in1=PB[4][:, 0:512], op=ALU.subtract)
                    B.op("dve", em1, r=src_tags + [pbt(4)], w=["SA"])
                else:
                    def em1(e, src_ap=src_ap):
                        return e.tensor_tensor(out=SA[:, 1:512], in0=src_ap,
                                               in1=PB[4][:, 1:512], op=ALU.subtract)
                    B.op("dve", em1, r=src_tags + [pbt(4)], w=["SA"])
                    def em0(e):
                        return e.tensor_scalar(out=SA[:, 0:1], in0=PB[4][:, 0:1],
                                               scalar1=-1.0, scalar2=None, op0=ALU.mult)
                    B.op("dve", em0, r=[pbt(4)], w=["SA"])
                dst_ap, dst_tags = dst_fn(c, nch)
                if gain_bias is None:
                    def em2(e, dst_ap=dst_ap):
                        return e.tensor_tensor(out=dst_ap, in0=SA[:, 0:512],
                                               in1=PB[5][:, 0:512], op=ALU.mult)
                    B.op("dve", em2, r=["SA", pbt(5)], w=dst_tags)
                else:
                    g_ap, b_ap = gain_bias(c)
                    def em2(e):
                        return e.tensor_tensor(out=SB_[:, 0:512], in0=SA[:, 0:512],
                                               in1=PB[5][:, 0:512], op=ALU.mult)
                    B.op("dve", em2, r=["SA", pbt(5)], w=["SB"])
                    def em3(e, dst_ap=dst_ap, g_ap=g_ap, b_ap=b_ap):
                        return e.activation(out=dst_ap, in_=SB_[:, 0:512],
                                            func=AF.Identity, scale=g_ap, bias=b_ap)
                    B.op("act", em3, r=["SB", "BCOL"], w=dst_tags)

    def norm_to_XN():
        normalize(lambda c, nch: (XNc(c)[:, nch * 512:(nch + 1) * 512], [f"XN:{c}"]))

    def norm_to_XF():
        normalize(lambda c, nch: (XFc(c)[:, nch * 512:(nch + 1) * 512], [f"XF:{c}"]))

    stats_init()
    for c in range(NCH):
        def eml(e, c=c):
            return e.dma_start(out=XFc(c), in_=xin[c * 128:(c + 1) * 128, :])
        B.dma("qsp", eml, r=["xin"], w=[f"XF:{c}"])
        stats_chunk(lambda c=c: XFc(c), [f"XF:{c}"])
    stats_rows()
    norm_to_XN()

    def attn_a():
        QT0, KT0, VT0, ET0, KV0 = 0, 4096, 8192, 13312, 14336
        kv_ap, kv_tags = ws(7168, 512)
        B.dma("qpool", lambda e: e.dma_start(
            out=kv_ap.rearrange("p (c n) -> p c n", c=8),
            in_=wkvT.rearrange("(c p) n -> p c n", p=128)),
            r=["wkvT"], w=kv_tags)
        for nch in range(4):
            for kc in range(NCH):
                def em(e, nch=nch, kc=kc):
                    a, _ = ws(7168 + kc * 64, 64)
                    return e.matmul(PB[0][0:64, :], a,
                                    XNc(kc)[:, nch * 512:(nch + 1) * 512],
                                    start=(kc == 0), stop=(kc == NCH - 1))
                B.op("pe", em, r=kv_tags + [f"XN:{kc}"], w=[pbt(0)])
            def eme(e, nch=nch):
                a, tg = sh(KV0 + nch * 512, 512, rows=(0, 64))
                return e.activation(out=a, in_=PB[0][0:64, :], func=AF.Copy)
            B.op("act", eme, r=[pbt(0)], w=sh(KV0, 2048)[1])
        kup_ap, kup_tags = ws(0, 2048, rows=(0, 64))
        B.dma("qpool", lambda e: e.dma_start(out=kup_ap, in_=wkupT),
              r=["wkupT"], w=kup_tags)
        vup_ap, vup_tags = ws(2048, 2048, rows=(0, 64))
        B.dma("qpool", lambda e: e.dma_start(out=vup_ap, in_=wvupT),
              r=["wvupT"], w=vup_tags)
        vt_tags = sh(VT0, 4112)[1]
        for jt in range(16):
            def emo(e, jt=jt):
                return e.memset(SH[:, VT0 + jt * 257 + 256:VT0 + jt * 257 + 257], 1.0)
            B.op("dve", emo, w=vt_tags)

        kvs_tags = sh(KV0, 2048)[1]
        qt_tags = sh(QT0, 4096)[1]
        kt_tags = sh(KT0, 4096)[1]
        for h in range(8):
            wq_ap, wq_tags = ws(4096, 2048)
            def emw(e, h=h):
                return e.dma_start(
                    out=wq_ap.rearrange("p (c n) -> p c n", c=8),
                    in_=wqT.rearrange("(c p) n -> p c n", p=128)[:, :, h * 256:(h + 1) * 256])
            B.dma("qpool", emw, r=["wqT"], w=wq_tags)
            for m in range(2):
                for nch in range(4):
                    for kc in range(NCH):
                        def em(e, m=m, nch=nch, kc=kc):
                            a, _ = ws(4096 + kc * 256 + m * 128, 128)
                            return e.matmul(PB[m][:, :], a,
                                            XNc(kc)[:, nch * 512:(nch + 1) * 512],
                                            start=(kc == 0), stop=(kc == NCH - 1))
                        B.op("pe", em, r=wq_tags + [f"XN:{kc}"], w=[pbt(m)])
                    def eme(e, m=m, nch=nch):
                        return e.activation(
                            out=SH[:, QT0 + m * T + nch * 512:QT0 + m * T + (nch + 1) * 512],
                            in_=PB[m][:, :], func=AF.Copy)
                    B.op("act", eme, r=[pbt(m)], w=qt_tags)
                for nch in range(4):
                    def em(e, m=m, nch=nch, h=h):
                        a, _ = ws(h * 256 + m * 128, 128, rows=(0, 64))
                        return e.matmul(PB[2][:, :], a,
                                        SH[0:64, KV0 + nch * 512:KV0 + (nch + 1) * 512],
                                        start=True, stop=True)
                    B.op("pe", em, r=kup_tags + kvs_tags, w=[pbt(2)])
                    def eme(e, m=m, nch=nch):
                        return e.activation(
                            out=SH[:, KT0 + m * T + nch * 512:KT0 + m * T + (nch + 1) * 512],
                            in_=PB[2][:, :], func=AF.Copy)
                    B.op("act", eme, r=[pbt(2)], w=kt_tags)
            for jt in range(16):
                def em(e, jt=jt, h=h):
                    a, _ = ws(2048 + h * 256, 256, rows=(0, 64))
                    return e.matmul(PB[3][:, 0:256],
                                    SH[0:64, KV0 + jt * 128:KV0 + (jt + 1) * 128],
                                    a, start=True, stop=True)
                B.op("pe", em, r=kvs_tags + vup_tags, w=[pbt(3)])
                def eme(e, jt=jt):
                    return e.activation(out=SH[:, VT0 + jt * 257:VT0 + jt * 257 + 256],
                                        in_=PB[3][:, 0:256], func=AF.Copy)
                B.op("act", eme, r=[pbt(3)], w=vt_tags)
            for ic in range(4):
                for jt in range(16):
                    pb = jt % 2
                    for kc in range(2):
                        def em(e, pb=pb, kc=kc, jt=jt, ic=ic):
                            return e.matmul(
                                PB[pb][:, :],
                                SH[:, KT0 + kc * T + jt * 128:KT0 + kc * T + (jt + 1) * 128],
                                SH[:, QT0 + kc * T + ic * 512:QT0 + kc * T + (ic + 1) * 512],
                                start=(kc == 0), stop=(kc == 1))
                        B.op("pe", em, r=kt_tags + qt_tags, w=[pbt(pb)])
                    def eme(e, pb=pb):
                        return e.activation(out=SH[:, ET0 + pb * 512:ET0 + (pb + 1) * 512],
                                            in_=PB[pb][:, :], func=AF.Exp, scale=1.0 / 16.0)
                    B.op("act", eme, r=[pbt(pb)], w=sh(ET0 + pb * 512, 512)[1])
                    for mt in range(3):
                        o0, wdt, opb = ((0, 128, 3), (128, 128, 4), (256, 1, 5))[mt]
                        def em(e, jt=jt, o0=o0, wdt=wdt, opb=opb, pb=pb):
                            return e.matmul(
                                PB[opb][0:(128 if wdt == 128 else 1), 0:512],
                                SH[:, VT0 + jt * 257 + o0:VT0 + jt * 257 + o0 + wdt],
                                SH[:, ET0 + pb * 512:ET0 + (pb + 1) * 512],
                                start=(jt == 0), stop=(jt == 15))
                        B.op("pe", em, r=vt_tags + sh(ET0 + pb * 512, 512)[1], w=[pbt(opb)])
                def emf(e):
                    return e.tensor_scalar(out=row(2, 0, 512), in0=PB[5][0:1, 0:512],
                                           scalar1=1e-35, scalar2=None, op0=ALU.max)
                B.op("dve", emf, r=[pbt(5)], w=["R2"])
                B.op("act", lambda e: e.activation(out=row(2, 0, 512), in_=row(2, 0, 512),
                                                   func=AF.Ln), r=["R2"], w=["R2"])
                B.op("act", lambda e: e.activation(out=row(2, 0, 512), in_=row(2, 0, 512),
                                                   func=AF.Exp, scale=-1.0),
                     r=["R2"], w=["R2"])
                def emb(e):
                    return e.matmul(PB[2][:, 0:512], onesrow_f(2), row(2, 0, 512),
                                    start=True, stop=True)
                B.op("pe", emb, r=["R2", "ONE"], w=[pbt(2)])
                B.op("act", lambda e: e.activation(out=SA[:, 0:512], in_=PB[2][:, 0:512],
                                                   func=AF.Copy), r=[pbt(2)], w=["SA"])
                for mt in range(2):
                    def em(e, mt=mt, ic=ic, h=h):
                        oc = 2 * h + mt
                        return e.tensor_tensor(
                            out=XFb[:, oc * T + ic * 512:oc * T + ic * 512 + 512],
                            in0=PB[3 + mt][:, 0:512], in1=SA[:, 0:512], op=ALU.mult)
                    B.op("dve", em, r=[pbt(3 + mt), "SA"], w=[f"XF:{h}"])
        stats_init()
        for m in range(NCH):
            def eml(e, m=m):
                return e.dma_start(out=XL[:, (m % 2) * T:(m % 2) * T + T],
                                   in_=xin[m * 128:(m + 1) * 128, :])
            B.dma("qsp", eml, r=["xin"], w=[f"XL:{m % 2}"])
            for kc in range(16):
                sl_ap, sl_tags = ws(6144 + (kc % 2) * 128, 128)
                def emw(e, m=m, kc=kc, sl_ap=sl_ap):
                    return e.dma_start(out=sl_ap,
                                       in_=woT[kc * 128:(kc + 1) * 128, m * 128:(m + 1) * 128])
                B.dma("qpool", emw, r=["woT"], w=sl_tags)
                for sp in range(4):
                    def em(e, m=m, kc=kc, sp=sp, sl_ap=sl_ap):
                        return e.matmul(PB[sp][:, :], sl_ap,
                                        XFb[:, kc * T + sp * 512:kc * T + sp * 512 + 512],
                                        start=(kc == 0), stop=(kc == 15))
                    B.op("pe", em, r=sl_tags + [f"XF:{kc // 2}"], w=[pbt(sp)])
            for sp in range(4):
                def emr(e, m=m, sp=sp):
                    return e.tensor_tensor(
                        out=SB_[:, sp * 512:(sp + 1) * 512], in0=PB[sp][:, :],
                        in1=XL[:, (m % 2) * T + sp * 512:(m % 2) * T + (sp + 1) * 512],
                        op=ALU.add)
                B.op("dve", emr, r=[pbt(sp), f"XL:{m % 2}"], w=["SB"])
            def emw2(e, m=m):
                return e.dma_start(out=xs1[m * 128:(m + 1) * 128, :], in_=SB_[:, 0:T])
            B.dma("qsp", emw2, r=["SB"], w=[f"xs1:{m}"])
            stats_chunk(lambda: SB_[:, 0:T], ["SB"])
        stats_rows()
        for c in range(NCH):
            def eml(e, c=c):
                return e.dma_start(out=XFc(c), in_=xs1[c * 128:(c + 1) * 128, :])
            B.dma("qsp", eml, r=[f"xs1:{c}"], w=[f"XF:{c}"])
            if debug:
                def emt(e, c=c):
                    return e.dma_start(out=xt1[c * 128:(c + 1) * 128, :], in_=XFc(c))
                B.dma("qsp", emt, r=[f"XF:{c}"], w=["xt1"])
        norm_to_XF()

    attn_a()

    def shift_to_XN():
        for c in range(NCH):
            if c < 4:
                def em(e, c=c):
                    return e.activation(out=XNc(c), in_=XFc(c), func=AF.Copy)
                B.op("act", em, r=[f"XF:{c}"], w=[f"XN:{c}"])
            else:
                def em(e, c=c):
                    return e.activation(out=XNc(c)[:, 1:T], in_=XFc(c)[:, 0:T - 1],
                                        func=AF.Copy)
                B.op("act", em, r=[f"XF:{c}"], w=[f"XN:{c}"])
                B.op("dve", lambda e, c=c: e.memset(XNc(c)[:, 0:1], 0.0), w=[f"XN:{c}"])

    def ffn_layer(l):
        QQ0, KK0, VS0, ET0, OT0 = 0, 2048, 4096, 7168, 8192
        qq_tags = sh(QQ0, 2048)[1]
        kk_tags = sh(KK0, 2048)[1]
        vs_tags = sh(VS0, 2080)[1]
        et_tags = ["SH:7"]
        wqkv_l = fwqkv[l].rearrange("(c p) n -> p c n", p=128)
        wperm_l = fwperm[l].rearrange("(c p) n -> p c n", p=128)

        shift_to_XN()
        for hp in range(4):
            slot_info = []
            for si, (src, base) in enumerate((
                    (wqkv_l, hp * 128), (wqkv_l, 512 + hp * 128),
                    (wqkv_l, 1024 + hp * 128),
                    (wperm_l, hp * 128), (wperm_l, 512 + hp * 128))):
                s_ap, s_tags = ws(si * 1024, 1024)
                slot_info.append((si, s_tags))
                def emw(e, s_ap=s_ap, src=src, base=base):
                    return e.dma_start(out=s_ap.rearrange("p (c n) -> p c n", c=8),
                                       in_=src[:, :, base:base + 128])
                B.dma("qpool", emw, r=["fwqkv"], w=s_tags)
            for which in range(2):
                dst0 = (QQ0, KK0)[which]
                dtags = (qq_tags, kk_tags)[which]
                cs_off = (0, 2 * T)[which]
                sc = (OFF_QS, OFF_KS)[which] + l
                scp = (OFF_QSP, OFF_KSP)[which] + l
                wtag = slot_info[which][1]
                wptag = slot_info[3 + which][1]
                for nch in range(4):
                    for kc in range(NCH):
                        def em(e, which=which, nch=nch, kc=kc):
                            a, _ = ws(which * 1024 + kc * 128, 128)
                            return e.matmul(PB[nch][:, :], a,
                                            XNc(kc)[:, nch * 512:(nch + 1) * 512],
                                            start=(kc == 0), stop=(kc == NCH - 1))
                        B.op("pe", em, r=wtag + [f"XN:{kc}"], w=[pbt(nch)])
                for nch in range(4):
                    def em(e, nch=nch):
                        return e.activation(out=SCb[:, nch * 512:(nch + 1) * 512],
                                            in_=PB[nch][:, :], func=AF.Square)
                    B.op("act", em, r=[pbt(nch)], w=["SC"])
                for nch in range(4):
                    def em(e, nch=nch):
                        return e.matmul(PB[4 + nch % 2][:, 0:512], BLK64[:, :],
                                        SCb[:, nch * 512:(nch + 1) * 512],
                                        start=True, stop=True)
                    B.op("pe", em, r=["SC", "BLK"], w=[pbt(4 + nch % 2)])
                    def el(e, nch=nch):
                        return e.activation(out=SA[:, nch * 512:(nch + 1) * 512],
                                            in_=PB[4 + nch % 2][:, 0:512], func=AF.Ln)
                    B.op("act", el, r=[pbt(4 + nch % 2)], w=["SA"])
                for nch in range(4):
                    def ex(e, nch=nch):
                        return e.activation(out=SA[:, nch * 512:(nch + 1) * 512],
                                            in_=SA[:, nch * 512:(nch + 1) * 512],
                                            func=AF.Exp, scale=-0.5)
                    B.op("act", ex, r=["SA"], w=["SA"])
                for nch in range(4):
                    def em(e, nch=nch):
                        return e.tensor_tensor(out=SB_[:, nch * 512:(nch + 1) * 512],
                                               in0=PB[nch][:, :],
                                               in1=SA[:, nch * 512:(nch + 1) * 512],
                                               op=ALU.mult)
                    B.op("dve", em, r=[pbt(nch), "SA"], w=["SB"])
                def esc(e, sc=sc):
                    return e.activation(out=SB_[:], in_=SB_[:], func=AF.Identity,
                                        scale=BCOL[:, sc:sc + 1])
                B.op("act", esc, r=["SB", "BCOL"], w=["SB"])
                for nch in range(4):
                    for kc in range(NCH):
                        def em(e, which=which, nch=nch, kc=kc):
                            a, _ = ws((3 + which) * 1024 + kc * 128, 128)
                            return e.matmul(PB[nch][:, :], a,
                                            XNc(kc)[:, nch * 512:(nch + 1) * 512],
                                            start=(kc == 0), stop=(kc == NCH - 1))
                        B.op("pe", em, r=wptag + [f"XN:{kc}"], w=[pbt(nch)])
                    def em2(e, nch=nch):
                        return e.tensor_tensor(out=SC[:, nch * 512:(nch + 1) * 512],
                                               in0=PB[nch][:, :],
                                               in1=SA[:, nch * 512:(nch + 1) * 512],
                                               op=ALU.mult)
                    B.op("dve", em2, r=[pbt(nch), "SA"], w=["SC"])
                def esc2(e, scp=scp):
                    return e.activation(out=SC[:], in_=SC[:], func=AF.Identity,
                                        scale=BCOL[:, scp:scp + 1])
                B.op("act", esc2, r=["SC", "BCOL"], w=["SC"])
                def eu(e, cs_off=cs_off):
                    return e.tensor_tensor(out=SB_[:], in0=SB_[:],
                                           in1=ROTB[:, cs_off:cs_off + T], op=ALU.mult)
                B.op("dve", eu, r=["SB", "XL:0", "XL:1"], w=["SB"])
                def ev2(e, cs_off=cs_off):
                    return e.tensor_tensor(out=SC[:], in0=SC[:],
                                           in1=ROTB[:, cs_off + T:cs_off + 2 * T],
                                           op=ALU.mult)
                B.op("dve", ev2, r=["SC", "XL:0", "XL:1"], w=["SC"])
                def eq(e, dst0=dst0):
                    return e.tensor_tensor(out=SH[:, dst0:dst0 + T], in0=SB_[:],
                                           in1=SC[:], op=ALU.add)
                B.op("dve", eq, r=["SB", "SC"], w=dtags)
            if hp == 0:
                for bi in range(32):
                    def emo(e, bi=bi):
                        return e.memset(SH[:, VS0 + bi * 65 + 64:VS0 + bi * 65 + 65], 1.0)
                    B.op("dve", emo, w=vs_tags)
            vtag = slot_info[2][1]
            for jt in range(16):
                for kc in range(NCH):
                    def em(e, jt=jt, kc=kc):
                        a, _ = ws(2 * 1024 + kc * 128, 128)
                        return e.matmul(PB[5][:, 0:128], XNc(kc)[:, jt * 128:(jt + 1) * 128],
                                        a, start=(kc == 0), stop=(kc == NCH - 1))
                    B.op("pe", em, r=vtag + [f"XN:{kc}"], w=[pbt(5)])
                for hh in range(2):
                    def emc(e, jt=jt, hh=hh):
                        off = VS0 + (hh * 16 + jt) * 65
                        return e.activation(out=SH[:, off:off + 64],
                                            in_=PB[5][:, hh * 64:hh * 64 + 64],
                                            func=AF.Copy)
                    B.op("act", emc, r=[pbt(5)], w=vs_tags)
            def foacc(gg):
                return (PB[4], pbt(4)) if gg % 2 == 0 else (PBS, "PBSa")

            def fepilogue(hh, w, gg):
                h = 2 * hp + hh
                acc, acct = foacc(gg)
                r0 = (gg % 2) * 512
                rt = ["R2"] if gg % 2 == 0 else ["R2b"]
                def emf(e, acc=acc, r0=r0):
                    return e.tensor_scalar(out=row(2, r0, 512), in0=acc[64:65, 0:512],
                                           scalar1=1e-35, scalar2=None, op0=ALU.max)
                B.op("dve", emf, r=[acct], w=rt)
                def er1(e, r0=r0):
                    return e.activation(out=row(2, r0, 512), in_=row(2, r0, 512),
                                        func=AF.Ln)
                B.op("act", er1, r=rt, w=rt)
                def er2(e, r0=r0):
                    return e.activation(out=row(2, r0, 512), in_=row(2, r0, 512),
                                        func=AF.Exp, scale=-1.0)
                B.op("act", er2, r=rt, w=rt)
                if gg % 2 == 0:
                    bt_ap, btt = PB[5][:, 0:512], pbt(5)
                else:
                    bt_ap, btt = PBS[:, 512:1024], "PBSb"
                def emb(e, r0=r0, bt_ap=bt_ap):
                    return e.matmul(bt_ap, onesrow_f(2), row(2, r0, 512),
                                    start=True, stop=True)
                B.op("pe", emb, r=rt + ["ONE"], w=[btt])
                def ecp(e, bt_ap=bt_ap):
                    return e.activation(out=SA[:, 0:512], in_=bt_ap, func=AF.Copy)
                B.op("act", ecp, r=[btt], w=["SA"])
                def emo2(e, h=h, w=w, acc=acc):
                    oc = h // 2
                    pr = (h % 2) * 64
                    off = OT0 + oc * T + w * 512
                    return e.tensor_tensor(out=SH[pr:pr + 64, off:off + 512],
                                           in0=acc[0:64, 0:512], in1=SA[0:64, 0:512],
                                           op=ALU.mult)
                B.op("dve", emo2, r=[acct, "SA"],
                     w=sh(OT0 + (h // 2) * T + w * 512, 512)[1])

            def fflush(t):
                hh, w, gg, jglob, pb, i0, N, first, last = t
                acc, acct = foacc(gg)
                def emo(e, jglob=jglob, pb=pb, hh=hh, i0=i0, N=N,
                        first=first, last=last, acc=acc):
                    off = VS0 + (hh * 16 + jglob) * 65
                    return e.matmul(acc[0:65, i0:i0 + N],
                                    SH[:, off:off + 65],
                                    SH[:, ET0 + pb * 512:ET0 + pb * 512 + N],
                                    start=first, stop=last)
                B.op("pe", emo, r=vs_tags + sh(ET0 + pb * 512, 512)[1], w=[acct])
                if last:
                    fepilogue(hh, w, gg)

            fpend = None
            for hh in range(2):
                for w in range(4):
                    gg = hh * 4 + w
                    qb = hh * 64
                    jtls = list(range(4, 8)) if w == 0 else list(range(8))
                    for ti, jtl in enumerate(jtls):
                        jglob = 4 * (w - 1) + jtl
                        pb = ti % 2
                        i0 = max(0, (jtl - 4) * 128)
                        i1 = min(512, (jtl + 1) * 128)
                        N = i1 - i0
                        def em(e, pb=pb, jglob=jglob, w=w, qb=qb, i0=i0, N=N):
                            return e.matmul(
                                PB[pb][:, 0:N],
                                SH[qb:qb + 64, KK0 + jglob * 128:KK0 + (jglob + 1) * 128],
                                SH[qb:qb + 64, QQ0 + w * 512 + i0:QQ0 + w * 512 + i0 + N],
                                start=True, stop=False)
                        B.op("pe", em, r=qq_tags + kk_tags, w=[pbt(pb)])
                        if w > 0 and jtl <= 3:
                            moff, mpos = 0, jtl * 128 - i0
                        elif jtl >= 4:
                            moff, mpos = 128, (jtl - 4) * 128 - i0
                        else:
                            moff = None
                        if moff is not None:
                            def emk(e, pb=pb, moff=moff, mpos=mpos):
                                return e.matmul(PB[pb][:, mpos:mpos + 128],
                                                IDENTB[:, :], MSK[:, moff:moff + 128],
                                                start=False, stop=True)
                            B.op("pe", emk, r=["IDENTB", "MSK"], w=[pbt(pb)])
                        else:
                            def emk(e, pb=pb):
                                return e.matmul(PB[pb][0:1, 0:1], IDENTB[0:1, 0:1],
                                                ZEROB[0:1, 0:1], start=False, stop=True)
                            B.op("pe", emk, r=["IDENTB", "ZEROB"], w=[pbt(pb)])
                        if fpend is not None:
                            fflush(fpend)
                        def eme(e, pb=pb, N=N):
                            return e.activation(out=SH[:, ET0 + pb * 512:ET0 + pb * 512 + N],
                                                in_=PB[pb][:, 0:N], func=AF.Exp, scale=8.0)
                        B.op("act", eme, r=[pbt(pb)], w=sh(ET0 + pb * 512, 512)[1])
                        fpend = (hh, w, gg, jglob, pb, i0, N,
                                 ti == 0, ti == len(jtls) - 1)
            fflush(fpend)
        for m in range(NCH):
            for kc in range(4):
                sl_ap, sl_tags = ws(6144 + (kc % 2) * 128, 128)
                def emw(e, m=m, kc=kc, sl_ap=sl_ap):
                    return e.dma_start(out=sl_ap,
                                       in_=fwo[l][kc * 128:(kc + 1) * 128, m * 128:(m + 1) * 128])
                B.dma("qpool", emw, r=["fwo"], w=sl_tags)
                for sp in range(4):
                    def em(e, m=m, kc=kc, sp=sp, sl_ap=sl_ap):
                        return e.matmul(PB[sp][:, :], sl_ap,
                                        SH[:, OT0 + kc * T + sp * 512:OT0 + kc * T + (sp + 1) * 512],
                                        start=(kc == 0), stop=(kc == 3))
                    B.op("pe", em, r=sl_tags + sh(8192 + kc * T, T)[1], w=[pbt(sp)])
            for sp in range(4):
                def ea(e, m=m, sp=sp):
                    return e.tensor_tensor(
                        out=XFc(m)[:, sp * 512:(sp + 1) * 512],
                        in0=XFc(m)[:, sp * 512:(sp + 1) * 512],
                        in1=PB[sp][:, :], op=ALU.add)
                B.op("dve", ea, r=[pbt(sp), f"XF:{m}"], w=[f"XF:{m}"])
        stats_init()
        shift_to_XN()
        for c in range(NCH):
            stats_from_bf16(lambda c=c: XNc(c), [f"XN:{c}"])
        stats_rows()

        def srcfn(c, nch):
            if c < 4:
                return (XFc(c)[:, nch * 512:(nch + 1) * 512], [f"XF:{c}"], False)
            if nch == 0:
                return (XFc(c)[:, 0:511], [f"XF:{c}"], True)
            return (XFc(c)[:, nch * 512 - 1:(nch + 1) * 512 - 1], [f"XF:{c}"], False)

        normalize(lambda c, nch: (XNc(c)[:, nch * 512:(nch + 1) * 512], [f"XN:{c}"]),
                  src_fn=srcfn,
                  gain_bias=lambda c: (BCOL[:, OFF_LNG + 8 * l + c:OFF_LNG + 8 * l + c + 1],
                                       BCOL[:, OFF_LNB + 8 * l + c:OFF_LNB + 8 * l + c + 1]))

        w1r = fw1[l].rearrange("(c p) n -> p c n", p=128)
        HID = 2730
        groups = [list(range(0, 8)), list(range(8, 16)), list(range(16, 22))]
        for g in groups:
            for gi, cch in enumerate(g):
                wdt = min(128, HID - cch * 128)
                wa0 = (5 * 1024) if cch % 2 == 0 else 0
                wg0 = (6 * 1024) if cch % 2 == 0 else 1024
                wa_ap, wa_tags = ws(wa0, 8 * wdt)
                wg_ap, wg_tags = ws(wg0, 8 * wdt)
                def emwa(e, wa_ap=wa_ap, cch=cch, wdt=wdt):
                    return e.dma_start(out=wa_ap.rearrange("p (c n) -> p c n", c=8),
                                       in_=w1r[:, :, cch * 128:cch * 128 + wdt])
                B.dma("qpool", emwa, r=["fw1"], w=wa_tags)
                def emwg(e, wg_ap=wg_ap, cch=cch, wdt=wdt):
                    return e.dma_start(out=wg_ap.rearrange("p (c n) -> p c n", c=8),
                                       in_=w1r[:, :, HID + cch * 128:HID + cch * 128 + wdt])
                B.dma("qpool", emwg, r=["fw1"], w=wg_tags)
                for nch in range(4):
                    for kc in range(NCH):
                        def ema(e, nch=nch, kc=kc, wdt=wdt, wa0=wa0):
                            a, _ = ws(wa0 + kc * wdt, wdt)
                            return e.matmul(PB[nch % 2][0:wdt, :], a,
                                            XNc(kc)[:, nch * 512:(nch + 1) * 512],
                                            start=(kc == 0), stop=(kc == NCH - 1))
                        B.op("pe", ema, r=wa_tags + [f"XN:{kc}"], w=[pbt(nch % 2)])
                        def emg(e, nch=nch, kc=kc, wdt=wdt, wg0=wg0):
                            a, _ = ws(wg0 + kc * wdt, wdt)
                            return e.matmul(PB[2 + nch % 2][0:wdt, :], a,
                                            XNc(kc)[:, nch * 512:(nch + 1) * 512],
                                            start=(kc == 0), stop=(kc == NCH - 1))
                        B.op("pe", emg, r=wg_tags + [f"XN:{kc}"], w=[pbt(2 + nch % 2)])
                    def eg1(e, nch=nch, wdt=wdt, cch=cch):
                        bcg = OFF_B1 + 44 * l + 22 + cch
                        return e.activation(out=SCb[0:wdt, nch * 512:(nch + 1) * 512],
                                            in_=PB[2 + nch % 2][0:wdt, :], func=AF.Gelu,
                                            bias=BCOL[0:wdt, bcg:bcg + 1])
                    B.op("act", eg1, r=[pbt(2 + nch % 2), "BCOL"], w=["SC"])
                    def eg2(e, nch=nch, wdt=wdt, cch=cch, gi=gi):
                        bca = OFF_B1 + 44 * l + cch
                        a, _ = sh(gi * T + nch * 512, 512, rows=(0, wdt))
                        return e.scalar_tensor_tensor(
                            out=a, in0=PB[nch % 2][0:wdt, :],
                            scalar=BCOL[0:wdt, bca:bca + 1],
                            in1=SCb[0:wdt, nch * 512:(nch + 1) * 512],
                            op0=ALU.add, op1=ALU.mult)
                    B.op("dve", eg2, r=[pbt(nch % 2), "SC", "BCOL"],
                         w=sh(gi * T, 2048)[1])
            last_group = (g[0] == 16)
            for m in range(NCH):
                for gi, cch in enumerate(g):
                    wdt = min(128, HID - cch * 128)
                    sl_ap, sl_tags = ws(7 * 1024 + (gi % 2) * 128, 128, rows=(0, wdt))
                    def emw(e, m=m, cch=cch, wdt=wdt, sl_ap=sl_ap):
                        return e.dma_start(out=sl_ap,
                                           in_=fw2[l][cch * 128:cch * 128 + wdt,
                                                      m * 128:(m + 1) * 128])
                    B.dma("qpool", emw, r=["fw2"], w=sl_tags)
                    for sp in range(4):
                        def em(e, m=m, sp=sp, gi=gi, wdt=wdt, sl_ap=sl_ap, n=len(g)):
                            a, _ = sh(gi * T + sp * 512, 512, rows=(0, wdt))
                            return e.matmul(PB[2 + sp][:, :], sl_ap, a,
                                            start=(gi == 0), stop=(gi == n - 1))
                        B.op("pe", em, r=sl_tags + sh(gi * T, 2048)[1], w=[pbt(2 + sp)])
                for sp in range(4):
                    if last_group:
                        def ea(e, m=m, sp=sp):
                            return e.scalar_tensor_tensor(
                                out=XFc(m)[:, sp * 512:(sp + 1) * 512],
                                in0=PB[2 + sp][:, :],
                                scalar=BCOL[:, OFF_B2 + 8 * l + m:OFF_B2 + 8 * l + m + 1],
                                in1=XFc(m)[:, sp * 512:(sp + 1) * 512],
                                op0=ALU.add, op1=ALU.add)
                        B.op("dve", ea, r=[pbt(2 + sp), f"XF:{m}", "BCOL"], w=[f"XF:{m}"])
                    else:
                        def ea(e, m=m, sp=sp):
                            return e.tensor_tensor(
                                out=XFc(m)[:, sp * 512:(sp + 1) * 512],
                                in0=XFc(m)[:, sp * 512:(sp + 1) * 512],
                                in1=PB[2 + sp][:, :], op=ALU.add)
                        B.op("dve", ea, r=[pbt(2 + sp), f"XF:{m}"], w=[f"XF:{m}"])

    def ffn_call(src_x, src_fmt, dst_x, dst_fmt, final):
        for bi in range(4):
            def emr(e, bi=bi):
                return e.dma_start(out=ROTB[:, bi * T:(bi + 1) * T], in_=rot_t[bi])
            B.dma("qsp", emr, r=["rot"], w=["XL:0" if bi < 2 else "XL:1"])
        for l in range(2):
            ffn_layer(l)
        if not final:
            stats_init()
        for c in range(NCH):
            def eml(e, c=c):
                return e.dma_start(out=XL[:, (c % 2) * T:(c % 2) * T + T],
                                   in_=src_x[c * 128:(c + 1) * 128, :])
            B.dma("qsp", eml, r=[src_fmt.format(c)], w=[f"XL:{c % 2}"])
            def emr2(e, c=c):
                return e.tensor_tensor(out=XFc(c), in0=XFc(c),
                                       in1=XL[:, (c % 2) * T:(c % 2) * T + T], op=ALU.add)
            B.op("dve", emr2, r=[f"XF:{c}", f"XL:{c % 2}"], w=[f"XF:{c}"])
            def emw(e, c=c):
                return e.dma_start(out=dst_x[c * 128:(c + 1) * 128, :], in_=XFc(c))
            B.dma("qsp", emw, r=[f"XF:{c}"], w=[dst_fmt.format(c)])
            if not final:
                stats_chunk(lambda c=c: XFc(c), [f"XF:{c}"])
        if not final:
            stats_rows()

    ffn_call(xs1, "xs1:{}", xs2, "xs2:{}", False)
    norm_to_XN()

    def attn_b():
        Q0, K0, V0, ET0, OT0 = 0, 4096, 8192, 11264, 12288
        EPS = 1e-6
        CEXP = 70.0
        wq = bw_qkv.rearrange("(c p) n -> p c n", p=128)
        v_tags = sh(V0, 2080)[1]
        if debug:
            for c in range(NCH):
                def emt(e, c=c):
                    return e.dma_start(out=xt2p[c * 128:(c + 1) * 128, :], in_=XFc(c))
                B.dma("qsp", emt, r=[f"XF:{c}"], w=["xt2p"])
        for grp in range(4):
            if debug and grp == 2:
                for c in range(NCH):
                    def emt(e, c=c):
                        return e.dma_start(out=xt3m[c * 128:(c + 1) * 128, :], in_=XFc(c))
                    B.dma("qsp", emt, r=[f"XF:{c}"], w=["xt3m"])
            ot_tags = [sh(OT0, 2048)[1], sh(OT0 + 2048, 2048)[1]]
            for gh in range(2):
                hp = grp * 2 + gh
                slot_tags = []
                for si, base in enumerate((hp * 128, 1024 + hp * 128, 2048 + hp * 128)):
                    s_ap, s_tags = ws(si * 1024, 1024)
                    slot_tags.append(s_tags)
                    def emw(e, s_ap=s_ap, base=base):
                        return e.dma_start(out=s_ap.rearrange("p (c n) -> p c n", c=8),
                                           in_=wq[:, :, base:base + 128])
                    B.dma("qpool", emw, r=["bw_qkv"], w=s_tags)
                for which in range(2):
                    for nch in range(4):
                        for kc in range(NCH):
                            def em(e, which=which, nch=nch, kc=kc):
                                a, _ = ws(which * 1024 + kc * 128, 128)
                                return e.matmul(PB[nch][:, :], a,
                                                XNc(kc)[:, nch * 512:(nch + 1) * 512],
                                                start=(kc == 0), stop=(kc == NCH - 1))
                            B.op("pe", em, r=slot_tags[which] + [f"XN:{kc}"], w=[pbt(nch)])
                    for hh in range(2):
                        dst0 = (Q0, K0)[which] + hh * 2048
                        dtags = sh(dst0, 2048)[1]
                        if which == 0:
                            bias_col = OFF_BQKV + hp
                            scl = 1.0
                        else:
                            bias_col = OFF_MBK + hp
                            scl = -2.0
                        for nch in range(4):
                            def ee(e, hh=hh, nch=nch, dst0=dst0, bias_col=bias_col, scl=scl):
                                return e.activation(
                                    out=SH[0:64, dst0 + nch * 512:dst0 + (nch + 1) * 512],
                                    in_=PB[nch][hh * 64:hh * 64 + 64, :],
                                    func=AF.Identity, scale=scl,
                                    bias=BCOL[hh * 64:hh * 64 + 64, bias_col:bias_col + 1])
                            B.op("act", ee, r=[pbt(nch), "BCOL"], w=dtags)
                for hh in range(2):
                    qd = Q0 + hh * 2048
                    kd = K0 + hh * 2048
                    qtags = sh(qd, 2048)[1]
                    ktags = sh(kd, 2048)[1]
                    B.op("dve", lambda e, qd=qd: e.memset(SH[64:68, qd:qd + T], 1.0),
                         w=qtags)
                    B.op("dve", lambda e, kd=kd: e.memset(SH[64:68, kd:kd + T], 1.0),
                         w=ktags)
                    B.op("act", lambda e, qd=qd: e.activation(out=SCb[0:64, 0:T],
                                                              in_=SH[0:64, qd:qd + T],
                                                              func=AF.Square),
                         r=qtags, w=["SC", "SC:0", "SC:1"])
                    for nch in range(4):
                        def em(e, nch=nch):
                            return e.matmul(PBS[0:1, 0:512], ONEB[0:64, 0:1],
                                            SCb[0:64, nch * 512:(nch + 1) * 512],
                                            start=True, stop=True)
                        B.op("pe", em, r=["SC", "SC:0", "SC:1", "ONE"], w=["PBSa"])
                        def ee(e, hh=hh, nch=nch):
                            return e.activation(out=row(hh, nch * 512, 512),
                                                in_=PBS[0:1, 0:512], func=AF.Identity,
                                                scale=1.0, bias=64.0 + EPS)
                        B.op("act", ee, r=["PBSa", "CONSTS"], w=[f"R{hh}"])
                    p0 = hh * 32
                    def eqh(e, hh=hh, p0=p0):
                        return e.activation(out=ROWSB[p0:p0 + 1, 0:T], in_=row(hh),
                                            func=AF.Copy)
                    B.op("act", eqh, r=[f"R{hh}"], w=["RBQ"])
                    def eql(e, hh=hh, p0=p0):
                        return e.tensor_tensor(out=ROWSB[p0:p0 + 1, T:2 * T], in0=row(hh),
                                               in1=ROWSB[p0:p0 + 1, 0:T], op=ALU.subtract)
                    B.op("dve", eql, r=[f"R{hh}", "RBQ"], w=["RBQ2"])
                    def ew1(e, hh=hh, p0=p0):
                        return e.dma_start(out=auxs[4 * hh + 0], in_=ROWSB[p0:p0 + 1, 0:T])
                    B.dma("qsp", ew1, r=["RBQ"], w=[f"aux{4 * hh}"])
                    def ew2(e, hh=hh, p0=p0):
                        return e.dma_start(out=auxs[4 * hh + 1],
                                           in_=ROWSB[p0:p0 + 1, T:2 * T])
                    B.dma("qsp", ew2, r=["RBQ2"], w=[f"aux{4 * hh + 1}"])
                    def edq(e, qd=qd, hh=hh):
                        return e.dma_start(out=SH[66:67, qd:qd + T], in_=auxs[4 * hh + 0])
                    B.dma("qsp", edq, r=[f"aux{4 * hh}"], w=qtags)
                    def edq2(e, qd=qd, hh=hh):
                        return e.dma_start(out=SH[67:68, qd:qd + T], in_=auxs[4 * hh + 1])
                    B.dma("qsp", edq2, r=[f"aux{4 * hh + 1}"], w=qtags)
                    B.op("act", lambda e, kd=kd: e.activation(out=SCb[0:64, 0:T],
                                                              in_=SH[0:64, kd:kd + T],
                                                              func=AF.Square),
                         r=ktags, w=["SC", "SC:0", "SC:1"])
                    for nch in range(4):
                        def em(e, nch=nch):
                            return e.matmul(PBS[0:1, 512:1024], ONEB[0:64, 0:1],
                                            SCb[0:64, nch * 512:(nch + 1) * 512],
                                            start=True, stop=True)
                        B.op("pe", em, r=["SC", "SC:0", "SC:1", "ONE"], w=["PBSb"])
                        def ee(e, nch=nch):
                            return e.activation(out=row(2, nch * 512, 512),
                                                in_=PBS[0:1, 512:1024], func=AF.Identity,
                                                scale=0.25, bias=-64.0)
                        B.op("act", ee, r=["PBSb", "CONSTS"], w=["R2", "R2b"])
                    def ekh(e, kd=kd):
                        return e.activation(out=SH[64:65, kd:kd + T], in_=row(2),
                                            func=AF.Copy)
                    B.op("act", ekh, r=["R2"], w=ktags)
                    def ekl(e, kd=kd):
                        return e.tensor_tensor(out=ROWSB[64:65, 1024:1024 + T], in0=row(2),
                                               in1=SH[64:65, kd:kd + T], op=ALU.subtract)
                    B.op("dve", ekl, r=["R2"] + ktags, w=["RBK"])
                    def ew3(e, hh=hh):
                        return e.dma_start(out=auxs[4 * hh + 2],
                                           in_=ROWSB[64:65, 1024:1024 + T])
                    B.dma("qsp", ew3, r=["RBK"], w=[f"aux{4 * hh + 2}"])
                    def edk(e, kd=kd, hh=hh):
                        return e.dma_start(out=SH[65:66, kd:kd + T], in_=auxs[4 * hh + 2])
                    B.dma("qsp", edk, r=[f"aux{4 * hh + 2}"], w=ktags)
                if hp == 0 and debug:
                    def edK(e):
                        return e.dma_start(out=dumpK, in_=SH[0:68, K0:K0 + T])
                    B.dma("qsp", edK, r=sh(K0, 2048)[1], w=["dumpK"])
                    def edQ(e):
                        return e.dma_start(out=dumpQ, in_=SH[0:68, Q0:Q0 + T])
                    B.dma("qsp", edQ, r=sh(Q0, 2048)[1], w=["dumpQ"])
                    for r_ in range(4):
                        def ed1(e, r_=r_):
                            return e.dma_start(out=auxdump[r_],
                                               in_=SH[64 + r_:65 + r_, K0:K0 + T])
                        B.dma("qsp", ed1, r=sh(K0, 2048)[1], w=["auxdump"])
                        def ed2(e, r_=r_):
                            return e.dma_start(out=auxdump[4 + r_],
                                               in_=SH[64 + r_:65 + r_, Q0:Q0 + T])
                        B.dma("qsp", ed2, r=sh(Q0, 2048)[1], w=["auxdump"])
                        def ed3(e, r_=r_):
                            return e.dma_start(out=auxdump[8 + r_],
                                               in_=SH[r_:r_ + 1, K0:K0 + T])
                        B.dma("qsp", ed3, r=sh(K0, 2048)[1], w=["auxdump"])
                        def ed4(e, r_=r_):
                            return e.dma_start(out=auxdump[12 + r_],
                                               in_=SH[r_:r_ + 1, Q0:Q0 + T])
                        B.dma("qsp", ed4, r=sh(Q0, 2048)[1], w=["auxdump"])
                if hp == 0:
                    for bi in range(32):
                        def emo(e, bi=bi):
                            return e.memset(SH[:, V0 + bi * 65 + 64:V0 + bi * 65 + 65], 1.0)
                        B.op("dve", emo, w=v_tags)
                for jt in range(16):
                    for kc in range(NCH):
                        def em(e, jt=jt, kc=kc):
                            a, _ = ws(2 * 1024 + kc * 128, 128)
                            return e.matmul(PB[5][:, 0:128],
                                            XNc(kc)[:, jt * 128:(jt + 1) * 128],
                                            a, start=(kc == 0), stop=(kc == NCH - 1))
                        B.op("pe", em, r=slot_tags[2] + [f"XN:{kc}"], w=[pbt(5)])
                    for hh in range(2):
                        def emc(e, jt=jt, hh=hh):
                            off = V0 + (hh * 16 + jt) * 65
                            return e.activation(out=SH[:, off:off + 64],
                                                in_=PB[5][:, hh * 64:hh * 64 + 64],
                                                func=AF.Copy)
                        B.op("act", emc, r=[pbt(5)], w=v_tags)
                for jt in range(16):
                    def emb2(e, hp=hp):
                        return e.matmul(PB[5][:, 0:128], ONEBR[64:65, 0:128],
                                        ROWSB[64:65, hp * 128:(hp + 1) * 128],
                                        start=True, stop=True)
                    B.op("pe", emb2, r=["ONE", "RBV"], w=[pbt(5)])
                    for hh in range(2):
                        def ea(e, jt=jt, hh=hh):
                            off = V0 + (hh * 16 + jt) * 65
                            return e.tensor_tensor(out=SH[:, off:off + 64],
                                                   in0=PB[5][:, hh * 64:hh * 64 + 64],
                                                   in1=SH[:, off:off + 64], op=ALU.add)
                        B.op("dve", ea, r=[pbt(5)] + v_tags, w=v_tags)
                tiles = []
                for hh in range(2):
                    for ic in range(4):
                        for jt in range(4 * ic + 4):
                            tiles.append((hh, ic, jt))

                def oacc(g):
                    return (PB[4], pbt(4)) if g % 2 == 0 else (PBS, "PBSa")

                def epilogue(hh, ic, g):
                    acc, acct = oacc(g)
                    r0 = (g % 2) * 512
                    rt = ["R2"] if g % 2 == 0 else ["R2b"]
                    def emf(e, acc=acc, r0=r0):
                        return e.tensor_scalar(out=row(2, r0, 512),
                                               in0=acc[64:65, 0:512],
                                               scalar1=1e-35, scalar2=2.0 ** 64,
                                               op0=ALU.max, op1=ALU.mult)
                    B.op("dve", emf, r=[acct], w=rt)
                    if debug and hp == 0 and hh == 0 and ic == 0:
                        def edS1(e):
                            return e.dma_start(out=dumpS[1:2, :], in_=row(2, 0, 512))
                        B.dma("qsp", edS1, r=rt, w=["dumpS2"])
                    def er1(e, r0=r0):
                        return e.activation(out=row(2, r0, 512), in_=row(2, r0, 512),
                                            func=AF.Ln)
                    B.op("act", er1, r=rt, w=rt)
                    def er2(e, r0=r0):
                        return e.activation(out=row(2, r0, 512), in_=row(2, r0, 512),
                                            func=AF.Exp, scale=-1.0)
                    B.op("act", er2, r=rt, w=rt)
                    if debug and hp == 0 and hh == 0 and ic == 0:
                        def edS0(e):
                            return e.dma_start(out=dumpS[0:1, :], in_=row(2, 0, 512))
                        B.dma("qsp", edS0, r=rt, w=["dumpS"])
                    if g % 2 == 0:
                        bt_ap, btt = PB[5][:, 0:512], pbt(5)
                    else:
                        bt_ap, btt = PBS[:, 512:1024], "PBSb"
                    def emb3(e, r0=r0, bt_ap=bt_ap):
                        return e.matmul(bt_ap, onesrow_f(2), row(2, r0, 512),
                                        start=True, stop=True)
                    B.op("pe", emb3, r=rt + ["ONE"], w=[btt])
                    sbo = (g % 2) * 512
                    sbt = [f"SB:{g % 2}"]
                    def ecp(e, bt_ap=bt_ap, sbo=sbo):
                        return e.activation(out=SB_[:, sbo:sbo + 512], in_=bt_ap,
                                            func=AF.Identity, scale=2.0 ** 64)
                    B.op("act", ecp, r=[btt], w=sbt)
                    def emo(e, hh=hh, ic=ic, acc=acc, sbo=sbo, gh=gh):
                        off = OT0 + gh * T + ic * 512
                        pr = hh * 64
                        return e.tensor_tensor(out=SH[pr:pr + 64, off:off + 512],
                                               in0=acc[0:64, 0:512],
                                               in1=SB_[0:64, sbo:sbo + 512],
                                               op=ALU.mult)
                    B.op("dve", emo, r=[acct] + sbt, w=ot_tags[gh])

                ET3 = 10752

                def flush5(t):
                    idx, hh, ic, jt = t
                    s = idx % 4
                    ep = idx % 3
                    sa_ap = SA[:, s * 512:(s + 1) * 512]
                    sat = [f"SA:{s}"]
                    ett = sh(ET3 + ep * 512, 512)[1]
                    def e5(e, sa_ap=sa_ap, ep=ep):
                        return e.activation(
                            out=SH[:, ET3 + ep * 512:ET3 + (ep + 1) * 512],
                            in_=sa_ap, func=AF.Exp, scale=0.25, bias=-CEXP)
                    B.op("act", e5, r=sat + ["CONSTS"], w=ett)
                    if debug and hp == 0 and idx == 0:
                        def edE(e):
                            return e.dma_start(out=dumpE, in_=SH[:, ET3:ET3 + 512])
                        B.dma("qsp", edE, r=ett, w=["dumpE"])

                def flush6(t):
                    idx, hh, ic, jt = t
                    g = hh * 4 + ic
                    n = 4 * ic + 4
                    ep = idx % 3
                    ett = sh(ET3 + ep * 512, 512)[1]
                    acc, acct = oacc(g)
                    def e6(e, jt=jt, ep=ep, hh=hh, n=n, acc=acc):
                        off = V0 + (hh * 16 + jt) * 65
                        return e.matmul(acc[0:65, 0:512], SH[:, off:off + 65],
                                        SH[:, ET3 + ep * 512:ET3 + (ep + 1) * 512],
                                        start=(jt == 0), stop=(jt == n - 1))
                    B.op("pe", e6, r=v_tags + ett, w=[acct])
                    if jt == n - 1:
                        epilogue(hh, ic, g)

                pend5 = None
                pend6 = None
                for idx, (hh, ic, jt) in enumerate(tiles):
                    qd = Q0 + hh * 2048
                    kd = K0 + hh * 2048
                    qtags = sh(qd, 2048)[1]
                    ktags = sh(kd, 2048)[1]
                    pbp = idx % 2
                    s = idx % 4
                    sa_ap = SA[:, s * 512:(s + 1) * 512]
                    sc_ap = SC[:, s * 512:(s + 1) * 512]
                    sat = [f"SA:{s}"]
                    sct = [f"SC:{s}"]
                    def emd(e, jt=jt, ic=ic, qd=qd, kd=kd, pbp=pbp):
                        return e.matmul(PB[pbp][:, :],
                                        SH[0:68, kd + jt * 128:kd + (jt + 1) * 128],
                                        SH[0:68, qd + ic * 512:qd + (ic + 1) * 512],
                                        start=True, stop=True)
                    B.op("pe", emd, r=ktags + qtags, w=[pbt(pbp)])
                    def emn(e, jt=jt, ic=ic, qd=qd, kd=kd, pbp=pbp):
                        return e.matmul(PB[2 + pbp][:, :],
                                        SH[0:64, kd + jt * 128:kd + (jt + 1) * 128],
                                        SH[0:64, qd + ic * 512:qd + (ic + 1) * 512],
                                        start=True, stop=True)
                    B.op("pe", emn, r=ktags + qtags, w=[pbt(2 + pbp)])
                    def e1b(e, pbp=pbp, sa_ap=sa_ap):
                        return e.activation(out=sa_ap, in_=PB[pbp][:, :], func=AF.Ln)
                    B.op("act", e1b, r=[pbt(pbp)], w=sat)
                    def e1c(e, sa_ap=sa_ap):
                        return e.activation(out=sa_ap, in_=sa_ap, func=AF.Exp,
                                            scale=-1.0)
                    B.op("act", e1c, r=sat, w=sat)
                    def e2(e, pbp=pbp, sc_ap=sc_ap):
                        return e.activation(out=sc_ap, in_=PB[2 + pbp][:, :],
                                            func=AF.Square)
                    B.op("act", e2, r=[pbt(2 + pbp)], w=sct)
                    def e3(e, sa_ap=sa_ap, sc_ap=sc_ap):
                        return e.tensor_tensor(out=sa_ap, in0=sc_ap, in1=sa_ap,
                                               op=ALU.mult)
                    B.op("dve", e3, r=sat + sct, w=sat)
                    if jt >= 4 * ic:
                        kk = jt - 4 * ic
                        def e4(e, kk=kk, sa_ap=sa_ap):
                            return e.tensor_tensor(
                                out=sa_ap, in0=sa_ap,
                                in1=MSK[:, 256 + kk * 512:256 + (kk + 1) * 512],
                                op=ALU.add)
                        B.op("dve", e4, r=sat + ["MSK"], w=sat)
                    if pend6 is not None:
                        flush6(pend6)
                    if pend5 is not None:
                        flush5(pend5)
                    pend6, pend5 = pend5, (idx, hh, ic, jt)
                flush6(pend6)
                flush5(pend5)
                flush6(pend5)
            for m in range(NCH):
                for kci in range(2):
                    kc = 2 * grp + kci
                    sl_ap, sl_tags = ws(3 * 1024 + kci * 128, 128)
                    def emw(e, m=m, kc=kc, sl_ap=sl_ap):
                        return e.dma_start(
                            out=sl_ap,
                            in_=bw_out[kc * 128:(kc + 1) * 128, m * 128:(m + 1) * 128])
                    B.dma("qpool", emw, r=["bw_out"], w=sl_tags)
                    for sp in range(4):
                        def em(e, m=m, sp=sp, kci=kci, sl_ap=sl_ap):
                            return e.matmul(
                                PB[sp][:, :], sl_ap,
                                SH[:, OT0 + kci * T + sp * 512:OT0 + kci * T + (sp + 1) * 512],
                                start=(kci == 0), stop=(kci == 1))
                        B.op("pe", em, r=sl_tags + ot_tags[kci], w=[pbt(sp)])
                for sp in range(4):
                    if grp == 3:
                        def ea(e, m=m, sp=sp):
                            return e.scalar_tensor_tensor(
                                out=XFc(m)[:, sp * 512:(sp + 1) * 512],
                                in0=PB[sp][:, :],
                                scalar=BCOL[:, OFF_BOUT + m:OFF_BOUT + m + 1],
                                in1=XFc(m)[:, sp * 512:(sp + 1) * 512],
                                op0=ALU.add, op1=ALU.add)
                        B.op("dve", ea, r=[pbt(sp), f"XF:{m}", "BCOL"], w=[f"XF:{m}"])
                    else:
                        def ea(e, m=m, sp=sp):
                            return e.tensor_tensor(
                                out=XFc(m)[:, sp * 512:(sp + 1) * 512],
                                in0=XFc(m)[:, sp * 512:(sp + 1) * 512],
                                in1=PB[sp][:, :], op=ALU.add)
                        B.op("dve", ea, r=[pbt(sp), f"XF:{m}"], w=[f"XF:{m}"])
        stats_init()
        for c in range(NCH):
            def emw(e, c=c):
                return e.dma_start(out=xs1[c * 128:(c + 1) * 128, :], in_=XFc(c))
            B.dma("qsp", emw, r=[f"XF:{c}"], w=[f"xs1:{c}"])
            stats_chunk(lambda c=c: XFc(c), [f"XF:{c}"])
        stats_rows()
        norm_to_XF()

    attn_b()
    ffn_call(xs1, "xs1:{}", yT, "yT:{}", True)

    extra = (["xt1", "xt2p", "xt3m", "auxdump", "dumpK", "dumpQ", "dumpE", "dumpO", "dumpS"] + [f"xs1:{c}" for c in range(NCH)] + [f"xs2:{c}" for c in range(NCH)]) if debug else []
    B.op("sp", lambda e: e.nop(), r=[f"yT:{c}" for c in range(NCH)] + extra)
    import os
    if os.environ.get("KSTRICT"):
        B.strict_ranges.append((0, len(B.ops)))
    B.finalize(nc)
    return nc


_CACHE = {}


def kernel(**inputs):
    x = np.asarray(inputs["L0_attn_a_x"], dtype=np.float32)
    nb = x.shape[0]
    rot, mska, mskb, mskc = _host_consts()

    def f32(a):
        return np.ascontiguousarray(np.asarray(a, dtype=np.float32))

    wqkv = f32(inputs["L0_ffn_wqkv"])
    perm = np.concatenate([(h * 64 + (np.arange(64) + 32) % 64) for h in range(8)])
    fwperm = np.ascontiguousarray(
        np.concatenate([wqkv[:, :, perm], wqkv[:, :, 512 + perm]], axis=2))
    b1 = f32(inputs["L0_ffn_b1"])

    common = {
        "wqT": f32(np.asarray(inputs["L0_attn_a_Wq"]).T),
        "wkvT": f32(np.asarray(inputs["L0_attn_a_Wkv_down"]).T),
        "wkupT": f32(np.asarray(inputs["L0_attn_a_Wk_up"]).T),
        "wvupT": f32(np.asarray(inputs["L0_attn_a_Wv_up"]).T),
        "woT": f32(np.asarray(inputs["L0_attn_a_Wo"]).T),
        "bw_qkv": f32(inputs["L0_attn_b_w_qkv"]),
        "bb_qkv": f32(inputs["L0_attn_b_b_qkv"]).reshape(1, -1),
        "bw_out": f32(inputs["L0_attn_b_w_out"]),
        "bb_out": f32(inputs["L0_attn_b_b_out"]).reshape(1, -1),
        "fwqkv": wqkv,
        "fwperm": fwperm,
        "fqs": f32(inputs["L0_ffn_q_scale"]).reshape(2, 1, 64),
        "fks": f32(inputs["L0_ffn_k_scale"]).reshape(2, 1, 64),
        "fwo": f32(inputs["L0_ffn_wo"]),
        "flng": f32(inputs["L0_ffn_ln_g"]).reshape(2, 1, D),
        "flnb": f32(inputs["L0_ffn_ln_b"]).reshape(2, 1, D),
        "fw1": f32(inputs["L0_ffn_w1"]),
        "fb1a": np.ascontiguousarray(b1[:, 0:2730]).reshape(2, 1, 2730),
        "fb1g": np.ascontiguousarray(b1[:, 2730:5460]).reshape(2, 1, 2730),
        "fw2": f32(inputs["L0_ffn_w2"]),
        "fb2": f32(inputs["L0_ffn_b2"]).reshape(2, 1, D),
        "rot": rot, "mska": mska, "mskb": mskb, "mskc": mskc,
        "identb": np.eye(128, dtype=BFNP),
        "onefc": _onefc(), "onebc": _onefc().astype(BFNP),
        "onefr": _onefr(), "onebr": _onebr(), "blk64": _blk64(),
    }

    debug = bool(inputs.pop("_debug", False))
    key = ("nc", debug)
    if key not in _CACHE:
        _CACHE[key] = build_program(debug)
    nc = _CACHE[key]

    in_maps = []
    for b in range(nb):
        m = dict(common)
        m["xT"] = np.ascontiguousarray(x[b].T)
        in_maps.append(m)

    res = run_bass_kernel_spmd(nc, in_maps, core_ids=list(range(nb)))
    if debug:
        taps = {nm: np.stack([np.ascontiguousarray(r[nm].T) for r in res.results])
                for nm in ("xt1", "xs2", "xs1", "xt2p", "xt3m")}
        for nm2 in ("auxdump", "dumpK", "dumpQ", "dumpE", "dumpO", "dumpS"):
            taps[nm2] = np.stack([np.asarray(r[nm2]) for r in res.results])
        taps["out"] = np.stack([np.ascontiguousarray(r["yT"].T) for r in res.results])
        return taps
    return np.stack([np.ascontiguousarray(r["yT"].T) for r in res.results])


# version 15
# speedup vs baseline: 1.4120x; 1.0866x over previous
import numpy as np
import ml_dtypes
import concourse.bass as bass
import concourse.mybir as mybir
from concourse.bass_utils import run_bass_kernel_spmd

F32 = mybir.dt.float32
BF16 = mybir.dt.bfloat16
AF = mybir.ActivationFunctionType
ALU = mybir.AluOpType
BFNP = ml_dtypes.bfloat16

T = 2048
D = 1024
NCH = 8

COMPUTE_ENGINES = ("pe", "act", "dve", "pool", "sp")
DMA_QUEUES = {"qsp": "sp", "qpool": "pool"}
ALL_ENTITIES = list(COMPUTE_ENGINES) + list(DMA_QUEUES)


class _Op:
    __slots__ = ("idx", "entity", "issuer", "emit", "reads", "writes", "waits",
                 "marked", "ent_idx", "know", "qwait")

    def __init__(self, idx, entity, issuer, emit, reads, writes):
        self.idx = idx
        self.entity = entity
        self.issuer = issuer
        self.emit = emit
        self.reads = reads
        self.writes = writes
        self.waits = []
        self.marked = False
        self.ent_idx = -1
        self.know = None


class Builder:
    def __init__(self):
        self.ops = []
        self.last_write = {}
        self.readers = {}
        self.strict_ranges = []

    def in_strict(self, idx):
        return any(lo <= idx < hi for lo, hi in self.strict_ranges)

    def op(self, eng, emit, r=(), w=()):
        o = _Op(len(self.ops), eng, eng, emit, tuple(r), tuple(w))
        self.ops.append(o)
        return o

    def dma(self, queue, emit, r=(), w=()):
        o = _Op(len(self.ops), queue, DMA_QUEUES[queue], emit, tuple(r), tuple(w))
        self.ops.append(o)
        return o

    def finalize(self, nc):
        for o in self.ops:
            deps = []
            for t in o.reads:
                p = self.last_write.get(t)
                if p is not None:
                    deps.append(p)
            for t in o.writes:
                p = self.last_write.get(t)
                if p is not None:
                    deps.append(p)
                deps.extend(self.readers.get(t, ()))
            strict = self.in_strict(o.idx)
            seen = {}
            for p in deps:
                if (p.entity == o.entity and o.entity not in DMA_QUEUES
                        and not strict):
                    continue
                if p is o:
                    continue
                cur = seen.get(p.entity)
                if cur is None or p.idx > cur.idx:
                    seen[p.entity] = p
            o.waits = list(seen.values())
            for t in o.reads:
                self.readers.setdefault(t, []).append(o)
            for t in o.writes:
                self.last_write[t] = o
                self.readers[t] = []

        import os as _os
        if _os.environ.get("KSERIAL"):
            for i in range(1, len(self.ops)):
                o = self.ops[i]
                p = self.ops[i - 1]
                if all(q is not p for q in o.waits):
                    o.waits.append(p)

        def dma_key(o):
            return (o.entity, o.writes)

        key_rank = {}
        key_id = {}
        for o in self.ops:
            if o.entity in DMA_QUEUES:
                o.marked = True
                k = dma_key(o)
                if k not in key_id:
                    key_id[k] = len(key_id)
                key_rank[k] = key_rank.get(k, 0) + 1
                o.ent_idx = key_rank[k]
                o.entity = f"D{key_id[k]}_{o.entity}"

        for o in self.ops:
            seen = {}
            for p in o.waits:
                cur = seen.get(p.entity)
                if cur is None or p.idx > cur.idx:
                    seen[p.entity] = p
            o.waits = list(seen.values())

        issuer_know = {e: {} for e in COMPUTE_ENGINES}
        for o in self.ops:
            kn = issuer_know[o.issuer]
            pruned = []
            for p in sorted(o.waits, key=lambda q: q.idx):
                if kn.get(p.entity, -1) >= p.idx:
                    continue
                pruned.append(p)
                p.marked = True
                if p.know:
                    for e2, v in p.know.items():
                        if kn.get(e2, -1) < v:
                            kn[e2] = v
                if kn.get(p.entity, -1) < p.idx:
                    kn[p.entity] = p.idx
            o.waits = pruned
            snap = dict(kn)
            snap[o.entity] = o.idx
            o.know = snap

        cnt = {}
        for o in self.ops:
            if o.issuer == o.entity or o.entity in COMPUTE_ENGINES:
                if o.marked:
                    cnt[o.entity] = cnt.get(o.entity, 0) + 1
                o.ent_idx = cnt.get(o.entity, 0)

        entities = sorted(set(o.entity for o in self.ops)
                          | set(e for o in self.ops for e in
                                [p.entity for p in o.waits]))
        sems = {e: nc.alloc_semaphore(f"sem_{e}"[:48]) for e in entities}
        streams = {e: [] for e in COMPUTE_ENGINES}
        for o in self.ops:
            streams[o.issuer].append(o)

        def amt(e):
            return 1 if e in COMPUTE_ENGINES else 16

        def run(ns, issuer):
            for o in streams[issuer]:
                for p in o.waits:
                    ns.wait_ge(sems[p.entity], p.ent_idx * amt(p.entity))
                instr = o.emit(ns)
                if o.marked:
                    instr.then_inc(sems[o.entity], amt(o.entity))

        with nc.Block() as block:
            @block.tensor
            def _(e):
                run(e, "pe")

            @block.scalar
            def _(e):
                run(e, "act")

            @block.vector
            def _(e):
                run(e, "dve")

            @block.gpsimd
            def _(e):
                run(e, "pool")

            @block.sync
            def _(e):
                run(e, "sp")


def _onefc():
    a = np.zeros((128, 4), np.float32)
    a[:, 0] = 1.0
    a[:, 1] = 1.0 / 1024.0
    return a


def _onefr():
    a = np.zeros((128, 128), np.float32)
    for p in (0, 32, 64):
        a[p, :] = 1.0
    return a


def _onebr():
    a = np.zeros((128, 384), np.float64)
    a[0, 0:128] = 1.0
    a[64, 0:128] = 1.0
    a[0, 128:192] = 1.0
    a[0, 320:384] = 1.0
    return a.astype(BFNP)


def _blk64():
    a = np.zeros((128, 128), np.float64)
    a[0:64, 0:64] = 1.0
    a[64:128, 64:128] = 1.0
    return a.astype(BFNP)


def _host_consts():
    DH, WIN = 64, 512
    half = np.arange(0, DH, 2, dtype=np.float64)
    inv_freq = 1.0 / 10000.0 ** (half / DH)
    t = np.arange(T, dtype=np.float64)
    freqs = np.concatenate([t[:, None] * inv_freq[None, :]] * 2, axis=-1)
    base = (half + 0.4 * DH) / (1.4 * DH)
    power = (t - T // 2) / float(WIN // 2)
    scale = np.concatenate([base[None, :] ** power[:, None]] * 2, axis=-1)
    cos, sin = np.cos(freqs), np.sin(freqs)
    sgn = np.where(np.arange(DH) < 32, -1.0, 1.0)[None, :]

    def dup(a):
        return np.concatenate([a.T, a.T], axis=0).astype(BFNP)

    rot = np.stack([dup(cos * scale), dup(sgn * sin * scale),
                    dup(cos / scale), dup(sgn * sin / scale)])

    p = np.arange(128)[:, None]
    c = np.arange(128)[None, :]
    mska = np.where(p >= c, 0.0, -1e30).astype(BFNP)
    mskb = np.where(p <= c, 0.0, -1e30).astype(BFNP)
    c2 = np.arange(512)[None, :]
    mskc = np.zeros((4, 128, 512), dtype=np.float64)
    for k in range(4):
        mskc[k] = np.where(128 * k + p > c2, -4e9, 0.0)
    mskc = mskc.reshape(512, 512).astype(BFNP)
    return rot, mska, mskb, mskc


def build_program(debug=False):
    nc = bass.Bass("TRN2", target_bir_lowering=False)
    B = Builder()

    def din(name, shape, dt=F32):
        return nc.dram_tensor(name, shape, dt, kind="ExternalInput").ap()

    xin = din("xT", [D, T])
    wqT = din("wqT", [D, 2048])
    wkvT = din("wkvT", [D, 64])
    wkupT = din("wkupT", [64, 2048])
    wvupT = din("wvupT", [64, 2048])
    woT = din("woT", [2048, D])
    bw_qkv = din("bw_qkv", [D, 3072])
    bb_qkv = din("bb_qkv", [1, 3072])
    bw_out = din("bw_out", [D, D])
    bb_out = din("bb_out", [1, D])
    fwqkv = din("fwqkv", [2, D, 1536])
    fwperm = din("fwperm", [2, D, 1024])
    fqs = din("fqs", [2, 1, 64])
    fks = din("fks", [2, 1, 64])
    fwo = din("fwo", [2, 512, D])
    flng = din("flng", [2, 1, D])
    flnb = din("flnb", [2, 1, D])
    fw1 = din("fw1", [2, D, 5460])
    fb1a = din("fb1a", [2, 1, 2730])
    fb1g = din("fb1g", [2, 1, 2730])
    fw2 = din("fw2", [2, 2730, D])
    fb2 = din("fb2", [2, 1, D])
    rot_t = din("rot", [4, 128, T], BF16)
    mska_t = din("mska", [128, 128], BF16)
    mskb_t = din("mskb", [128, 128], BF16)
    mskc_t = din("mskc", [512, 512], BF16)
    identb_t = din("identb", [128, 128], BF16)
    onefc_t = din("onefc", [128, 4], F32)
    onebc_t = din("onebc", [128, 4], BF16)
    onefr_t = din("onefr", [128, 128], F32)
    onebr_t = din("onebr", [128, 384], BF16)
    blk64_t = din("blk64", [128, 128], BF16)

    yT = nc.dram_tensor("yT", [D, T], F32, kind="ExternalOutput").ap()
    kind_dbg = "ExternalOutput" if debug else "Internal"
    xs1 = nc.dram_tensor("xs1", [D, T], F32, kind=kind_dbg).ap()
    auxs = nc.dram_tensor("auxs", [8, T], BF16, kind="Internal").ap()
    auxdump = nc.dram_tensor("auxdump", [16, T], BF16, kind="ExternalOutput").ap() if debug else None
    dumpK = nc.dram_tensor("dumpK", [68, T], BF16, kind="ExternalOutput").ap() if debug else None
    dumpQ = nc.dram_tensor("dumpQ", [68, T], BF16, kind="ExternalOutput").ap() if debug else None
    dumpE = nc.dram_tensor("dumpE", [128, 512], BF16, kind="ExternalOutput").ap() if debug else None
    dumpO = nc.dram_tensor("dumpO", [128, 2 * T], BF16, kind="ExternalOutput").ap() if debug else None
    dumpS = nc.dram_tensor("dumpS", [2, 512], F32, kind="ExternalOutput").ap() if debug else None
    xs2 = nc.dram_tensor("xs2", [D, T], F32, kind=kind_dbg).ap()
    xt1 = nc.dram_tensor("xt1", [D, T], F32, kind=kind_dbg).ap() if debug else None
    xt2p = nc.dram_tensor("xt2p", [D, T], F32, kind=kind_dbg).ap() if debug else None
    xt3m = nc.dram_tensor("xt3m", [D, T], F32, kind=kind_dbg).ap() if debug else None

    def sbt(name, shape, dt):
        return nc.alloc_sbuf_tensor(name, shape, dt)

    XN = sbt("XN", [128, 8 * T], BF16)
    XF = sbt("XF", [128, 8 * T], F32)
    XFb = XF.bitcast(BF16)
    SH = sbt("SH", [128, 16384], BF16)
    WS = sbt("WS", [128, 8192], BF16)
    SA = sbt("SA", [128, T], F32)
    SB_ = sbt("SBs", [128, T], F32)
    SC = sbt("SCs", [128, T], F32)
    SCb = SC.bitcast(BF16)
    XL = sbt("XL", [128, 2 * T], F32)
    ROTB = XL.bitcast(BF16)
    ROWS = sbt("ROWS", [128, T], F32)
    ROWSB = sbt("ROWSB", [128, 2 * T], BF16)
    MSK = sbt("MSK", [128, 2304], BF16)
    BCOL = sbt("BCOL", [128, 192], F32)
    ONEF = sbt("ONEF", [128, 4], F32)
    ONEB = sbt("ONEB", [128, 4], BF16)
    ONEFR = sbt("ONEFR", [128, 128], F32)
    ONEBR = sbt("ONEBR", [128, 384], BF16)
    IDENTB = sbt("IDENTB", [128, 128], BF16)
    BLK64 = sbt("BLK64", [128, 128], BF16)
    ZEROB = sbt("ZEROB", [128, 4], BF16)

    PB = [nc.alloc_psum_tensor(f"PB{i}", [128, 512], F32) for i in range(6)]
    PBS = nc.alloc_psum_tensor("PBS", [128, 1024], F32)

    CONSTS = sbt("CONSTS", [128, 4], F32)
    CONST_VALS = (64.0 + 1e-6, -64.0, -70.0)
    for j, v in enumerate(CONST_VALS):
        nc.const_aps.register(F32, v, CONSTS[:, j:j + 1])

    def pbt(i):
        return f"PB{i}"

    def sh(c0, n, rows=None):
        ap = SH[:, c0:c0 + n] if rows is None else SH[rows[0]:rows[1], c0:c0 + n]
        return ap, [f"SH:{i}" for i in range(c0 // 512, (c0 + n + 511) // 512)]

    def ws(c0, n, rows=None):
        ap = WS[:, c0:c0 + n] if rows is None else WS[rows[0]:rows[1], c0:c0 + n]
        return ap, [f"WS:{i}" for i in range(c0 // 128, (c0 + n + 127) // 128)]

    def XFc(c):
        return XF[:, c * T:(c + 1) * T]

    def XNc(c):
        return XN[:, c * T:(c + 1) * T]

    ROWP = (0, 32, 64)

    def row(i, off=0, n=T):
        p = ROWP[i]
        return ROWS[p:p + 1, off:off + n]

    def onesrow_f(i):
        return ONEFR[ROWP[i]:ROWP[i] + 1, 0:128]

    def rowb(i, off=0, n=T):
        return ROWSB[0:1, i * T + off:i * T + off + n]

    BVROW = ROWSB

    OFF_BQKV, OFF_BOUT, OFF_B2 = 0, 24, 32
    OFF_LNG, OFF_LNB = 48, 64
    OFF_B1 = 80
    OFF_QS, OFF_KS, OFF_QSP, OFF_KSP, OFF_MBK = 168, 170, 172, 174, 176

    B.dma("qsp", lambda e: e.dma_start(out=MSK[:, 0:128], in_=mska_t), w=["MSK"])
    B.dma("qsp", lambda e: e.dma_start(out=MSK[:, 128:256], in_=mskb_t), w=["MSK"])
    B.dma("qsp", lambda e: e.dma_start(
        out=MSK[:, 256:2304].rearrange("p (k n) -> p k n", k=4),
        in_=mskc_t.rearrange("(k p) n -> p k n", p=128)), w=["MSK"])
    B.dma("qsp", lambda e: e.dma_start(out=IDENTB[:], in_=identb_t), w=["IDENTB"])
    B.op("dve", lambda e: e.memset(ZEROB[:], 0.0), w=["ZEROB"])
    B.dma("qpool", lambda e: e.dma_start(out=ROWSB[64:65, 0:1024],
                                         in_=bb_qkv[:, 2048:3072]),
          r=["bb_qkv"], w=["RBV"])
    B.dma("qsp", lambda e: e.dma_start(out=ONEF[:], in_=onefc_t), w=["ONE"])
    B.dma("qsp", lambda e: e.dma_start(out=ONEB[:], in_=onebc_t), w=["ONE"])
    B.dma("qsp", lambda e: e.dma_start(out=ONEFR[:], in_=onefr_t), w=["ONE"])
    B.dma("qsp", lambda e: e.dma_start(out=ONEBR[:], in_=onebr_t), w=["ONE"])
    B.dma("qsp", lambda e: e.dma_start(out=BLK64[:], in_=blk64_t), w=["BLK"])

    for j, v in enumerate(CONST_VALS):
        B.op("dve", lambda e, j=j, v=v: e.memset(CONSTS[:, j:j + 1], v), w=["CONSTS"])

    def load_row_to_cols(src_ap, src_tag, n, bcol_off):
        for s0 in range(0, n, 2048):
            seg = min(2048, n - s0)
            def eml(e, s0=s0, seg=seg):
                return e.dma_start(out=SA[0:1, 0:seg], in_=src_ap[:, s0:s0 + seg])
            B.dma("qsp", eml, r=[src_tag], w=["SA"])
            for c in range((seg + 127) // 128):
                wdt = min(128, seg - c * 128)
                def emm(e, c=c, wdt=wdt):
                    return e.matmul(PB[5][0:wdt, 0:1], SA[0:1, c * 128:c * 128 + wdt],
                                    ONEFR[0:1, 0:1], start=True, stop=True)
                B.op("pe", emm, r=["SA", "ONE"], w=[pbt(5)])
                def emv(e, c=c, wdt=wdt, s0=s0):
                    col = bcol_off + s0 // 128 + c
                    return e.tensor_copy(out=BCOL[0:wdt, col:col + 1],
                                         in_=PB[5][0:wdt, 0:1])
                B.op("dve", emv, r=[pbt(5)], w=["BCOL"])

    load_row_to_cols(bb_qkv, "bb_qkv", 3072, OFF_BQKV)
    load_row_to_cols(bb_out, "bb_out", D, OFF_BOUT)
    for l in range(2):
        load_row_to_cols(fb1a[l], "fb1a", 2730, OFF_B1 + 44 * l)
        load_row_to_cols(fb1g[l], "fb1g", 2730, OFF_B1 + 44 * l + 22)
        load_row_to_cols(fb2[l], "fb2", D, OFF_B2 + 8 * l)
        load_row_to_cols(flng[l], "flng", D, OFF_LNG + 8 * l)
        load_row_to_cols(flnb[l], "flnb", D, OFF_LNB + 8 * l)
        load_row_to_cols(fqs[l], "fqs", 64, OFF_QS + l)
        load_row_to_cols(fks[l], "fks", 64, OFF_KS + l)
    for l in range(2):
        for src, dst in ((OFF_QS + l, OFF_QSP + l), (OFF_KS + l, OFF_KSP + l)):
            def e1(e, src=src, dst=dst):
                return e.tensor_copy(out=BCOL[0:32, dst:dst + 1],
                                     in_=BCOL[32:64, src:src + 1])
            B.op("dve", e1, r=["BCOL"], w=["BCOL"])
            def e2(e, src=src, dst=dst):
                return e.tensor_copy(out=BCOL[32:64, dst:dst + 1],
                                     in_=BCOL[0:32, src:src + 1])
            B.op("dve", e2, r=["BCOL"], w=["BCOL"])
        for cc in (OFF_QS + l, OFF_KS + l, OFF_QSP + l, OFF_KSP + l):
            def e3(e, cc=cc):
                return e.tensor_copy(out=BCOL[64:128, cc:cc + 1],
                                     in_=BCOL[0:64, cc:cc + 1])
            B.op("dve", e3, r=["BCOL"], w=["BCOL"])
    for hp in range(8):
        def e4(e, hp=hp):
            return e.activation(out=BCOL[:, OFF_MBK + hp:OFF_MBK + hp + 1],
                                in_=BCOL[:, OFF_BQKV + 8 + hp:OFF_BQKV + 8 + hp + 1],
                                func=AF.Identity, scale=-2.0)
        B.op("act", e4, r=["BCOL"], w=["BCOL"])

    def stats_init():
        B.op("dve", lambda e: e.memset(row(0), 0.0),
             w=["R0"] + [f"R0c{i}" for i in range(4)])
        B.op("dve", lambda e: e.memset(row(1), 0.0),
             w=["R1"] + [f"R1c{i}" for i in range(4)])

    def stats_from_bf16(src_ap_fn, src_tags):
        B.op("act", lambda e: e.activation(out=SCb[:, T:2 * T], in_=src_ap_fn(),
                                           func=AF.Square), r=list(src_tags), w=["SC"])
        for nch in range(4):
            def em1(e, nch=nch, fn=src_ap_fn):
                full = fn()
                return e.matmul(PBS[0:1, 0:512], ONEB[:, 1:2],
                                full[:, nch * 512:(nch + 1) * 512], start=True, stop=True)
            B.op("pe", em1, r=list(src_tags) + ["ONE"], w=["PBSa"])
            def em2(e, nch=nch):
                return e.matmul(PBS[0:1, 512:1024], ONEB[:, 1:2],
                                SCb[:, T + nch * 512:T + (nch + 1) * 512],
                                start=True, stop=True)
            B.op("pe", em2, r=["SC", "ONE"], w=["PBSb"])
            def ev1(e, nch=nch):
                return e.tensor_tensor(out=row(0, nch * 512, 512),
                                       in0=row(0, nch * 512, 512),
                                       in1=PBS[0:1, 0:512], op=ALU.add)
            B.op("dve", ev1, r=[f"R0c{nch}", "PBSa"], w=["R0", f"R0c{nch}"])
            def ev2(e, nch=nch):
                return e.tensor_tensor(out=row(1, nch * 512, 512),
                                       in0=row(1, nch * 512, 512),
                                       in1=PBS[0:1, 512:1024], op=ALU.add)
            B.op("dve", ev2, r=[f"R1c{nch}", "PBSb"], w=["R1", f"R1c{nch}"])

    def stats_chunk(src_ap_fn, src_tags):
        B.op("act", lambda e: e.activation(out=SCb[:, 0:T], in_=src_ap_fn(),
                                           func=AF.Copy), r=list(src_tags), w=["SC"])
        stats_from_bf16(lambda: SCb[:, 0:T], ["SC"])

    def stats_rows(eps=1e-5):
        for nch in range(4):
            s0 = nch * 512
            def ev1(e, s0=s0):
                return e.tensor_tensor(out=SA[0:1, s0:s0 + 512], in0=row(0, s0, 512),
                                       in1=row(0, s0, 512), op=ALU.mult)
            B.op("dve", ev1, r=[f"R0c{nch}"], w=["SA"])
            def ev2(e, s0=s0):
                return e.activation(out=SB_[0:1, s0:s0 + 512], in_=row(1, s0, 512),
                                    func=AF.Copy)
            B.op("act", ev2, r=[f"R1c{nch}"], w=["SB"])
            def ev3(e, s0=s0):
                return e.scalar_tensor_tensor(out=row(1, s0, 512),
                                              in0=SB_[0:1, s0:s0 + 512], scalar=eps,
                                              in1=SA[0:1, s0:s0 + 512],
                                              op0=ALU.add, op1=ALU.subtract)
            B.op("dve", ev3, r=["SA", "SB"], w=["R1", f"R1c{nch}"])
            def ev4(e, s0=s0):
                return e.activation(out=row(1, s0, 512), in_=row(1, s0, 512),
                                    func=AF.Ln)
            B.op("act", ev4, r=[f"R1c{nch}"], w=["R1", f"R1c{nch}"])
            def ev5(e, s0=s0):
                return e.activation(out=row(1, s0, 512), in_=row(1, s0, 512),
                                    func=AF.Exp, scale=-0.5)
            B.op("act", ev5, r=[f"R1c{nch}"], w=["R1", f"R1c{nch}"])

    def normalize(dst_fn, src_fn=None, gain_bias=None):
        for nch in range(4):
            def emb(e, nch=nch):
                return e.matmul(PB[4][:, 0:512], onesrow_f(0),
                                row(0, nch * 512, 512), start=True, stop=True)
            B.op("pe", emb, r=[f"R0c{nch}", "ONE"], w=[pbt(4)])
            def emb2(e, nch=nch):
                return e.matmul(PB[5][:, 0:512], onesrow_f(1),
                                row(1, nch * 512, 512), start=True, stop=True)
            B.op("pe", emb2, r=[f"R1c{nch}", "ONE"], w=[pbt(5)])
            for c in range(NCH):
                if src_fn is None:
                    src_ap = XFc(c)[:, nch * 512:(nch + 1) * 512]
                    src_tags, shifted = [f"XF:{c}"], False
                else:
                    src_ap, src_tags, shifted = src_fn(c, nch)
                if not shifted:
                    def em1(e, src_ap=src_ap):
                        return e.tensor_tensor(out=SA[:, 0:512], in0=src_ap,
                                               in1=PB[4][:, 0:512], op=ALU.subtract)
                    B.op("dve", em1, r=src_tags + [pbt(4)], w=["SA"])
                else:
                    def em1(e, src_ap=src_ap):
                        return e.tensor_tensor(out=SA[:, 1:512], in0=src_ap,
                                               in1=PB[4][:, 1:512], op=ALU.subtract)
                    B.op("dve", em1, r=src_tags + [pbt(4)], w=["SA"])
                    def em0(e):
                        return e.tensor_scalar(out=SA[:, 0:1], in0=PB[4][:, 0:1],
                                               scalar1=-1.0, scalar2=None, op0=ALU.mult)
                    B.op("dve", em0, r=[pbt(4)], w=["SA"])
                dst_ap, dst_tags = dst_fn(c, nch)
                if gain_bias is None:
                    def em2(e, dst_ap=dst_ap):
                        return e.tensor_tensor(out=dst_ap, in0=SA[:, 0:512],
                                               in1=PB[5][:, 0:512], op=ALU.mult)
                    B.op("dve", em2, r=["SA", pbt(5)], w=dst_tags)
                else:
                    g_ap, b_ap = gain_bias(c)
                    def em2(e):
                        return e.tensor_tensor(out=SB_[:, 0:512], in0=SA[:, 0:512],
                                               in1=PB[5][:, 0:512], op=ALU.mult)
                    B.op("dve", em2, r=["SA", pbt(5)], w=["SB"])
                    def em3(e, dst_ap=dst_ap, g_ap=g_ap, b_ap=b_ap):
                        return e.activation(out=dst_ap, in_=SB_[:, 0:512],
                                            func=AF.Identity, scale=g_ap, bias=b_ap)
                    B.op("act", em3, r=["SB", "BCOL"], w=dst_tags)

    def norm_to_XN():
        normalize(lambda c, nch: (XNc(c)[:, nch * 512:(nch + 1) * 512], [f"XN:{c}"]))

    def norm_to_XF():
        normalize(lambda c, nch: (XFc(c)[:, nch * 512:(nch + 1) * 512], [f"XF:{c}"]))

    stats_init()
    for c in range(NCH):
        def eml(e, c=c):
            return e.dma_start(out=XFc(c), in_=xin[c * 128:(c + 1) * 128, :])
        B.dma("qsp", eml, r=["xin"], w=[f"XF:{c}"])
        stats_chunk(lambda c=c: XFc(c), [f"XF:{c}"])
    stats_rows()
    norm_to_XN()

    def attn_a():
        QT0, KT0, VT0, ET0, KV0 = 0, 4096, 8192, 13312, 14336
        kv_ap, kv_tags = ws(7168, 512)
        B.dma("qpool", lambda e: e.dma_start(
            out=kv_ap.rearrange("p (c n) -> p c n", c=8),
            in_=wkvT.rearrange("(c p) n -> p c n", p=128)),
            r=["wkvT"], w=kv_tags)
        for nch in range(4):
            for kc in range(NCH):
                def em(e, nch=nch, kc=kc):
                    a, _ = ws(7168 + kc * 64, 64)
                    return e.matmul(PB[0][0:64, :], a,
                                    XNc(kc)[:, nch * 512:(nch + 1) * 512],
                                    start=(kc == 0), stop=(kc == NCH - 1))
                B.op("pe", em, r=kv_tags + [f"XN:{kc}"], w=[pbt(0)])
            def eme(e, nch=nch):
                a, tg = sh(KV0 + nch * 512, 512, rows=(0, 64))
                return e.activation(out=a, in_=PB[0][0:64, :], func=AF.Copy)
            B.op("act", eme, r=[pbt(0)], w=sh(KV0, 2048)[1])
        kup_ap, kup_tags = ws(0, 2048, rows=(0, 64))
        B.dma("qpool", lambda e: e.dma_start(out=kup_ap, in_=wkupT),
              r=["wkupT"], w=kup_tags)
        vup_ap, vup_tags = ws(2048, 2048, rows=(0, 64))
        B.dma("qpool", lambda e: e.dma_start(out=vup_ap, in_=wvupT),
              r=["wvupT"], w=vup_tags)
        vt_tags = sh(VT0, 4112)[1]
        for jt in range(16):
            def emo(e, jt=jt):
                return e.memset(SH[:, VT0 + jt * 257 + 256:VT0 + jt * 257 + 257], 1.0)
            B.op("dve", emo, w=vt_tags)

        kvs_tags = sh(KV0, 2048)[1]
        qt_tags = sh(QT0, 4096)[1]
        kt_tags = sh(KT0, 4096)[1]
        for h in range(8):
            wq_ap, wq_tags = ws(4096, 2048)
            def emw(e, h=h):
                return e.dma_start(
                    out=wq_ap.rearrange("p (c n) -> p c n", c=8),
                    in_=wqT.rearrange("(c p) n -> p c n", p=128)[:, :, h * 256:(h + 1) * 256])
            B.dma("qpool", emw, r=["wqT"], w=wq_tags)
            for m in range(2):
                for nch in range(4):
                    for kc in range(NCH):
                        def em(e, m=m, nch=nch, kc=kc):
                            a, _ = ws(4096 + kc * 256 + m * 128, 128)
                            return e.matmul(PB[m][:, :], a,
                                            XNc(kc)[:, nch * 512:(nch + 1) * 512],
                                            start=(kc == 0), stop=(kc == NCH - 1))
                        B.op("pe", em, r=wq_tags + [f"XN:{kc}"], w=[pbt(m)])
                    def eme(e, m=m, nch=nch):
                        return e.activation(
                            out=SH[:, QT0 + m * T + nch * 512:QT0 + m * T + (nch + 1) * 512],
                            in_=PB[m][:, :], func=AF.Copy)
                    B.op("act", eme, r=[pbt(m)], w=qt_tags)
                for nch in range(4):
                    def em(e, m=m, nch=nch, h=h):
                        a, _ = ws(h * 256 + m * 128, 128, rows=(0, 64))
                        return e.matmul(PB[2][:, :], a,
                                        SH[0:64, KV0 + nch * 512:KV0 + (nch + 1) * 512],
                                        start=True, stop=True)
                    B.op("pe", em, r=kup_tags + kvs_tags, w=[pbt(2)])
                    def eme(e, m=m, nch=nch):
                        return e.activation(
                            out=SH[:, KT0 + m * T + nch * 512:KT0 + m * T + (nch + 1) * 512],
                            in_=PB[2][:, :], func=AF.Copy)
                    B.op("act", eme, r=[pbt(2)], w=kt_tags)
            for jt in range(16):
                def em(e, jt=jt, h=h):
                    a, _ = ws(2048 + h * 256, 256, rows=(0, 64))
                    return e.matmul(PB[3][:, 0:256],
                                    SH[0:64, KV0 + jt * 128:KV0 + (jt + 1) * 128],
                                    a, start=True, stop=True)
                B.op("pe", em, r=kvs_tags + vup_tags, w=[pbt(3)])
                def eme(e, jt=jt):
                    return e.activation(out=SH[:, VT0 + jt * 257:VT0 + jt * 257 + 256],
                                        in_=PB[3][:, 0:256], func=AF.Copy)
                B.op("act", eme, r=[pbt(3)], w=vt_tags)
            def aepilogue(ic):
                def emf(e):
                    return e.tensor_scalar(out=row(2, 0, 512), in0=PB[5][0:1, 0:512],
                                           scalar1=1e-35, scalar2=None, op0=ALU.max)
                B.op("dve", emf, r=[pbt(5)], w=["R2"])
                B.op("act", lambda e: e.activation(out=row(2, 0, 512), in_=row(2, 0, 512),
                                                   func=AF.Ln), r=["R2"], w=["R2"])
                B.op("act", lambda e: e.activation(out=row(2, 0, 512), in_=row(2, 0, 512),
                                                   func=AF.Exp, scale=-1.0),
                     r=["R2"], w=["R2"])
                def emb(e):
                    return e.matmul(PB[2][:, 0:512], onesrow_f(2), row(2, 0, 512),
                                    start=True, stop=True)
                B.op("pe", emb, r=["R2", "ONE"], w=[pbt(2)])
                B.op("act", lambda e: e.activation(out=SA[:, 0:512], in_=PB[2][:, 0:512],
                                                   func=AF.Copy), r=[pbt(2)], w=["SA"])
                for mt in range(2):
                    def em(e, mt=mt, ic=ic, h=h):
                        oc = 2 * h + mt
                        return e.tensor_tensor(
                            out=XFb[:, oc * T + ic * 512:oc * T + ic * 512 + 512],
                            in0=PB[3 + mt][:, 0:512], in1=SA[:, 0:512], op=ALU.mult)
                    B.op("dve", em, r=[pbt(3 + mt), "SA"], w=[f"XF:{h}"])

            def aflush(t):
                ic, jt = t
                pb = jt % 2
                for mt in range(3):
                    o0, wdt, opb = ((0, 128, 3), (128, 128, 4), (256, 1, 5))[mt]
                    def em(e, jt=jt, o0=o0, wdt=wdt, opb=opb, pb=pb):
                        return e.matmul(
                            PB[opb][0:(128 if wdt == 128 else 1), 0:512],
                            SH[:, VT0 + jt * 257 + o0:VT0 + jt * 257 + o0 + wdt],
                            SH[:, ET0 + pb * 512:ET0 + (pb + 1) * 512],
                            start=(jt == 0), stop=(jt == 15))
                    B.op("pe", em, r=vt_tags + sh(ET0 + pb * 512, 512)[1], w=[pbt(opb)])
                if jt == 15:
                    aepilogue(ic)

            apend = None
            for ic in range(4):
                for jt in range(16):
                    pb = jt % 2
                    for kc in range(2):
                        def em(e, pb=pb, kc=kc, jt=jt, ic=ic):
                            return e.matmul(
                                PB[pb][:, :],
                                SH[:, KT0 + kc * T + jt * 128:KT0 + kc * T + (jt + 1) * 128],
                                SH[:, QT0 + kc * T + ic * 512:QT0 + kc * T + (ic + 1) * 512],
                                start=(kc == 0), stop=(kc == 1))
                        B.op("pe", em, r=kt_tags + qt_tags, w=[pbt(pb)])
                    if apend is not None:
                        aflush(apend)
                    def eme(e, pb=pb):
                        return e.activation(out=SH[:, ET0 + pb * 512:ET0 + (pb + 1) * 512],
                                            in_=PB[pb][:, :], func=AF.Exp, scale=1.0 / 16.0)
                    B.op("act", eme, r=[pbt(pb)], w=sh(ET0 + pb * 512, 512)[1])
                    apend = (ic, jt)
            aflush(apend)
        stats_init()
        for m in range(NCH):
            def eml(e, m=m):
                return e.dma_start(out=XL[:, (m % 2) * T:(m % 2) * T + T],
                                   in_=xin[m * 128:(m + 1) * 128, :])
            B.dma("qsp", eml, r=["xin"], w=[f"XL:{m % 2}"])
            for kc in range(16):
                sl_ap, sl_tags = ws(6144 + (kc % 2) * 128, 128)
                def emw(e, m=m, kc=kc, sl_ap=sl_ap):
                    return e.dma_start(out=sl_ap,
                                       in_=woT[kc * 128:(kc + 1) * 128, m * 128:(m + 1) * 128])
                B.dma("qpool", emw, r=["woT"], w=sl_tags)
                for sp in range(4):
                    def em(e, m=m, kc=kc, sp=sp, sl_ap=sl_ap):
                        return e.matmul(PB[sp][:, :], sl_ap,
                                        XFb[:, kc * T + sp * 512:kc * T + sp * 512 + 512],
                                        start=(kc == 0), stop=(kc == 15))
                    B.op("pe", em, r=sl_tags + [f"XF:{kc // 2}"], w=[pbt(sp)])
            for sp in range(4):
                def emr(e, m=m, sp=sp):
                    return e.tensor_tensor(
                        out=SB_[:, sp * 512:(sp + 1) * 512], in0=PB[sp][:, :],
                        in1=XL[:, (m % 2) * T + sp * 512:(m % 2) * T + (sp + 1) * 512],
                        op=ALU.add)
                B.op("dve", emr, r=[pbt(sp), f"XL:{m % 2}"], w=["SB"])
            def emw2(e, m=m):
                return e.dma_start(out=xs1[m * 128:(m + 1) * 128, :], in_=SB_[:, 0:T])
            B.dma("qsp", emw2, r=["SB"], w=[f"xs1:{m}"])
            stats_chunk(lambda: SB_[:, 0:T], ["SB"])
        stats_rows()
        for c in range(NCH):
            def eml(e, c=c):
                return e.dma_start(out=XFc(c), in_=xs1[c * 128:(c + 1) * 128, :])
            B.dma("qsp", eml, r=[f"xs1:{c}"], w=[f"XF:{c}"])
            if debug:
                def emt(e, c=c):
                    return e.dma_start(out=xt1[c * 128:(c + 1) * 128, :], in_=XFc(c))
                B.dma("qsp", emt, r=[f"XF:{c}"], w=["xt1"])
        norm_to_XF()

    attn_a()

    def shift_to_XN():
        for c in range(NCH):
            if c < 4:
                def em(e, c=c):
                    return e.activation(out=XNc(c), in_=XFc(c), func=AF.Copy)
                B.op("act", em, r=[f"XF:{c}"], w=[f"XN:{c}"])
            else:
                def em(e, c=c):
                    return e.activation(out=XNc(c)[:, 1:T], in_=XFc(c)[:, 0:T - 1],
                                        func=AF.Copy)
                B.op("act", em, r=[f"XF:{c}"], w=[f"XN:{c}"])
                B.op("dve", lambda e, c=c: e.memset(XNc(c)[:, 0:1], 0.0), w=[f"XN:{c}"])

    def ffn_layer(l):
        QQ0, KK0, VS0, ET0, OT0 = 0, 2048, 4096, 7168, 8192
        qq_tags = sh(QQ0, 2048)[1]
        kk_tags = sh(KK0, 2048)[1]
        vs_tags = sh(VS0, 2080)[1]
        et_tags = ["SH:7"]
        wqkv_l = fwqkv[l].rearrange("(c p) n -> p c n", p=128)
        wperm_l = fwperm[l].rearrange("(c p) n -> p c n", p=128)

        shift_to_XN()
        for hp in range(4):
            slot_info = []
            for si, (src, base) in enumerate((
                    (wqkv_l, hp * 128), (wqkv_l, 512 + hp * 128),
                    (wqkv_l, 1024 + hp * 128),
                    (wperm_l, hp * 128), (wperm_l, 512 + hp * 128))):
                s_ap, s_tags = ws(si * 1024, 1024)
                slot_info.append((si, s_tags))
                def emw(e, s_ap=s_ap, src=src, base=base):
                    return e.dma_start(out=s_ap.rearrange("p (c n) -> p c n", c=8),
                                       in_=src[:, :, base:base + 128])
                B.dma("qpool", emw, r=["fwqkv"], w=s_tags)
            for which in range(2):
                dst0 = (QQ0, KK0)[which]
                dtags = (qq_tags, kk_tags)[which]
                cs_off = (0, 2 * T)[which]
                sc = (OFF_QS, OFF_KS)[which] + l
                scp = (OFF_QSP, OFF_KSP)[which] + l
                wtag = slot_info[which][1]
                wptag = slot_info[3 + which][1]
                for nch in range(4):
                    for kc in range(NCH):
                        def em(e, which=which, nch=nch, kc=kc):
                            a, _ = ws(which * 1024 + kc * 128, 128)
                            return e.matmul(PB[nch][:, :], a,
                                            XNc(kc)[:, nch * 512:(nch + 1) * 512],
                                            start=(kc == 0), stop=(kc == NCH - 1))
                        B.op("pe", em, r=wtag + [f"XN:{kc}"], w=[pbt(nch)])
                for nch in range(4):
                    def em(e, nch=nch):
                        return e.activation(out=SCb[:, nch * 512:(nch + 1) * 512],
                                            in_=PB[nch][:, :], func=AF.Square)
                    B.op("act", em, r=[pbt(nch)], w=["SC"])
                for nch in range(4):
                    def em(e, nch=nch):
                        return e.matmul(PB[4 + nch % 2][:, 0:512], BLK64[:, :],
                                        SCb[:, nch * 512:(nch + 1) * 512],
                                        start=True, stop=True)
                    B.op("pe", em, r=["SC", "BLK"], w=[pbt(4 + nch % 2)])
                    def el(e, nch=nch):
                        return e.activation(out=SA[:, nch * 512:(nch + 1) * 512],
                                            in_=PB[4 + nch % 2][:, 0:512], func=AF.Ln)
                    B.op("act", el, r=[pbt(4 + nch % 2)], w=["SA"])
                for nch in range(4):
                    def ex(e, nch=nch):
                        return e.activation(out=SA[:, nch * 512:(nch + 1) * 512],
                                            in_=SA[:, nch * 512:(nch + 1) * 512],
                                            func=AF.Exp, scale=-0.5)
                    B.op("act", ex, r=["SA"], w=["SA"])
                for nch in range(4):
                    def em(e, nch=nch):
                        return e.tensor_tensor(out=SB_[:, nch * 512:(nch + 1) * 512],
                                               in0=PB[nch][:, :],
                                               in1=SA[:, nch * 512:(nch + 1) * 512],
                                               op=ALU.mult)
                    B.op("dve", em, r=[pbt(nch), "SA"], w=["SB"])
                def esc(e, sc=sc):
                    return e.activation(out=SB_[:], in_=SB_[:], func=AF.Identity,
                                        scale=BCOL[:, sc:sc + 1])
                B.op("act", esc, r=["SB", "BCOL"], w=["SB"])
                for nch in range(4):
                    for kc in range(NCH):
                        def em(e, which=which, nch=nch, kc=kc):
                            a, _ = ws((3 + which) * 1024 + kc * 128, 128)
                            return e.matmul(PB[nch][:, :], a,
                                            XNc(kc)[:, nch * 512:(nch + 1) * 512],
                                            start=(kc == 0), stop=(kc == NCH - 1))
                        B.op("pe", em, r=wptag + [f"XN:{kc}"], w=[pbt(nch)])
                    def em2(e, nch=nch):
                        return e.tensor_tensor(out=SC[:, nch * 512:(nch + 1) * 512],
                                               in0=PB[nch][:, :],
                                               in1=SA[:, nch * 512:(nch + 1) * 512],
                                               op=ALU.mult)
                    B.op("dve", em2, r=[pbt(nch), "SA"], w=["SC"])
                def esc2(e, scp=scp):
                    return e.activation(out=SC[:], in_=SC[:], func=AF.Identity,
                                        scale=BCOL[:, scp:scp + 1])
                B.op("act", esc2, r=["SC", "BCOL"], w=["SC"])
                def eu(e, cs_off=cs_off):
                    return e.tensor_tensor(out=SB_[:], in0=SB_[:],
                                           in1=ROTB[:, cs_off:cs_off + T], op=ALU.mult)
                B.op("dve", eu, r=["SB", "XL:0", "XL:1"], w=["SB"])
                def ev2(e, cs_off=cs_off):
                    return e.tensor_tensor(out=SC[:], in0=SC[:],
                                           in1=ROTB[:, cs_off + T:cs_off + 2 * T],
                                           op=ALU.mult)
                B.op("dve", ev2, r=["SC", "XL:0", "XL:1"], w=["SC"])
                def eq(e, dst0=dst0):
                    return e.tensor_tensor(out=SH[:, dst0:dst0 + T], in0=SB_[:],
                                           in1=SC[:], op=ALU.add)
                B.op("dve", eq, r=["SB", "SC"], w=dtags)
            if hp == 0:
                for bi in range(32):
                    def emo(e, bi=bi):
                        return e.memset(SH[:, VS0 + bi * 65 + 64:VS0 + bi * 65 + 65], 1.0)
                    B.op("dve", emo, w=vs_tags)
            vtag = slot_info[2][1]
            for jt in range(16):
                for kc in range(NCH):
                    def em(e, jt=jt, kc=kc):
                        a, _ = ws(2 * 1024 + kc * 128, 128)
                        return e.matmul(PB[5][:, 0:128], XNc(kc)[:, jt * 128:(jt + 1) * 128],
                                        a, start=(kc == 0), stop=(kc == NCH - 1))
                    B.op("pe", em, r=vtag + [f"XN:{kc}"], w=[pbt(5)])
                for hh in range(2):
                    def emc(e, jt=jt, hh=hh):
                        off = VS0 + (hh * 16 + jt) * 65
                        return e.activation(out=SH[:, off:off + 64],
                                            in_=PB[5][:, hh * 64:hh * 64 + 64],
                                            func=AF.Copy)
                    B.op("act", emc, r=[pbt(5)], w=vs_tags)
            def foacc(gg):
                return (PB[4], pbt(4)) if gg % 2 == 0 else (PBS, "PBSa")

            def fepilogue(hh, w, gg):
                h = 2 * hp + hh
                acc, acct = foacc(gg)
                r0 = (gg % 2) * 512
                rt = ["R2"] if gg % 2 == 0 else ["R2b"]
                def emf(e, acc=acc, r0=r0):
                    return e.tensor_scalar(out=row(2, r0, 512), in0=acc[64:65, 0:512],
                                           scalar1=1e-35, scalar2=None, op0=ALU.max)
                B.op("dve", emf, r=[acct], w=rt)
                def er1(e, r0=r0):
                    return e.activation(out=row(2, r0, 512), in_=row(2, r0, 512),
                                        func=AF.Ln)
                B.op("act", er1, r=rt, w=rt)
                def er2(e, r0=r0):
                    return e.activation(out=row(2, r0, 512), in_=row(2, r0, 512),
                                        func=AF.Exp, scale=-1.0)
                B.op("act", er2, r=rt, w=rt)
                if gg % 2 == 0:
                    bt_ap, btt = PB[5][:, 0:512], pbt(5)
                else:
                    bt_ap, btt = PBS[:, 512:1024], "PBSb"
                def emb(e, r0=r0, bt_ap=bt_ap):
                    return e.matmul(bt_ap, onesrow_f(2), row(2, r0, 512),
                                    start=True, stop=True)
                B.op("pe", emb, r=rt + ["ONE"], w=[btt])
                def ecp(e, bt_ap=bt_ap):
                    return e.activation(out=SA[:, 0:512], in_=bt_ap, func=AF.Copy)
                B.op("act", ecp, r=[btt], w=["SA"])
                def emo2(e, h=h, w=w, acc=acc):
                    oc = h // 2
                    pr = (h % 2) * 64
                    off = OT0 + oc * T + w * 512
                    return e.tensor_tensor(out=SH[pr:pr + 64, off:off + 512],
                                           in0=acc[0:64, 0:512], in1=SA[0:64, 0:512],
                                           op=ALU.mult)
                B.op("dve", emo2, r=[acct, "SA"],
                     w=sh(OT0 + (h // 2) * T + w * 512, 512)[1])

            def fflush(t):
                hh, w, gg, jglob, pb, i0, N, first, last = t
                acc, acct = foacc(gg)
                def emo(e, jglob=jglob, pb=pb, hh=hh, i0=i0, N=N,
                        first=first, last=last, acc=acc):
                    off = VS0 + (hh * 16 + jglob) * 65
                    return e.matmul(acc[0:65, i0:i0 + N],
                                    SH[:, off:off + 65],
                                    SH[:, ET0 + pb * 512:ET0 + pb * 512 + N],
                                    start=first, stop=last)
                B.op("pe", emo, r=vs_tags + sh(ET0 + pb * 512, 512)[1], w=[acct])
                if last:
                    fepilogue(hh, w, gg)

            fpend = None
            for hh in range(2):
                for w in range(4):
                    gg = hh * 4 + w
                    qb = hh * 64
                    jtls = list(range(4, 8)) if w == 0 else list(range(8))
                    for ti, jtl in enumerate(jtls):
                        jglob = 4 * (w - 1) + jtl
                        pb = ti % 2
                        i0 = max(0, (jtl - 4) * 128)
                        i1 = min(512, (jtl + 1) * 128)
                        N = i1 - i0
                        def em(e, pb=pb, jglob=jglob, w=w, qb=qb, i0=i0, N=N):
                            return e.matmul(
                                PB[pb][:, 0:N],
                                SH[qb:qb + 64, KK0 + jglob * 128:KK0 + (jglob + 1) * 128],
                                SH[qb:qb + 64, QQ0 + w * 512 + i0:QQ0 + w * 512 + i0 + N],
                                start=True, stop=False)
                        B.op("pe", em, r=qq_tags + kk_tags, w=[pbt(pb)])
                        if w > 0 and jtl <= 3:
                            moff, mpos = 0, jtl * 128 - i0
                        elif jtl >= 4:
                            moff, mpos = 128, (jtl - 4) * 128 - i0
                        else:
                            moff = None
                        if moff is not None:
                            def emk(e, pb=pb, moff=moff, mpos=mpos):
                                return e.matmul(PB[pb][:, mpos:mpos + 128],
                                                IDENTB[:, :], MSK[:, moff:moff + 128],
                                                start=False, stop=True)
                            B.op("pe", emk, r=["IDENTB", "MSK"], w=[pbt(pb)])
                        else:
                            def emk(e, pb=pb):
                                return e.matmul(PB[pb][0:1, 0:1], IDENTB[0:1, 0:1],
                                                ZEROB[0:1, 0:1], start=False, stop=True)
                            B.op("pe", emk, r=["IDENTB", "ZEROB"], w=[pbt(pb)])
                        if fpend is not None:
                            fflush(fpend)
                        def eme(e, pb=pb, N=N):
                            return e.activation(out=SH[:, ET0 + pb * 512:ET0 + pb * 512 + N],
                                                in_=PB[pb][:, 0:N], func=AF.Exp, scale=8.0)
                        B.op("act", eme, r=[pbt(pb)], w=sh(ET0 + pb * 512, 512)[1])
                        fpend = (hh, w, gg, jglob, pb, i0, N,
                                 ti == 0, ti == len(jtls) - 1)
            fflush(fpend)
        for m in range(NCH):
            for kc in range(4):
                sl_ap, sl_tags = ws(6144 + (kc % 2) * 128, 128)
                def emw(e, m=m, kc=kc, sl_ap=sl_ap):
                    return e.dma_start(out=sl_ap,
                                       in_=fwo[l][kc * 128:(kc + 1) * 128, m * 128:(m + 1) * 128])
                B.dma("qpool", emw, r=["fwo"], w=sl_tags)
                for sp in range(4):
                    def em(e, m=m, kc=kc, sp=sp, sl_ap=sl_ap):
                        return e.matmul(PB[sp][:, :], sl_ap,
                                        SH[:, OT0 + kc * T + sp * 512:OT0 + kc * T + (sp + 1) * 512],
                                        start=(kc == 0), stop=(kc == 3))
                    B.op("pe", em, r=sl_tags + sh(8192 + kc * T, T)[1], w=[pbt(sp)])
            for sp in range(4):
                def ea(e, m=m, sp=sp):
                    return e.tensor_tensor(
                        out=XFc(m)[:, sp * 512:(sp + 1) * 512],
                        in0=XFc(m)[:, sp * 512:(sp + 1) * 512],
                        in1=PB[sp][:, :], op=ALU.add)
                B.op("dve", ea, r=[pbt(sp), f"XF:{m}"], w=[f"XF:{m}"])
        stats_init()
        shift_to_XN()
        for c in range(NCH):
            stats_from_bf16(lambda c=c: XNc(c), [f"XN:{c}"])
        stats_rows()

        def srcfn(c, nch):
            if c < 4:
                return (XFc(c)[:, nch * 512:(nch + 1) * 512], [f"XF:{c}"], False)
            if nch == 0:
                return (XFc(c)[:, 0:511], [f"XF:{c}"], True)
            return (XFc(c)[:, nch * 512 - 1:(nch + 1) * 512 - 1], [f"XF:{c}"], False)

        normalize(lambda c, nch: (XNc(c)[:, nch * 512:(nch + 1) * 512], [f"XN:{c}"]),
                  src_fn=srcfn,
                  gain_bias=lambda c: (BCOL[:, OFF_LNG + 8 * l + c:OFF_LNG + 8 * l + c + 1],
                                       BCOL[:, OFF_LNB + 8 * l + c:OFF_LNB + 8 * l + c + 1]))

        w1r = fw1[l].rearrange("(c p) n -> p c n", p=128)
        HID = 2730
        groups = [list(range(0, 8)), list(range(8, 16)), list(range(16, 22))]
        for g in groups:
            for gi, cch in enumerate(g):
                wdt = min(128, HID - cch * 128)
                wa0 = (5 * 1024) if cch % 2 == 0 else 0
                wg0 = (6 * 1024) if cch % 2 == 0 else 1024
                wa_ap, wa_tags = ws(wa0, 8 * wdt)
                wg_ap, wg_tags = ws(wg0, 8 * wdt)
                def emwa(e, wa_ap=wa_ap, cch=cch, wdt=wdt):
                    return e.dma_start(out=wa_ap.rearrange("p (c n) -> p c n", c=8),
                                       in_=w1r[:, :, cch * 128:cch * 128 + wdt])
                B.dma("qpool", emwa, r=["fw1"], w=wa_tags)
                def emwg(e, wg_ap=wg_ap, cch=cch, wdt=wdt):
                    return e.dma_start(out=wg_ap.rearrange("p (c n) -> p c n", c=8),
                                       in_=w1r[:, :, HID + cch * 128:HID + cch * 128 + wdt])
                B.dma("qpool", emwg, r=["fw1"], w=wg_tags)
                for nch in range(4):
                    for kc in range(NCH):
                        def ema(e, nch=nch, kc=kc, wdt=wdt, wa0=wa0):
                            a, _ = ws(wa0 + kc * wdt, wdt)
                            return e.matmul(PB[nch % 2][0:wdt, :], a,
                                            XNc(kc)[:, nch * 512:(nch + 1) * 512],
                                            start=(kc == 0), stop=(kc == NCH - 1))
                        B.op("pe", ema, r=wa_tags + [f"XN:{kc}"], w=[pbt(nch % 2)])
                        def emg(e, nch=nch, kc=kc, wdt=wdt, wg0=wg0):
                            a, _ = ws(wg0 + kc * wdt, wdt)
                            return e.matmul(PB[2 + nch % 2][0:wdt, :], a,
                                            XNc(kc)[:, nch * 512:(nch + 1) * 512],
                                            start=(kc == 0), stop=(kc == NCH - 1))
                        B.op("pe", emg, r=wg_tags + [f"XN:{kc}"], w=[pbt(2 + nch % 2)])
                    def eg1(e, nch=nch, wdt=wdt, cch=cch):
                        bcg = OFF_B1 + 44 * l + 22 + cch
                        return e.activation(out=SCb[0:wdt, nch * 512:(nch + 1) * 512],
                                            in_=PB[2 + nch % 2][0:wdt, :], func=AF.Gelu,
                                            bias=BCOL[0:wdt, bcg:bcg + 1])
                    B.op("act", eg1, r=[pbt(2 + nch % 2), "BCOL"], w=["SC"])
                    def eg2(e, nch=nch, wdt=wdt, cch=cch, gi=gi):
                        bca = OFF_B1 + 44 * l + cch
                        a, _ = sh(gi * T + nch * 512, 512, rows=(0, wdt))
                        return e.scalar_tensor_tensor(
                            out=a, in0=PB[nch % 2][0:wdt, :],
                            scalar=BCOL[0:wdt, bca:bca + 1],
                            in1=SCb[0:wdt, nch * 512:(nch + 1) * 512],
                            op0=ALU.add, op1=ALU.mult)
                    B.op("dve", eg2, r=[pbt(nch % 2), "SC", "BCOL"],
                         w=sh(gi * T, 2048)[1])
            last_group = (g[0] == 16)
            for m in range(NCH):
                for gi, cch in enumerate(g):
                    wdt = min(128, HID - cch * 128)
                    sl_ap, sl_tags = ws(7 * 1024 + (gi % 2) * 128, 128, rows=(0, wdt))
                    def emw(e, m=m, cch=cch, wdt=wdt, sl_ap=sl_ap):
                        return e.dma_start(out=sl_ap,
                                           in_=fw2[l][cch * 128:cch * 128 + wdt,
                                                      m * 128:(m + 1) * 128])
                    B.dma("qpool", emw, r=["fw2"], w=sl_tags)
                    for sp in range(4):
                        def em(e, m=m, sp=sp, gi=gi, wdt=wdt, sl_ap=sl_ap, n=len(g)):
                            a, _ = sh(gi * T + sp * 512, 512, rows=(0, wdt))
                            return e.matmul(PB[2 + sp][:, :], sl_ap, a,
                                            start=(gi == 0), stop=(gi == n - 1))
                        B.op("pe", em, r=sl_tags + sh(gi * T, 2048)[1], w=[pbt(2 + sp)])
                for sp in range(4):
                    if last_group:
                        def ea(e, m=m, sp=sp):
                            return e.scalar_tensor_tensor(
                                out=XFc(m)[:, sp * 512:(sp + 1) * 512],
                                in0=PB[2 + sp][:, :],
                                scalar=BCOL[:, OFF_B2 + 8 * l + m:OFF_B2 + 8 * l + m + 1],
                                in1=XFc(m)[:, sp * 512:(sp + 1) * 512],
                                op0=ALU.add, op1=ALU.add)
                        B.op("dve", ea, r=[pbt(2 + sp), f"XF:{m}", "BCOL"], w=[f"XF:{m}"])
                    else:
                        def ea(e, m=m, sp=sp):
                            return e.tensor_tensor(
                                out=XFc(m)[:, sp * 512:(sp + 1) * 512],
                                in0=XFc(m)[:, sp * 512:(sp + 1) * 512],
                                in1=PB[2 + sp][:, :], op=ALU.add)
                        B.op("dve", ea, r=[pbt(2 + sp), f"XF:{m}"], w=[f"XF:{m}"])

    def ffn_call(src_x, src_fmt, dst_x, dst_fmt, final):
        for bi in range(4):
            def emr(e, bi=bi):
                return e.dma_start(out=ROTB[:, bi * T:(bi + 1) * T], in_=rot_t[bi])
            B.dma("qsp", emr, r=["rot"], w=["XL:0" if bi < 2 else "XL:1"])
        for l in range(2):
            ffn_layer(l)
        if not final:
            stats_init()
        for c in range(NCH):
            def eml(e, c=c):
                return e.dma_start(out=XL[:, (c % 2) * T:(c % 2) * T + T],
                                   in_=src_x[c * 128:(c + 1) * 128, :])
            B.dma("qsp", eml, r=[src_fmt.format(c)], w=[f"XL:{c % 2}"])
            def emr2(e, c=c):
                return e.tensor_tensor(out=XFc(c), in0=XFc(c),
                                       in1=XL[:, (c % 2) * T:(c % 2) * T + T], op=ALU.add)
            B.op("dve", emr2, r=[f"XF:{c}", f"XL:{c % 2}"], w=[f"XF:{c}"])
            def emw(e, c=c):
                return e.dma_start(out=dst_x[c * 128:(c + 1) * 128, :], in_=XFc(c))
            B.dma("qsp", emw, r=[f"XF:{c}"], w=[dst_fmt.format(c)])
            if not final:
                stats_chunk(lambda c=c: XFc(c), [f"XF:{c}"])
        if not final:
            stats_rows()

    ffn_call(xs1, "xs1:{}", xs2, "xs2:{}", False)
    norm_to_XN()

    def attn_b():
        Q0, K0, V0, ET0, OT0 = 0, 4096, 8192, 11264, 12288
        EPS = 1e-6
        CEXP = 70.0
        wq = bw_qkv.rearrange("(c p) n -> p c n", p=128)
        v_tags = sh(V0, 2080)[1]
        if debug:
            for c in range(NCH):
                def emt(e, c=c):
                    return e.dma_start(out=xt2p[c * 128:(c + 1) * 128, :], in_=XFc(c))
                B.dma("qsp", emt, r=[f"XF:{c}"], w=["xt2p"])
        for grp in range(4):
            if debug and grp == 2:
                for c in range(NCH):
                    def emt(e, c=c):
                        return e.dma_start(out=xt3m[c * 128:(c + 1) * 128, :], in_=XFc(c))
                    B.dma("qsp", emt, r=[f"XF:{c}"], w=["xt3m"])
            ot_tags = [sh(OT0, 2048)[1], sh(OT0 + 2048, 2048)[1]]
            for gh in range(2):
                hp = grp * 2 + gh
                slot_tags = []
                for si, base in enumerate((hp * 128, 1024 + hp * 128, 2048 + hp * 128)):
                    s_ap, s_tags = ws(si * 1024, 1024)
                    slot_tags.append(s_tags)
                    def emw(e, s_ap=s_ap, base=base):
                        return e.dma_start(out=s_ap.rearrange("p (c n) -> p c n", c=8),
                                           in_=wq[:, :, base:base + 128])
                    B.dma("qpool", emw, r=["bw_qkv"], w=s_tags)
                for which in range(2):
                    for nch in range(4):
                        for kc in range(NCH):
                            def em(e, which=which, nch=nch, kc=kc):
                                a, _ = ws(which * 1024 + kc * 128, 128)
                                return e.matmul(PB[nch][:, :], a,
                                                XNc(kc)[:, nch * 512:(nch + 1) * 512],
                                                start=(kc == 0), stop=(kc == NCH - 1))
                            B.op("pe", em, r=slot_tags[which] + [f"XN:{kc}"], w=[pbt(nch)])
                    for hh in range(2):
                        dst0 = (Q0, K0)[which] + hh * 2048
                        dtags = sh(dst0, 2048)[1]
                        if which == 0:
                            bias_col = OFF_BQKV + hp
                            scl = 1.0
                        else:
                            bias_col = OFF_MBK + hp
                            scl = -2.0
                        for nch in range(4):
                            def ee(e, hh=hh, nch=nch, dst0=dst0, bias_col=bias_col, scl=scl):
                                return e.activation(
                                    out=SH[0:64, dst0 + nch * 512:dst0 + (nch + 1) * 512],
                                    in_=PB[nch][hh * 64:hh * 64 + 64, :],
                                    func=AF.Identity, scale=scl,
                                    bias=BCOL[hh * 64:hh * 64 + 64, bias_col:bias_col + 1])
                            B.op("act", ee, r=[pbt(nch), "BCOL"], w=dtags)
                for hh in range(2):
                    qd = Q0 + hh * 2048
                    kd = K0 + hh * 2048
                    qtags = sh(qd, 2048)[1]
                    ktags = sh(kd, 2048)[1]
                    B.op("dve", lambda e, qd=qd: e.memset(SH[64:68, qd:qd + T], 1.0),
                         w=qtags)
                    B.op("dve", lambda e, kd=kd: e.memset(SH[64:68, kd:kd + T], 1.0),
                         w=ktags)
                    B.op("act", lambda e, qd=qd: e.activation(out=SCb[0:64, 0:T],
                                                              in_=SH[0:64, qd:qd + T],
                                                              func=AF.Square),
                         r=qtags, w=["SC", "SC:0", "SC:1"])
                    for nch in range(4):
                        def em(e, nch=nch):
                            return e.matmul(PBS[0:1, 0:512], ONEB[0:64, 0:1],
                                            SCb[0:64, nch * 512:(nch + 1) * 512],
                                            start=True, stop=True)
                        B.op("pe", em, r=["SC", "SC:0", "SC:1", "ONE"], w=["PBSa"])
                        def ee(e, hh=hh, nch=nch):
                            return e.activation(out=row(hh, nch * 512, 512),
                                                in_=PBS[0:1, 0:512], func=AF.Identity,
                                                scale=1.0, bias=64.0 + EPS)
                        B.op("act", ee, r=["PBSa", "CONSTS"], w=[f"R{hh}"])
                    p0 = hh * 32
                    def eqh(e, hh=hh, p0=p0):
                        return e.activation(out=ROWSB[p0:p0 + 1, 0:T], in_=row(hh),
                                            func=AF.Copy)
                    B.op("act", eqh, r=[f"R{hh}"], w=["RBQ"])
                    def eql(e, hh=hh, p0=p0):
                        return e.tensor_tensor(out=ROWSB[p0:p0 + 1, T:2 * T], in0=row(hh),
                                               in1=ROWSB[p0:p0 + 1, 0:T], op=ALU.subtract)
                    B.op("dve", eql, r=[f"R{hh}", "RBQ"], w=["RBQ2"])
                    def ew1(e, hh=hh, p0=p0):
                        return e.dma_start(out=auxs[4 * hh + 0], in_=ROWSB[p0:p0 + 1, 0:T])
                    B.dma("qsp", ew1, r=["RBQ"], w=[f"aux{4 * hh}"])
                    def ew2(e, hh=hh, p0=p0):
                        return e.dma_start(out=auxs[4 * hh + 1],
                                           in_=ROWSB[p0:p0 + 1, T:2 * T])
                    B.dma("qsp", ew2, r=["RBQ2"], w=[f"aux{4 * hh + 1}"])
                    def edq(e, qd=qd, hh=hh):
                        return e.dma_start(out=SH[66:67, qd:qd + T], in_=auxs[4 * hh + 0])
                    B.dma("qsp", edq, r=[f"aux{4 * hh}"], w=qtags)
                    def edq2(e, qd=qd, hh=hh):
                        return e.dma_start(out=SH[67:68, qd:qd + T], in_=auxs[4 * hh + 1])
                    B.dma("qsp", edq2, r=[f"aux{4 * hh + 1}"], w=qtags)
                    B.op("act", lambda e, kd=kd: e.activation(out=SCb[0:64, 0:T],
                                                              in_=SH[0:64, kd:kd + T],
                                                              func=AF.Square),
                         r=ktags, w=["SC", "SC:0", "SC:1"])
                    for nch in range(4):
                        def em(e, nch=nch):
                            return e.matmul(PBS[0:1, 512:1024], ONEB[0:64, 0:1],
                                            SCb[0:64, nch * 512:(nch + 1) * 512],
                                            start=True, stop=True)
                        B.op("pe", em, r=["SC", "SC:0", "SC:1", "ONE"], w=["PBSb"])
                        def ee(e, nch=nch):
                            return e.activation(out=row(2, nch * 512, 512),
                                                in_=PBS[0:1, 512:1024], func=AF.Identity,
                                                scale=0.25, bias=-64.0)
                        B.op("act", ee, r=["PBSb", "CONSTS"], w=["R2", "R2b"])
                    def ekh(e, kd=kd):
                        return e.activation(out=SH[64:65, kd:kd + T], in_=row(2),
                                            func=AF.Copy)
                    B.op("act", ekh, r=["R2"], w=ktags)
                    def ekl(e, kd=kd):
                        return e.tensor_tensor(out=ROWSB[64:65, 1024:1024 + T], in0=row(2),
                                               in1=SH[64:65, kd:kd + T], op=ALU.subtract)
                    B.op("dve", ekl, r=["R2"] + ktags, w=["RBK"])
                    def ew3(e, hh=hh):
                        return e.dma_start(out=auxs[4 * hh + 2],
                                           in_=ROWSB[64:65, 1024:1024 + T])
                    B.dma("qsp", ew3, r=["RBK"], w=[f"aux{4 * hh + 2}"])
                    def edk(e, kd=kd, hh=hh):
                        return e.dma_start(out=SH[65:66, kd:kd + T], in_=auxs[4 * hh + 2])
                    B.dma("qsp", edk, r=[f"aux{4 * hh + 2}"], w=ktags)
                if hp == 0 and debug:
                    def edK(e):
                        return e.dma_start(out=dumpK, in_=SH[0:68, K0:K0 + T])
                    B.dma("qsp", edK, r=sh(K0, 2048)[1], w=["dumpK"])
                    def edQ(e):
                        return e.dma_start(out=dumpQ, in_=SH[0:68, Q0:Q0 + T])
                    B.dma("qsp", edQ, r=sh(Q0, 2048)[1], w=["dumpQ"])
                    for r_ in range(4):
                        def ed1(e, r_=r_):
                            return e.dma_start(out=auxdump[r_],
                                               in_=SH[64 + r_:65 + r_, K0:K0 + T])
                        B.dma("qsp", ed1, r=sh(K0, 2048)[1], w=["auxdump"])
                        def ed2(e, r_=r_):
                            return e.dma_start(out=auxdump[4 + r_],
                                               in_=SH[64 + r_:65 + r_, Q0:Q0 + T])
                        B.dma("qsp", ed2, r=sh(Q0, 2048)[1], w=["auxdump"])
                        def ed3(e, r_=r_):
                            return e.dma_start(out=auxdump[8 + r_],
                                               in_=SH[r_:r_ + 1, K0:K0 + T])
                        B.dma("qsp", ed3, r=sh(K0, 2048)[1], w=["auxdump"])
                        def ed4(e, r_=r_):
                            return e.dma_start(out=auxdump[12 + r_],
                                               in_=SH[r_:r_ + 1, Q0:Q0 + T])
                        B.dma("qsp", ed4, r=sh(Q0, 2048)[1], w=["auxdump"])
                if hp == 0:
                    for bi in range(32):
                        def emo(e, bi=bi):
                            return e.memset(SH[:, V0 + bi * 65 + 64:V0 + bi * 65 + 65], 1.0)
                        B.op("dve", emo, w=v_tags)
                for jt in range(16):
                    for kc in range(NCH):
                        def em(e, jt=jt, kc=kc):
                            a, _ = ws(2 * 1024 + kc * 128, 128)
                            return e.matmul(PB[5][:, 0:128],
                                            XNc(kc)[:, jt * 128:(jt + 1) * 128],
                                            a, start=(kc == 0), stop=(kc == NCH - 1))
                        B.op("pe", em, r=slot_tags[2] + [f"XN:{kc}"], w=[pbt(5)])
                    for hh in range(2):
                        def emc(e, jt=jt, hh=hh):
                            off = V0 + (hh * 16 + jt) * 65
                            return e.activation(out=SH[:, off:off + 64],
                                                in_=PB[5][:, hh * 64:hh * 64 + 64],
                                                func=AF.Copy)
                        B.op("act", emc, r=[pbt(5)], w=v_tags)
                for jt in range(16):
                    def emb2(e, hp=hp):
                        return e.matmul(PB[5][:, 0:128], ONEBR[64:65, 0:128],
                                        ROWSB[64:65, hp * 128:(hp + 1) * 128],
                                        start=True, stop=True)
                    B.op("pe", emb2, r=["ONE", "RBV"], w=[pbt(5)])
                    for hh in range(2):
                        def ea(e, jt=jt, hh=hh):
                            off = V0 + (hh * 16 + jt) * 65
                            return e.tensor_tensor(out=SH[:, off:off + 64],
                                                   in0=PB[5][:, hh * 64:hh * 64 + 64],
                                                   in1=SH[:, off:off + 64], op=ALU.add)
                        B.op("dve", ea, r=[pbt(5)] + v_tags, w=v_tags)
                tiles = []
                for hh in range(2):
                    for ic in range(4):
                        for jt in range(4 * ic + 4):
                            tiles.append((hh, ic, jt))

                def oacc(g):
                    return (PB[4], pbt(4)) if g % 2 == 0 else (PBS, "PBSa")

                def epilogue(hh, ic, g):
                    acc, acct = oacc(g)
                    r0 = (g % 2) * 512
                    rt = ["R2"] if g % 2 == 0 else ["R2b"]
                    def emf(e, acc=acc, r0=r0):
                        return e.tensor_scalar(out=row(2, r0, 512),
                                               in0=acc[64:65, 0:512],
                                               scalar1=1e-35, scalar2=2.0 ** 64,
                                               op0=ALU.max, op1=ALU.mult)
                    B.op("dve", emf, r=[acct], w=rt)
                    if debug and hp == 0 and hh == 0 and ic == 0:
                        def edS1(e):
                            return e.dma_start(out=dumpS[1:2, :], in_=row(2, 0, 512))
                        B.dma("qsp", edS1, r=rt, w=["dumpS2"])
                    def er1(e, r0=r0):
                        return e.activation(out=row(2, r0, 512), in_=row(2, r0, 512),
                                            func=AF.Ln)
                    B.op("act", er1, r=rt, w=rt)
                    def er2(e, r0=r0):
                        return e.activation(out=row(2, r0, 512), in_=row(2, r0, 512),
                                            func=AF.Exp, scale=-1.0)
                    B.op("act", er2, r=rt, w=rt)
                    if debug and hp == 0 and hh == 0 and ic == 0:
                        def edS0(e):
                            return e.dma_start(out=dumpS[0:1, :], in_=row(2, 0, 512))
                        B.dma("qsp", edS0, r=rt, w=["dumpS"])
                    if g % 2 == 0:
                        bt_ap, btt = PB[5][:, 0:512], pbt(5)
                    else:
                        bt_ap, btt = PBS[:, 512:1024], "PBSb"
                    def emb3(e, r0=r0, bt_ap=bt_ap):
                        return e.matmul(bt_ap, onesrow_f(2), row(2, r0, 512),
                                        start=True, stop=True)
                    B.op("pe", emb3, r=rt + ["ONE"], w=[btt])
                    sbo = (g % 2) * 512
                    sbt = [f"SB:{g % 2}"]
                    def ecp(e, bt_ap=bt_ap, sbo=sbo):
                        return e.activation(out=SB_[:, sbo:sbo + 512], in_=bt_ap,
                                            func=AF.Identity, scale=2.0 ** 64)
                    B.op("act", ecp, r=[btt], w=sbt)
                    def emo(e, hh=hh, ic=ic, acc=acc, sbo=sbo, gh=gh):
                        off = OT0 + gh * T + ic * 512
                        pr = hh * 64
                        return e.tensor_tensor(out=SH[pr:pr + 64, off:off + 512],
                                               in0=acc[0:64, 0:512],
                                               in1=SB_[0:64, sbo:sbo + 512],
                                               op=ALU.mult)
                    B.op("dve", emo, r=[acct] + sbt, w=ot_tags[gh])

                ET3 = 10752

                def flush5(t):
                    idx, hh, ic, jt = t
                    s = idx % 4
                    ep = idx % 3
                    sa_ap = SA[:, s * 512:(s + 1) * 512]
                    sat = [f"SA:{s}"]
                    ett = sh(ET3 + ep * 512, 512)[1]
                    def e5(e, sa_ap=sa_ap, ep=ep):
                        return e.activation(
                            out=SH[:, ET3 + ep * 512:ET3 + (ep + 1) * 512],
                            in_=sa_ap, func=AF.Exp, scale=0.25, bias=-CEXP)
                    B.op("act", e5, r=sat + ["CONSTS"], w=ett)
                    if debug and hp == 0 and idx == 0:
                        def edE(e):
                            return e.dma_start(out=dumpE, in_=SH[:, ET3:ET3 + 512])
                        B.dma("qsp", edE, r=ett, w=["dumpE"])

                def flush6(t):
                    idx, hh, ic, jt = t
                    g = hh * 4 + ic
                    n = 4 * ic + 4
                    ep = idx % 3
                    ett = sh(ET3 + ep * 512, 512)[1]
                    acc, acct = oacc(g)
                    def e6(e, jt=jt, ep=ep, hh=hh, n=n, acc=acc):
                        off = V0 + (hh * 16 + jt) * 65
                        return e.matmul(acc[0:65, 0:512], SH[:, off:off + 65],
                                        SH[:, ET3 + ep * 512:ET3 + (ep + 1) * 512],
                                        start=(jt == 0), stop=(jt == n - 1))
                    B.op("pe", e6, r=v_tags + ett, w=[acct])
                    if jt == n - 1:
                        epilogue(hh, ic, g)

                pend5 = None
                pend6 = None
                for idx, (hh, ic, jt) in enumerate(tiles):
                    qd = Q0 + hh * 2048
                    kd = K0 + hh * 2048
                    qtags = sh(qd, 2048)[1]
                    ktags = sh(kd, 2048)[1]
                    pbp = idx % 2
                    s = idx % 4
                    sa_ap = SA[:, s * 512:(s + 1) * 512]
                    sc_ap = SC[:, s * 512:(s + 1) * 512]
                    sat = [f"SA:{s}"]
                    sct = [f"SC:{s}"]
                    def emd(e, jt=jt, ic=ic, qd=qd, kd=kd, pbp=pbp):
                        return e.matmul(PB[pbp][:, :],
                                        SH[0:68, kd + jt * 128:kd + (jt + 1) * 128],
                                        SH[0:68, qd + ic * 512:qd + (ic + 1) * 512],
                                        start=True, stop=True)
                    B.op("pe", emd, r=ktags + qtags, w=[pbt(pbp)])
                    def emn(e, jt=jt, ic=ic, qd=qd, kd=kd, pbp=pbp):
                        return e.matmul(PB[2 + pbp][:, :],
                                        SH[0:64, kd + jt * 128:kd + (jt + 1) * 128],
                                        SH[0:64, qd + ic * 512:qd + (ic + 1) * 512],
                                        start=True, stop=True)
                    B.op("pe", emn, r=ktags + qtags, w=[pbt(2 + pbp)])
                    def e1b(e, pbp=pbp, sa_ap=sa_ap):
                        return e.activation(out=sa_ap, in_=PB[pbp][:, :], func=AF.Ln)
                    B.op("act", e1b, r=[pbt(pbp)], w=sat)
                    def e1c(e, sa_ap=sa_ap):
                        return e.activation(out=sa_ap, in_=sa_ap, func=AF.Exp,
                                            scale=-1.0)
                    B.op("act", e1c, r=sat, w=sat)
                    def e2(e, pbp=pbp, sc_ap=sc_ap):
                        return e.activation(out=sc_ap, in_=PB[2 + pbp][:, :],
                                            func=AF.Square)
                    B.op("act", e2, r=[pbt(2 + pbp)], w=sct)
                    def e3(e, sa_ap=sa_ap, sc_ap=sc_ap):
                        return e.tensor_tensor(out=sa_ap, in0=sc_ap, in1=sa_ap,
                                               op=ALU.mult)
                    B.op("dve", e3, r=sat + sct, w=sat)
                    if jt >= 4 * ic:
                        kk = jt - 4 * ic
                        def e4(e, kk=kk, sa_ap=sa_ap):
                            return e.tensor_tensor(
                                out=sa_ap, in0=sa_ap,
                                in1=MSK[:, 256 + kk * 512:256 + (kk + 1) * 512],
                                op=ALU.add)
                        B.op("dve", e4, r=sat + ["MSK"], w=sat)
                    if pend6 is not None:
                        flush6(pend6)
                    if pend5 is not None:
                        flush5(pend5)
                    pend6, pend5 = pend5, (idx, hh, ic, jt)
                flush6(pend6)
                flush5(pend5)
                flush6(pend5)
            for m in range(NCH):
                for kci in range(2):
                    kc = 2 * grp + kci
                    sl_ap, sl_tags = ws(3 * 1024 + kci * 128, 128)
                    def emw(e, m=m, kc=kc, sl_ap=sl_ap):
                        return e.dma_start(
                            out=sl_ap,
                            in_=bw_out[kc * 128:(kc + 1) * 128, m * 128:(m + 1) * 128])
                    B.dma("qpool", emw, r=["bw_out"], w=sl_tags)
                    for sp in range(4):
                        def em(e, m=m, sp=sp, kci=kci, sl_ap=sl_ap):
                            return e.matmul(
                                PB[sp][:, :], sl_ap,
                                SH[:, OT0 + kci * T + sp * 512:OT0 + kci * T + (sp + 1) * 512],
                                start=(kci == 0), stop=(kci == 1))
                        B.op("pe", em, r=sl_tags + ot_tags[kci], w=[pbt(sp)])
                for sp in range(4):
                    if grp == 3:
                        def ea(e, m=m, sp=sp):
                            return e.scalar_tensor_tensor(
                                out=XFc(m)[:, sp * 512:(sp + 1) * 512],
                                in0=PB[sp][:, :],
                                scalar=BCOL[:, OFF_BOUT + m:OFF_BOUT + m + 1],
                                in1=XFc(m)[:, sp * 512:(sp + 1) * 512],
                                op0=ALU.add, op1=ALU.add)
                        B.op("dve", ea, r=[pbt(sp), f"XF:{m}", "BCOL"], w=[f"XF:{m}"])
                    else:
                        def ea(e, m=m, sp=sp):
                            return e.tensor_tensor(
                                out=XFc(m)[:, sp * 512:(sp + 1) * 512],
                                in0=XFc(m)[:, sp * 512:(sp + 1) * 512],
                                in1=PB[sp][:, :], op=ALU.add)
                        B.op("dve", ea, r=[pbt(sp), f"XF:{m}"], w=[f"XF:{m}"])
        stats_init()
        for c in range(NCH):
            def emw(e, c=c):
                return e.dma_start(out=xs1[c * 128:(c + 1) * 128, :], in_=XFc(c))
            B.dma("qsp", emw, r=[f"XF:{c}"], w=[f"xs1:{c}"])
            stats_chunk(lambda c=c: XFc(c), [f"XF:{c}"])
        stats_rows()
        norm_to_XF()

    attn_b()
    ffn_call(xs1, "xs1:{}", yT, "yT:{}", True)

    extra = (["xt1", "xt2p", "xt3m", "auxdump", "dumpK", "dumpQ", "dumpE", "dumpO", "dumpS"] + [f"xs1:{c}" for c in range(NCH)] + [f"xs2:{c}" for c in range(NCH)]) if debug else []
    B.op("sp", lambda e: e.nop(), r=[f"yT:{c}" for c in range(NCH)] + extra)
    import os
    if os.environ.get("KSTRICT"):
        B.strict_ranges.append((0, len(B.ops)))
    B.finalize(nc)
    return nc


_CACHE = {}


def kernel(**inputs):
    x = np.asarray(inputs["L0_attn_a_x"], dtype=np.float32)
    nb = x.shape[0]
    rot, mska, mskb, mskc = _host_consts()

    def f32(a):
        return np.ascontiguousarray(np.asarray(a, dtype=np.float32))

    wqkv = f32(inputs["L0_ffn_wqkv"])
    perm = np.concatenate([(h * 64 + (np.arange(64) + 32) % 64) for h in range(8)])
    fwperm = np.ascontiguousarray(
        np.concatenate([wqkv[:, :, perm], wqkv[:, :, 512 + perm]], axis=2))
    b1 = f32(inputs["L0_ffn_b1"])

    common = {
        "wqT": f32(np.asarray(inputs["L0_attn_a_Wq"]).T),
        "wkvT": f32(np.asarray(inputs["L0_attn_a_Wkv_down"]).T),
        "wkupT": f32(np.asarray(inputs["L0_attn_a_Wk_up"]).T),
        "wvupT": f32(np.asarray(inputs["L0_attn_a_Wv_up"]).T),
        "woT": f32(np.asarray(inputs["L0_attn_a_Wo"]).T),
        "bw_qkv": f32(inputs["L0_attn_b_w_qkv"]),
        "bb_qkv": f32(inputs["L0_attn_b_b_qkv"]).reshape(1, -1),
        "bw_out": f32(inputs["L0_attn_b_w_out"]),
        "bb_out": f32(inputs["L0_attn_b_b_out"]).reshape(1, -1),
        "fwqkv": wqkv,
        "fwperm": fwperm,
        "fqs": f32(inputs["L0_ffn_q_scale"]).reshape(2, 1, 64),
        "fks": f32(inputs["L0_ffn_k_scale"]).reshape(2, 1, 64),
        "fwo": f32(inputs["L0_ffn_wo"]),
        "flng": f32(inputs["L0_ffn_ln_g"]).reshape(2, 1, D),
        "flnb": f32(inputs["L0_ffn_ln_b"]).reshape(2, 1, D),
        "fw1": f32(inputs["L0_ffn_w1"]),
        "fb1a": np.ascontiguousarray(b1[:, 0:2730]).reshape(2, 1, 2730),
        "fb1g": np.ascontiguousarray(b1[:, 2730:5460]).reshape(2, 1, 2730),
        "fw2": f32(inputs["L0_ffn_w2"]),
        "fb2": f32(inputs["L0_ffn_b2"]).reshape(2, 1, D),
        "rot": rot, "mska": mska, "mskb": mskb, "mskc": mskc,
        "identb": np.eye(128, dtype=BFNP),
        "onefc": _onefc(), "onebc": _onefc().astype(BFNP),
        "onefr": _onefr(), "onebr": _onebr(), "blk64": _blk64(),
    }

    debug = bool(inputs.pop("_debug", False))
    key = ("nc", debug)
    if key not in _CACHE:
        _CACHE[key] = build_program(debug)
    nc = _CACHE[key]

    in_maps = []
    for b in range(nb):
        m = dict(common)
        m["xT"] = np.ascontiguousarray(x[b].T)
        in_maps.append(m)

    res = run_bass_kernel_spmd(nc, in_maps, core_ids=list(range(nb)))
    if debug:
        taps = {nm: np.stack([np.ascontiguousarray(r[nm].T) for r in res.results])
                for nm in ("xt1", "xs2", "xs1", "xt2p", "xt3m")}
        for nm2 in ("auxdump", "dumpK", "dumpQ", "dumpE", "dumpO", "dumpS"):
            taps[nm2] = np.stack([np.asarray(r[nm2]) for r in res.results])
        taps["out"] = np.stack([np.ascontiguousarray(r["yT"].T) for r in res.results])
        return taps
    return np.stack([np.ascontiguousarray(r["yT"].T) for r in res.results])


# version 16
# speedup vs baseline: 1.4131x; 1.0008x over previous
import numpy as np
import ml_dtypes
import concourse.bass as bass
import concourse.mybir as mybir
from concourse.bass_utils import run_bass_kernel_spmd

F32 = mybir.dt.float32
BF16 = mybir.dt.bfloat16
AF = mybir.ActivationFunctionType
ALU = mybir.AluOpType
BFNP = ml_dtypes.bfloat16

T = 2048
D = 1024
NCH = 8

COMPUTE_ENGINES = ("pe", "act", "dve", "pool", "sp")
DMA_QUEUES = {"qsp": "sp", "qpool": "pool"}
ALL_ENTITIES = list(COMPUTE_ENGINES) + list(DMA_QUEUES)


class _Op:
    __slots__ = ("idx", "entity", "issuer", "emit", "reads", "writes", "waits",
                 "marked", "ent_idx", "know", "qwait")

    def __init__(self, idx, entity, issuer, emit, reads, writes):
        self.idx = idx
        self.entity = entity
        self.issuer = issuer
        self.emit = emit
        self.reads = reads
        self.writes = writes
        self.waits = []
        self.marked = False
        self.ent_idx = -1
        self.know = None


class Builder:
    def __init__(self):
        self.ops = []
        self.last_write = {}
        self.readers = {}
        self.strict_ranges = []

    def in_strict(self, idx):
        return any(lo <= idx < hi for lo, hi in self.strict_ranges)

    def op(self, eng, emit, r=(), w=()):
        o = _Op(len(self.ops), eng, eng, emit, tuple(r), tuple(w))
        self.ops.append(o)
        return o

    def dma(self, queue, emit, r=(), w=()):
        o = _Op(len(self.ops), queue, DMA_QUEUES[queue], emit, tuple(r), tuple(w))
        self.ops.append(o)
        return o

    def finalize(self, nc):
        for o in self.ops:
            deps = []
            for t in o.reads:
                p = self.last_write.get(t)
                if p is not None:
                    deps.append(p)
            for t in o.writes:
                p = self.last_write.get(t)
                if p is not None:
                    deps.append(p)
                deps.extend(self.readers.get(t, ()))
            strict = self.in_strict(o.idx)
            seen = {}
            for p in deps:
                if (p.entity == o.entity and o.entity not in DMA_QUEUES
                        and not strict):
                    continue
                if p is o:
                    continue
                cur = seen.get(p.entity)
                if cur is None or p.idx > cur.idx:
                    seen[p.entity] = p
            o.waits = list(seen.values())
            for t in o.reads:
                self.readers.setdefault(t, []).append(o)
            for t in o.writes:
                self.last_write[t] = o
                self.readers[t] = []

        import os as _os
        if _os.environ.get("KSERIAL"):
            for i in range(1, len(self.ops)):
                o = self.ops[i]
                p = self.ops[i - 1]
                if all(q is not p for q in o.waits):
                    o.waits.append(p)

        def dma_key(o):
            return (o.entity, o.writes)

        key_rank = {}
        key_id = {}
        for o in self.ops:
            if o.entity in DMA_QUEUES:
                o.marked = True
                k = dma_key(o)
                if k not in key_id:
                    key_id[k] = len(key_id)
                key_rank[k] = key_rank.get(k, 0) + 1
                o.ent_idx = key_rank[k]
                o.entity = f"D{key_id[k]}_{o.entity}"

        for o in self.ops:
            seen = {}
            for p in o.waits:
                cur = seen.get(p.entity)
                if cur is None or p.idx > cur.idx:
                    seen[p.entity] = p
            o.waits = list(seen.values())

        issuer_know = {e: {} for e in COMPUTE_ENGINES}
        for o in self.ops:
            kn = issuer_know[o.issuer]
            pruned = []
            for p in sorted(o.waits, key=lambda q: q.idx):
                if kn.get(p.entity, -1) >= p.idx:
                    continue
                pruned.append(p)
                p.marked = True
                if p.know:
                    for e2, v in p.know.items():
                        if kn.get(e2, -1) < v:
                            kn[e2] = v
                if kn.get(p.entity, -1) < p.idx:
                    kn[p.entity] = p.idx
            o.waits = pruned
            snap = dict(kn)
            snap[o.entity] = o.idx
            o.know = snap

        cnt = {}
        for o in self.ops:
            if o.issuer == o.entity or o.entity in COMPUTE_ENGINES:
                if o.marked:
                    cnt[o.entity] = cnt.get(o.entity, 0) + 1
                o.ent_idx = cnt.get(o.entity, 0)

        entities = sorted(set(o.entity for o in self.ops)
                          | set(e for o in self.ops for e in
                                [p.entity for p in o.waits]))
        sems = {e: nc.alloc_semaphore(f"sem_{e}"[:48]) for e in entities}
        streams = {e: [] for e in COMPUTE_ENGINES}
        for o in self.ops:
            streams[o.issuer].append(o)

        def amt(e):
            return 1 if e in COMPUTE_ENGINES else 16

        def run(ns, issuer):
            for o in streams[issuer]:
                for p in o.waits:
                    ns.wait_ge(sems[p.entity], p.ent_idx * amt(p.entity))
                instr = o.emit(ns)
                if o.marked:
                    instr.then_inc(sems[o.entity], amt(o.entity))

        with nc.Block() as block:
            @block.tensor
            def _(e):
                run(e, "pe")

            @block.scalar
            def _(e):
                run(e, "act")

            @block.vector
            def _(e):
                run(e, "dve")

            @block.gpsimd
            def _(e):
                run(e, "pool")

            @block.sync
            def _(e):
                run(e, "sp")


def _onefc():
    a = np.zeros((128, 4), np.float32)
    a[:, 0] = 1.0
    a[:, 1] = 1.0 / 1024.0
    return a


def _onefr():
    a = np.zeros((128, 128), np.float32)
    for p in (0, 32, 64):
        a[p, :] = 1.0
    return a


def _onebr():
    a = np.zeros((128, 384), np.float64)
    a[0, 0:128] = 1.0
    a[64, 0:128] = 1.0
    a[0, 128:192] = 1.0
    a[0, 320:384] = 1.0
    return a.astype(BFNP)


def _blk64():
    a = np.zeros((128, 128), np.float64)
    a[0:64, 0:64] = 1.0
    a[64:128, 64:128] = 1.0
    return a.astype(BFNP)


def _host_consts():
    DH, WIN = 64, 512
    half = np.arange(0, DH, 2, dtype=np.float64)
    inv_freq = 1.0 / 10000.0 ** (half / DH)
    t = np.arange(T, dtype=np.float64)
    freqs = np.concatenate([t[:, None] * inv_freq[None, :]] * 2, axis=-1)
    base = (half + 0.4 * DH) / (1.4 * DH)
    power = (t - T // 2) / float(WIN // 2)
    scale = np.concatenate([base[None, :] ** power[:, None]] * 2, axis=-1)
    cos, sin = np.cos(freqs), np.sin(freqs)
    sgn = np.where(np.arange(DH) < 32, -1.0, 1.0)[None, :]

    def dup(a):
        return np.concatenate([a.T, a.T], axis=0).astype(BFNP)

    rot = np.stack([dup(cos * scale), dup(sgn * sin * scale),
                    dup(cos / scale), dup(sgn * sin / scale)])

    p = np.arange(128)[:, None]
    c = np.arange(128)[None, :]
    mska = np.where(p >= c, 0.0, -1e30).astype(BFNP)
    mskb = np.where(p <= c, 0.0, -1e30).astype(BFNP)
    c2 = np.arange(512)[None, :]
    mskc = np.zeros((4, 128, 512), dtype=np.float64)
    for k in range(4):
        mskc[k] = np.where(128 * k + p > c2, -4e9, 0.0)
    mskc = mskc.reshape(512, 512).astype(BFNP)
    return rot, mska, mskb, mskc


def build_program(debug=False):
    nc = bass.Bass("TRN2", target_bir_lowering=False)
    B = Builder()

    def din(name, shape, dt=F32):
        return nc.dram_tensor(name, shape, dt, kind="ExternalInput").ap()

    xin = din("xT", [D, T])
    wqT = din("wqT", [D, 2048])
    wkvT = din("wkvT", [D, 64])
    wkupT = din("wkupT", [64, 2048])
    wvupT = din("wvupT", [64, 2048])
    woT = din("woT", [2048, D])
    bw_qkv = din("bw_qkv", [D, 3072])
    bb_qkv = din("bb_qkv", [1, 3072])
    bw_out = din("bw_out", [D, D])
    bb_out = din("bb_out", [1, D])
    fwqkv = din("fwqkv", [2, D, 1536])
    fwperm = din("fwperm", [2, D, 1024])
    fqs = din("fqs", [2, 1, 64])
    fks = din("fks", [2, 1, 64])
    fwo = din("fwo", [2, 512, D])
    flng = din("flng", [2, 1, D])
    flnb = din("flnb", [2, 1, D])
    fw1 = din("fw1", [2, D, 5460])
    fb1a = din("fb1a", [2, 1, 2730])
    fb1g = din("fb1g", [2, 1, 2730])
    fw2 = din("fw2", [2, 2730, D])
    fb2 = din("fb2", [2, 1, D])
    rot_t = din("rot", [4, 128, T], BF16)
    mska_t = din("mska", [128, 128], BF16)
    mskb_t = din("mskb", [128, 128], BF16)
    mskc_t = din("mskc", [512, 512], BF16)
    identb_t = din("identb", [128, 128], BF16)
    onefc_t = din("onefc", [128, 4], F32)
    onebc_t = din("onebc", [128, 4], BF16)
    onefr_t = din("onefr", [128, 128], F32)
    onebr_t = din("onebr", [128, 384], BF16)
    blk64_t = din("blk64", [128, 128], BF16)

    yT = nc.dram_tensor("yT", [D, T], F32, kind="ExternalOutput").ap()
    kind_dbg = "ExternalOutput" if debug else "Internal"
    xs1 = nc.dram_tensor("xs1", [D, T], F32, kind=kind_dbg).ap()
    auxs = nc.dram_tensor("auxs", [8, T], BF16, kind="Internal").ap()
    auxdump = nc.dram_tensor("auxdump", [16, T], BF16, kind="ExternalOutput").ap() if debug else None
    dumpK = nc.dram_tensor("dumpK", [68, T], BF16, kind="ExternalOutput").ap() if debug else None
    dumpQ = nc.dram_tensor("dumpQ", [68, T], BF16, kind="ExternalOutput").ap() if debug else None
    dumpE = nc.dram_tensor("dumpE", [128, 512], BF16, kind="ExternalOutput").ap() if debug else None
    dumpO = nc.dram_tensor("dumpO", [128, 2 * T], BF16, kind="ExternalOutput").ap() if debug else None
    dumpS = nc.dram_tensor("dumpS", [2, 512], F32, kind="ExternalOutput").ap() if debug else None
    xs2 = nc.dram_tensor("xs2", [D, T], F32, kind=kind_dbg).ap()
    xt1 = nc.dram_tensor("xt1", [D, T], F32, kind=kind_dbg).ap() if debug else None
    xt2p = nc.dram_tensor("xt2p", [D, T], F32, kind=kind_dbg).ap() if debug else None
    xt3m = nc.dram_tensor("xt3m", [D, T], F32, kind=kind_dbg).ap() if debug else None

    def sbt(name, shape, dt):
        return nc.alloc_sbuf_tensor(name, shape, dt)

    XN = sbt("XN", [128, 8 * T], BF16)
    XF = sbt("XF", [128, 8 * T], F32)
    XFb = XF.bitcast(BF16)
    SH = sbt("SH", [128, 16384], BF16)
    WS = sbt("WS", [128, 8192], BF16)
    SA = sbt("SA", [128, T], F32)
    SB_ = sbt("SBs", [128, T], F32)
    SC = sbt("SCs", [128, T], F32)
    SCb = SC.bitcast(BF16)
    XL = sbt("XL", [128, 2 * T], F32)
    ROTB = XL.bitcast(BF16)
    ROWS = sbt("ROWS", [128, T], F32)
    ROWSB = sbt("ROWSB", [128, 2 * T], BF16)
    MSK = sbt("MSK", [128, 2304], BF16)
    BCOL = sbt("BCOL", [128, 192], F32)
    ONEF = sbt("ONEF", [128, 4], F32)
    ONEB = sbt("ONEB", [128, 4], BF16)
    ONEFR = sbt("ONEFR", [128, 128], F32)
    ONEBR = sbt("ONEBR", [128, 384], BF16)
    IDENTB = sbt("IDENTB", [128, 128], BF16)
    BLK64 = sbt("BLK64", [128, 128], BF16)
    ZEROB = sbt("ZEROB", [128, 4], BF16)

    PB = [nc.alloc_psum_tensor(f"PB{i}", [128, 512], F32) for i in range(6)]
    PBS = nc.alloc_psum_tensor("PBS", [128, 1024], F32)

    CONSTS = sbt("CONSTS", [128, 4], F32)
    CONST_VALS = (64.0 + 1e-6, -64.0, -70.0)
    for j, v in enumerate(CONST_VALS):
        nc.const_aps.register(F32, v, CONSTS[:, j:j + 1])

    def pbt(i):
        return f"PB{i}"

    def sh(c0, n, rows=None):
        ap = SH[:, c0:c0 + n] if rows is None else SH[rows[0]:rows[1], c0:c0 + n]
        return ap, [f"SH:{i}" for i in range(c0 // 512, (c0 + n + 511) // 512)]

    def ws(c0, n, rows=None):
        ap = WS[:, c0:c0 + n] if rows is None else WS[rows[0]:rows[1], c0:c0 + n]
        return ap, [f"WS:{i}" for i in range(c0 // 128, (c0 + n + 127) // 128)]

    def XFc(c):
        return XF[:, c * T:(c + 1) * T]

    def XNc(c):
        return XN[:, c * T:(c + 1) * T]

    ROWP = (0, 32, 64)

    def row(i, off=0, n=T):
        p = ROWP[i]
        return ROWS[p:p + 1, off:off + n]

    def onesrow_f(i):
        return ONEFR[ROWP[i]:ROWP[i] + 1, 0:128]

    def rowb(i, off=0, n=T):
        return ROWSB[0:1, i * T + off:i * T + off + n]

    BVROW = ROWSB

    OFF_BQKV, OFF_BOUT, OFF_B2 = 0, 24, 32
    OFF_LNG, OFF_LNB = 48, 64
    OFF_B1 = 80
    OFF_QS, OFF_KS, OFF_QSP, OFF_KSP, OFF_MBK = 168, 170, 172, 174, 176

    B.dma("qsp", lambda e: e.dma_start(out=MSK[:, 0:128], in_=mska_t), w=["MSK"])
    B.dma("qsp", lambda e: e.dma_start(out=MSK[:, 128:256], in_=mskb_t), w=["MSK"])
    B.dma("qsp", lambda e: e.dma_start(
        out=MSK[:, 256:2304].rearrange("p (k n) -> p k n", k=4),
        in_=mskc_t.rearrange("(k p) n -> p k n", p=128)), w=["MSK"])
    B.dma("qsp", lambda e: e.dma_start(out=IDENTB[:], in_=identb_t), w=["IDENTB"])
    B.op("dve", lambda e: e.memset(ZEROB[:], 0.0), w=["ZEROB"])
    B.dma("qpool", lambda e: e.dma_start(out=ROWSB[64:65, 0:1024],
                                         in_=bb_qkv[:, 2048:3072]),
          r=["bb_qkv"], w=["RBV"])
    B.dma("qsp", lambda e: e.dma_start(out=ONEF[:], in_=onefc_t), w=["ONE"])
    B.dma("qsp", lambda e: e.dma_start(out=ONEB[:], in_=onebc_t), w=["ONE"])
    B.dma("qsp", lambda e: e.dma_start(out=ONEFR[:], in_=onefr_t), w=["ONE"])
    B.dma("qsp", lambda e: e.dma_start(out=ONEBR[:], in_=onebr_t), w=["ONE"])
    B.dma("qsp", lambda e: e.dma_start(out=BLK64[:], in_=blk64_t), w=["BLK"])

    for j, v in enumerate(CONST_VALS):
        B.op("dve", lambda e, j=j, v=v: e.memset(CONSTS[:, j:j + 1], v), w=["CONSTS"])

    def load_row_to_cols(src_ap, src_tag, n, bcol_off):
        for s0 in range(0, n, 2048):
            seg = min(2048, n - s0)
            def eml(e, s0=s0, seg=seg):
                return e.dma_start(out=SA[0:1, 0:seg], in_=src_ap[:, s0:s0 + seg])
            B.dma("qsp", eml, r=[src_tag], w=["SA"])
            for c in range((seg + 127) // 128):
                wdt = min(128, seg - c * 128)
                def emm(e, c=c, wdt=wdt):
                    return e.matmul(PB[5][0:wdt, 0:1], SA[0:1, c * 128:c * 128 + wdt],
                                    ONEFR[0:1, 0:1], start=True, stop=True)
                B.op("pe", emm, r=["SA", "ONE"], w=[pbt(5)])
                def emv(e, c=c, wdt=wdt, s0=s0):
                    col = bcol_off + s0 // 128 + c
                    return e.tensor_copy(out=BCOL[0:wdt, col:col + 1],
                                         in_=PB[5][0:wdt, 0:1])
                B.op("dve", emv, r=[pbt(5)], w=["BCOL"])

    load_row_to_cols(bb_qkv, "bb_qkv", 3072, OFF_BQKV)
    load_row_to_cols(bb_out, "bb_out", D, OFF_BOUT)
    for l in range(2):
        load_row_to_cols(fb1a[l], "fb1a", 2730, OFF_B1 + 44 * l)
        load_row_to_cols(fb1g[l], "fb1g", 2730, OFF_B1 + 44 * l + 22)
        load_row_to_cols(fb2[l], "fb2", D, OFF_B2 + 8 * l)
        load_row_to_cols(flng[l], "flng", D, OFF_LNG + 8 * l)
        load_row_to_cols(flnb[l], "flnb", D, OFF_LNB + 8 * l)
        load_row_to_cols(fqs[l], "fqs", 64, OFF_QS + l)
        load_row_to_cols(fks[l], "fks", 64, OFF_KS + l)
    for l in range(2):
        for src, dst in ((OFF_QS + l, OFF_QSP + l), (OFF_KS + l, OFF_KSP + l)):
            def e1(e, src=src, dst=dst):
                return e.tensor_copy(out=BCOL[0:32, dst:dst + 1],
                                     in_=BCOL[32:64, src:src + 1])
            B.op("dve", e1, r=["BCOL"], w=["BCOL"])
            def e2(e, src=src, dst=dst):
                return e.tensor_copy(out=BCOL[32:64, dst:dst + 1],
                                     in_=BCOL[0:32, src:src + 1])
            B.op("dve", e2, r=["BCOL"], w=["BCOL"])
        for cc in (OFF_QS + l, OFF_KS + l, OFF_QSP + l, OFF_KSP + l):
            def e3(e, cc=cc):
                return e.tensor_copy(out=BCOL[64:128, cc:cc + 1],
                                     in_=BCOL[0:64, cc:cc + 1])
            B.op("dve", e3, r=["BCOL"], w=["BCOL"])
    for hp in range(8):
        def e4(e, hp=hp):
            return e.activation(out=BCOL[:, OFF_MBK + hp:OFF_MBK + hp + 1],
                                in_=BCOL[:, OFF_BQKV + 8 + hp:OFF_BQKV + 8 + hp + 1],
                                func=AF.Identity, scale=-2.0)
        B.op("act", e4, r=["BCOL"], w=["BCOL"])

    def stats_init():
        B.op("dve", lambda e: e.memset(row(0), 0.0),
             w=["R0"] + [f"R0c{i}" for i in range(4)])
        B.op("dve", lambda e: e.memset(row(1), 0.0),
             w=["R1"] + [f"R1c{i}" for i in range(4)])

    def stats_from_bf16(src_ap_fn, src_tags):
        B.op("act", lambda e: e.activation(out=SCb[:, T:2 * T], in_=src_ap_fn(),
                                           func=AF.Square), r=list(src_tags), w=["SC"])
        for nch in range(4):
            def em1(e, nch=nch, fn=src_ap_fn):
                full = fn()
                return e.matmul(PBS[0:1, 0:512], ONEB[:, 1:2],
                                full[:, nch * 512:(nch + 1) * 512], start=True, stop=True)
            B.op("pe", em1, r=list(src_tags) + ["ONE"], w=["PBSa"])
            def em2(e, nch=nch):
                return e.matmul(PBS[0:1, 512:1024], ONEB[:, 1:2],
                                SCb[:, T + nch * 512:T + (nch + 1) * 512],
                                start=True, stop=True)
            B.op("pe", em2, r=["SC", "ONE"], w=["PBSb"])
            def ev1(e, nch=nch):
                return e.tensor_tensor(out=row(0, nch * 512, 512),
                                       in0=row(0, nch * 512, 512),
                                       in1=PBS[0:1, 0:512], op=ALU.add)
            B.op("dve", ev1, r=[f"R0c{nch}", "PBSa"], w=["R0", f"R0c{nch}"])
            def ev2(e, nch=nch):
                return e.tensor_tensor(out=row(1, nch * 512, 512),
                                       in0=row(1, nch * 512, 512),
                                       in1=PBS[0:1, 512:1024], op=ALU.add)
            B.op("dve", ev2, r=[f"R1c{nch}", "PBSb"], w=["R1", f"R1c{nch}"])

    def stats_chunk(src_ap_fn, src_tags):
        B.op("act", lambda e: e.activation(out=SCb[:, 0:T], in_=src_ap_fn(),
                                           func=AF.Copy), r=list(src_tags), w=["SC"])
        stats_from_bf16(lambda: SCb[:, 0:T], ["SC"])

    def stats_rows(eps=1e-5):
        for nch in range(4):
            s0 = nch * 512
            def ev1(e, s0=s0):
                return e.tensor_tensor(out=SA[0:1, s0:s0 + 512], in0=row(0, s0, 512),
                                       in1=row(0, s0, 512), op=ALU.mult)
            B.op("dve", ev1, r=[f"R0c{nch}"], w=["SA"])
            def ev2(e, s0=s0):
                return e.activation(out=SB_[0:1, s0:s0 + 512], in_=row(1, s0, 512),
                                    func=AF.Copy)
            B.op("act", ev2, r=[f"R1c{nch}"], w=["SB"])
            def ev3(e, s0=s0):
                return e.scalar_tensor_tensor(out=row(1, s0, 512),
                                              in0=SB_[0:1, s0:s0 + 512], scalar=eps,
                                              in1=SA[0:1, s0:s0 + 512],
                                              op0=ALU.add, op1=ALU.subtract)
            B.op("dve", ev3, r=["SA", "SB"], w=["R1", f"R1c{nch}"])
            def ev4(e, s0=s0):
                return e.activation(out=row(1, s0, 512), in_=row(1, s0, 512),
                                    func=AF.Ln)
            B.op("act", ev4, r=[f"R1c{nch}"], w=["R1", f"R1c{nch}"])
            def ev5(e, s0=s0):
                return e.activation(out=row(1, s0, 512), in_=row(1, s0, 512),
                                    func=AF.Exp, scale=-0.5)
            B.op("act", ev5, r=[f"R1c{nch}"], w=["R1", f"R1c{nch}"])

    def normalize(dst_fn, src_fn=None, gain_bias=None):
        for nch in range(4):
            def emb(e, nch=nch):
                return e.matmul(PB[4][:, 0:512], onesrow_f(0),
                                row(0, nch * 512, 512), start=True, stop=True)
            B.op("pe", emb, r=[f"R0c{nch}", "ONE"], w=[pbt(4)])
            def emb2(e, nch=nch):
                return e.matmul(PB[5][:, 0:512], onesrow_f(1),
                                row(1, nch * 512, 512), start=True, stop=True)
            B.op("pe", emb2, r=[f"R1c{nch}", "ONE"], w=[pbt(5)])
            for c in range(NCH):
                if src_fn is None:
                    src_ap = XFc(c)[:, nch * 512:(nch + 1) * 512]
                    src_tags, shifted = [f"XF:{c}"], False
                else:
                    src_ap, src_tags, shifted = src_fn(c, nch)
                if not shifted:
                    def em1(e, src_ap=src_ap):
                        return e.tensor_tensor(out=SA[:, 0:512], in0=src_ap,
                                               in1=PB[4][:, 0:512], op=ALU.subtract)
                    B.op("dve", em1, r=src_tags + [pbt(4)], w=["SA"])
                else:
                    def em1(e, src_ap=src_ap):
                        return e.tensor_tensor(out=SA[:, 1:512], in0=src_ap,
                                               in1=PB[4][:, 1:512], op=ALU.subtract)
                    B.op("dve", em1, r=src_tags + [pbt(4)], w=["SA"])
                    def em0(e):
                        return e.tensor_scalar(out=SA[:, 0:1], in0=PB[4][:, 0:1],
                                               scalar1=-1.0, scalar2=None, op0=ALU.mult)
                    B.op("dve", em0, r=[pbt(4)], w=["SA"])
                dst_ap, dst_tags = dst_fn(c, nch)
                if gain_bias is None:
                    def em2(e, dst_ap=dst_ap):
                        return e.tensor_tensor(out=dst_ap, in0=SA[:, 0:512],
                                               in1=PB[5][:, 0:512], op=ALU.mult)
                    B.op("dve", em2, r=["SA", pbt(5)], w=dst_tags)
                else:
                    g_ap, b_ap = gain_bias(c)
                    def em2(e):
                        return e.tensor_tensor(out=SB_[:, 0:512], in0=SA[:, 0:512],
                                               in1=PB[5][:, 0:512], op=ALU.mult)
                    B.op("dve", em2, r=["SA", pbt(5)], w=["SB"])
                    def em3(e, dst_ap=dst_ap, g_ap=g_ap, b_ap=b_ap):
                        return e.activation(out=dst_ap, in_=SB_[:, 0:512],
                                            func=AF.Identity, scale=g_ap, bias=b_ap)
                    B.op("act", em3, r=["SB", "BCOL"], w=dst_tags)

    def norm_to_XN():
        normalize(lambda c, nch: (XNc(c)[:, nch * 512:(nch + 1) * 512],
                                  [f"XN:{c}", f"XN:{c}:{nch}"]))

    def norm_to_XF():
        normalize(lambda c, nch: (XFc(c)[:, nch * 512:(nch + 1) * 512], [f"XF:{c}"]))

    stats_init()
    for c in range(NCH):
        def eml(e, c=c):
            return e.dma_start(out=XFc(c), in_=xin[c * 128:(c + 1) * 128, :])
        B.dma("qsp", eml, r=["xin"], w=[f"XF:{c}"])
        stats_chunk(lambda c=c: XFc(c), [f"XF:{c}"])
    stats_rows()
    norm_to_XN()

    def attn_a():
        QT0, KT0, VT0, ET0, KV0 = 0, 4096, 8192, 13312, 14336
        kv_ap, kv_tags = ws(7168, 512)
        B.dma("qpool", lambda e: e.dma_start(
            out=kv_ap.rearrange("p (c n) -> p c n", c=8),
            in_=wkvT.rearrange("(c p) n -> p c n", p=128)),
            r=["wkvT"], w=kv_tags)
        for nch in range(4):
            for kc in range(NCH):
                def em(e, nch=nch, kc=kc):
                    a, _ = ws(7168 + kc * 64, 64)
                    return e.matmul(PB[0][0:64, :], a,
                                    XNc(kc)[:, nch * 512:(nch + 1) * 512],
                                    start=(kc == 0), stop=(kc == NCH - 1))
                B.op("pe", em, r=kv_tags + [f"XN:{kc}:{nch}"], w=[pbt(0)])
            def eme(e, nch=nch):
                a, tg = sh(KV0 + nch * 512, 512, rows=(0, 64))
                return e.activation(out=a, in_=PB[0][0:64, :], func=AF.Copy)
            B.op("act", eme, r=[pbt(0)], w=sh(KV0, 2048)[1])
        kup_ap, kup_tags = ws(0, 2048, rows=(0, 64))
        B.dma("qpool", lambda e: e.dma_start(out=kup_ap, in_=wkupT),
              r=["wkupT"], w=kup_tags)
        vup_ap, vup_tags = ws(2048, 2048, rows=(0, 64))
        B.dma("qpool", lambda e: e.dma_start(out=vup_ap, in_=wvupT),
              r=["wvupT"], w=vup_tags)
        vt_tags = sh(VT0, 4112)[1]
        for jt in range(16):
            def emo(e, jt=jt):
                return e.memset(SH[:, VT0 + jt * 257 + 256:VT0 + jt * 257 + 257], 1.0)
            B.op("dve", emo, w=vt_tags)

        kvs_tags = sh(KV0, 2048)[1]
        qt_tags = sh(QT0, 4096)[1]
        kt_tags = sh(KT0, 4096)[1]
        for h in range(8):
            wq_ap, wq_tags = ws(4096, 2048)
            def emw(e, h=h):
                return e.dma_start(
                    out=wq_ap.rearrange("p (c n) -> p c n", c=8),
                    in_=wqT.rearrange("(c p) n -> p c n", p=128)[:, :, h * 256:(h + 1) * 256])
            B.dma("qpool", emw, r=["wqT"], w=wq_tags)
            for m in range(2):
                for nch in range(4):
                    for kc in range(NCH):
                        def em(e, m=m, nch=nch, kc=kc):
                            a, _ = ws(4096 + kc * 256 + m * 128, 128)
                            return e.matmul(PB[m][:, :], a,
                                            XNc(kc)[:, nch * 512:(nch + 1) * 512],
                                            start=(kc == 0), stop=(kc == NCH - 1))
                        B.op("pe", em, r=wq_tags + [f"XN:{kc}:{nch}"], w=[pbt(m)])
                    def eme(e, m=m, nch=nch):
                        return e.activation(
                            out=SH[:, QT0 + m * T + nch * 512:QT0 + m * T + (nch + 1) * 512],
                            in_=PB[m][:, :], func=AF.Copy)
                    B.op("act", eme, r=[pbt(m)], w=qt_tags)
                for nch in range(4):
                    def em(e, m=m, nch=nch, h=h):
                        a, _ = ws(h * 256 + m * 128, 128, rows=(0, 64))
                        return e.matmul(PB[2][:, :], a,
                                        SH[0:64, KV0 + nch * 512:KV0 + (nch + 1) * 512],
                                        start=True, stop=True)
                    B.op("pe", em, r=kup_tags + kvs_tags, w=[pbt(2)])
                    def eme(e, m=m, nch=nch):
                        return e.activation(
                            out=SH[:, KT0 + m * T + nch * 512:KT0 + m * T + (nch + 1) * 512],
                            in_=PB[2][:, :], func=AF.Copy)
                    B.op("act", eme, r=[pbt(2)], w=kt_tags)
            for jt in range(16):
                def em(e, jt=jt, h=h):
                    a, _ = ws(2048 + h * 256, 256, rows=(0, 64))
                    return e.matmul(PB[3][:, 0:256],
                                    SH[0:64, KV0 + jt * 128:KV0 + (jt + 1) * 128],
                                    a, start=True, stop=True)
                B.op("pe", em, r=kvs_tags + vup_tags, w=[pbt(3)])
                def eme(e, jt=jt):
                    return e.activation(out=SH[:, VT0 + jt * 257:VT0 + jt * 257 + 256],
                                        in_=PB[3][:, 0:256], func=AF.Copy)
                B.op("act", eme, r=[pbt(3)], w=vt_tags)
            def aepilogue(ic):
                def emf(e):
                    return e.tensor_scalar(out=row(2, 0, 512), in0=PB[5][0:1, 0:512],
                                           scalar1=1e-35, scalar2=None, op0=ALU.max)
                B.op("dve", emf, r=[pbt(5)], w=["R2"])
                B.op("act", lambda e: e.activation(out=row(2, 0, 512), in_=row(2, 0, 512),
                                                   func=AF.Ln), r=["R2"], w=["R2"])
                B.op("act", lambda e: e.activation(out=row(2, 0, 512), in_=row(2, 0, 512),
                                                   func=AF.Exp, scale=-1.0),
                     r=["R2"], w=["R2"])
                def emb(e):
                    return e.matmul(PB[2][:, 0:512], onesrow_f(2), row(2, 0, 512),
                                    start=True, stop=True)
                B.op("pe", emb, r=["R2", "ONE"], w=[pbt(2)])
                B.op("act", lambda e: e.activation(out=SA[:, 0:512], in_=PB[2][:, 0:512],
                                                   func=AF.Copy), r=[pbt(2)], w=["SA"])
                for mt in range(2):
                    def em(e, mt=mt, ic=ic, h=h):
                        oc = 2 * h + mt
                        return e.tensor_tensor(
                            out=XFb[:, oc * T + ic * 512:oc * T + ic * 512 + 512],
                            in0=PB[3 + mt][:, 0:512], in1=SA[:, 0:512], op=ALU.mult)
                    B.op("dve", em, r=[pbt(3 + mt), "SA"], w=[f"XF:{h}"])

            def aflush(t):
                ic, jt = t
                pb = jt % 2
                for mt in range(3):
                    o0, wdt, opb = ((0, 128, 3), (128, 128, 4), (256, 1, 5))[mt]
                    def em(e, jt=jt, o0=o0, wdt=wdt, opb=opb, pb=pb):
                        return e.matmul(
                            PB[opb][0:(128 if wdt == 128 else 1), 0:512],
                            SH[:, VT0 + jt * 257 + o0:VT0 + jt * 257 + o0 + wdt],
                            SH[:, ET0 + pb * 512:ET0 + (pb + 1) * 512],
                            start=(jt == 0), stop=(jt == 15))
                    B.op("pe", em, r=vt_tags + sh(ET0 + pb * 512, 512)[1], w=[pbt(opb)])
                if jt == 15:
                    aepilogue(ic)

            apend = None
            for ic in range(4):
                for jt in range(16):
                    pb = jt % 2
                    for kc in range(2):
                        def em(e, pb=pb, kc=kc, jt=jt, ic=ic):
                            return e.matmul(
                                PB[pb][:, :],
                                SH[:, KT0 + kc * T + jt * 128:KT0 + kc * T + (jt + 1) * 128],
                                SH[:, QT0 + kc * T + ic * 512:QT0 + kc * T + (ic + 1) * 512],
                                start=(kc == 0), stop=(kc == 1))
                        B.op("pe", em, r=kt_tags + qt_tags, w=[pbt(pb)])
                    if apend is not None:
                        aflush(apend)
                    def eme(e, pb=pb):
                        return e.activation(out=SH[:, ET0 + pb * 512:ET0 + (pb + 1) * 512],
                                            in_=PB[pb][:, :], func=AF.Exp, scale=1.0 / 16.0)
                    B.op("act", eme, r=[pbt(pb)], w=sh(ET0 + pb * 512, 512)[1])
                    apend = (ic, jt)
            aflush(apend)
        stats_init()
        for m in range(NCH):
            def eml(e, m=m):
                return e.dma_start(out=XL[:, (m % 2) * T:(m % 2) * T + T],
                                   in_=xin[m * 128:(m + 1) * 128, :])
            B.dma("qsp", eml, r=["xin"], w=[f"XL:{m % 2}"])
            for kc in range(16):
                sl_ap, sl_tags = ws(6144 + (kc % 2) * 128, 128)
                def emw(e, m=m, kc=kc, sl_ap=sl_ap):
                    return e.dma_start(out=sl_ap,
                                       in_=woT[kc * 128:(kc + 1) * 128, m * 128:(m + 1) * 128])
                B.dma("qpool", emw, r=["woT"], w=sl_tags)
                for sp in range(4):
                    def em(e, m=m, kc=kc, sp=sp, sl_ap=sl_ap):
                        return e.matmul(PB[sp][:, :], sl_ap,
                                        XFb[:, kc * T + sp * 512:kc * T + sp * 512 + 512],
                                        start=(kc == 0), stop=(kc == 15))
                    B.op("pe", em, r=sl_tags + [f"XF:{kc // 2}"], w=[pbt(sp)])
            for sp in range(4):
                def emr(e, m=m, sp=sp):
                    return e.tensor_tensor(
                        out=SB_[:, sp * 512:(sp + 1) * 512], in0=PB[sp][:, :],
                        in1=XL[:, (m % 2) * T + sp * 512:(m % 2) * T + (sp + 1) * 512],
                        op=ALU.add)
                B.op("dve", emr, r=[pbt(sp), f"XL:{m % 2}"], w=["SB"])
            def emw2(e, m=m):
                return e.dma_start(out=xs1[m * 128:(m + 1) * 128, :], in_=SB_[:, 0:T])
            B.dma("qsp", emw2, r=["SB"], w=[f"xs1:{m}"])
            stats_chunk(lambda: SB_[:, 0:T], ["SB"])
        stats_rows()
        for c in range(NCH):
            def eml(e, c=c):
                return e.dma_start(out=XFc(c), in_=xs1[c * 128:(c + 1) * 128, :])
            B.dma("qsp", eml, r=[f"xs1:{c}"], w=[f"XF:{c}"])
            if debug:
                def emt(e, c=c):
                    return e.dma_start(out=xt1[c * 128:(c + 1) * 128, :], in_=XFc(c))
                B.dma("qsp", emt, r=[f"XF:{c}"], w=["xt1"])
        norm_to_XF()

    attn_a()

    def shift_to_XN():
        for c in range(NCH):
            if c < 4:
                def em(e, c=c):
                    return e.activation(out=XNc(c), in_=XFc(c), func=AF.Copy)
                B.op("act", em, r=[f"XF:{c}"],
                     w=[f"XN:{c}"] + [f"XN:{c}:{n}" for n in range(4)])
            else:
                def em(e, c=c):
                    return e.activation(out=XNc(c)[:, 1:T], in_=XFc(c)[:, 0:T - 1],
                                        func=AF.Copy)
                B.op("act", em, r=[f"XF:{c}"],
                     w=[f"XN:{c}"] + [f"XN:{c}:{n}" for n in range(4)])
                B.op("dve", lambda e, c=c: e.memset(XNc(c)[:, 0:1], 0.0),
                     w=[f"XN:{c}", f"XN:{c}:0"])

    def ffn_layer(l):
        QQ0, KK0, VS0, ET0, OT0 = 0, 2048, 4096, 7168, 8192
        qq_tags = sh(QQ0, 2048)[1]
        kk_tags = sh(KK0, 2048)[1]
        vs_tags = sh(VS0, 2080)[1]
        et_tags = ["SH:7"]
        wqkv_l = fwqkv[l].rearrange("(c p) n -> p c n", p=128)
        wperm_l = fwperm[l].rearrange("(c p) n -> p c n", p=128)

        shift_to_XN()
        for hp in range(4):
            slot_info = []
            for si, (src, base) in enumerate((
                    (wqkv_l, hp * 128), (wqkv_l, 512 + hp * 128),
                    (wqkv_l, 1024 + hp * 128),
                    (wperm_l, hp * 128), (wperm_l, 512 + hp * 128))):
                s_ap, s_tags = ws(si * 1024, 1024)
                slot_info.append((si, s_tags))
                def emw(e, s_ap=s_ap, src=src, base=base):
                    return e.dma_start(out=s_ap.rearrange("p (c n) -> p c n", c=8),
                                       in_=src[:, :, base:base + 128])
                B.dma("qpool", emw, r=["fwqkv"], w=s_tags)
            for which in range(2):
                dst0 = (QQ0, KK0)[which]
                dtags = (qq_tags, kk_tags)[which]
                cs_off = (0, 2 * T)[which]
                sc = (OFF_QS, OFF_KS)[which] + l
                scp = (OFF_QSP, OFF_KSP)[which] + l
                wtag = slot_info[which][1]
                wptag = slot_info[3 + which][1]
                for nch in range(4):
                    for kc in range(NCH):
                        def em(e, which=which, nch=nch, kc=kc):
                            a, _ = ws(which * 1024 + kc * 128, 128)
                            return e.matmul(PB[nch][:, :], a,
                                            XNc(kc)[:, nch * 512:(nch + 1) * 512],
                                            start=(kc == 0), stop=(kc == NCH - 1))
                        B.op("pe", em, r=wtag + [f"XN:{kc}:{nch}"], w=[pbt(nch)])
                for nch in range(4):
                    def em(e, nch=nch):
                        return e.activation(out=SCb[:, nch * 512:(nch + 1) * 512],
                                            in_=PB[nch][:, :], func=AF.Square)
                    B.op("act", em, r=[pbt(nch)], w=["SC"])
                for nch in range(4):
                    def em(e, nch=nch):
                        return e.matmul(PB[4 + nch % 2][:, 0:512], BLK64[:, :],
                                        SCb[:, nch * 512:(nch + 1) * 512],
                                        start=True, stop=True)
                    B.op("pe", em, r=["SC", "BLK"], w=[pbt(4 + nch % 2)])
                    def el(e, nch=nch):
                        return e.activation(out=SA[:, nch * 512:(nch + 1) * 512],
                                            in_=PB[4 + nch % 2][:, 0:512], func=AF.Ln)
                    B.op("act", el, r=[pbt(4 + nch % 2)], w=["SA"])
                for nch in range(4):
                    def ex(e, nch=nch):
                        return e.activation(out=SA[:, nch * 512:(nch + 1) * 512],
                                            in_=SA[:, nch * 512:(nch + 1) * 512],
                                            func=AF.Exp, scale=-0.5)
                    B.op("act", ex, r=["SA"], w=["SA"])
                for nch in range(4):
                    def em(e, nch=nch):
                        return e.tensor_tensor(out=SB_[:, nch * 512:(nch + 1) * 512],
                                               in0=PB[nch][:, :],
                                               in1=SA[:, nch * 512:(nch + 1) * 512],
                                               op=ALU.mult)
                    B.op("dve", em, r=[pbt(nch), "SA"], w=["SB"])
                def esc(e, sc=sc):
                    return e.activation(out=SB_[:], in_=SB_[:], func=AF.Identity,
                                        scale=BCOL[:, sc:sc + 1])
                B.op("act", esc, r=["SB", "BCOL"], w=["SB"])
                for nch in range(4):
                    for kc in range(NCH):
                        def em(e, which=which, nch=nch, kc=kc):
                            a, _ = ws((3 + which) * 1024 + kc * 128, 128)
                            return e.matmul(PB[nch][:, :], a,
                                            XNc(kc)[:, nch * 512:(nch + 1) * 512],
                                            start=(kc == 0), stop=(kc == NCH - 1))
                        B.op("pe", em, r=wptag + [f"XN:{kc}:{nch}"], w=[pbt(nch)])
                    def em2(e, nch=nch):
                        return e.tensor_tensor(out=SC[:, nch * 512:(nch + 1) * 512],
                                               in0=PB[nch][:, :],
                                               in1=SA[:, nch * 512:(nch + 1) * 512],
                                               op=ALU.mult)
                    B.op("dve", em2, r=[pbt(nch), "SA"], w=["SC"])
                def esc2(e, scp=scp):
                    return e.activation(out=SC[:], in_=SC[:], func=AF.Identity,
                                        scale=BCOL[:, scp:scp + 1])
                B.op("act", esc2, r=["SC", "BCOL"], w=["SC"])
                def eu(e, cs_off=cs_off):
                    return e.tensor_tensor(out=SB_[:], in0=SB_[:],
                                           in1=ROTB[:, cs_off:cs_off + T], op=ALU.mult)
                B.op("dve", eu, r=["SB", "XL:0", "XL:1"], w=["SB"])
                def ev2(e, cs_off=cs_off):
                    return e.tensor_tensor(out=SC[:], in0=SC[:],
                                           in1=ROTB[:, cs_off + T:cs_off + 2 * T],
                                           op=ALU.mult)
                B.op("dve", ev2, r=["SC", "XL:0", "XL:1"], w=["SC"])
                def eq(e, dst0=dst0):
                    return e.tensor_tensor(out=SH[:, dst0:dst0 + T], in0=SB_[:],
                                           in1=SC[:], op=ALU.add)
                B.op("dve", eq, r=["SB", "SC"], w=dtags)
            if hp == 0:
                for bi in range(32):
                    def emo(e, bi=bi):
                        return e.memset(SH[:, VS0 + bi * 65 + 64:VS0 + bi * 65 + 65], 1.0)
                    B.op("dve", emo, w=vs_tags)
            vtag = slot_info[2][1]
            for jt in range(16):
                for kc in range(NCH):
                    def em(e, jt=jt, kc=kc):
                        a, _ = ws(2 * 1024 + kc * 128, 128)
                        return e.matmul(PB[5][:, 0:128], XNc(kc)[:, jt * 128:(jt + 1) * 128],
                                        a, start=(kc == 0), stop=(kc == NCH - 1))
                    B.op("pe", em, r=vtag + [f"XN:{kc}:{jt // 4}"], w=[pbt(5)])
                for hh in range(2):
                    def emc(e, jt=jt, hh=hh):
                        off = VS0 + (hh * 16 + jt) * 65
                        return e.activation(out=SH[:, off:off + 64],
                                            in_=PB[5][:, hh * 64:hh * 64 + 64],
                                            func=AF.Copy)
                    B.op("act", emc, r=[pbt(5)], w=vs_tags)
            def foacc(gg):
                return (PB[4], pbt(4)) if gg % 2 == 0 else (PBS, "PBSa")

            def fepilogue(hh, w, gg):
                h = 2 * hp + hh
                acc, acct = foacc(gg)
                r0 = (gg % 2) * 512
                rt = ["R2"] if gg % 2 == 0 else ["R2b"]
                def emf(e, acc=acc, r0=r0):
                    return e.tensor_scalar(out=row(2, r0, 512), in0=acc[64:65, 0:512],
                                           scalar1=1e-35, scalar2=None, op0=ALU.max)
                B.op("dve", emf, r=[acct], w=rt)
                def er1(e, r0=r0):
                    return e.activation(out=row(2, r0, 512), in_=row(2, r0, 512),
                                        func=AF.Ln)
                B.op("act", er1, r=rt, w=rt)
                def er2(e, r0=r0):
                    return e.activation(out=row(2, r0, 512), in_=row(2, r0, 512),
                                        func=AF.Exp, scale=-1.0)
                B.op("act", er2, r=rt, w=rt)
                if gg % 2 == 0:
                    bt_ap, btt = PB[5][:, 0:512], pbt(5)
                else:
                    bt_ap, btt = PBS[:, 512:1024], "PBSb"
                def emb(e, r0=r0, bt_ap=bt_ap):
                    return e.matmul(bt_ap, onesrow_f(2), row(2, r0, 512),
                                    start=True, stop=True)
                B.op("pe", emb, r=rt + ["ONE"], w=[btt])
                def ecp(e, bt_ap=bt_ap):
                    return e.activation(out=SA[:, 0:512], in_=bt_ap, func=AF.Copy)
                B.op("act", ecp, r=[btt], w=["SA"])
                def emo2(e, h=h, w=w, acc=acc):
                    oc = h // 2
                    pr = (h % 2) * 64
                    off = OT0 + oc * T + w * 512
                    return e.tensor_tensor(out=SH[pr:pr + 64, off:off + 512],
                                           in0=acc[0:64, 0:512], in1=SA[0:64, 0:512],
                                           op=ALU.mult)
                B.op("dve", emo2, r=[acct, "SA"],
                     w=sh(OT0 + (h // 2) * T + w * 512, 512)[1])

            def fflush(t):
                hh, w, gg, jglob, pb, i0, N, first, last = t
                acc, acct = foacc(gg)
                def emo(e, jglob=jglob, pb=pb, hh=hh, i0=i0, N=N,
                        first=first, last=last, acc=acc):
                    off = VS0 + (hh * 16 + jglob) * 65
                    return e.matmul(acc[0:65, i0:i0 + N],
                                    SH[:, off:off + 65],
                                    SH[:, ET0 + pb * 512:ET0 + pb * 512 + N],
                                    start=first, stop=last)
                B.op("pe", emo, r=vs_tags + sh(ET0 + pb * 512, 512)[1], w=[acct])
                if last:
                    fepilogue(hh, w, gg)

            fpend = None
            for hh in range(2):
                for w in range(4):
                    gg = hh * 4 + w
                    qb = hh * 64
                    jtls = list(range(4, 8)) if w == 0 else list(range(8))
                    for ti, jtl in enumerate(jtls):
                        jglob = 4 * (w - 1) + jtl
                        pb = ti % 2
                        i0 = max(0, (jtl - 4) * 128)
                        i1 = min(512, (jtl + 1) * 128)
                        N = i1 - i0
                        def em(e, pb=pb, jglob=jglob, w=w, qb=qb, i0=i0, N=N):
                            return e.matmul(
                                PB[pb][:, 0:N],
                                SH[qb:qb + 64, KK0 + jglob * 128:KK0 + (jglob + 1) * 128],
                                SH[qb:qb + 64, QQ0 + w * 512 + i0:QQ0 + w * 512 + i0 + N],
                                start=True, stop=False)
                        B.op("pe", em, r=qq_tags + kk_tags, w=[pbt(pb)])
                        if w > 0 and jtl <= 3:
                            moff, mpos = 0, jtl * 128 - i0
                        elif jtl >= 4:
                            moff, mpos = 128, (jtl - 4) * 128 - i0
                        else:
                            moff = None
                        if moff is not None:
                            def emk(e, pb=pb, moff=moff, mpos=mpos):
                                return e.matmul(PB[pb][:, mpos:mpos + 128],
                                                IDENTB[:, :], MSK[:, moff:moff + 128],
                                                start=False, stop=True)
                            B.op("pe", emk, r=["IDENTB", "MSK"], w=[pbt(pb)])
                        else:
                            def emk(e, pb=pb):
                                return e.matmul(PB[pb][0:1, 0:1], IDENTB[0:1, 0:1],
                                                ZEROB[0:1, 0:1], start=False, stop=True)
                            B.op("pe", emk, r=["IDENTB", "ZEROB"], w=[pbt(pb)])
                        if fpend is not None:
                            fflush(fpend)
                        def eme(e, pb=pb, N=N):
                            return e.activation(out=SH[:, ET0 + pb * 512:ET0 + pb * 512 + N],
                                                in_=PB[pb][:, 0:N], func=AF.Exp, scale=8.0)
                        B.op("act", eme, r=[pbt(pb)], w=sh(ET0 + pb * 512, 512)[1])
                        fpend = (hh, w, gg, jglob, pb, i0, N,
                                 ti == 0, ti == len(jtls) - 1)
            fflush(fpend)
        for m in range(NCH):
            for kc in range(4):
                sl_ap, sl_tags = ws(6144 + (kc % 2) * 128, 128)
                def emw(e, m=m, kc=kc, sl_ap=sl_ap):
                    return e.dma_start(out=sl_ap,
                                       in_=fwo[l][kc * 128:(kc + 1) * 128, m * 128:(m + 1) * 128])
                B.dma("qpool", emw, r=["fwo"], w=sl_tags)
                for sp in range(4):
                    def em(e, m=m, kc=kc, sp=sp, sl_ap=sl_ap):
                        return e.matmul(PB[sp][:, :], sl_ap,
                                        SH[:, OT0 + kc * T + sp * 512:OT0 + kc * T + (sp + 1) * 512],
                                        start=(kc == 0), stop=(kc == 3))
                    B.op("pe", em, r=sl_tags + sh(8192 + kc * T, T)[1], w=[pbt(sp)])
            for sp in range(4):
                def ea(e, m=m, sp=sp):
                    return e.tensor_tensor(
                        out=XFc(m)[:, sp * 512:(sp + 1) * 512],
                        in0=XFc(m)[:, sp * 512:(sp + 1) * 512],
                        in1=PB[sp][:, :], op=ALU.add)
                B.op("dve", ea, r=[pbt(sp), f"XF:{m}"], w=[f"XF:{m}"])
        stats_init()
        shift_to_XN()
        for c in range(NCH):
            stats_from_bf16(lambda c=c: XNc(c), [f"XN:{c}"])
        stats_rows()

        def srcfn(c, nch):
            if c < 4:
                return (XFc(c)[:, nch * 512:(nch + 1) * 512], [f"XF:{c}"], False)
            if nch == 0:
                return (XFc(c)[:, 0:511], [f"XF:{c}"], True)
            return (XFc(c)[:, nch * 512 - 1:(nch + 1) * 512 - 1], [f"XF:{c}"], False)

        normalize(lambda c, nch: (XNc(c)[:, nch * 512:(nch + 1) * 512],
                                  [f"XN:{c}", f"XN:{c}:{nch}"]),
                  src_fn=srcfn,
                  gain_bias=lambda c: (BCOL[:, OFF_LNG + 8 * l + c:OFF_LNG + 8 * l + c + 1],
                                       BCOL[:, OFF_LNB + 8 * l + c:OFF_LNB + 8 * l + c + 1]))

        w1r = fw1[l].rearrange("(c p) n -> p c n", p=128)
        HID = 2730
        groups = [list(range(0, 8)), list(range(8, 16)), list(range(16, 22))]
        for g in groups:
            for gi, cch in enumerate(g):
                wdt = min(128, HID - cch * 128)
                wa0 = (5 * 1024) if cch % 2 == 0 else 0
                wg0 = (6 * 1024) if cch % 2 == 0 else 1024
                wa_ap, wa_tags = ws(wa0, 8 * wdt)
                wg_ap, wg_tags = ws(wg0, 8 * wdt)
                def emwa(e, wa_ap=wa_ap, cch=cch, wdt=wdt):
                    return e.dma_start(out=wa_ap.rearrange("p (c n) -> p c n", c=8),
                                       in_=w1r[:, :, cch * 128:cch * 128 + wdt])
                B.dma("qpool", emwa, r=["fw1"], w=wa_tags)
                def emwg(e, wg_ap=wg_ap, cch=cch, wdt=wdt):
                    return e.dma_start(out=wg_ap.rearrange("p (c n) -> p c n", c=8),
                                       in_=w1r[:, :, HID + cch * 128:HID + cch * 128 + wdt])
                B.dma("qpool", emwg, r=["fw1"], w=wg_tags)
                for nch in range(4):
                    for kc in range(NCH):
                        def ema(e, nch=nch, kc=kc, wdt=wdt, wa0=wa0):
                            a, _ = ws(wa0 + kc * wdt, wdt)
                            return e.matmul(PB[nch % 2][0:wdt, :], a,
                                            XNc(kc)[:, nch * 512:(nch + 1) * 512],
                                            start=(kc == 0), stop=(kc == NCH - 1))
                        B.op("pe", ema, r=wa_tags + [f"XN:{kc}:{nch}"], w=[pbt(nch % 2)])
                        def emg(e, nch=nch, kc=kc, wdt=wdt, wg0=wg0):
                            a, _ = ws(wg0 + kc * wdt, wdt)
                            return e.matmul(PB[2 + nch % 2][0:wdt, :], a,
                                            XNc(kc)[:, nch * 512:(nch + 1) * 512],
                                            start=(kc == 0), stop=(kc == NCH - 1))
                        B.op("pe", emg, r=wg_tags + [f"XN:{kc}:{nch}"],
                             w=[pbt(2 + nch % 2)])
                    def eg1(e, nch=nch, wdt=wdt, cch=cch):
                        bcg = OFF_B1 + 44 * l + 22 + cch
                        return e.activation(out=SCb[0:wdt, nch * 512:(nch + 1) * 512],
                                            in_=PB[2 + nch % 2][0:wdt, :], func=AF.Gelu,
                                            bias=BCOL[0:wdt, bcg:bcg + 1])
                    B.op("act", eg1, r=[pbt(2 + nch % 2), "BCOL"], w=["SC"])
                    def eg2(e, nch=nch, wdt=wdt, cch=cch, gi=gi):
                        bca = OFF_B1 + 44 * l + cch
                        a, _ = sh(gi * T + nch * 512, 512, rows=(0, wdt))
                        return e.scalar_tensor_tensor(
                            out=a, in0=PB[nch % 2][0:wdt, :],
                            scalar=BCOL[0:wdt, bca:bca + 1],
                            in1=SCb[0:wdt, nch * 512:(nch + 1) * 512],
                            op0=ALU.add, op1=ALU.mult)
                    B.op("dve", eg2, r=[pbt(nch % 2), "SC", "BCOL"],
                         w=sh(gi * T, 2048)[1])
            last_group = (g[0] == 16)
            for m in range(NCH):
                for gi, cch in enumerate(g):
                    wdt = min(128, HID - cch * 128)
                    sl_ap, sl_tags = ws(7 * 1024 + (gi % 2) * 128, 128, rows=(0, wdt))
                    def emw(e, m=m, cch=cch, wdt=wdt, sl_ap=sl_ap):
                        return e.dma_start(out=sl_ap,
                                           in_=fw2[l][cch * 128:cch * 128 + wdt,
                                                      m * 128:(m + 1) * 128])
                    B.dma("qpool", emw, r=["fw2"], w=sl_tags)
                    for sp in range(4):
                        def em(e, m=m, sp=sp, gi=gi, wdt=wdt, sl_ap=sl_ap, n=len(g)):
                            a, _ = sh(gi * T + sp * 512, 512, rows=(0, wdt))
                            return e.matmul(PB[2 + sp][:, :], sl_ap, a,
                                            start=(gi == 0), stop=(gi == n - 1))
                        B.op("pe", em, r=sl_tags + sh(gi * T, 2048)[1], w=[pbt(2 + sp)])
                for sp in range(4):
                    if last_group:
                        def ea(e, m=m, sp=sp):
                            return e.scalar_tensor_tensor(
                                out=XFc(m)[:, sp * 512:(sp + 1) * 512],
                                in0=PB[2 + sp][:, :],
                                scalar=BCOL[:, OFF_B2 + 8 * l + m:OFF_B2 + 8 * l + m + 1],
                                in1=XFc(m)[:, sp * 512:(sp + 1) * 512],
                                op0=ALU.add, op1=ALU.add)
                        B.op("dve", ea, r=[pbt(2 + sp), f"XF:{m}", "BCOL"], w=[f"XF:{m}"])
                    else:
                        def ea(e, m=m, sp=sp):
                            return e.tensor_tensor(
                                out=XFc(m)[:, sp * 512:(sp + 1) * 512],
                                in0=XFc(m)[:, sp * 512:(sp + 1) * 512],
                                in1=PB[2 + sp][:, :], op=ALU.add)
                        B.op("dve", ea, r=[pbt(2 + sp), f"XF:{m}"], w=[f"XF:{m}"])

    def ffn_call(src_x, src_fmt, dst_x, dst_fmt, final):
        for bi in range(4):
            def emr(e, bi=bi):
                return e.dma_start(out=ROTB[:, bi * T:(bi + 1) * T], in_=rot_t[bi])
            B.dma("qsp", emr, r=["rot"], w=["XL:0" if bi < 2 else "XL:1"])
        for l in range(2):
            ffn_layer(l)
        if not final:
            stats_init()
        for c in range(NCH):
            def eml(e, c=c):
                return e.dma_start(out=XL[:, (c % 2) * T:(c % 2) * T + T],
                                   in_=src_x[c * 128:(c + 1) * 128, :])
            B.dma("qsp", eml, r=[src_fmt.format(c)], w=[f"XL:{c % 2}"])
            def emr2(e, c=c):
                return e.tensor_tensor(out=XFc(c), in0=XFc(c),
                                       in1=XL[:, (c % 2) * T:(c % 2) * T + T], op=ALU.add)
            B.op("dve", emr2, r=[f"XF:{c}", f"XL:{c % 2}"], w=[f"XF:{c}"])
            def emw(e, c=c):
                return e.dma_start(out=dst_x[c * 128:(c + 1) * 128, :], in_=XFc(c))
            B.dma("qsp", emw, r=[f"XF:{c}"], w=[dst_fmt.format(c)])
            if not final:
                stats_chunk(lambda c=c: XFc(c), [f"XF:{c}"])
        if not final:
            stats_rows()

    ffn_call(xs1, "xs1:{}", xs2, "xs2:{}", False)
    norm_to_XN()

    def attn_b():
        Q0, K0, V0, ET0, OT0 = 0, 4096, 8192, 11264, 12288
        EPS = 1e-6
        CEXP = 70.0
        wq = bw_qkv.rearrange("(c p) n -> p c n", p=128)
        v_tags = sh(V0, 2080)[1]
        if debug:
            for c in range(NCH):
                def emt(e, c=c):
                    return e.dma_start(out=xt2p[c * 128:(c + 1) * 128, :], in_=XFc(c))
                B.dma("qsp", emt, r=[f"XF:{c}"], w=["xt2p"])
        for grp in range(4):
            if debug and grp == 2:
                for c in range(NCH):
                    def emt(e, c=c):
                        return e.dma_start(out=xt3m[c * 128:(c + 1) * 128, :], in_=XFc(c))
                    B.dma("qsp", emt, r=[f"XF:{c}"], w=["xt3m"])
            ot_tags = [sh(OT0, 2048)[1], sh(OT0 + 2048, 2048)[1]]
            for gh in range(2):
                hp = grp * 2 + gh
                slot_tags = []
                for si, base in enumerate((hp * 128, 1024 + hp * 128, 2048 + hp * 128)):
                    s_ap, s_tags = ws(si * 1024, 1024)
                    slot_tags.append(s_tags)
                    def emw(e, s_ap=s_ap, base=base):
                        return e.dma_start(out=s_ap.rearrange("p (c n) -> p c n", c=8),
                                           in_=wq[:, :, base:base + 128])
                    B.dma("qpool", emw, r=["bw_qkv"], w=s_tags)
                for which in range(2):
                    for nch in range(4):
                        for kc in range(NCH):
                            def em(e, which=which, nch=nch, kc=kc):
                                a, _ = ws(which * 1024 + kc * 128, 128)
                                return e.matmul(PB[nch][:, :], a,
                                                XNc(kc)[:, nch * 512:(nch + 1) * 512],
                                                start=(kc == 0), stop=(kc == NCH - 1))
                            B.op("pe", em, r=slot_tags[which] + [f"XN:{kc}:{nch}"],
                                 w=[pbt(nch)])
                    for hh in range(2):
                        dst0 = (Q0, K0)[which] + hh * 2048
                        dtags = sh(dst0, 2048)[1]
                        if which == 0:
                            bias_col = OFF_BQKV + hp
                            scl = 1.0
                        else:
                            bias_col = OFF_MBK + hp
                            scl = -2.0
                        for nch in range(4):
                            def ee(e, hh=hh, nch=nch, dst0=dst0, bias_col=bias_col, scl=scl):
                                return e.activation(
                                    out=SH[0:64, dst0 + nch * 512:dst0 + (nch + 1) * 512],
                                    in_=PB[nch][hh * 64:hh * 64 + 64, :],
                                    func=AF.Identity, scale=scl,
                                    bias=BCOL[hh * 64:hh * 64 + 64, bias_col:bias_col + 1])
                            B.op("act", ee, r=[pbt(nch), "BCOL"], w=dtags)
                for hh in range(2):
                    qd = Q0 + hh * 2048
                    kd = K0 + hh * 2048
                    qtags = sh(qd, 2048)[1]
                    ktags = sh(kd, 2048)[1]
                    B.op("dve", lambda e, qd=qd: e.memset(SH[64:68, qd:qd + T], 1.0),
                         w=qtags)
                    B.op("dve", lambda e, kd=kd: e.memset(SH[64:68, kd:kd + T], 1.0),
                         w=ktags)
                    B.op("act", lambda e, qd=qd: e.activation(out=SCb[0:64, 0:T],
                                                              in_=SH[0:64, qd:qd + T],
                                                              func=AF.Square),
                         r=qtags, w=["SC", "SC:0", "SC:1"])
                    for nch in range(4):
                        def em(e, nch=nch):
                            return e.matmul(PBS[0:1, 0:512], ONEB[0:64, 0:1],
                                            SCb[0:64, nch * 512:(nch + 1) * 512],
                                            start=True, stop=True)
                        B.op("pe", em, r=["SC", "SC:0", "SC:1", "ONE"], w=["PBSa"])
                        def ee(e, hh=hh, nch=nch):
                            return e.activation(out=row(hh, nch * 512, 512),
                                                in_=PBS[0:1, 0:512], func=AF.Identity,
                                                scale=1.0, bias=64.0 + EPS)
                        B.op("act", ee, r=["PBSa", "CONSTS"], w=[f"R{hh}"])
                    p0 = hh * 32
                    def eqh(e, hh=hh, p0=p0):
                        return e.activation(out=ROWSB[p0:p0 + 1, 0:T], in_=row(hh),
                                            func=AF.Copy)
                    B.op("act", eqh, r=[f"R{hh}"], w=["RBQ"])
                    def eql(e, hh=hh, p0=p0):
                        return e.tensor_tensor(out=ROWSB[p0:p0 + 1, T:2 * T], in0=row(hh),
                                               in1=ROWSB[p0:p0 + 1, 0:T], op=ALU.subtract)
                    B.op("dve", eql, r=[f"R{hh}", "RBQ"], w=["RBQ2"])
                    def ew1(e, hh=hh, p0=p0):
                        return e.dma_start(out=auxs[4 * hh + 0], in_=ROWSB[p0:p0 + 1, 0:T])
                    B.dma("qsp", ew1, r=["RBQ"], w=[f"aux{4 * hh}"])
                    def ew2(e, hh=hh, p0=p0):
                        return e.dma_start(out=auxs[4 * hh + 1],
                                           in_=ROWSB[p0:p0 + 1, T:2 * T])
                    B.dma("qsp", ew2, r=["RBQ2"], w=[f"aux{4 * hh + 1}"])
                    def edq(e, qd=qd, hh=hh):
                        return e.dma_start(out=SH[66:67, qd:qd + T], in_=auxs[4 * hh + 0])
                    B.dma("qsp", edq, r=[f"aux{4 * hh}"], w=qtags)
                    def edq2(e, qd=qd, hh=hh):
                        return e.dma_start(out=SH[67:68, qd:qd + T], in_=auxs[4 * hh + 1])
                    B.dma("qsp", edq2, r=[f"aux{4 * hh + 1}"], w=qtags)
                    B.op("act", lambda e, kd=kd: e.activation(out=SCb[0:64, 0:T],
                                                              in_=SH[0:64, kd:kd + T],
                                                              func=AF.Square),
                         r=ktags, w=["SC", "SC:0", "SC:1"])
                    for nch in range(4):
                        def em(e, nch=nch):
                            return e.matmul(PBS[0:1, 512:1024], ONEB[0:64, 0:1],
                                            SCb[0:64, nch * 512:(nch + 1) * 512],
                                            start=True, stop=True)
                        B.op("pe", em, r=["SC", "SC:0", "SC:1", "ONE"], w=["PBSb"])
                        def ee(e, nch=nch):
                            return e.activation(out=row(2, nch * 512, 512),
                                                in_=PBS[0:1, 512:1024], func=AF.Identity,
                                                scale=0.25, bias=-64.0)
                        B.op("act", ee, r=["PBSb", "CONSTS"], w=["R2", "R2b"])
                    def ekh(e, kd=kd):
                        return e.activation(out=SH[64:65, kd:kd + T], in_=row(2),
                                            func=AF.Copy)
                    B.op("act", ekh, r=["R2"], w=ktags)
                    def ekl(e, kd=kd):
                        return e.tensor_tensor(out=ROWSB[64:65, 1024:1024 + T], in0=row(2),
                                               in1=SH[64:65, kd:kd + T], op=ALU.subtract)
                    B.op("dve", ekl, r=["R2"] + ktags, w=["RBK"])
                    def ew3(e, hh=hh):
                        return e.dma_start(out=auxs[4 * hh + 2],
                                           in_=ROWSB[64:65, 1024:1024 + T])
                    B.dma("qsp", ew3, r=["RBK"], w=[f"aux{4 * hh + 2}"])
                    def edk(e, kd=kd, hh=hh):
                        return e.dma_start(out=SH[65:66, kd:kd + T], in_=auxs[4 * hh + 2])
                    B.dma("qsp", edk, r=[f"aux{4 * hh + 2}"], w=ktags)
                if hp == 0 and debug:
                    def edK(e):
                        return e.dma_start(out=dumpK, in_=SH[0:68, K0:K0 + T])
                    B.dma("qsp", edK, r=sh(K0, 2048)[1], w=["dumpK"])
                    def edQ(e):
                        return e.dma_start(out=dumpQ, in_=SH[0:68, Q0:Q0 + T])
                    B.dma("qsp", edQ, r=sh(Q0, 2048)[1], w=["dumpQ"])
                    for r_ in range(4):
                        def ed1(e, r_=r_):
                            return e.dma_start(out=auxdump[r_],
                                               in_=SH[64 + r_:65 + r_, K0:K0 + T])
                        B.dma("qsp", ed1, r=sh(K0, 2048)[1], w=["auxdump"])
                        def ed2(e, r_=r_):
                            return e.dma_start(out=auxdump[4 + r_],
                                               in_=SH[64 + r_:65 + r_, Q0:Q0 + T])
                        B.dma("qsp", ed2, r=sh(Q0, 2048)[1], w=["auxdump"])
                        def ed3(e, r_=r_):
                            return e.dma_start(out=auxdump[8 + r_],
                                               in_=SH[r_:r_ + 1, K0:K0 + T])
                        B.dma("qsp", ed3, r=sh(K0, 2048)[1], w=["auxdump"])
                        def ed4(e, r_=r_):
                            return e.dma_start(out=auxdump[12 + r_],
                                               in_=SH[r_:r_ + 1, Q0:Q0 + T])
                        B.dma("qsp", ed4, r=sh(Q0, 2048)[1], w=["auxdump"])
                if hp == 0:
                    for bi in range(32):
                        def emo(e, bi=bi):
                            return e.memset(SH[:, V0 + bi * 65 + 64:V0 + bi * 65 + 65], 1.0)
                        B.op("dve", emo, w=v_tags)
                for jt in range(16):
                    for kc in range(NCH):
                        def em(e, jt=jt, kc=kc):
                            a, _ = ws(2 * 1024 + kc * 128, 128)
                            return e.matmul(PB[5][:, 0:128],
                                            XNc(kc)[:, jt * 128:(jt + 1) * 128],
                                            a, start=(kc == 0), stop=(kc == NCH - 1))
                        B.op("pe", em, r=slot_tags[2] + [f"XN:{kc}:{jt // 4}"],
                             w=[pbt(5)])
                    for hh in range(2):
                        def emc(e, jt=jt, hh=hh):
                            off = V0 + (hh * 16 + jt) * 65
                            return e.activation(out=SH[:, off:off + 64],
                                                in_=PB[5][:, hh * 64:hh * 64 + 64],
                                                func=AF.Copy)
                        B.op("act", emc, r=[pbt(5)], w=v_tags)
                for jt in range(16):
                    def emb2(e, hp=hp):
                        return e.matmul(PB[5][:, 0:128], ONEBR[64:65, 0:128],
                                        ROWSB[64:65, hp * 128:(hp + 1) * 128],
                                        start=True, stop=True)
                    B.op("pe", emb2, r=["ONE", "RBV"], w=[pbt(5)])
                    for hh in range(2):
                        def ea(e, jt=jt, hh=hh):
                            off = V0 + (hh * 16 + jt) * 65
                            return e.tensor_tensor(out=SH[:, off:off + 64],
                                                   in0=PB[5][:, hh * 64:hh * 64 + 64],
                                                   in1=SH[:, off:off + 64], op=ALU.add)
                        B.op("dve", ea, r=[pbt(5)] + v_tags, w=v_tags)
                tiles = []
                for hh in range(2):
                    for ic in range(4):
                        for jt in range(4 * ic + 4):
                            tiles.append((hh, ic, jt))

                def oacc(g):
                    return (PB[4], pbt(4)) if g % 2 == 0 else (PBS, "PBSa")

                def epilogue(hh, ic, g):
                    acc, acct = oacc(g)
                    r0 = (g % 2) * 512
                    rt = ["R2"] if g % 2 == 0 else ["R2b"]
                    def emf(e, acc=acc, r0=r0):
                        return e.tensor_scalar(out=row(2, r0, 512),
                                               in0=acc[64:65, 0:512],
                                               scalar1=1e-35, scalar2=2.0 ** 64,
                                               op0=ALU.max, op1=ALU.mult)
                    B.op("dve", emf, r=[acct], w=rt)
                    if debug and hp == 0 and hh == 0 and ic == 0:
                        def edS1(e):
                            return e.dma_start(out=dumpS[1:2, :], in_=row(2, 0, 512))
                        B.dma("qsp", edS1, r=rt, w=["dumpS2"])
                    def er1(e, r0=r0):
                        return e.activation(out=row(2, r0, 512), in_=row(2, r0, 512),
                                            func=AF.Ln)
                    B.op("act", er1, r=rt, w=rt)
                    def er2(e, r0=r0):
                        return e.activation(out=row(2, r0, 512), in_=row(2, r0, 512),
                                            func=AF.Exp, scale=-1.0)
                    B.op("act", er2, r=rt, w=rt)
                    if debug and hp == 0 and hh == 0 and ic == 0:
                        def edS0(e):
                            return e.dma_start(out=dumpS[0:1, :], in_=row(2, 0, 512))
                        B.dma("qsp", edS0, r=rt, w=["dumpS"])
                    if g % 2 == 0:
                        bt_ap, btt = PB[5][:, 0:512], pbt(5)
                    else:
                        bt_ap, btt = PBS[:, 512:1024], "PBSb"
                    def emb3(e, r0=r0, bt_ap=bt_ap):
                        return e.matmul(bt_ap, onesrow_f(2), row(2, r0, 512),
                                        start=True, stop=True)
                    B.op("pe", emb3, r=rt + ["ONE"], w=[btt])
                    sbo = (g % 2) * 512
                    sbt = [f"SB:{g % 2}"]
                    def ecp(e, bt_ap=bt_ap, sbo=sbo):
                        return e.activation(out=SB_[:, sbo:sbo + 512], in_=bt_ap,
                                            func=AF.Identity, scale=2.0 ** 64)
                    B.op("act", ecp, r=[btt], w=sbt)
                    def emo(e, hh=hh, ic=ic, acc=acc, sbo=sbo, gh=gh):
                        off = OT0 + gh * T + ic * 512
                        pr = hh * 64
                        return e.tensor_tensor(out=SH[pr:pr + 64, off:off + 512],
                                               in0=acc[0:64, 0:512],
                                               in1=SB_[0:64, sbo:sbo + 512],
                                               op=ALU.mult)
                    B.op("dve", emo, r=[acct] + sbt, w=ot_tags[gh])

                ET3 = 10752

                def flush5(t):
                    idx, hh, ic, jt = t
                    s = idx % 4
                    ep = idx % 3
                    sa_ap = SA[:, s * 512:(s + 1) * 512]
                    sat = [f"SA:{s}"]
                    ett = sh(ET3 + ep * 512, 512)[1]
                    def e5(e, sa_ap=sa_ap, ep=ep):
                        return e.activation(
                            out=SH[:, ET3 + ep * 512:ET3 + (ep + 1) * 512],
                            in_=sa_ap, func=AF.Exp, scale=0.25, bias=-CEXP)
                    B.op("act", e5, r=sat + ["CONSTS"], w=ett)
                    if debug and hp == 0 and idx == 0:
                        def edE(e):
                            return e.dma_start(out=dumpE, in_=SH[:, ET3:ET3 + 512])
                        B.dma("qsp", edE, r=ett, w=["dumpE"])

                def flush6(t):
                    idx, hh, ic, jt = t
                    g = hh * 4 + ic
                    n = 4 * ic + 4
                    ep = idx % 3
                    ett = sh(ET3 + ep * 512, 512)[1]
                    acc, acct = oacc(g)
                    def e6(e, jt=jt, ep=ep, hh=hh, n=n, acc=acc):
                        off = V0 + (hh * 16 + jt) * 65
                        return e.matmul(acc[0:65, 0:512], SH[:, off:off + 65],
                                        SH[:, ET3 + ep * 512:ET3 + (ep + 1) * 512],
                                        start=(jt == 0), stop=(jt == n - 1))
                    B.op("pe", e6, r=v_tags + ett, w=[acct])
                    if jt == n - 1:
                        epilogue(hh, ic, g)

                pend5 = None
                pend6 = None
                for idx, (hh, ic, jt) in enumerate(tiles):
                    qd = Q0 + hh * 2048
                    kd = K0 + hh * 2048
                    qtags = sh(qd, 2048)[1]
                    ktags = sh(kd, 2048)[1]
                    pbp = idx % 2
                    s = idx % 4
                    sa_ap = SA[:, s * 512:(s + 1) * 512]
                    sc_ap = SC[:, s * 512:(s + 1) * 512]
                    sat = [f"SA:{s}"]
                    sct = [f"SC:{s}"]
                    def emd(e, jt=jt, ic=ic, qd=qd, kd=kd, pbp=pbp):
                        return e.matmul(PB[pbp][:, :],
                                        SH[0:68, kd + jt * 128:kd + (jt + 1) * 128],
                                        SH[0:68, qd + ic * 512:qd + (ic + 1) * 512],
                                        start=True, stop=True)
                    B.op("pe", emd, r=ktags + qtags, w=[pbt(pbp)])
                    def emn(e, jt=jt, ic=ic, qd=qd, kd=kd, pbp=pbp):
                        return e.matmul(PB[2 + pbp][:, :],
                                        SH[0:64, kd + jt * 128:kd + (jt + 1) * 128],
                                        SH[0:64, qd + ic * 512:qd + (ic + 1) * 512],
                                        start=True, stop=True)
                    B.op("pe", emn, r=ktags + qtags, w=[pbt(2 + pbp)])
                    def e1b(e, pbp=pbp, sa_ap=sa_ap):
                        return e.activation(out=sa_ap, in_=PB[pbp][:, :], func=AF.Ln)
                    B.op("act", e1b, r=[pbt(pbp)], w=sat)
                    def e1c(e, sa_ap=sa_ap):
                        return e.activation(out=sa_ap, in_=sa_ap, func=AF.Exp,
                                            scale=-1.0)
                    B.op("act", e1c, r=sat, w=sat)
                    def e2(e, pbp=pbp, sc_ap=sc_ap):
                        return e.activation(out=sc_ap, in_=PB[2 + pbp][:, :],
                                            func=AF.Square)
                    B.op("act", e2, r=[pbt(2 + pbp)], w=sct)
                    def e3(e, sa_ap=sa_ap, sc_ap=sc_ap):
                        return e.tensor_tensor(out=sa_ap, in0=sc_ap, in1=sa_ap,
                                               op=ALU.mult)
                    B.op("dve", e3, r=sat + sct, w=sat)
                    if jt >= 4 * ic:
                        kk = jt - 4 * ic
                        def e4(e, kk=kk, sa_ap=sa_ap):
                            return e.tensor_tensor(
                                out=sa_ap, in0=sa_ap,
                                in1=MSK[:, 256 + kk * 512:256 + (kk + 1) * 512],
                                op=ALU.add)
                        B.op("dve", e4, r=sat + ["MSK"], w=sat)
                    if pend6 is not None:
                        flush6(pend6)
                    if pend5 is not None:
                        flush5(pend5)
                    pend6, pend5 = pend5, (idx, hh, ic, jt)
                flush6(pend6)
                flush5(pend5)
                flush6(pend5)
            for m in range(NCH):
                for kci in range(2):
                    kc = 2 * grp + kci
                    sl_ap, sl_tags = ws(3 * 1024 + kci * 128, 128)
                    def emw(e, m=m, kc=kc, sl_ap=sl_ap):
                        return e.dma_start(
                            out=sl_ap,
                            in_=bw_out[kc * 128:(kc + 1) * 128, m * 128:(m + 1) * 128])
                    B.dma("qpool", emw, r=["bw_out"], w=sl_tags)
                    for sp in range(4):
                        def em(e, m=m, sp=sp, kci=kci, sl_ap=sl_ap):
                            return e.matmul(
                                PB[sp][:, :], sl_ap,
                                SH[:, OT0 + kci * T + sp * 512:OT0 + kci * T + (sp + 1) * 512],
                                start=(kci == 0), stop=(kci == 1))
                        B.op("pe", em, r=sl_tags + ot_tags[kci], w=[pbt(sp)])
                for sp in range(4):
                    if grp == 3:
                        def ea(e, m=m, sp=sp):
                            return e.scalar_tensor_tensor(
                                out=XFc(m)[:, sp * 512:(sp + 1) * 512],
                                in0=PB[sp][:, :],
                                scalar=BCOL[:, OFF_BOUT + m:OFF_BOUT + m + 1],
                                in1=XFc(m)[:, sp * 512:(sp + 1) * 512],
                                op0=ALU.add, op1=ALU.add)
                        B.op("dve", ea, r=[pbt(sp), f"XF:{m}", "BCOL"], w=[f"XF:{m}"])
                    else:
                        def ea(e, m=m, sp=sp):
                            return e.tensor_tensor(
                                out=XFc(m)[:, sp * 512:(sp + 1) * 512],
                                in0=XFc(m)[:, sp * 512:(sp + 1) * 512],
                                in1=PB[sp][:, :], op=ALU.add)
                        B.op("dve", ea, r=[pbt(sp), f"XF:{m}"], w=[f"XF:{m}"])
        stats_init()
        for c in range(NCH):
            def emw(e, c=c):
                return e.dma_start(out=xs1[c * 128:(c + 1) * 128, :], in_=XFc(c))
            B.dma("qsp", emw, r=[f"XF:{c}"], w=[f"xs1:{c}"])
            stats_chunk(lambda c=c: XFc(c), [f"XF:{c}"])
        stats_rows()
        norm_to_XF()

    attn_b()
    ffn_call(xs1, "xs1:{}", yT, "yT:{}", True)

    extra = (["xt1", "xt2p", "xt3m", "auxdump", "dumpK", "dumpQ", "dumpE", "dumpO", "dumpS"] + [f"xs1:{c}" for c in range(NCH)] + [f"xs2:{c}" for c in range(NCH)]) if debug else []
    B.op("sp", lambda e: e.nop(), r=[f"yT:{c}" for c in range(NCH)] + extra)
    import os
    if os.environ.get("KSTRICT"):
        B.strict_ranges.append((0, len(B.ops)))
    B.finalize(nc)
    return nc


_CACHE = {}


def kernel(**inputs):
    x = np.asarray(inputs["L0_attn_a_x"], dtype=np.float32)
    nb = x.shape[0]
    rot, mska, mskb, mskc = _host_consts()

    def f32(a):
        return np.ascontiguousarray(np.asarray(a, dtype=np.float32))

    wqkv = f32(inputs["L0_ffn_wqkv"])
    perm = np.concatenate([(h * 64 + (np.arange(64) + 32) % 64) for h in range(8)])
    fwperm = np.ascontiguousarray(
        np.concatenate([wqkv[:, :, perm], wqkv[:, :, 512 + perm]], axis=2))
    b1 = f32(inputs["L0_ffn_b1"])

    common = {
        "wqT": f32(np.asarray(inputs["L0_attn_a_Wq"]).T),
        "wkvT": f32(np.asarray(inputs["L0_attn_a_Wkv_down"]).T),
        "wkupT": f32(np.asarray(inputs["L0_attn_a_Wk_up"]).T),
        "wvupT": f32(np.asarray(inputs["L0_attn_a_Wv_up"]).T),
        "woT": f32(np.asarray(inputs["L0_attn_a_Wo"]).T),
        "bw_qkv": f32(inputs["L0_attn_b_w_qkv"]),
        "bb_qkv": f32(inputs["L0_attn_b_b_qkv"]).reshape(1, -1),
        "bw_out": f32(inputs["L0_attn_b_w_out"]),
        "bb_out": f32(inputs["L0_attn_b_b_out"]).reshape(1, -1),
        "fwqkv": wqkv,
        "fwperm": fwperm,
        "fqs": f32(inputs["L0_ffn_q_scale"]).reshape(2, 1, 64),
        "fks": f32(inputs["L0_ffn_k_scale"]).reshape(2, 1, 64),
        "fwo": f32(inputs["L0_ffn_wo"]),
        "flng": f32(inputs["L0_ffn_ln_g"]).reshape(2, 1, D),
        "flnb": f32(inputs["L0_ffn_ln_b"]).reshape(2, 1, D),
        "fw1": f32(inputs["L0_ffn_w1"]),
        "fb1a": np.ascontiguousarray(b1[:, 0:2730]).reshape(2, 1, 2730),
        "fb1g": np.ascontiguousarray(b1[:, 2730:5460]).reshape(2, 1, 2730),
        "fw2": f32(inputs["L0_ffn_w2"]),
        "fb2": f32(inputs["L0_ffn_b2"]).reshape(2, 1, D),
        "rot": rot, "mska": mska, "mskb": mskb, "mskc": mskc,
        "identb": np.eye(128, dtype=BFNP),
        "onefc": _onefc(), "onebc": _onefc().astype(BFNP),
        "onefr": _onefr(), "onebr": _onebr(), "blk64": _blk64(),
    }

    debug = bool(inputs.pop("_debug", False))
    key = ("nc", debug)
    if key not in _CACHE:
        _CACHE[key] = build_program(debug)
    nc = _CACHE[key]

    in_maps = []
    for b in range(nb):
        m = dict(common)
        m["xT"] = np.ascontiguousarray(x[b].T)
        in_maps.append(m)

    res = run_bass_kernel_spmd(nc, in_maps, core_ids=list(range(nb)))
    if debug:
        taps = {nm: np.stack([np.ascontiguousarray(r[nm].T) for r in res.results])
                for nm in ("xt1", "xs2", "xs1", "xt2p", "xt3m")}
        for nm2 in ("auxdump", "dumpK", "dumpQ", "dumpE", "dumpO", "dumpS"):
            taps[nm2] = np.stack([np.asarray(r[nm2]) for r in res.results])
        taps["out"] = np.stack([np.ascontiguousarray(r["yT"].T) for r in res.results])
        return taps
    return np.stack([np.ascontiguousarray(r["yT"].T) for r in res.results])
